# version 12 (final)
# speedup vs baseline: 1.1670x; 1.0120x over previous
import base64
import math
import zlib
from contextlib import ExitStack

import numpy as np
import ml_dtypes

import concourse.bass as bass
import concourse.mybir as mybir
from concourse.bass_utils import run_bass_kernel_spmd

F32 = mybir.dt.float32
BF16 = mybir.dt.bfloat16
I32 = mybir.dt.int32
AF = mybir.ActivationFunctionType
ALU = mybir.AluOpType
AX = mybir.AxisListType
BFD = ml_dtypes.bfloat16

D = 1024
S = 2048
NCH = 8
HD = 32
NPAIR = 16
NDUO = 8
QT = 512
NQT = S // QT
NKC = S // 128
INNER = 4096
NIC = INNER // 128
LAMBDA_INIT = 0.8 - 0.6 * math.exp(-0.3 * 12)
LI_HI = float(np.float32(LAMBDA_INIT))
LI_LO = float(np.float32(LAMBDA_INIT - LI_HI))
OMLI = 1.0 - LAMBDA_INIT
OMLI_HI = float(np.float32(OMLI))
OMLI_LO = float(np.float32(OMLI - OMLI_HI))
SCAL = 1.0 / math.sqrt(32.0)
SCAL_HI = float(np.float32(SCAL))
SCAL_LO = float(np.float32(SCAL - SCAL_HI))
C127 = 1.0 / 127.0
C127_HI = float(np.float32(C127))
C127_LO = float(np.float32(C127 - C127_HI))
EPS = 1e-5
THETA = 1e-4


_ROPE_B64 = "eNqku3c81XH/Pm5ESEYpW6iQqKyyzutpz4a9yd7j2BzOebdpJ6m0p1JKKVrnvJ6lqSGV0NaQFikpKvmd+/4+Pnff+/e97+8/3z/84cHD8VrP67qez+sSEVkEIv8PX3beMjDmiw/99YXBCS4eEHsnAHtcfpDeVRy4yc6G3othsPrjAToq8xstHWHwyjd5UNjAhd9iK6jLPQbMFB/SAedk7LnOQHQFj+j2tRCpcin0SEiGvX/mwxH2EDne6YEz545BnT8MrDLlU525ecD+FIaGzgzosblobz8BnixmsDL2FlX5pkmmrWdwnrMXiLImwqB4PT/SwB92WTNw6xeDLU2FaOydDEBnw3Y/Hszy1xd+thHcmcRAQkgk5vAN8c7ap/SnvTHoFO0k9wslMVpzKTnb0kJXDDLw/L413vkUDB93/CTnJnDByLaMbGeHYfNeHlq0F6O4VD4kykyATU/EIJX74/xXx2hUj2VQfqU6en9j8PELLdIV5wyrLAvQiCmCs3dyofvjfto9mgeHzGZiwL1slGbxQNB4kDzw3k1FLzLY0XeVHP5YDMafGUjLSYCRPiMov7zDdjiwEAQjdqBgwQMTHy6+NLkosNDRhYZxl0jXp58sLQllXH03AGE3QHLvUar2nsHaHWHQHqYIiTdy4YJMPnwfVqJ3pjcI+IuKodpWGurHvRIMD2uj+dOjghfrF+Agu4vmpkbgxDZDVHnFwKLM5SzrsW/JRj8JfJOQAw8hCYMf5OCe8iJU72LDu6QuKpApRp+2jfQ9PxtDakpY19Pt6bGJhaD/mMGMHncMED9IGjVmYQw781/3ZeC/3Jfxg/JgWPb3vjyrycZHtfZkRmgU2E1NhyyJMhvt1/bwOF54rs3y0IoMHokxgr67U/H65Gxs80+AObJvaY26FeZfZ/Cd5wb69kQAa1JKEinVToYWySB4scwTw/ULIGP5YoFhuhcE/SyAOn8P3Js6HyeLMHC20Aw3X8qGs3wG9q3QhdjTdQLRFgbLz8ZBxvxEqB3/lDbezsNnMdNwx1g+fWZeSoxs4jEwZwJezPKC97vj4aMPoTInGQzt54FbbSzNrthKpJOioH2ULKl9XACTAnOhmluEB9Z70FiTORjrVYBP9m+n6/+0EOO+76wWRxdMP8BAQ94gaWqrp4lfAyHpVhi84zhgWGSf7Y5r2ThFKhbVxMzAXC4X9zcz2L7znE3ng5eC8lIOHvmZCgoVDK5mv6XHXw1QCV0Wzn8ZCEbTcrH3axHGq2fB+LOPaEdjIdrLBlDRuGLs099Lgmeo4sZFYlDy+YUgsZSBkgdc1Nyngoq2H2hyjQ/MfrWG1m26SGr82WgkEUnStNlQFW4PfMt0PH5bGeS/peCfQj0IcvRBmSIGjSYa4qI0M+zw48DqGfNgmcw00HX1gz256UDDnOG4uSp6erAh9bU/hml7wbWLXBw6n4as87FQlSjcfwsOCbnngc5yoYLKJFdAdS4E4xzMLUrFSS8ZeHOrhKXxVY28zDTHqVI5QI+545rDfoJJV8QF4/wZjM5kQ8T2CIw3zsaNC+3hf+pN7H+pN4cn8ynP62+9cXi9AGqOp2FLoyysH2cDv+fI4thRwjVbJaJKzjpyyzkaK+qS8UfWWdr5MAvgqiaJDFPEzomhsNacQf79B/ysnQV4/IUBzFMxg4xhBsUaJ5LCr3Eg18mn7BEeNO2igtD7U2jno3kgraILkvtV0FTbhvVNU/j5/oH4bgsPi+LicIgYASZI4fTVDGpujMHVeWGYWeWOFbW+kKVrxdI9qgReGsI3eS4b/jjEo6EkgyPlKSjZwWCMiiP9OW0q/DF+RH7OLCGbVaKgsZ+Ld29XEZG5sXArhCEStyvJarFi2DZ5NX+6qReIC+/urlUuVEKtGJXsGMh31YAC5MLouhjYvyICZzwyx/1X88jh6LGsbbdfkQfH1PhvlzrAym42Kmb+Zhm84OHEi2bgg9lY6V5NXwRMhMU7psKEBB30fhoKA7V/BKK91th1ORQvamhB/vA+evJDFckpmQRqXukYcoUHu+8WwYe8eFQp8cHKflmwLK+gh0x18INqMMqsEoNckWKYpRGOB1Sc4ES5CtZ4e9ADWjKQKucEYYd5eLu/mRCWP8r3O8DyU5IQmJaHT1pigA8BsGBSOmj/SMHL76/a1p+JgoknvPDYCUNMXVeA/IS5ZLCMB29SR6NHCqDTgQ7y6+wQObUI8OyxFUR5vi/ipVwI+iUPiybGQFZ9LhT75ODy29nYdKGWNbE1Bzs5PBx1W3gHduuhs2QxGh39RP6BV3r/F7xacK0Q5SL+4tVNNxWc/kEaRU9fYD2d1UHWGSWDxh1lcmrqPORlSwhmciWxklMt+PiMC8olWXhn0Bo39Rnhk+NrSK0NDzVHV5P4jjb6zI0NnEM3yZH1/rDyUwbs/ewOs1Mmo2VwAvTtSUKx45mw0GUeOkeU0aNvBmh3p/D/0OXB+sB8CPldThXbldCVW0/OrAjma92Wwsz9LpjdXYgWyMXawrl4yiQUol+Oxy8FeeTpzGK4PdUAzo/mYXuoPYhzNKhbjA7WfN9Num2C0WvDHFxk7wjPeQm4u9ABarMUyMRQG9jVv4X+7DeFJZ80qfXNOlL3iQtX330W9KSs51eVJ8G85jKyMCgB21P9wOgmoFR7N422LKPZ4RkwrmQ8GK4Jsm0Vy4cDZ6KxOZ2L+sJacnZiDC7vGGLxijVIxbp88kFKnK5TvkjaWLbwgdiTASijlyc6wqeWmyTu/gFiOjYfskusoOSsCszdz4WE9Cg05S8hmit5eOZICDH+6ggv/xTh6x2WeNPRDkyKHQAPe9LEn55kcogqWCmYgVpUG9kY1E7oci7O4T8QZGZeI3MPXSDvO4xJPZMDb68P2p7pNodlyxi4INy/ny45qF9998JY8Aa/cfnk5C4ehhYwaOzPhY3VjnBksjhk/JkHfSsVSOryUnL3ZgFc0DJBstwXSgIT4NVoK9Q9HkT7tucAb1wZ1baJIdZRUnDHgoszVuVB4GcteKPvjs9ir7H+h+8EOfxnvtP8zBo3D//lO7bO+8mVYCuITrpJu+1Xs5wai/FHqCf+9pmG/cbH6cIl9fxZwQUQZDIRU+ZkwX5BIep7O2PZEINfvxWjx1k1tNXzAvv1XIhVumjbefM2nSXvAEreU2FxeBDu+CMP7/1ngPNaFmrVzYX1HdnY3VFPt4j0koIHxRB+WMihXkUDuTOK7rY7xn+nLgdzWr4TBZYOyj7g4eHBFPz1YTed+zgMlZXHg2azIf5oZ2DXLDmo3lGMng9HoZhpJnwfSMEI63mCCwMMiHXnYKyyJij5WuPlkSJgNhShNT8Lq9U2ky2fPKj4YDll9/IworAYnBe3kvIYDqRoqoNB/SXBoiJJ8FQ8LVjaqYLzCrTo/I/W6CwzGQ2dePjEfDX/5+4qstjaCyPITRLpX4zpgcVAFwpxxUEd/E7+FqS5PKMTjIV1ZasSPMr4QeY15JE90/nE+hsPz1VVUfNkB/io5AWvLbVgg1oxqG40gSmRDNgkp+HHFdPpKWkenkg5QP/YKeA47RRwba0iih+4ANdzcLS2Glb8KoJd2ZvI4bBivPytCKHAh16K5UCb50dBR3gQXdQ5GWfOD8PcjjGQ0p4PnYfd8dMMZbCbFol2d7xRzJANs4S8zdblIHFsLoLqi6nIWcEGtigX5scsoU7dlWTDkBFkRI3GzSUa0CTEr8Rr7VTmtz3y3uTgjtm6MD2yntzgGoPioSKUrPHHwq5oCB5/k8b+LqX/iS+3xvzly3FR/86X+e9i+aXTvKBT0g0DOnbRe8MsMItKxpvrftPJnpp4OWmIeDfLoM67p/TZ20w0eHyBdt0MxmN7eWQqKUL9Km+oCbcCD8UFkBhzmsp169Hs9970kkc/6VcqwPSPJayRfV+JyJUsqqk5F+Mr3LHuVCwdX2cD2UpFsCkqA270a0FNyWmyf2MTrVBKwR8jDCh9ek8vCZKxY44qFv0Wno+YhMBp7jhcdYGLoQd58HBfFsp94ODsmHRSYiAOqsuEGN52l1bN88PhIV96PbuTyNMBekuHDefe/iLvjpkQA/tNNFI8Fe9+fEhPzjIhsgc5EBFshK+KrhGvI5OJJ+8NNXKst71smIxdr9roGA9d8vNSIT71k8T5yyiZNPmI7cg8VfxwygpPb3GDJwUcPJrFxanXl5DGB4lw7eQg3ZcaApoFE6DQ+g7J/W0NriOmwJxi8IWUHGRxDhNJudeCJYejwasyBQc9OfA64yX1UfrKMvWfgt5mklgTHg+fKoxgU9Fl6vKBB3qLPOh1n5c04sJsCO1Sg6SGFqKRPIlERF5lxWpxMNV/mJjWH+ffP6OIKU3yeEbpG6t9Yxg6HzpHWjsC4bsSn957OQ4qqwOhNsQHmkO9oNe6GL+5+4FWn1BDzGJgb4wGPWnJhp+V43BpnDO2p+yne2ktNQqrIblsL9BZyYO4uAwwMMqF/b95sDB7gFzVi8HeA4VYHaoNS76U04lWx1hSLer/1Fs/bf+73sL4BDDUNv6X3vp0sIJOam6jv+uKsLFhCr6duow6Op8V1J3eTp0U/LCXhsH8Hi28cFio1SoyQaCiAF+9kvFNQwRoOhVi2LIcbOHLYEvpC7Lb3Rh/JBvBWszByooy8jxHyNmju2gCrxivLMqHiBJPEHSXslIEdYKpQcE49K0Amk3dEDumYpyiI+xvmolnY4rhJP8HWRtfQkfvAZxk+kqwp7uaBt+Mhe5FiuBt2EUf7SyAY5l5gtLOArzN38YqSQCMW+eLMu/n4NK9N4j4GXsI9eaQ2CwWdZqiC/WphmDs4waaCRuJzPkAdHwnjWuaE8B0fQEYfAsHDWt/5D4+SJuPAP78LY+dxzzR5Hwc/bz+PVl9PxsnVRSClf9kuKLQxAfKYO+5SZhnOx81bAswYF48FL6Ox9VLGRiWccWudwx+r3xCn0t7E413CfBTMh8Ouq0lnmd7ydp5R2hKQDQufcqGgSxZelSvAJ6EKZDoMTl4S96HtkrMxw1rRiH3QwFUOdjS+ScLYck5Q+zyt8DfW9aQSUfUMWO7JLIsttgOL8mA2Hv5OMeZhdKqxWjHiUO2lBemhbwkje+6be/G65OkGl3gvNEkj+7mosHrMlZItw+O9J/jLxGuSanMHuMu5EPYUAE+aGfQ1NYQ/DgB+HA+D769LcJ9BeOo9K+PtsE/WohTSCQ1SJKhPVm5eGC6kKVnqIBrVgjtjcnH5pJ6ymzlYThrDIxiwvAfer1J8RJ5+O0/6/UPNWGwuOyvXt/yahy4RxWgqHAPssZHo7XlLrpYxw8+PG0RbJfMw3/0ZN6ZZMLuaglUd8zEYWFtIo55KCZtjhf68/DcEAMx8ukY8zNccPFwDPJ35qN/7xzUOl7FEruShuRWAHLDp5KWwzPxZZ8H9rSGwdK981lpZeU0ZGseLJIcC4a7IhANOKB2KBG1823RP0QKdR4XCLpPi+LHdy2kDtzR09AHLRoVsLp+iFx4ngYmFRywtclD94ZG2tzqga25PVRkZhrWtM0km9XZcNV2CktihjipF9ciEZiCmrUpuHLNBtoz/gS/0VYP5dKmIpmcB2IlHJw20QwnWNngFJKJb36kwPOGCeSR2m7+qFILEKoc7N75QCA/qwgLht/bmgZJotpZMawYfY5+u5yLNmmW+KRDGWWHgiHnfg5GjfYFrRIhzvgdPL9yHA/GSufDpRk+wP/JoLR2DckxicaNPA4Mv+RCX3QuGEYYkV73fvpeYgspu52LITjMP15XQpyunLINGgwF3ZZslOLGwcPHhVgVpQGCSwyu6hkQ7HgwAfvdc1G2KxEu6tZR/U4Gt9EC/L01FCeXx8CN9TMEB8ZfIHpaurB9zRzW7PFJCNm+INkVgJeb4tE8+hm9+jYHzjwJgRU7k6k3l01GTIowJCIOXbbroEHFYpbTgg3Us/+lYG1sCCqEzkc3QS5sO9NFbe3MYOYJISbdycEQdz9W4KPzgnWikcCbysX/6ffYtP7nfs839grWWuZvvyfqsyfuCnVhXf45E3/t5eKbVhfY/TkG2FJtLPlHwrfeGAbLFGNsx0qJ4IIBNsS3y0PYSi7eNBfBhyI56HgwC3EfkId7R+HTHh4qaWXDu+krSJWMCL1EnfAQCvndoyhYU7qbVgV7QLdTAozU2hGHtwzMf5sNUj9W0p2DB0iPWx58OcNgVglD1MoZHJaZQJpIGH2R4gWR2wPA506nIPSJPOwUZ8B8RwSwx72jKtxs3CQ3FRq635JiOwnb1hEGJSbOQMMOFaLSP43W5D8hDpJ8eiRzkyBk11VW2sl15PhKL2hKAswJV0SJ6Gyo28uDfYbjIE/4liOvMqghUowyDzOgoIUQi4JY0N3tQceXJuI3yUoikismOOFyj1TnXaPmm3hQ/T0LdWq/ktmjZ0DFWFFYvY3BSTlX6BYJaVQu86UD84ugZ5QZ6I6djycm/SRvvhyj9ZWdgm4hX9RUEf7eyiyIm/eJxvJc4NdPU+hv4KHBE0WwjpqD4Q6JZLjaABouF6LeiAR55WIAZUQTa+WEfASX0H2WOrgqLQubhBoweTgIHvSFCrUxF3vaGAjZ6AeJ94spb1s2S24SGwKT0qHFay4s/OOL596k4C5bf5TiF4Hfn0y4f1sfne4kQe7YUpqSUoTft4lBRiUSh0Ol5Mg2BVxqoorOTU4w40ACBA7l4u+rXNx1xAPmfdlFU9Zm4qIRoXYamwe/hXXJQKIQ/6dfyL3/t18YsuBvv3DcNXcsLv7bLywuyITLDBeeBJXQcvMCfL6Oh7XbRVgFX3aQ9xvYuCF/iOQWZOO99wyEbmbjlDQ+HRSes0Z0PgQPs7FDZf4/+9YntcPwybRcPBKhjb/PF0DuryrKTZHCA67R2N9ujWoPiuDYJXecME2TiJQeII+PZdIdpWw4u0lNELiYC+dDLWHBvkwctWIf2Tv6Gf38/QSpf3OdpSzkv9MHiojhSCq8M5DHrCEdiNH2BPF+Fm5PZuPp70HIUsiDzc3fCN2dhtwZBdBj6okuT87Tpu6ZIHbAHYdO5EOHih/cebiGbpNjg/yAH6qtzQTsz4CWiGR8eS8JnrEKcdaUJDz8yRqeWztA5/ErZMlwMQRcrSJ99rNgUNsUHD6MZ3Gki0Hr3FLqPlaXmj/MwOYKa5omHgz2qeqsxfUJeLnRknYGfyfdqc309quF8LM1j2Q/30bTNkiBxuVasmmnA9xOZMC0ygNtwjKgrNsC2gYd4YRyDjC3IrCgLRUP/MzBaoUO8nNnEzndZ42G8kX4I7iOxL3ThBiLQ4RbepX2OyfDxpfpeJoUwTL5GVDnPQqXKIdjJ98bLX+dFETueUnfTltM9UWLYaPCVBz4rEzOlfoBv8GdshFQ7XaVQL4xHdoimshns3FwgczCjOBQtFSfzPoyhQe3xn8kic88cKtQ1921OkMGrdMEli55YFVuiH5hEdCuG4BzndNx64g09vx6S7RpFBSe8cL/1G/+3+cTORf/vd+c/FaoTeQOko/+0UT77HxMD4kFCaMw7K8dJqOOu2PZ8tP8+nmqUHboHY2aK+QtzYUYuJGL6ztLLkRXpeENIQ7AXSBr+QwG1c5FQ9kW1vHpxuCRIo5j2jh09SJJPNJ/n6qf0MYhA3e4Md4GY/S6SHBgpJD2pUFn43GyW8GV9bx/JxmdZYfudm7Ie6AE3N7RUGh4h7wPSYA8EwYNTV3whZ0cSB0po5+azOCORyC0jUrDYfUC0GDOC4pe+EOdEcEvcVmoXZyMTKwsmtyLhFrKxTPiY1HdKgYPD60iK1/kkt7ceLR+XMmyWJIKJ44DpGYm0/wgOzQ/MAsHBXlU94UXzW1SgN+1xSCX7wAV/T1kZWUe7l+oIzjTbgoin4cESuZJ+H44Bf3GitExpedoausL2qc3AysEM+GpsBZYaE/AV7wZ0DyrjPZ8YaApl8GT5ScpezAPJlZyYGmfJ/Q/SYFUu1gcseinN5U4cG7vdMzcUgRzRPypSJwqBPPOsla6ltF0lghop3FR/IoWrlLOBzF3MYzYr07K/VNQfE4k8u8qo7ZukKCWauAth4sE5JNx6zZN9B8rg5d5DvBSYRQ2VrBx7jh/zDYyg85dephzlI2r7yRDTTEKFOXMsOFjFsaPVsPkhvHYfXwCBAt0IMUyE+0uCrGuRaifioKxYVEenjyaT14dT4OGcIB9R5Lxu4czdN3xwCwXOVioLIP/mFckxP73ecXpa/8+rzjdngsbt0bg8CJ9WG02GrFKBP3rw7CjeRoYispjbp4krj+lANvbFDDrYwbebLRGUf88lKrLQ+XhJPzO3kUsx6RjnUICrnEfhTdy3pNHH/fRKr4x6rFWsq6LNPCnN/Ngz40rtHS9G345XYjvQxXhyEET6GEngaJAA491FGDnBHM6sPUtvdXEg0dOPBioMAEtTUPoFFWAO/dvkpQRQpetHYub3mSjoYwqVL7dRrReJyJnKYM/bvJQuZcHeYWdtHurFujHnBfYh1tg9tEimPUzBevaNPACf7utmOZKuvKjIUZtEeK2ULvzaSJIHhLDmCsMilmdpuwBPl3dUkXPHs1BF3krIFtiQGGkCBreF2N/UBzZ9cCUxS3xpDcm82zjnunCixmJ6DPpERHNKsbxl33x7eYG6nS+AFtNB0nicVs8/qOdbH9njTveHqe259rI7phTxM6/ikyWygQnw4tkUDoRPl0qhma5HDzV6gBSs07QjAgrNF5tDyvm2IHgZgz9XpVHduYawLtmFWhLnQRcLwfo85uE2h48tJNJwMfnZ0L4u3hUH+ogR+1W06lH7Qm2eeL7GkcU++mMF59UkWYhtx7gA6hs94fKb0Xw208G16mPh2Vv4mFhchiNtPPH+QHFeLp6GZ17Ig36Jz2h5WNcwU6opRYcdUf7yI3kCjcC1t/Kg2Nz2PCZVwD5rm/IcRqPYfVpKJsByBjdZEWoFtN/zLs0/y/zrun+ujD77PF/zbvOOvjhwpdTsa8vHY4YzqI8X1eBSuEL2/TSeRC/dQ1VdYhGg1V56LKHB193Z0DJsBbRT0lGjwIxeF8Wh6Guyy6IKUtjR48y+tbq0uw34fj6cRE+OuiF149S8jR6iFw6mUz19/MgbMAV9m+4QKf1GUHaUnGYkhIHBqq+sAEe0BmJWlg9+6ig1zsGu9ZeInZHfaGvPwwGZUpZ+16wQO1mNngdkAVprhDDr30nEZcZdHKKw6b8dLhusoFWdaWAw7f1gmxRKkj45QuV2wJQrpULYjaqaDYuC/LKI+AplBC3Bi7uEepFE7VXRGVOLIQXxRKl/ClUTmMq2RSQJKg+5A5OmpYYnBkMtdPMwXAyG3RbZxKpxwAfXe0ENeNS8X66BtmqqgTx8rG4x3IGrp50gq8q5C07U7fZfj2cCdN1pDDQRIghYj9YGpcKceYuV9T8qgJdpvV0X5oqpr6JhqVTebitOwYax3Mx6PwXlr7XBSLXKc5a/TmNDJmywbQjAuoixDHntyl0XknFD1LtZKqrNuq7vRH0hVih33sB0WqNxudPnlJycgn5+G4u/HrylLU7ygRPHFUiexKiUFCYjofZHlTxzDvBmFHxRHOyUIumt5CgtjbacnsyLDkaDZ+ei4BIwjXi5eKB2hKDrAvTivD3Egb9n8fAcFEuLtDg0xs1XiBhtde2aFQ+tjt5wY8tPDAXK8JbS6KRRjJwW0sc1kReppwyafKf5qWPq/7OSzWHeKDGj/7XvFQtQgvki7RhvxwPahUCSP26CTAzJwaOMonw850Sy/gig4VfoliTKq5QKVYG7stPgENyITh6uSkshSgMX7yNbq+3BrGe8fRXUgyp3M/Azrs6+LQ6GZOnTYFefhhkiwdjYp0C5oa4YqyVAsTbOoNDRB6+mhAFPfdzMcHTGxxzkrHUYgt5s3CYFDYJ9/lCKhzX4MD10C66QrAQE6SsUL5vDGpMXsX6/qieLK6NJW8GFuJzf2d0/RIFBoNq0HRuJ/nJ6aG+crFw6WQ63lBKgFXerwRRU+Ns6mODcU70cnrg9Rsa9IDBE9PtIO/WQgj8RlmVS9jwMkEdOmYfIDtHeLipejnpW8cGN823xFdfFx5uiwTLNWyMOO/NWnYgAG+IlpMbdmzs2bMQL86IhLhoDpwKSsMV3Bvk+10NHNjMoLJ3Fl6Qcab4IBt/ywVBdzwXl4qeIflCnrJbeCbrO8cRz7yFYOQWC8FCPdKHnwSDjc3kvPJRMvO1KgkayoaOt3b4RSIfVqzkCbbkM9jfqw3PE6Nxx9MF+OzoXEhaEIEsbzGyUngHNkhxwaD6C5GzL4Q4QQa4rMhFllDHlTUYIis1ARcczMPgIwEgLZDEmZ83Ub1TifhzcgQE7gf8HMcBa8V71PrID8LcngPbiDR2CnlC1pAPPmvVJKL7FmCNXwx6zhNi/xvKD37KAJlRysocDEdz7QJMFvL48N3OOLJ7mPxj3q5sOge7F/zneXvl20ES0336X/P2sa+PU6ehKPju7g3H5spD6PJ0/HLGT7jfhTBxpJacvBeARzLTwOm+NooJa2Tn1jD4tc4Zey4uxAm7w7D76gSYI7oAroYfIwu1xoGL+gIco4WksYaDMS+D/qlNxIT6zv7RO/qAusA3ITeqvBkMi46Y8A9lhcEboQ62vmQNYxgeJrxXgqslKazSicUIdUXw5jwPVJoC8EfnbHo6cAU9JTEGut+FQXd9LJHpjYLlEIpicjLgWaoNiV90ibfjFFgmvhCiH4xiNW8vRolqKzj9soXcEmp03mC6YMKypWT/zyGSUcfDh/lCzBkXCmHQRGONxWF2dyJsfa4HL5yiQXCkACSSufDroBrxGa9Fxh61RV8HZRJrvFDwbPJRvrLYQ9KWd1ow/CYEU2yK0GfFNeJmMgW7VC2g5+tRViXzjJpo8JCvPUQDGw3x4OFMcnRqHblSfpaumy2JPrLWkLc+GW4eDYYd/Raocn8JWfBcETUXzYFhGU98KO2J1+t4ULI4BSfK5IPsqHzwTJGCz9d18ItWtBDXonFPtzue5AZjsY81mW//lbVtaSZ8f2YFat1sLExwACdjBn9vzcfe5my8bWIKnwdM4M6sQMwS4oZ5kjZ9f8+A8hYEQ09gMkiIHOcrXF/H0qxxgWCNOGrVNUB+rGYg/7I6KqvrQvWdCmon4gcHl+Uj92cA9jkXQ1ZTJrgqBGHTiPCMzafQ/CEO+t51g//xa6z///k11vX+L7/G7D9dVFzsr18jukqd1XFuC90f+ZacdvSD3w0cmPqoVjB7gAENOS2QLuiiHHV9VGouAu3V6bjaYjmrfoEx1sflUZu6QDwhNx8bYh7T+pmO0KsfAIdaz5APGkL9JNQ+5d4F0HslDAYc6sjXDwwU6LpgfYM8vJ+dDJ825MOYtACYzstEtzhZHJXji9FiQSAaq4yhKQepXy8DFSHJwBHyWedZXFDXLIT1XjLYb5YAvZfqWFutZ6IZKwC3ai+lAU8MUMmsiu64HQVEiB1S0mH4Sli3F475RsqTvOBppjo8aUiA6NeLqcJoBn7vLMAfsJc6jA2AN6gPrRIOeGARA3UxMTA7bYgeDDOGgHwGTNT16Ps/F6i2ejPdGTUX5HaxWZv3eEFnkTREyhSD5TN/vMniwZiEQMwP8qZHDqZAUvEXsnKHEvQpMqh/0Ab3HPegtwZ5+C3BAM7vOE9ur7tHfLW0ocpFGzPK/OByyieSOykOKxtjUO97KnjPYmDzlmTM2MCDjSIMXhxtBjYgj6FeP8mZdF1YtvCFoO0EG3/6X6P7Iv3w94av9NTtLJysaAz3bYT1QVwd9j9eQB37uRjuG4jlVwpxD+s9ORNSSU5eDwTRzyo47LKb/+kiD/gz/aBnNw9/84uQLSEF9lU5wAmtpSeENULCmgtXVx6htZps2NTJgG3RdVbyrXwIf5iC+m+zQS1cAcc998Fq9alY/Y87PT0X5+7Lhf/x+wT8F7+P0S0uutn89fsotK0kO27zsPBtpECnPA9ezJ6NLt+S8FiOUNdv84OWNSUs5yZRnGSxm1bYpkPkc3esvKaIykJNV83xwTu7s6D6aQHWmHLgmlIRKLF9yeGuCZCRyEG3WOEb4g2RB0KNffPFCBVUOsOOBj4tN8mDI0YjdPpkbzg35AkFiunYodtDz50rgn3+GXDexAjO2xXCxl8EuO7R6Hn2A42pN0S5OGl4t0aD5G2dL9gRLtTVfV74ZI6vYGBPBIoG26BhKQ9srswD0fYwPPonH2cctCKCPzzwWywHaz3m4/U1DGl/rANfnqehfCUDCtQL2u6HoGOHB3I6g+HjCx5UPykR3EvZRxftyYPfrh9p+V53XDKGgTGiSbhNnid4JsIGX1s9UFd5Sm9GLkC9T4looWaG7g3riayR8M1qReLVuTxw1c/FxV6J+ExQRdbtuUsuthbi/QN8GqtQDBV64uB+dyJmGi+AQ3uNaPIzZfT7IeT6wnN3MYtC2Vy+wLA8G+w4hej9Jo9cu6CGhj8ZDHOZjKwzDtAbw8GQc1wQ/TkfQ6Msocu3n94wiSTcYQb6OysFD7bk4IGuTBTv0cGEfmuMWcDFLcI6HWQThIqLDhG/vi8ssdZ6WnBtPqxYxYEI4zpqqBsB2yYwAMlzoerIIAnTYIO13B3+I1GhPrp3irIupqCJaAFeV3enNV3x8DxYD+9Fz8MvDxsJWzWU0o6Z+OQqD/7hFxuI+He/WFvaX7/YxVxD1Bz31y82Z1AWjSZvF9he/UWLk3mwzimMlI144m/rVCjzyIO73+Rh+BQHVAUzkf8wDSNOZ2Pr17f0j+sGavxzLhYPMbiufwlL1zYPdpUUwZ9d5jjuwGPaNyUZOxLSYNro0/x1sVuI7F0J3NLthBXthRi7iAulXeb4aIMnbO8bh2PIAmK7+Axf7QUHLIQVr0m6EI72J8DcFyLwbqkkmq81wQHdGipTK4V0hg361AHls3PhYqgn3k6oJ7yiGsLNyMdo9zwQ23qKRki32w4cjsMK8dF0W1gC7LTJR3XlMbTSu4jeGCijxTsisFHgS9freULkQAFw6l6QE+39RN4/Fxxnq8LG5iKUc46BWcK3OI1y8dzrHFoxqoyliSUsNZdckpgEMBLqgJVn3FHgxYJzHXKQrWEGq/xz4HIoAy9lGsnkm5EYM4uHxTkOoKhgC2WOa4ilBSUZF0yhPv8aWa6UAbWL3WG8qgLdLDcdHnWE4t72NCjNHSDLH/rCWXMhNysMRetpZdT7fRb0regl6p8mg2RvHtwOYFDxkAQ95umO6x/GYXePI5xvy6Sb16XCSMEAfTTXEntXhOJql3NUc0EZLdKzJslmkyH+eBA81cmHu9cWk72H5+OUdle4VBSBST+D4LPw7ivkxUJ3KIND3aNR4VIQfJi2kfgtdgDDHmd8bOIGn5wLoNrNDA4t9Yc+9wRI3uOKurBf8P1mEiiFbaaab6P/5Tfc1fXXb/ju4l+/oZRzAdFb+tdvOKXUFdaOyYMKyzBs8YyChu2nSNqB8aSiT1gbWxiY4pcAPquGBdzsfDi+PA3ecjbQo98P0oqzwbjqszPWtebAII8LP06y4HpGIOwGDtzgMZj0ZgGO1XYC1WxJbFZfAO6HGFhu7gSNh6zxhXCtfRoRIHbWGQZrt5B/cMQH2gPEZaM/bNlpDuJnSknGpbmwNT+IzGAa+J+vJUD612JIaBwNVjWFeGrsAXp712XbmEXOmLRiEmov4cIsyMfRQs3sMjeM7vOPgfxV7lg4+xmdIqEA5u+Ws5wzMuDw2AJS3pct5GWe2M2JwaciTuDtzOAdReH6GRWWiJ4baOiw8b24Lg5buwC7OxbP/UrBN+WBaHspEMbmrxN8kZqO2475Qfl2X1h5wxFXzRyhM1uTQDMjDwZGnEE+aS5+2KSNVx86oP+kIoz4UwRHRKLwbfNsUL6KdKHuXFpychnZOuYua9IGRxhY9oKkSwbD1a8y6K1rBQ19laylRrEwIxRg0Wd9fNdlja0izy6c15MTrn8KLpGpIvGZDB5xTMVdgY6435MLnN+OcDxyFj65NBtmm0fBbP3lxLvZBq2mxQp2b8kj3+cthN1vKLngEIyNsx1g+vYrpP+hK8jft4fPusbAnjQdWnzioOmgARhoRgLdxqCZoSXsiVcEy9dVZFxOMHjamJJJnALc0lMEK5TsQH3Igjzdao+2tsl0+Uc5KJeNIZLvTP7pVzX/8H/6VaOviP7Tr+rrzIalGn/9qiCSgdZPXhBXRwa1siZDwtV5WHswA64cnAVtNmwoFmHAzDEel1lm0ecaadiQrQv8w45U9bk1HG4FTHXwQZXFO8m1bTvJtjFK8NkvA8vzpdH1mhHeqxwNdnbR2KzvjZ+2PqUPqx2xqnISebWfCzeOLCbHtwGU7tO2tTGYRXw43jD1ljgEJ/rS7eJz8FGbATRvvisYCB0inWqvWG+z1HHFmNGo5nqBRk7torx9BbgzElB7gj/EH17KWmdlh/xkOXCwnA2cZ37w5d4UdNk+C5cYlrI4aakQ88IeJZty6KFYd/S9aYZd140groUFUgZpwEnfJqhM76Ux136RpMZCeL2Lku/nteGVLheNiSrOnOwG19Qj0dJ4C2vkBBeP58aD18ZD5GAFCz8eCCA9F0No/T7hPXN6RC4H1pIJusUYvFEDI5MXYukUNiivNaJR53Pw881L5ItOKjL9Q6yKbcXYEsKCoduzsEusjn6z8BesuvGZZEmoo+wYadb1UBXY03adkouFGHY9E1d85qFawlSwX6uKZpUc7DmsCcFXbbBzWTFKGfXT4EA2BrTcJtNmMVi9rQD8FJVwn9RSsmfIFPTTCfq58XBsWzA4LOkSLHcTA/nC26RtpQ00ZrZR41vKuGSSHOw+cpj0FHyhC4ae0f1Cnv4n2xMMozzooZwEUifkmHf4BSCv10Lm79aDLfIMkvk2uOKjOEjICPdzqQ7snX2R/Ce/8/+er3hi7PVvfudjQh7EvumBK37HY/ip7SR8CQOR5Q7w+KEkzJngDrK5XuCkaA66J4R65ZYQkwd48PVcgICj10d8861wVZk6XKuIwCNWE+i1ilSSFK0MF9MHSOQ8Mbw5IZ+ECHWA6o0NtNd1Im7XcoRFpQlg4J8HOCys89ZWME2oAyZLp2PwVx7U7V5HAjvO0FM6uWiR9otUfm6mPefDoPlHIT7VvkevakrC6BAFSC9VwD87f5BVfyxxvF0eBrUL32XLATqt25tEn+HgH3pMoDzlKzVwjcIGvy56ZMI0NFIPpu9XsUlBxGqWfqYqfjWRhhkRc8Ah2AmTb/KQ67wAe4MlyAkRMzylxIOV19+RtpJi/MZ+JXAIOEZY27suWL3ZKfi64w09k5ENe2IYnKgzB0vILdaeK0Lu1uAAG3UiWWOnxBHtjwdpgscLymkyxihDXdC7Xgzj1nqSF3cv06F1AZgoEsH/YN1K3r8WvpVPkaByqQg3Ndwkg3GtrKvdbOC9DcVrpx8SvZsnBP2t2Th6uSaIrr9JpG31sPo8gwFmmZjCU4UiYoGem2Kg80EWxs8vRp0LXbZN0p4wd50x1I++TRc9k4FuoXbe69lEq8LGw1eXEJxbFI3WfVrgqcHFKhEL0IoeZK2bFodr9EaRd+ccBTViNYLVcUpQ7dlIHq/KhN2nDLH2DA93O9hAyucC3PNcElUvyYBt9kkiaDTHlMN2mCTpi+dFM0Cu/IT1f/LLr07665cXbb3M/7Dwr1/+riAfZa/aopaeFt5gbxWsEVuI0+/Pp9IrjpM9ZnJwTeIe2bj+DSnenwmrI1OxdO8gCTpzhLV1RyHSR6a47tdRGvbJAL+t0cLzUap0dmKKIEuIrz+WH6Dh3zYI2h8E4OnrDFxo54JlkgPeWxYGp3clg/uzlUR9uynsk1AGqwJZtKxMhFvvmgUZGgvARymQcoq2kUsPDf/pv3lzZxRcsxVyh3gJ7BL+bbcyIyzvmwNnP87Cy48ZqF3zmO4yliT6oeb0wLJ0aO1IwoKgDdS7owi/XAtAOZkRevFSLhzzTKVPf4fB+OvfqapRDmbWzIL9F6VAo2sLHVvOQzONRbSrxINEaEyEPVNXkKVTY5D7uIU4fxe+lcYo0KysYm2Gn6REtRhKKzzIxLRZKDXhNs1YO4VqXGsiR8wW0Bd35iCOTIfJN/IpM7uN9ulqkVDOZ+pi6gT3wq+SJrMqfkWqBn281Anunp0BeiuK8EzrCf7QqHkQO3ct3WZXiJOGwjDLKoZw1icQ96uG+NxbFk66aUHtCz3oVpLEE2+i8fGzNHxQMANbn5pj2NovrLLB+fhlRh0VUTpK+R/30vlL5tO1m11g0ph8sHQU6hLZ18T2TAgkykbjmhArWHa3j3XZcgYcjrEjnlVLiOqZy3TmmDa64p4GLPLjwbLTViSrl4Fzd7Lx0fJW0qW4iMjOKgTdqwwubG8jkbO8YdegMb4cSIb1YddI2Fse6H6m/8xbPF733/MWvfL/nrfQNxZyT1VxOGI2nsZFv6Kc/B7ydGkODvTIExP+KpJ3+qXAqJaLLhOmUM7YVLjj9Ja+ORdN2ussWBZ/puPKQTHWwTxt2L8rDU2dxNA3Zpge0h4i3x8MCJrzLlJVpouOajtLc+ecpb119qD1cymrXy4EXHNDIUIwHWYK9+3CSWtw1tIBjw299PR+Bk+Y2sOYk7dYJexw7L0WBgfSpkCI/hg0zh8FopvlQdrWGd0ywnDgzzTsGWLjVDdvUFqrDjlqUqi+dgL6GczFG/l7BamnuDgPGVxuUwTKEeZ4aFQSGdJNAIkn9fTDTGXBYsE00JmyiMo2xYC+IRvjp6iguyKffJn2mO4JTmeVFJqhW4IXOCzdQbh3/FF5wTEBT4aBwmYGnmfEwVDRNNz43gBEOjJhlHmvoLN9LGoaFkGMaiHw3qsKpP0/C9b/uED9Ds3E3zPTcc7Ha7TwjBckjNlIO3W98flZfYiU48G1GUVw8HwR7KjQRNNIE6iobLY19l5l22k6CrSoB01/m4+HI4oRTunj67Z7RF/ghUtXfhHw8/RRcOw9adglvCcXWmhEihHa/EijEY1FmPOn0MbzQTqUy5iByjyG7m1hcNA1FOdPfS7IK1sAa/TThHxcHxpdB2iVIgNd6+1wrqM1VlZn4t2SRhJ25wY/wCIXMq8W4kwXa9gTVIATogrR664KandVkE0XnPFG6lQ8cJ2HtuZF8GuGN7xJNPlXXmfJK8t/y+sMev2vvM6vqymY3/k3r/Pl02R0e5kJm2v4ZM0rgrMUKcuTY4kSV1cIPoZPEcycrooH6pyIdZIHXt6TgoEtXLz4/RTRNssB5e4pmNe8hnx/GANWVkLO6uiJ7T4xIDg+GkVlTrIkhD93SSthPbZzwq6vU/HqVHs88t0dqzqdwaNzDuoemAJnNLIhK8gbii11yfcIR8x7HQ5rothQZvNEMOLFwfDlQ2RdQzqO3veNVlJxlBcR0I8rgjFKZhX1atdFSRlPUL1qDc/XxIFr0AL8ZObKMqwcR2q2iZHH1fEY4BmNMdOl6Sa9cvqbH0cb+tSJ4tF4GjKjEERH64B+Z4rg9BptCBo7BUO1Q9ArwwuXcBicWqlMT0a/obF/eDCybB7s21cj2CTWwLLi6wCvjgfH5xnit0ZtfLErFO1MxWF35Bhc5uqLG2rTcJLVSjK8wog8fOhGJu92R3Z3IiifHWLd77lB1PszQF70IV347Bht5WmDtmsiXt5wjcjnZILrnnCM3m9P18+OgXdR9tRspT3ENVSR077bWRh2kLT6G4CbcjHUy1rj7Ulc2GutjbKr9ajssyVEs0abdaMuHm8vFHIaS10cVA4gknsZONSfR6T1A/A8XwxOtIfCyxYHiEiNBi9LNQz7Pgm2ZlmBR/pX1nGxImz8EYm3zxXAgkhtdLiWTDtTfUDtuzVeYUXjcw1jOLS7EPg3D5HMFG2yIq6VjqqbhGaWHHjkcYy/6WA7mbwp6Z95LxXP/573al7kRqVd/ua9XH8iOaA5hV6SJnCkg40527Tw7oFlpPr9Slpx9yQt+ZqCzT/DwccvGUznpoBTkCFKiA6Q/DsyqCumgeH2cji6dgvdHuyL+8/n4bYmHubtYnCK/CZSFCvUyF/kYfeTEHi2jAPjttqBw+FsnOQl1FyS3+kFvjo8E/LA/ODHdE7AFnp6fDY6Zb6nkwfUyQSjXfQfGdzG0lP8CY3TBP/wlx1qFoOZnYVYOzsZHYXf2+erY3SNIq6JlMV6LgOn8wow57AIvFubAcfmHCbuQn7oXCKJ99Xi0GcDB/5wosgJdxvMvFYn+HjxC20wUYdu97NEqd0A3z7KoJ9TivEDMrD0hDTen1VHVR5MpJ4RCfDnkA/uyk+G1DN8wTL7jVRrVwGEt8miyic1tPMSYpKYA/5oj8F0CQ7eOT0eFrbOxiE5PjnT8pPsF3BxCdxh7fDxAu7nm7SqJoeEyktjxVgGkqrUoMvZFn7FBGIHNYXkAi7mulykA2L+INp5lWq/SoEPmQ6wPzoFmHBjEDWYhlN2m8L6JdqY/+MuFZ2ujKoWE2mvRhyqGkuBQx6Dd7cU4SodSvy23yUHg/OhSqeMPrztDtOlxOGWcxhueZ0HPcvY8EJPA8PeTITFS+Kx+2YWhpzIwv0BPJzKn4kC42J8CY/otT9qcDKgjMYO6NDnYlVE/W4hGhjlg31QrW0fFkPVzonofCYcv10owjlPr/N9hTX2H3nBYw5jWelX/3Ne8EmssB6PzflXXnDz582C5MsMbrTlQNo9BjfFZ8GyVQUQbPWJLmnRwrmXuDjZyA57u44IVj9LxuDnBaT9gTKMqBuhpMJERJ4bGNzh4cyXP2lvMA/lwA2+Zz2noeUDxHNtAWb7JECmdTiJ5m2kzAvACTM30DmzxsGHHQWg3jQBtCNzYPhZAXYX2KBuAYOv1VsEnDMemH7/D137PQn5GZLotjYX/sA3ajtBDOvXWKOFXB4ahS6nDmQclvA3kbXj03GkMxw4xVz0DY1FK3SA4B/C/Q78SiuGG/jTfliA+Imp2D8nnBbtL8SPZS6sB+usMPm7ItSe18ZL+yPQUGEDqR7DwV4Mw9mD3wiHI4X8j9NIkpQCPNu9msq5emB00lXWmt/ZGCSXDnW52bDkuCL66TF45okH+g0xeIVdjPt/6Qp0LHJxX68sTJtnC/bDKbgzphC3GMbja85tEqA3HQsfT8WQNVa0NF4R3Bf/pvZaZjh/K4+c5BagGPWGmweOCvx9FfHzBAZqhovAe8x0dHQswGVN08BGMJcq6oujvZMrcrUU8ZHHKzL0UhmtJkZDs0Y2nn2YCsmyPFA+PAG07gSA+Wtr7I49TGzvMOh5Kgwu4kHSEsOBg9apEN2mAIKlXHga3k9fZ1rgwsY8HD5vSTqSqyjrvSpiw2HiVppH5oYVg19aMe4awwFRTQd4+PGLALybBRbrFHCikOv+nlNH334ZhcwrNv6nvClnz9+86fq3RfCx62/e9AiLTctmhlJ1hwIQ35OEojMK8B0aw0eii38m+6PxCXc0Dr5Czw37Y3FJMvT7sKFB0QzigmOh9IgsNqzJwCCv7YIX1Rq2NsXp6LxkH10yQwk8FujBFR8GH4owMD2Th9XLzXG0HgBroy60978lU2o7+QlRsjA20huDri1hoVeu8OTj0TxSG5beScZ0D0NMDiCYQKJxYP14nBYwBudMEYVLFtrEaxEXPx8ORwsVWZz4UIT1VH8BOcoXgebj4Xh4HQ8ONs2jny1mgHu7BZX3HyLRT14Q6ydnqUxvMFkzfJ6qngBSbhoCX5vHwF21IGhl1xBupjF0B/riwvguoiOsrX5ypmg88onIepSy3KoZaMuYSkU0bwvs5rrjlR0R4L8sjLVn+RjMExdiQl8PuYQLcc3MEKzeIYb+GkHU9IEFjD2bCIH2qlioPBrWHZjDOivRTE8e5aJUEuCzvHjc83Q0TPlgRyRb5WD4Uimdm7QAR9oY9BUkYdhiK3R3KITuqWxoifhJr/ryiOc2QzQxSUU/rxt02fUFkLlxNG69Yo6RMtOBvGHgTKYjrnykiNK+E+GPJgtGG+vAQizE+D8Mrsx4RRLswvFZjSqy5zGQJytGTN1Gw9I1RejwwRFmHlhBO/w88Wr+FXLmznR4ddAOa7SP2975bgrrVL/RanEt4FhxMIj/ViDWvIj+upiOr4cl8Z7vbZr4IwiUdIPw9QDBf+SVZ6j/97zy42d3Cefm37zyyf1ScOhGEpxWs4QWrgluVJoBO1P20qhsV9yzNg/FpvPpwUwGVlZGQdfEZHw80QvizT0gStWPKCdJ4rZHDPyWzgM3j24yO9AaVdcz1DCMB/fOe8BdmTQMzPeCaXGPiEZ1LqRuJjjyhQdt76tITH4uVmpJgrOOMszgcSHVoBDvLzLCOq1s3HH4omDUp3loKS2CKy4yaFM4EbMtGRhKEsHgygTgjTD4nTUbfsSLofyFo3REeF8W161iFddoYtUDF3yYkYN7JL3hUIcM6TkbBm0SPPz8nAsl+wPoIY4CyKjtJ+yV68hBdTEwrs/HD0u48FjIZ6X7xmKy7ivW5s1DRFHKD0+9ng0H53XReWcENOViJiy408q6V7mG9TLdE65O5qBziyh+o3YYPj4PeKKjEG1U0GZhHIRL2sOGbwGwYB4PIja8Ekgl6YKtUygMjmoh1gFvqPsxP4yR0YeHy0TB2ecHi9eWBBl7DVHGTAoHt4SCtPdcLGoScpehEGAb64J+QxEcSLSB/4+wt46q6uvav0FKaRGkJG1KSnqvSUiKhCAl3d3NOWcb2GJ3dxciYrDXVEFExMDGRBQMRFQQxMD3fJ/feLzf+/fez/v8wR9nMAbssdZcc17XPmN9rvN2hpBSEoQ7T5fTa3+qSZCtKG4rj4OsEaHQuToGDtd9ogZzfjtcPtlK5y0Mh4biGjLxYR6u3L6bLKGjoKw5Au6fYzFbuJ9FtXxs7BaBll/RoCMyDZiCaWRKxDBkhwdg96QSiJschVLi6+hzsSL4dcoYLU1vkO68UqiuiEWdLfVUIlQNTvj/oMN9m5kDYw2hVrhPRRva6T/33enkBoe71f/5vruliBR2Kjv+ve/u6TINrT/LQ1zSMSI6HunrW1Oo2tkyFEyOR70GFndXa5Hac+X0+9nrF0MwBX7UNdbaj4mE7ctYNNQdpOmehZjS8ZK8UQqCbnhIi4od4JR4HRlfngi757jgb9kWMqPIBgbqgqmXJIGc8O/k2xEXEjlWCsii72TsnN305bLtZKS3K0YlnKYO24tgw5IAeFSXjNtPBtLfD4Kwb34+rJhzh17vGSJcdjh828vH75t+E8tPAzSkUBOWEVPSwmuj5uUVtNhyB1E+bIOXulKhqO/Ff3234mLrSy5rjMZ3ETPJbWGtho/rIBs+syg3a4Cs38rC1gfzmTMipWCyYD+t1dfH4gIpfD2Yjg9nJcCN+iB8We6MB5cyuJl9yt1QjIALTRbgt7Ob7K3soc+Gi8MMpy1c0Z9DXGrgHLrVR4C1xdkw4lMm3PhUCm737pC0o1k44lI4OordpeXlP0hWQSJuaJSH/tE9ZL3FQ9p8JIxajMjDgpuW9FXQK/q8YiTZ3iD040J/WmGuTZxvjqJ3hPpMVdoIK1121w4NDHLZjwRozPtECctDt++VlK5Qhie3b3OOK8TBUKgDbRbmgm5NIGPXaICXfk6HJ04CvGYdiUyvBI69WQrzhzUSz1fd5HyHKSrq99L6axNh6X112vuOheCaJcTWMB8VzJVwsmwRJPalQMAPAZrvKsXWwRhMSRNFqdU64OHWTSV0gsiOBlXcAKkYG7KZ+0+8hIcXTv3lJTQLa6ouZuJfXoK+ZgYMSljgxoVTqapkht34/Y00Ml0fa8YV4euOLPyhrA8LjwZijVU03vdOwbDxwfjWR+jdR+0mfQofqeqkWfCzwQu3aAvg5RJXTi6eh7bGuvihqBgOXpNEmfWvOD1eHDaLOePuAgajlnXST7WJzKQHxnB55hsy02yyw+CaSNy/pIryNNod5qYYYXlqPLw9VYLjPR4xkxd2UCWxV7XbzmlilvRvepLOY8q3FOJSmWJUOfuavjaZiXJSmbhvIAAHBPeYmsnlJFK/nMy5VQraPmbwqTEcdlaGg6/1c+qj6kM9CxSgf5EizPxYQnzVX5O5slko/rAYVU+7Q51/JPO5JQ9E9rC48U4ZBrnzQRdZrDQKhxkj86iOyRvmx8gk2JqjDt/LHSCT104XXk6lsq/5WLWnlTR1ihOneFfokdIi6Wd5MPZLJPyO94OTx3/XdsEITmz5PdIp9B1Ls/KwVCUX816+ImdHGMOLehb75vGRLfAnyeP6uJGFWeA4l49L5RvI1M8XqWhHHJa8MiWZh40g5JYfGD6LcFhaMIlEOr+iG/348FyFj2+drzHdSSvIkshEvPB2BLqZCuB1ZznddtOLlhoG4lEnSgr3RMI37btkkUUO3DnvAR0ZrWTZlZfk6INcjFRLh+vj0smWtDA8tdcK0+oE+HCENybKWcD4lzXMzO9y0HCjFEZ1PeOu92jjAv80zLj9mBr1DdGhncmYuUcVZXlbyX/zNnz+L95G0ur/w9vYelIN116W/svbuODM4o4RAagZrIa3RS6Ryz6B8OQgRzauZ/FEmQd+1ciGqIjp0LM6GGxbk2Hd2TRsiBJ6h4FAMF/ziBrv1kFvNWecejkNbuQdIs2j0uCoLQ8f32dhT2AeFZ+kjvqhqnTe7Cz4VuEAZQ/4eP/0EU52dixm9j4kazI3EvFWfTxbMYFobW0mzi8nEJfSkczw7yx+r7pDamcsYo7/zMKW9zxIW/uT9Mh74acdqWhyx9BhdeBDeqopH77xZPHWbj7ar+8h4/4weL20GAyEfW5GbQxMWz5IUnqc8MZRFqqavIjAk6PPVxlBX0skfEm5RdZvdwcX8WZ6ffcFoqLYSlipIpz4+QX1+MDi1UgeTJkRi0cuziTdS4pg6MUHbm6gJhnZ2E9cB0Kw/vpNOtZkD9fds5G6Ew+4K3w+9vVmarNC6EfOs5A1yhikhwTQficfXO/Fg5nyA7phux5kqvMwt3sOx06+Se6kRqLmT3H6TNYLt86UB/H2sXD+vTMEpLM4Q+Uwc2abJx66bA3pPA/w8jDGgI4sCEkdQ+76J4Pekma6+00CvllSWdtlXE0dT4tB/gih9xD0kYNiLEw+LoMfTxwgVXkLqVjRWFgkG4UTtHnQ+yYWiPUMPPbuBtm7S5GeSFQj8mICeN87Cp47PycCjwKkAgEmP8vBoWFF5Ft9Me59/4k0K5Th3A/OMCnMFhlmFNXxuEFP2TIQsnUeyVAyp1dnj4V/eC28Lf/OazEcbvyX1xKrcoicP/T4L68l+Hcx9H+/T25tTsVx98yg/pvQwwufBfT4uCZBDh3EeHDGby9RvLyWCWOT8U6HO13OE4B2hhOeOHCF3jlYTZRUxGDtcQ34snISuGjI48IPByjfkwdyRzcxfz6n4fSOMjC7OIJ++WmPdXYGeOhOJH3nEEFviF8ha+4r4QZ9DYh2VYEFx62AvdVMpu2ZjQWzyrCuwhDcPinAjh168OL8UXpG8wfV25aHPeuEOl+7EFpiL9NvQt/59LgdLJuagcbUBdZf4UHPV2N47loK08R5ECZdVWsRg7RspwQu5rvT+nclqLppGuQaWuDlS5fIdWU59FrpDf6/T3KrEu3h5wUDyBwYBS0aJdh3OhZ2L5XEDj0B2vGnoNXKL8yOeHsMulBDDFo30K3al+jODVdIYUMcnB3Nh45RHmC/ZDHd8DYVcx8UweL8++RXVBLEograjpyJGuWPCFOxhubdYvGmZBlE8pDw5peCs1oWvH73glZuCYDpc/LR7rw3/Z5XiJEBaVSL46OL+FNSfCQNy6eZgNGofOIvvYbouozHK0k1VL3EBk4Yl8GjJ66o4eoNbz+7IA4J/cVAPkSau2NIljOEzmOhZFwZ+p+JgvvzzKGM84MF4RdoaP1ZclzDiKI8CzkbZ4FqdDBcUhQjRWNCqUp0Hlb3sqiksZoS5xHgLuxla2zLQL+1EPRW+6KBsM6f6J6ln58no74SC/1yJ4gkiYH/5v0EnvzPvJ9MqyKY2rn+L+9n05swDJhaBo/zBWg6FAZP7ULh6JLRcK89C0tfLqbFC51hi1Dv66ml4MNxyTBirgWUtbFQXNxGd7SeoY/DZnGpE7JgvpgT+ZYcD17DQx10thuC6K8M0HO6Qlou89Hg1nfKE/hgUoo97CsqIae+VdGoehbapp4hwzumw7LHcSApnwK8+elwadUsnEIicPTPNPT9GAFX/JJg/BMpqO83QO+934nj65V02CtXDMoSQ7k1VbTmSC5oHfSG5sHJ+MApD64YCve+axd1b0uB56v5EJUuif5CjWepK4K3p0wj+3LscOfCUBi6K4L1KUfJH9WlZLDRDuwNlTBbJgs8P2rRpXEC4VlIwnON5vCrrKa2UvwFQbVddLT1L65mSwm6HU8kr4Rr8Lj+CBUIRbOdswaM+Sz8394FMGweALl7heuNSAA48JY5LqMIngnKuPRwO0elWyk7PB8rTguw9G0HfaxzhBTuFaCWtRgknV7NfB9XCrfSC4GbnY9Xv8fiGkkJvD6UhmywAB0v1TN67SZY7WKIw0UekO/hSlhfepi63vxKq+tqyPbZJdhlUATZK9eT/uE/yKPSaPhpKNRPv8pArfIKdfq8iTSOicZRITOoupwdrIgXwPYHh4jklI/kz9t5xPd8B7GaVwZiO8bjsceOkNsWgU3d+Ti3zQ43BujCuJpBciC2DH/k7yca6bH4bAhpjOkh2mArgIGvp5mcszbglCSA/40XxYg5w4kM3b+8KGydCMfWH6tNPxyMKyp5kGrQSnVtW6l8Wgi63LVgJn89QG4UHiAijt5w4noSzn1ZBpNyeLBoSxmk+u2kyS/m0VcR4tTszQE6bhQLaSnD4YlNCSQungWS+8bDttOeWHbTELV3JsFlgR1WRGfDd6aXnt+9jaoc2UGac3PQat9GGvCcBfvfLOQJf1rPcw61a2ZiYrnQE4iykD6lBO8NL4E3nv308il92CZrgu9+m+IJsR30j74PLrl4g55770tLMlmYZSULOkfKMH2uL6iQRKhwiEWTziX0sFBX7/hA6OMWXXLwVgpc7SmEqRHbidx8debObFnULEjC/m8scL13qNWCOjI80htdD7aTST6DxGmTFWg9LoP6XFEyWHWBNl65wPQopNDDZ7fRHd0hkH5BH9842cCedgHIjOHj4rgAmD3OE8x3edMNKYsZCWFdtgh1duOqJ+SBhSfefFKMLhmAMorbyOmEVMgZcsLlAlFQCeHB8BGTseWRDR677YqPDZxx7jR1HHQIJSlmgeDMToGIA3xskxf2I78csLy6luq02hHD3ploqliAHpMEMKxzOSm0k8LIVGdoO86i68RsKHrsiH8szWFneTSkO6ajyHxdOGSnQ/dqrSULn07FrJXJuJImcpkaZUAnF8CzldnwvIfg0pf+OJeW4NLN2cKZOhyGLyuD/GZ1vFiejZvjpqNI4Rr6y6kMNa/nYsODMvi0pRD+4Y0tnPfvvLHBVO2/vLHk/rdcVOrJv7wxX8F1+lqBj3fNxLHvRAkclZWlu3JYtKqxxoydVcTE2YueCvWCtdvViKx1Euxaf5tMk8uHfJ3N3OIJq6nxlxFgVMfiMEUrbD6RDy6+yWjHLCXbBi2gsTMUZpvXUqXhaZDNT+PaPtjC/T++EHtPFYuCg/G0YBXp6WBB+x0fHU+GQo5bJlyWCEHb77HQb2yKTYsEsKfED0y1rzJSHxfQ6tt9JPGTAE6YKqGmUww8H7eYHnqlAv4vi1GicicR35AL64avuqi8Rx/VfY2gbaYNVF1iUcshAB86vKPzttoToyNJICdfCFpHKi9M4C0ip/ecoAFSaWhPWRStTIOiW0Ew2K+D6cGT8NAfJfLxbDionYnH7gpd3H1dnLpsVYSXi924ko/RGH1vIb2qWozLD0Ti8JgVRConDiQgBZQWx4NPwDhoYH/RdW4dVDNTk5ur64bWo5/RiQ8qmBXymdi3OwiWGSwkwy8Avn/jiof9IsG+PRTG9ysTqW3zyTRGF585pWPjyXbOe+ZsEGjFYaemKfIV3lDpJ+NBb+leZv/cBTT1uiiZoW+Be9dNQQfbaBgh6oSn9rOYpjAGktSLUGCqD0aLI4CrKMRnk2LwflwQrpgoBtWaqXjqcTk5ZhMP1120MeWFDN63DgLDmwXYJq1Nmh6q4gJbDZz9KA8f3T5a+z2PxYHrwv53VujH4krR+/Z00Ggvpq4QDfeSeuld2VJQ+ukF/xuvTm1XLM56++Uvr05U4hKTYXiZ6t08WhuUOBNOBk0EN31prDJWwMA1OnDxuAEOHZqFFkey0e5JItbrSaFBZyJwyjn41a6Qyju6YqXrbJq7qAhTqRu8zOLD5QIrPF8vDRqNxSB+QItsK9nDiK8NxbIgW+ypulnrM1K4XisImL/NJyXPCnCszVPuk3kv6fpiC1vlHpDHPrOg2VUSj65OAQvlFuLjlgI9JrbYGthLA1W/k1DuHU0Yn0mC1ubR+at3UP3N8xj97RNBEOEMH2ZK48kJHPlGRWHW3s9k7OMgbP/pDaltivhqug3dGx4O8uF80HEvwgjIJYtPjoHgPVOIY04cpphM/ee+FLZO4+FP1x/08EXj/3q3mGgtR02tr1I93eHkzD/3p4TVWidjCp5d2XRROQsGicfIUec8urVvAmz0lcamFhWmQuQL4fU54L4rQThPOQs7jgKV2fqcVI+LAAnNKPTw3UoKRaMJrR4GGjtekovCvyt5VRWalB8QydRieLTrLY00YFGh+Q6xO+6C4LjFYZ2yGaRd/0E8KYF5JWXYfTSS+ou00q5oFZiwJoPK1+hBRjYfVBJf0nz1F0TFLQgNnbXJNysPkpMVT1RXxsD6gG21Uf6NRLpbD7yCIogl8nCypCH8ehSAU62U4HTARNBL9EKdxfepxusD1OiGAVaI58PilIdk6ecyWNUigQkzWFRLvUIqRMKp97Up6NvnAVJ5DGi/EIX/5h1uf/kv3qFg+794h25vY/BD85y/vMP5E8NJYl8IWFz9RqqXy4LbrQLske4jhkqX6aljs8DvRh4enzuXKlcDXOUlQlYji01Ns8DEayQ0BIZS0ZMZcLArHR5PKMIRzwdIzpAj2swuwNmkhvg7stA1Uh8MJhajdnQkrL1sA5ftglHO0gvrD78hvoEexNM7ENKHC/uR+HXmmsUDck1ChUD2QW6++3G66hQDuwtecRNEpuDkM3ng3/qFdLV00ndCf2PwlMWRv92ow8Pl9pfLBOBDc2DrnS5ywDEF9vzkw1nvveTicHNiV9FB31nvIDHbBVAjZkEc5s9lBMK9TkoXhYv7XUmxQgD+U2PK+6Tx6+lG4t8cBNGJB6mG42YacvAT/dQUDuZJwtlyRAApOXJ0RaACfFfbS29ViIOPzzQ6TJCJke48mL/PFB9k7CN+w9ZcXCCYjemjVhLUTsFeYV9lKsfC7Gd5sFKijfaLJIFt1nR0rmbxRr8z0aDxtFu5EIbWRwBfrpa8lArBn/omkNxZTldIlWBlrB9k77xAHXZHQt9JM3QfycL3Y4FAtV8Q051OdLSRCdz8lQ5iJ8cTkYyNZMuMZOy9PApuzSog+enCHvruIj14m49vXwFU9sWgWYcf1FSVkxg3ZZi/1JQECX1Tlmwo+HmxaPE0joj9lqGmYml4d180zTZfVav+3Itu/vAPX0Ed0kbzcFjSXdI6l8U32gK8ZDaFXvrxjEQoesKid9tI/3JV8g8vM+Pcv/Myf0z4Fy/z7qmn3Be88peXqXdHBJ9NNISSvDDg63iSLZMKwFfon1w1rOn7lkIwvlaCNoEsXMlOJMe0E/H1SjXUp47geXEcDLFGdHkUi1ULVIA3xQsPd31nUl4sJ0ZGYXShqz55fywd1qpnQ3C5DMyp3uMwycgGJx5MQ9ttERhbX4pDEwzJ8jRt3NzzkhTMuktvBjgwIrKTUXdhCo6OkqN2eUPkyTh1bDT8QhMPMLS/tIdufsLH7s3u9HjoOvLh5mR6IXoLKbl4jKSvLoOuJX21eZp83EtHw7s1nxmr4/LUzGQRI/2DRflp9bTx9BRqOtELrRv4cEDdEuryJxFjRqh3Ak3JwghfLlduh8PCj3doeYohNC9v4LQ2WKPB2kHSUPabbj6kiO8bFMm7NRwdEfqa+vkWYuwMVeq+3w1ahpUzvLICbFBczuy1lQXtqAiUztnGzEgR4HDfWLy3WgI+3rPBxsWviMc5DSI+1xLXfI0i9r0qZAbvBNdwu5Urf6IBr6dI0VSh/h2RtYvuGx7PafDD8NRvAdzgWCifEYoPvMzhs0APXCWEuva7IwzbrEyTjaPw6LowOr9ZD2x8j3CjEoeDV5oA+tcrMIFHqqnrFk/sftROTPnZeCw0FhY8yoLgfhFy4X4WniOjyMWpCTjn0SU6+n4i1h6R4daYtdKBp/Xcap/U2rcf15O8gweIu4USWiXwcYs9D4bG8cDxZin6jpQn4W9G0tj50lRNrZeO2DeLdsZm0n94q0Ye/1/eag7v//BWx4i7wMdU0b+81Y/OBAx1tdFSng+thgwV2ToeHef9IBb9zxx2X2Rhzk87TN5/jcpVFmLkogSQiD5EzGIngv22ZLxyYIgL21UM3aIW2Kp6hT5gZeifDevJquOO8DzfhWPbnOGTGA/22mrD2LA4/ORgDTe93Okpnwy8W7SQsfr8kzGoP00sl/tiIs8XQ5Yepm0ZhaC82xtPfFzGLLrmRwp70rBoDw+Cprth1e9uAloG+ONqInd1UwhU2A5yy1SV0XhVJBYKde/OzatpHAtgfvExTfM9y+2rbKOiXQowavFRktynixExRmRwZR4eFepkm8lR2DCvn0nqZVGlNRvv7j1Onm/+TDt/atFm9xLIiGxjXmAC9i47Xbvv5mRMn/ieDhujTGd+KUF2kSLejmNrX9/o5bKaZPDk/WIULROgueYTGmaciq+XVDvwDCRoECc8YxWSGH+lBA9mLqSxMq/pQtYeFXLVsbA/Fl4rf2H2ZF8lM9v5KFM1jBxXqaNrV8zCkO1l+MJKg2zLnYDHY5LA8+4KeuhMHmfbYY5fH6+hkVl58NwhEK9FfOJO3CsFzassWo/Ngok35XGsRJiD11AdOas0hcI0V25DLR8Vz1jCsXWjiSSNxWMleaS7JqJWYHSW+/zgBjM90RpqRvNx9IAinjIVx68OoVBBAqDwqQtJ6leChnRnMJErhPB7lSS9gIdzEs3hnEUOBsukQa7Wda5lijkteJSLnb9G4s/yr/Q/8Xq7a/7F6z1w6995vb/mpODsPcnYfSoTXG6ORWVRMdIyTArrPTMIszwLHiespg9HAkY9NIOdExPwpGMRyF2RBrVtxtTsWB13enQY3pMNwL5dUmTqZ3XM7XVBhwPZ4PhtDh3qk4ARS5zBpTAH974NALMDVmix3QL2RhVj0txcyOi9xNBTDtxvTWd8sk2Ag1UKqMPmo22WAvk9cRlRPv+CC77Cxx8LJlHZD+nQyH6k7xVLid+JI8yEPfa41o5yf+56Q+xzfdTX7iT1yta4fe5i6pbBYsX09dRPxBwVfZOgu9IDwi6VQZHCZJq8cSWdtDMVuuyKaPmOrQxnmAuVc2Wxyd0FDo40wW7OAKIvLiUymTvJnzQ+DkuRRF2VdJD5yEJVtipZGWSHuauMMPxlEWx4s4nL0F5EDsg308TLPvhhzzjsu8wD6cex8DjrMFUpisS4HzW10mVPGMuJJRCQ20E+rS5Bhev9NC3YBNcbbGBumUxA+mMj1Y2vp7ezPNDpWjM5eF8fDaOqyPzAevpFXhF6z/jj+4tJMP2JBaQr5+MsdR5UVeyl87PXcutLBLh6pASd+ZwHryIdcbmWPN6s08Gbnhw9VL+Q+Iuk45JHCei+0htOhsfByBhD1EweQ3UdeRB9dhWTpfyZKsLp2lMNflT0kTWe/sZixMF4uK14jdC93uD+qwzeBE3jRJZMghubeND5dQa16ohF+WXaaPirghvR9Yl4+KmRH+FJWH0yGP8T7/n/nW8xxWD/v/GepZIEEL9qHsl/IPTkB/xwyPM5UT3MonbaE5La6Q5bUgpJ2vNISM7ZTJhr8ZD33gkyWx8SvwYBnDP05/admACHP90naY80YfMYT/wyQgDrtXRIVfdrumFNFtkzeID4PzWsrXzcyPTKW0HG4zK4vlGAXbMV8FD1dKaudQGdqigGpboFmH00Hn++GAumblmwNaGbGCy8S2WqPTGgIAbL1rzgpop2Eb3MbCipjiUh8R/pHHDjfvXloKK2BtjaxnFmXCSO6NpMU0OlcUZ1P91hGouLRFkYpxCM5do7qQY3njje1odlCmHwq58Fp+uOzCXrGfjzsB1sv1EGdq4huHh4EX6fZoWqH/6QiVY8PGMXi6/lVzBu4X+oGI6hXRl6xN/eFUnAT/pxpj23YYo583qZN+xTvENVHLNQ8ItB6bxZMGuxONaFG6GT+6DwLCXD0XUG6NnbSWNTL9BPm87Tl2Q4zn02lUmmodBWFwfSX71wEy8Vn3yaQDqP1RJGcgQM193N6SxZzOxa5wPLTX0wIc8S4zU9cbCOhW/rWTjb1lZ76WoKOIzOgo/NoaAzKQEjVNJgjx8P28rk0I8nC0fd1VF97TCq/SsZkzEOh+3mo7X2Kire2UpHfH3gENFgSTYYlUGT605i4l+Ey4V6fOZ7dWbm05FQZpALeydlQ6lmJk6oL8HCizzc7ZuOm/39Ua1JFZ3FbtRe+WyFzvIsvDBj0flqHv7DC58cVgwvDP4zL3zER3Ws+OHzlxeu/a0UtyuxsKZBlywtEuCRjDCc8PolTcvWAL9VsjDxizkkGBDSxpShbWI85pm+51be2UiMV1TRrPHVtR8PNNA0qzI4bBkJUtd5eCc8EZfYe+GBHdpoErOLO2XuRWeb5cHL+jS8P2Yqxsy4TXresjhFahImWh1xiGZkIGtSFpzgpqCNoBQlJfq4yU4usP6iPtiBEVbb1tIx+fYQFT4TZXa8o9ff+cJX5dMk0LMU9vuUXnzdw4Lu0zgYd1IG04dYvJroCsdX9ZJHzRNw+hQedpf5wfxtxVT0+RQ8ka1LGz8IIHG+G6x/sZLOaNG3P7hLGcqPeENbYA583MRi7Z08kMgqQO6qAdy1j8WXl1hsLS1C01YJ3KykSwYvJEHXUAiKX56KVw5uuFC9uJI+braD/e/uMnHrBaiiX06eJuznnlWaYFPlXfrxTDiMemaKbZXH6La8d+TZRUU4sy6Vts3NB6vNzufli4sR3TRAqqQIa+4LMCPdCF9/5cNguTl4DmSScuig54M3E+mrQm0jMxVKFFcynTcEEPzLh8r5XLMLqCL4w14Mpm7oJdPPsxgXx0Pu/UG6U3iGCm66g/ntdk6yvRT8D6xkWL14uND4mg4Z3aMXWGfMHGd/kf9OirjfqiH5HgTq/rDYtr2RNoTPxjffbGGSsH9VWOgD4HioNFpOsoQ91s4nCAdq7XC5eya0+S2x3/YhHr5+nSf09QG4/Hfpf+TNK23/F29eQ+i7vvTG/OXN72OjQE23iklLyqW3J/Px8TsWJi8ZBfYqTnB65CLSJFoEabl83DhyLLg3xUHh6Vjsix1H9Jb64d4p5xxMz1cxbNvR2n/u9C49L8CXS8VhrFYa1s2bhihdT68fMcDBAXXMNvAAz3JL8HeRQsccHv6pS4TV2nW1v+3cMM1dnB7KyKRO8/NxMOIV7TjoRZ8/cQcmLRITN2iRkYIW0nEmHpq03pKXe+/WDjr0kj1bkrgUuVsOTEYBxm/eSLcMi8ewL0Wo4CvABqGGl9afjjFZfEzVbCHvtFmwSy6C8VpaxOzuAJGUM4EzbWE4+dfj2rTdW+n4yhtU/9gEGNdQiid09NHMaDZV8vEBUS1z9HwShPOuyMHvfSxwCWPpAo9wSL+VgttEI+H2h9cOzm+1YFSwLE6J/Uaci9Lwl/0Osi0lBbfG+eP2/pfcP95aK/0jde7jgVfve7pG+Hn6tQvcIjsJxi/GmlNPY0HurD4umlqE/VrZ6LCjCG72j4aGxEDY3CYBgktBaNjN4uilWaC+bypusYmAnifF8Hx4PFY/cmBSY0QBfzih8qp7tRam+dhU4AAPbAzB2CcYb8wX6qf6O4T3TQRbHwfD8tp4nCH/kyTKj0DjRn+Q8HHmRtx4Svt7ZuI0v1Lo+BKFBS+mUJF5EThdPwzOHMyHRn9tYh0jBtu7rWCsEh8L4zbW5szXxV3Vjxmm3p1p+cXDkRXJyN6Txsr6hP+YVzCq9V95BS9PSuCS8uy/eQXMQRP0vZaLh8pMMWNEGjpLBuDCXAFYl0bBzasunL18EVzb95aMtEKyKykOj9+PQT7vpENVjhsknZrJ9BrHkuEKfLQ7HQETlGbikqHfDjyJa9xCiVhMj56Amk15OPG3Ahq9MydMlwXKNrGYLZePni1WRFZ8OXfYMRO6r7D4rG4ViRUNRZEtsyHUQvj7/Hhwqy7FZ/L6ENEuQEtJUa5zWwcd6xeMn+6qAnmWh950BTM3MwAevePj1JMsxi+3xZKIBGD7fhD9pkQsbklAq6BXXHpoM10wMocaX1enOWpv6XgzJZAZYQe84tXch+ftF/a1FWN3wHYS8tsJRcTamTrWEfSkU2D/szbqta6DbrAcA5Wez2jGuQmkwX0O0yFXiH5ti8gW83WM6qcAXCNctxFBYTC73xPHPnTDu03e6GSchndS/Min5nDol+ODuJ8mBuR2kdsWCnCXdWPudhXjwY8bOO/uDAw5HIkNU6bigfIJWKCVhwXTHlNqmwEbG5yhgbK4aN0psuORE85dbwUfexKhTj4TwnYawd4Dm5mKc5F07jtX7HCSQO7FFLzBsnR9cAlYTM7CG+dZSIj2A4nrE1FuIApFtBJgrzIL21rHo9qlo6Qiehv3yHsKhE6wwCdby0Cl2QaLVU3gXesxh/VD2fDA2x/mvbpIj88qRd7xfPQy54PkLz+QniSGpXt5GPiLMhdaecB4aKPa65mcCDfyv/IumNX/c96F9Mm7/5Z30Wn6hDTl9NL8tAJUfKuJJdn1pEpQT86E58OhqGoqO8oCTr0yhypPAYaIx8Hywifcg/Yv3NrTt0hHRj3zY9ww/DD1Mj2VJgIXT43Hxmcj0VE1AJa0F6BnchhKN5Zg6rVC2PQkF0/7W8CEbDVM7kzEUUMCXDbhGve7hMW7s2dTW21bOPaknz57yOLYqba4y7cIvqUJ605NqPeMMsmWYREYpfmGnNmYhm8VLeAMlQGNl9eZXEstNEl4yhV9TMQR3pLosnkY1l6TxAU7i7HpgDWe1FPH+67CWXCOQZdSZVL5ko/PvnWS4kWfyeqNzVyO/3byMW0e430tkJ5+pIjZGR3kPZcNsg9LIGq6Aj0ov4jRC87FkAWjsVDMiKaBF06aw//nnSHK6fQwcccLQHL5FNLZyULI3mt01EoBXEiTJ5k9pfjBeIgobhkkZzQs6GB2C110r5vOeMbR/UeB3GMGiN7DL5zeUVeYM/sYye5ZQJesfEF1I3RhYzKLed1Cnbw3HV75xeJ+NXeobQil3kbWKDtGgwRsj4RPHTKgXtHHPG68wL1f+5IEzvZFra99tG7JHmrqtIw4tfDQ+XEhBP0WwGLNDEit6LrYwktAt2g98EgGzJjBYp/LANfdHYX+4Sqgv8MZJAI20EGTYtx/PhaUNvFBs1QdHmz0RGzpYRbGeRHqz8fT17PAcnIT3XvoAan6MchUlcfjV1FDCPjyloSYnviPeSntE/+Vl3LHkAcDtv/KS3kUL06bP3vAE6USHNh9lS5usiGFFTqYc08AN0N18YheEXkzLAk/zhwFWbti8bSWCyie41NtO+GsTvrDxHkBJHwJgUcKPPuTz9yozaN0aK6Ph09uLIoOK0GLITtsss3i5tvZw8Eac9x75BBhg4PwWdJR4vxBkco5FcO89+kgr5AF1oNFlF70h+8b5hGJnSw4TEnBXeLC87UmBCaLzIalVa/o1ZNu1GeDJ7xsnAxHpimRJy9OkqeShbB2/Ei0JuvoLbux3J5jLO55wuLv7k+0uzsNe8ZEw491OTB1/Eh69s9kPH77EHkh7HmiO0fRcaEqOK5MAFPfGuDD7xz36XI45ujqktAfLBQeMSPcWwXo+r7qrOlDPvQETiHN8/Pwx28WxRYuIm6DyqTwA4tDwpm6IiEGavsV6dLQeMw2zgI1oRbV3srAMrnTtUeMonG9MwvP/HrIk8ESBPPdZE68NZQGjKbXBdIY6MEDclmSbLyYzeT8us29XiuNLR0sODIshK2WxBdzQlDqVgGsv2INaVvscY20N/y4mYfbjquS2x6NhDe6EPo3CLB8qrZDuZMHfBlthVDJ4jQ6CyLikqDPKYc59CgG/jxNgH3j1lKqFYiPW1xJ7f1JdKleGXwZSocfZ48Sr/N6nE9CLsRtngWSMSxU/9lHFGv52KWYBB9spwGbzYeJbblEfu9C6rrSDZ4bG5BGfyvIVggFM7cg8Jg/7m/eTu//lbdjkvZ/8nYmeZlD+9x/5e14u9jQF6M8hfrXDiX8OxgJFOr02Tw0SxeArWkQfpBeTUvvf+Z2zH5N2v/EwPj6QqEivkJPyA4wHS+NyFO9VFQyMoSOn3X0lNxcRnljGd4W0WZSGvnYY8RiesZqipoZ+HHSHKJ22QzeexXBXOqIf9r94YaNCa13C0XvpSqg0cmDtSsWEH24RJ09i+HbcD7sEmdw6zJnGP7ZDY2eAxNr2kZCzlvA560RUEATcPvqKaRdfBzXOPiSNCgf4Fxsw4jUuVHwLaSNLjHi4UJbE/r8Ch/7tliQ/qSx9NJtOTLpSCmZsM2dnJ0xjAn7YEoVcAboNxwjKyfkYc6tSYQiC0GjvPDB1DK4H/6ZRM1IAr/gXFjaUUO79pvTiukrqVcDH2uVw+HikBOZdyQHjNJMySxRPXDJryEj9ouAaqYLDNcvxY3lwhmeIIkdUxiI1JLBxTc/0wJ7O5QMeU2SGsPQN8iFbhlbTqRMy5l9RzRhPu4k3sXvabxWL5nTVwzczWK4UXyu9rhJJrWcZInhITaovaQE0+ZawsMtBpxynyVZGzYRL+xp5S6pp+Lvg4vIr+sC3DG3AFuOFuH0ozowVUkRRt4pxKDqWbjMLhEDdfNRQiIRE0cnwOLKWXTWDAEuFYRRqwErWpb+loRt1MaF9nu5XtdCKF0U6/BYNB3jmsyxXScTR+7iY4rAEDd+F6HfMv2wViqYRIZ202E3L1FYx0LlmTvkn7ymdf9XXpP64Nq/eU1nKnxoxM1/5TXd2/eF6Mn30gtFd+izsvOk9kIxzE04RJ8a58H4UUWo8dQOy8JLweMui6FbYjDm5T6SlSqKB0Iy4WlVKOlaLYAaTW3cfdsLNcqkQSnCFt7ZzMbpRTk4wScdXcIKyabGieASV4wN481w/20n2BE7ERd9m4b1I2bTX3PHQ9tvc8wpchfqsT/k7Wkn+nyBCXQeygLm6i+qIXqQHGo5RScHJeLWYy9oQEsZlPpkQF2VL10bFU5e3FtMXy/2RRPvJ+RW+HsyvDIDtwSMgiQuA53mKpHSGk+UlefjoY1e+LZEmp4qywbJ24lM7cp88KyOoB/zczGmIhL9rVgsuaED4hl+uNUmDW9eSAMXZxtoE2UhRXQM+ihPwJT+qaS2Wx/ubinE72k22CEdRTZ5eaPBo2zM2a9HKhpUQDHPnfl9wJMKdqRgZLkAntjE4tj1zUSvp49+NuglP0boE6lFitAcvZwmRkfTwideTNi9YpznkA4SCTKQ1REBlkL/5JmQBBd+hkKY4yBJbRdAOLOWyrnZwjXdC3R+rjp05ZfAreoYYlgRBpte8fFzkCqKZ1qApH8cXLMPpzqjE9D2aRU5fGgxY9uuAOv/nOA+3U6EnmGeaJaVh62vJ6KmTBzdPTYFQq+yaG80Hn8xBlDzhWOO+aTgj+Ox8P7qRvrASx2v739LV8z+zohwfNizlY+n1kY41I0/RX2i48nHShOyWpWPx5NyILvf+7/yvk4b/s95X3ci0/GdRc7fvK/p4jbo2JqLBya2MB+NXWH6oRzUkzCE5pUR8Fyo6a/WlqCLqxjOOT0MTC1jYLOMBnZ80sYqUEee7HzStb8Uz+1LxtwTfOR3OcCztxn0zp4IlH4ShgfAGQM+mcPgNxFY8MsI+peYQnHqB27hFWkMG7hLt1bPp5U366iUYgAG9J4nuRtmg2ycGfAyd9J0xRD49dubmo31ojEfWdiu5gdWk54TbYnbZE5UMTwrYeFoZDlxXywLJ3Sc8ZQ+gZyngRA6uQz0SwUAU2ahSGY7kXeqpZppH8li31Q4OlaKnG/xhZuHj3B3ro3Ao1ILaMEAC5df6ePGaQF4f9sM6BlRSzQVrnPO5jNhu1Ei2Aj8YOdzObQoKIVnjD0d91YAsTtTsWXfT7rt1ALiEjkKMhbKovzzanrjrbBXrlLBQzr5eE3fFweWloLsZaG30mbxi3BGv9zfRy/fSAKpVUYgJ6wxfuhCOjLjDB2h3kyDB/iQ7B0Lgr0s7s/4SV4NTIdpZ8dD1+lwsB7eQn02l8KcJ7aY80wR5Jr6id7YXuozbBGReliMKyU20VVr2jn18eFQEP6bVgQ6QvZFDzxwZyqTZ9NBf4/UgHheKtw1SsLG6asYB3UebmmVQ8Wji+jjIwxKTZKg3e/iMOx6DpQKuonmV28ck+IKjZdG4bZP7VxQLh9OCNfBrpWPX+e+pi8SxpAV2ixcCVhMvCXliUPf6dovDyfAkfCM/zUvbotN0L/lxaVuSYBXtmokbbUKWEUkQ+KtMZCQWALvBATmW2ZhXWsensDxGKckBsk3o9GiisX6fmu8NUYL+/ctId1cJGQvmEemKWXgUt0SyNM8Sp8c3ucg0m2HrXoSeEmkCJY+LcVI9QNk06opeKs+FqH0CdWx4UHRjKW0Ku0gE/7mPjGY31yb96kA5jqVYEB5Geo5WcPj5E7m0UEDXKu+mc6/+ZtpvfiEPo2XQt1oAZChA3SS1xaibOmOwTJikDqtFGaMywWdG9Z4bKCecAst8FmlOTQu1CK8Z6YwbPFhzrlFlDatfVQ7600Y7VloiAOft9Can/m4VF0DrBxf0G2tfIjb6EUmf50J305PgTUrhBpAv4XMyy2CjUlrqFQFIcMrBkiYxmyUmVkM1Ru3kpcnN1HNA3ZgPdoNaz6UwaiFmaCRYIvnWgzR5lcEZD0LQr8txaTiQi/V0+0nMZbhINrrAsnD5tNoso2Wa+lA2WJvuD7WGxSjS0BLwxfEV0kiPp8Iw5ZdIf+8v/FYao/LVozH7Ew7SBN+/r35PBm6bI+Db7YRL18WvC7HwdbgVeTZWxYCDqZxk+9nw8yWsdizezENpkFoYpaH44y8UGlJEpR4R0BvHYsvlfNgzrLt9JWMCFgpZ8EvoV6McGKBJx6FISFyZNR5HRhpMwN3c3fIZiOGxo5PxMt9fJgU8JPoaa0igtdGtM9sB6le+4IKDMtg7Vog0r/58N/53Ckr/3PeYEDemtpuf/+/eYOjbPjYIMdCjWUK1LmyYPz6KOeMy8gsH1XInu6OCi0G6KqZCYIbLHpOigajFRJo88MHWxelw37BTlKxwBjVRrGwI3gqprxnwUx2OszaHAuaYqPwyO00Ok2hCN6MHEZ8s3i4oNcE5K/E4L6WjbRnfz3XPm4X9TRPJnOty6B6pAROziiBrM5sqHMYi1a75eGnx33ytjEP+9f5oE50JrZNbSX7hM/RK1znOOsAlL2xi2jcyYR8gyzYey8PJL6VwX7lBDI1TAd9raRRa18s3JTRByeNZLypEoQT+n4xcxKCsX3UaXq9LRbG5uyml1fMhOrncXB0+zpG7RUf5otZ4MDreAiqlQBXGR7csHrF/ZCu4K7us8SjT53o18i39PJ8N6T3z9dOVN1Djkwbuth13BtwOQ91rHNh/dlSdFKbR9qMRLFgvyGEbe2guQrB8KNUBs9sHqBb1QTMjqEQUJoWgWfvVVKuKRBr7tgBb5sILB8qgIE7NfSdcO74GA2S/pTNzKdb4fBG6G2Ujc/Tp6L2KD9KAd5vSCVBK0PhStFpEhmWgdOma4DDgFArGJfA5gRVULDQIydf99BeLzncTFlUrjF0UNWaQG8I1+HZi0TouyHAuQO+zHnNKtp9KoQ+chSDd4XBuLIgGXTmsSA3Ogt25HYS79Jc1Ij0g6zuSBzstGRMdAT4XZWFfsN7pOxxEm7FKtKfZgqtj2uIzfqDNC5XqItSBklfkzSu6v9XXuXcj//Kq3yg5A9hnN3fvMr6LTzo+D2P+An9VtKzHOhgv1C5zqn4MOAesbosgyq/vCiN9GCadNtJB0ah3MBdcrw3Bpua5ImuVDWRSHhEbGdUMcHhC+lZ53AoMRXqKG4WzDl0nZ4ffYh5oWIBLzoSQOeVFvTPMkEnh1bOLGEcLTMV9tYj1XRbsRjONDxaezw0GhcnxcKMYZrIVV4kxm+RiPYbwbvGEtx1xh9Mv7rAjJOPaKKfGobk8GBigQv4+5wl8mtY7DMVp18WGkGESDKYRp0kgUk8HBQ9S5NCefBgdDYoC55y18pjQMW1n3NYnIY/7HsprzWENH05S1/91MS1KzbSsiAFSBR6Efe6Epgaos9cUT1G1l+LgQXa6hiiko6XthXA1cmu5MobPubImaAoKUDj9TXktOMucvnADfpcpwSP/wwE3lQrmGZbLNQpe6j7pztEq2chc2a7B4pNZWGT0iDZE+yFZrdSIPy7BZTanKO+moUQPFMWHzBb7F3PzIbbRQG080EbffRrIyk5n0sHdg2SQsnJZPUbPjR02cOJ8ZTeunGB8Xyxipx+eo6MOzkNjlfqo9jKYjioxEBQUBbufueJ855X0LtJHnhhZiyeXJYDj2/lQIywvppZdTLyaTFev1mKDwTn6f5R1zjZASfc/LCLfprpCvJ1MzB2kwuQnfPJgp8s/v4lgEm/CDy1nQ3tNgLY5KQHH+VGwHTtCUC/NJLwY9F46eNMNOEM0FJ9BvynvNOxF+L+5p0+vpgF7fN1/+adLm2Jw6qlyeg5JwYCUwxgpE40eKwvQNkpkUTUcgGdt+kAmTy6EKPaR8GroCjYas6DkPJs/P3PPlg0kbFLf3HlV3Nx/Jm5JHWjBcwyjwEZs90cP9aT7r93lYiaFZEjVZZYP+cG6aoxhmnWLrB1xcna6khVmHC/iSr5OYKx0B+IC3t1fpADmLVsZiqFfchqpyWJ6oiEThF7dPdZSa6MmkH0Jj0kzLdDZKtcPmjmvSBX82+Q7JdFcPIKiwuDIsjObyqQleaNDe8O0V2HdOjgdD7MlODB4usl8KS+6iJs62LOKbvRkruqmLSdxekqzfTdwDGy6j0fvScGwd3vOsDpZKOs6mzUvyRN3qeaQcLHNLS+ORk2KoXQgAwvatA+GbeZKeGvm1JwobqZxA2Nwe+vivHoTjscX/+ePHIIp6fPmYDE6i7OwO0xU/xGAbKvHqUe++/QTyk/6aaiPOTkCiH3zB+iGHSTBgpYvCOThhNlflP1539I/C5LuKAxCqI9LjM+VZI4bmVl7co9PPSpU8FWvgNGDCvFq4Vu3D7hevn23yQfoobh2GORuG94IXw8Kg1NpBQyZ6ahncFGrvZJFtaskxT6UG0c6cxgwO1knF5tj9/lM/H0u7ekcMMt+udMK1H76Ir7Hx7klN+dJ77rqkjIBg9qNhDKJEbmo7zQB+UduEaaD4dju78AA7LXUuPW8QBNClBudZv8usWDeAMLbFfOw7PaLeSfvNyF/z95uVvdcv8tL7daMAV2pY9EZx0pkLn/luyczeIJ+wga9sPJ4ZJNALMnyRmWqinBeQcBOgr78A1jcyiIK8W5V6+S0MI7xPp4BL1U1ktV9lhAr+piopj7jUqap2D4ZCUH6zETIUhsNf3yuJ0u+i7ARHVj3HG+EIJefGE292qD64aXdOheBiY5XqaHimfjPeNdpIj3jTTEqcJ67a+1bZ2lIFKwmjoii1Nv5WNszT2qPasIDnYmwsCtMtxj0UaKlAPxa85sOlbdih5qqyHS4SzMvm8AMotmMBLxCUK94AwhqyWwOjUa87recCUPLMB8txkqrN5DOu60UbXOqRdt1zzlfq9iYU7VZKJuoQ5rrkbgcMvV3HKdlyS2Trh+Z34THS9nHDo/g8S4lJHehR8oGpuCXOYr8tI9AKrJPObht3V04pjZJGVnFQ35sIM2BV5kjpoH0hPTk2DjICv0Rpoo+eoXOfFnBT1vxYcurkjoP9vp8dBMqC6YQp5u/IeZvoBIO5aiw1wBaMd0cI8SXdAzVhKvG3whRQuf00QFBoJ17XBsszW1zsogCofaibhtHHnod4xc3+IOTs/nMWbbLHG+zQ3u83Up0ljPovnAudp9O2Xw4cU8unNSCnjMKcFjm6LwWhAD76a9ppMKTPD8kpdkdGcrWav9lvkdEE4qpfOx+swY+oXzwxIVRTAJZDF6SIozSRWAlVE+pioRWBeUiwdfvSZOUnw8F6wCKXWlOCd5PPknb/nKy/85b5nbaohg0PQ3b/m42zNa+9sATjfLMn/UtLkXSd6wf4IjjMl3o1NNakj7jzKQ71IFkSRKNrtEAn+SL/1oK6yHUk+4r9FB/CeIgle3B6zbXQizzhs5FIts4Wo1Z+CxcbPJ0LswSHpsh+bX+MDfIQfnNY1g1bp95EU1n+DwVHSp6aB5iUIt9ykEMstVsT8sn5k4NQYrfB/R3GdZVN5PAB+9Cklj7AWyK9cKTprcJfeqneBWcBCslDzNbAzoJLpa2hDxNh3GiI7H24qK1HJMNMoUlkCy8Twy4ocVrBl+kByrHIuOpcEgYfeCsWwqg2/CmVB4MQju9HTSL9pbqZdEIcz8Ew8zDWo5ndpF3LXBrQ7XMpbTN4Uy5N15T5SJKcPu6VmgKulPbZdkA71ziE6+HoNbqr6Ri6EFuLD0n7veCWQ2fwq3rmEGqBSXYeTKDUT36R9qO4yF+1Y7qOhrHqQFD1FL2bEwKPS+U5QjHRRn9FOPRLf/en/deoCPT1SLqH1gFvwoukqW7NpFr4pmoKh1LH5QnQJXj7CQuYtBiQP6xMhsN90ZGw3Jnf2kYmMl4+XJg9zwC+TBd1e6PTqA0KaZ4HbyCLHeUIh5toNkkk4xNJcXgdvcFJz06Dz9IO2NO/1DcB87QG0io0BWrRCudvjDXaurdL+tFTqtsUWrbWcorVJDi8bFzLo5c8lqsQTIXynAgg8emKEVCH/WbeHq1L4TuxEJoCKaAQc5QN264/S/87rzC/5zXrf9Bj6O37fwb173+4HhJEcpFJ6c76F/Xq+ncoEn6ZmubDjo0UizNPQg2jAbYsXzsXHTWIh8HIGDGenYNZdFy5Mz8KjXINm5gGCUqiekDvFBa3kflZDShIZAWTJ5oIX8sSuBzoslaFHjTqdZ95D+MkOcYKiBrdwVsk5Gjea9HqRzFxejh6khjA94w32a8IKKmLLgHsWi3fan9HhjMszpMoeNx13ASayCJAlnWNG+Lq6NOsJUsWzQ+n/Yeu+4mv///x8VaWkqSkbSQklG4zzupahEe2lq773HOU9kbxElOzObl1Wdx01kz8jeEkL2ivA978/l8vvvdzl/nb+ez8e63a/X53qU/mIDex/iL5bqUNWaTMSOP8ez8j6xvy6L2SmDNnZ2XAc7rDWZx/1Jpw2DVoo6592R3mq4ynZUh8PUk7Bb+zdvTdSmH3JP2coyVbq6Qx0Dcjt5c7GMQ8vec52iMTzBrYFHHlVhL5SU+Hoff7a1zosiN7jyil9RXH/vbxbxSaBL1/ew9TE6tKZ/T4hnyTwt2wh3Hv1gak4MZTO8ybZrCua+r+WRAd2oqFVC4WfFpN+VT+kTu/jVfGUqnWnFFpikoT33O2+erotzRgNRNa8b7nsJGN8o4Pn4JCQ4K2DOXEZlDxOoR1Arvz/FGnP9FOig6VnuED5bdMxQj+fONkavh8eYa/c4mqCpQHvGBODtgTDSzdsidQ63hKNqMsn9UacbM22wL7iCW25QwPJXxdiXd0+qJifQ3+P6OH+smJpWzGSmFb1xy7yGR49q4+/qAmjgTx+8k9VRrXkrRGfl40nDuYTOv1agTYPGITExhDem1XDHJRr4337vnocFjOvx/7/fu7F9K7/9qxv7//Z7P3d8NJt96iTP+zwRz2vG46RFHL89ZggLjdeU1btAqrUdQiekux0iTUswNzWcvg8JRLGMXWsDTvKL4p50fVcSmcl/4dcnTqF5/1ki83UmLGTj/fRwX9qoJ9CP+zkorZTVMxnL1j0wJ1eZ539O6k5PR0rI5nxPRD2fRs66hghVu8zmBw9DhaIHpiko4/vH/ogMd6Bj//Kph28oXH4WYZH+NTb4TjRUYkxo4VhNLNXsRZ15Y0UtndZIH1mKnTfckaq5R/T+2BikGg2k/UnmrG2rOS7kDxDtdEmC/63roui5veDimYbj25/zu2G9cFFtKsrneGFCliPLvOOE45OiaeazADJjRsjQ+MwtCgawaf/FUkqlJ74vn8YPlD1s0LwbLy2tF0hrRy/yUc7GqoURSBmQRBGxtljZPZLCrOpZP1cTWt2ehDtmzWydrh+l6lpge5/uuOf6g/1cm0/GsT0g+tQL7eFzObuVid3lA7D0eTB2uqWIRNmlkDpbQH5lISoeB2LG4LNS60ndYNmXMH5QGC2bqw6vJ90RbK9IO4qHUc8bhbg7cCA+f/RB5QAvrM8YJtphYkmL2w3/L1s9FFOw+PNGbhebDNH/vo2c3Q+hp2R1Z44iNqdL6PFTMfjkRJornox3ZemwMy0kq7/WcNw1tGFUQyDqJ05Gbe/n0v31EvhPDyf5jByyrG/l4dcUySD2A1t9RuacD8pFGrvjMGzGZF7p8YxH1I+l0Lb+lHT0OhubZsgipp8RxRoW4++a3yw7I7zhxjENpFzog7zqDzxFTwxJmazNtyJB18aR/0l3bIq1Q5FfAfXIMmQ7t2bSxCGVTKNbOIaXLWnoZ5MFJScJndFQJ51sCZQfZGN92nnGDXNh4lhENmec6e1fEc19n0LdWsyhfNqRJMmdLPCBGfJSFZD73pAycxp5nxB1PIweBp0diejInUpXe2fCVX03q7D8zOQEJ/j1/c2mqheQFU3jB+VNacn2yzxlfxS6nVaiK4+N4LZFnQLWzmX2Wn4IrQhCzvQU9L1pgJM78tD9jQud9u/LD53LgX9AHuYZTyE5k7PSMVcEhEcWIn16CWW+UcdMnWyKNJqAVeY7eHZdMZz/SbCx50ZpmYsbpB3jwRWH0IM1Y5G0OIE+3g9jd4uD4L6tVlQrrObzrDVIqWsKDXw5BGiR0JPnBfiV08Hpnpg0ZnvQjA0STNd9Jn01dCErUS2ilRe6U9q2V7wyOYWUHo/CkmINpCePl5ouUoGupzpXzM7Fqfg/7F/KLm5yyBPLiqYheUsbX2JZRNbtAvlsI9q6uUSkoW2BmUMLaNYpDbr2eZpsrV3m2i/ScOKwBNdqBBr8LQR3m3woX/kQf389jCrXW1DM2xJaUpdMbp/VaMrkX3y8fyyPV9KE/PokTFev5OsODCfzwWLMXKRDy16PpNqT/gjfUMVSuYCo/7bzkOFpMu6Q0Lkr8TSAl9CcVwzjfTWpWDSalzIFftykD+l9LsZ37Qbe/EyTqnYF0f4QX7oa6k3fLEvx1d2fDD8WkZu+QLnF2vzg+Ez6VaWJGWPHUeKuww2q7Rn/+7Y3BPMSWKemYHl+CMxkbbl6o55b2w2C4G9NHT5htNHMh9RkzlyT1p/vlh0n8FkxlRjpsX9yX6Urjwo4GzEKARWuXGoZTnobnKhc2YMPPXSKVUGRntqakTLdZJHfx5Gt31RMNtPC5/dnmLh4FiudWUjG1yR4+zeNnfkt0MbLkTg42p0d1BtNo0IltGjrIWbQ8zL71P2udGLJfXZCS5Y3r/tRnxXK6HO/hTXYveP51TJHqs2iufYRvL9PNM0ze8HPXCiSrW1tUvbUY/7pp0U1VnpQny5BVMxGKY3sj56fBRT3lCMDZSeYrBNgu82dvvdNwoyTsZhbGY98nUzqVfSYf2iMlZ7Tjea8MAUDn4ew33+DoJbbnyK9tPF08TC6eMqbLv8V8G4Ql16r1qWxWgm0L6gYIfr9MEDm2kf153CrTz1wVeba2/qH0vE90+niif5ocT7E7LbY0WZFzoYvLAb8YrCwxpzqpiTBr3MoVl6bK6q91wtr3r/iUfMdcTNjMrZ+dkbz7HEOef/6U/flhaQ1ZwppfhMw4kYpip+HY+LLcHzOSKHYsGHSRzs6mU6vTlafMgjXJ6RAOYNz5eoT3HNGFNQm6uO/TEb/8Rh6YB3BHoolCNF2lZrGb+Cj89OwYaE2lftHo0qWmYfkJ+HBRwnmt+qxjtOyTO0TRKsW6pNCbm9uqZlBC5/E0Y/bOcjL9KZJ8yUUF5dBpsPzqKZLQpE539iZYTF4v7UIL4cMJmWD5byv7V6R4nV9jNGNw4CKAooWG2HY9AK07EujXhOuSIVxWWh/kE7fy91RH94bxivz8bw1FL/2ZpCjrw/ULyYhWceA3npGoqlDIMVdXTzKuxibqp3Yrm85mPIhm8Z5ylPvFduZt4yJbqUK+LLGFFG2pdTa6Ul299ypLsQAPTYckK7UEmiMjIEfr0yGWZQtrvp0sZ9Lc3FG9700yLGARV+4z7+LxfBedoE3ezvTiRtr2ED3weznRgNSL55EleNLKDtiuOiYfwYCThTC/zdRrkY+NUaPRqWcPe5Ch4adyMQkflzaQrvYj1cCDg7NhlGbAS5451F2Yi6NMynBpNhgaF7YzqdOnAqDogKMPyCGXM9S+j5AQoNrp0E1LYbtKJrIb2geYvlcRA7mA8i5dS0zNA6jKp/b7OK9CHRfZ0hb797mtecKadGMcDx8kyLrVzD7qbYIdJCjqiYzDJg9iR70TWETao2AMYbwPTiFHheNpjlnYpF0QUInZheRcmcRLp88yW2e9aE91wdjjbwG/jU60vSUHPj8Dcaz1QWQdBjRlFgBq1vsaF9FKGY4uuNkWjvb6b7SoV69hI5JFrIG9YMNxU9KoKV5hjvFpdA9pWIMqErmC9Zb0U6roXBV7o7GpmH0rr6Su573hlltN5xLKYGPcim2XZpKWpU9sdmrhj8PI46fEjK5bo8elyWUkLqZIWcojRYesqaVb9iWLIG+Xw1C95MC7n48w1LelNKIDwIhPoHMB40gx6bFDn+CiuhP/Ur+3vwm7zpUglNHh6K7iphSWo+xC1EVPO7fOFgd1oJLjS/J7xdDun8SydeLcbOpkNyaYpjt6FBS2pvFpFI7JKQOpEOdQ0kUMZLuKpYxm5IwWMxxhOHkkywctjgaKKHnyolMPW4C3beZy670u82erjKhZA1rSvoSRVd3NrFlDsYYfDiG107qx1aOt6b2lW4YXllCmgN6Yu9zCRSUb/PG38u5391xmFyhR/11zzGN/fmUljCUBeRIECwYk15cOppTaqXLLilS2LWhOFpnScPlsijdT6BzfWOwsnQFP3Y9F7/WEL1V2SeaGllAmVmlOO2uTmqnjGE11g+/XJy5ODcbYx1GwutKPjf3UWaVOwQUysZNunwCWTr4cq3kCHqxI5FXfBpNv+2+MPGtl+yg/DBKHhNg31i2hrf5qosCROMwNiEbu0IEDHkq0LwOc7w6Lcb5D07kU5dDl6PlsephCXugOpJWfxLg6DoM/4z9KXzdJZ5RLuPcF31wQ+SG/RcXs9AuAf6HV0i73BPI/2MG6ZRLKLTGCZcPL+XfryThTGQ3algwjBLOp5GH1x8eubaTVxj2xzqL6yIKi6baLoE0+6nRu4ieaAvUZTH7JZT5NgU2CnbwuV4n3fhMjmrPD8dH63BW/d9IOP20x+NOI4wPi6WDW41w/anMAS9kU1XeiYblsz1wqD2HqvsKSCuQ+eSDQtq7TcDI3iNoRoUlQr44Ya5eHNm9/czf8zDy7DDE0F0CpVVkkVRPnT57J+PF0Qga4FIEzfG5yFLtjevzn7CN7iOgerEUi9p98UkpjsXL+kf14RDRfMM4kg8QsPaTEr2WT8WW+8ekn/8S/VIIhcoJC/jNHgSb533p94IR5F16j6vGRqFB5hf+f9Qgsxp0ac5k+W0S2nFpv9TqVwkdFPmTnGY/OuRrAqfTXnzY7Exa+lOMCRdH4EFxDyoLL0D4qRKmMPM0/9mSis2yOj4gRIJhe0bgv1t2wIsY0rG5w+7mneV17lvZiU37HRx2HmH5SiNp8gyBfFXFuJsWTQ6z7/Nxh8fzFeON2LeDYlozqxQV9vlMdV2nwwidRCbx2y3tqnaic6q5uLbTmComj8SONgPKPCVAvpUQ3O0TD/1gRzva0knlQynm+EZBSX0HmyoKg/eoAw7bupKY2ukSkr+hS/PeybJ+4UhZHdrAj+QKMAz8wD8qlvM5IwrpMCljqXopdTe/xoMa3BF4MAwPOh349r3yJF2dSVrWmqTds5ytvjgSYRfyIN2Zi8HahfTyxy4+R7KFnVowBvI6JThbnM/Ke8j8WrvN4cfn0zy0bALtNDeiCtV8zF84mgqVLGl3ykT0UUtBdKmHw6s4Fc6jBmFPr0AS/hVRTnd3MhkTjdjSGJo+Wx3fM1JJdYIRO542SaSdnIeJky1R2yoVvd8XgSmzM6Gjznj2Ugk69tWx25cL8fmPQIusO1lslT/dejYee6wFKltygq+/kE7BNqHs7wJLGmc4nbYd8MKqQ6rEXTL4neg2riMuRdOMAoqY50HSV/NF03BAahw8DZ1fC+lHsxvSfIYiTsOZai5YYmdQLIUlmUDz9wiy9OgjDdvQg96dNRcd6ZePloAyZubTDxVqyTAMzkS0XwhdWy7GvifycE/IwOvGMaR6rwdzz9JAWWwk9Az3SDXuCGSoW0i5Fbl0o/oc58MzSeO1B84GiGHsOYwGiFPR9L2IakdosOJ/Y1CzmHBszhd+23wh1+2ryNvvaJBd4zPe8DMbByIioVhfhBnXtNgYQcCz36fZ/sn9sf3rM2nWUU/64TCWNpgW4sXkRq79V0LBTgtFKSVNfFaFO+Y4ZVGNiRh7XlvTg6MZ1NSQj/WbNomyHr+TjjyVgBP9EhFRPBaxRvu5+MJivu9uDT97abXD1W6l+PnJmczTMjD8uBXGyE/mM2dPoKnpMUgbk8Hn5V5lFzaNIpvKaaIDx8bSkFodWu86Xmo3V1a7L/1uGKg1EvoNn1j0zDTqWHVdetKrgd887sV1Eibh4w0FGdt50bd5b9hz0TTq+DoWLa8EEjRmiSIqx9HokBHs5GqBTarPw3IZZ86e04uuzPKmyiW5aHoxCKMaJuDCEwnVjJX54opwaFuPo1vmShQbacnubr/EVhn+EPkMzCGTh+ukqtOS6LJVJsU2CnidPx7r5dp5zopUOHYk0OsL49iNJdsbhmWMQ3y3C7z+xhJmOL4UPz6UixJuTCLpdTHpDUrAaO1cerMvjGp+zGY32Dy+wFbAigEa9KUykoZXRZPRHtnxbUrge+s3i38wW3Ts6QI20Pq5dNOrXXzaxVh6NUODfMzb+LKlsrzMypfOf1oo8+5yUW9NhrilflBqH4f9ymNh6KdDV1JyCT+jmf70OHS7FgK9yGh01EZLM0e0StWqRlD3Vzpsu3EIfL8+5+4WF3nJPUgfGo+nW5ds2ZxOc3o6XQ71NaPY19WhmK5nSaYPIinujgmOOwyhLRs2c8WZB1mqmzGmqAbima1Ae6ekUmX0eBhNTqbqdYw2fzFE/uZ69svxT4PRHk26YBeO5bolsHiYgxeNMn5xUKT5Mxq5/34Bew6L+b94N7CN4+mTxjQ62jWdMqIT6e6/Gra+tx/1XpACi7khVP8tFnv/5tMDryG49WIUVb+rkKa3R8vWUiDGl5lzx6kMRy4MEI1rSyUff1d28+ciFvY9hSaumsRTjgRhS9Qj/mWzOUKPS9mbz340NzWXvdzRC9UV7uTzIpvq5jGK+DUVdXL5dOi8gJlJb1n/c5bs7y+Zqzb708lfQWxJwgD2+H/fhGxMpck/etKKnGcN15t3smPmDjA8xTBnQRZOOZRS+q3zbMTFFnalyAGLQiJ599s+mN7ThVRHNbOzhX1p+vESKlrrTFt7/GA//wjQvjuTp3QxOhNdxmfOEoj8ppLbDimrX1eAO78T0Q+edOuqQPVPAzFo0z1uIGP0Vq006q0SRv2sNrPI4YWyeUzotd2EHCeX0qwWF5iHqdBRqw284rou2a32wOSR3vS/6zfncwRMGlVE02RM01v2f4LyFPLttEBTJtEFJwlV98unCVeecc093cll6BA6kmpOZkFuNCChnCnVBcL5dW8svppA1ssKySw3nAwG+EN8fxu/Wkso2fmO2cqlkdHdIqoK6mBRnhJq7xkBoz8u6Nt1nFvvv86u+H5llmVTaFDWNLr68DcTbZvL5w8opdMviPb7f+bCkCiaNUKD91o4mAL8nrH57ev5+nInepkYio0zDFnCX4EeFZnSSgVj6rFyfsPXmQKVrB6AzYaOiFwngaSZ0bhraXBcrUuflslDb7mENqky2I6W4NNtc/wx/sW6TfvNSh+fZPWTFXCl4RTHj/M8MdiRmscYQ2JjQSrftWjIIRF1axdQ6nFNalxkR+unjEWgRgF91e2F8bSWzbNeKWpcFwVH2W+NhkD9yh0xcNUCJl6pQ2oBphSpMIaOdPbB2Wc5CLw+Fq829+Liszl4L2OdQWMvirxP5vGTVwVEVofT62lqOHzJkZ6uyqaAyC52rCsDB15b0Pmd59nh5i72VHU0/Xe6G+1XGEJXNj3hSWMeSdc69YL58iBUbnJleVqf+TgFJySoTSH1lKWs2fmzVHeKJ0XFltKPa0RikSVaI9Lo+4+n0pTlWqR5aQN/csMZrW9f8cyzYiT+FmhO3CS6r+0jvbzGiv2zkdDlqkmknDicUlOnQsXfgib65VPxrja+Mk3mT8I/ftcmDdvNi0R/A1qZ18Eu3rRuEvo0hiB74WzRNvND/Ggqo/kjZvNhfrf59RgT3HynSx03wmhIcBPbWWOOiBQBQrMa2c6yoaMTe+PdzUk07VciQv1HS3u/dIRRfRgtupGDoJVFZBtgRE3qFxqIC3h/YiDmfZQ5nUMh9jTE01z5eCwqE+iPkis2mI7gcqMW87RWb5LXC6R/0RNR3G+RwzqbgSi+Pxhl07ypObD0/+5da9UHY4WhQDp/uvGp6i38u5MrrXUfTz2Pn+QtG6VszkdP6mfzVdQWXoL0HvqknVOE3Ev1/MObHXxWXRHi/hohztGH1lwpJsUmPdwrdMXiFcU06tdnXpMoRs7dRIqLjKfapLNM+3JfuEQlQl3fgy8ysqAPPwVkRnEuP+8my0vqju21YqrZ70qOOqeZvJUG4jMaRaGBMVSeHELdJb4UWqhJh2a85bP1R5M4pojcJpVgrHi+dN2529JiWd96uR7kQS9dseWBCryUGfafjaNDruHU1PSfw7tPtijSVoOSqy2ruVgE66hF7PutFDh+K2YP1fWg6plGbhkm+GLgRn9l5+U0ywsXdhohuyxblrX+oDYFJO7rTVNfMtLVzqRlP8ag3dsXp2IVyS4pBYbP4kWxa4qo0M8NW1S0cH5eBIo69GA1Kp3ktVJg3y+LblsoYE6UQJZ6/aA+Ow+xVy2Rtt+FWt6LMbA6H1ELlvC2BW7UZHKEVb9PxveCFNrpkUT5d4bx5hFKbOXabjRljz/1PjBXRPVh3GBMT1gdSaNu8ydDb646Peg2HvYn3RDXUUiTdUtRctdUZLCliAq6MfolY3+R5k8eHrCZnd7XycqGO9Is+b18tMxxY45znpLTj0ykuVT83Y3OJohIkiHAxUDmzIvy6VfPAtrmtoR5HH/Plkyt5SmB0Sh7mEnfslX47mGF5OSqx+6k5+BSH1/eouCJGSv68JtnBPQ5fpv1fZhN9U+q2BPbaOru0sZPiYLg/r/nb74MZr3dloryFwfTKPnBNMYj3OF+uQuO2XpS0dk4TOwnxulDbjR1sIQWtBpB9Y4R/G4+YFFyAuYtd8ejp+5wuruIjag3Ip1NGfTpURadWZ5Os9I8sX9xBJ5nFWBLlTvcPEtpTG8J3p0wxPkRN3jaEXssX7CdJUenYNTQ9+wbl7LBjqUUEzoAZTofpTabPcnQe6t07YIojPau5BOeFNFUesTdKY6Us86z8VXD+F6vArZEdEnad7ARPVxSIpqW7EzbM1XgqOONqy2eOJDtSpl9/eCfJ9Dbr7KclvQki1Uz2It6zk3PfGEePe0hrvMlg1euNFZ7Bo9fuIs5PCsgndA48lb0ogOPc0ndZAJyrSz5qmSBrnwT6NAJmdu2/MdOTE2mIz1LaUX/OAqaHUtf5/bGiscu/PjetdK1DjbSOnsvhH5aw+a+H4m5L5Ixv05MG/TU6G7IULbjbi2PCSokv/YUFJwwEX0IK6CqhRMhv8MMzckpMCkMgUqGF21OGQjle3+4hV5/UtGTsfeNy1z7ajqNvbiJ2ZoY0odHfejNgxDs2iJr2+h0DI7MwTYtC5oeE0aTtnjhhauAayMu834LhtBdGbu3nSrFznxlVJ9fze+tumLvX36gYeWbcPz7VUrf5wrw0iqGu8cvti/gk0PCLS9qTf7HrfQr+eTkeKxoL6RZJ8zRFjAGXWsWs4G1+sio7onyMcsc/szKoNjmAvx9Yw85XgLH4jhkKnojxPUJo1IXrnlPSyS/pISUlsxnaywC6X7QLP6yvAAZiQmkyKJJ8WoQzWgJwrZeq/jYQm1mYe9L0+29qXaSmJxOG4tEZfky3hbTsxm59MKmjr3wzmRKS1NkYyeGpU4g9bIdR0sODUGOVhbzbo6hmc6fRbbp3hjqMIK0VfXY065OrlGSRVs+ONOIBaXYWxXKJfGdXBw9gWyDq6WTF6qRvWM9V/g2in59G4dwqzQsNwugjqNJfEHGLPa+2R7B4Xm0Wk+Oix6PFNldnMSOm/RnI74u55+/JkI8D6xaL18q2ZpPHqX3ueTjJPoxLQCdX9KR4HWM/bQMoPYdhaQekoB3PQvpiZE1X1hpSyazV/Bds6eiV8VoViWdIcveAuo2XUI+uc5s4sIA+hTgiyel6qSZvp8plCXQddtpMJ34nGtbDGBHjknIa+UE8rRQYvOXdzUcu96b9qna0OsnIfz0hhwU1wWgQUvPYeX8YaQ0XYJ9Van0eEFwQ780d3wapMjXfA+mtCYxvE+l04HILD761ERqzCtlH/uH4t+1cbT6TQAZ9wigkA12dPnOH5GqYRbdWRFA+gYu1LT0pUjV/gCb7JZAYkcX+pP2kD/0S6eXCUu562dnSgmby6r6B+KNTRrtnjmFLhRlkMgglL6ZJ1Ou3Sc2zblBOiG/mi8L7Yki64m823R9jF8skOobW/Rbd4ItbyhE2eRAumIexNZ1hdI0RUdUToui4hcD2PoreTBtXSEKeeWLf19ONHTyHGivcEJcfQEtSi/Cq9MCrB3Myb84EGP+s8J171wK+vyeb1BOIN8ywuLmtdKGNk/p3AoB3Q1kWT3UDueeRUJBP4heO+fiW//lrPfEWqZcG0iShaWYalZCvYcUknbiFHJJzceAc2LKePjN4dZvR2T2TIXLEoEt35SL8zsHU/BtC6zy9SEPIzFq7wXi3f7b/IqKQDuiRA7T6v1o16IKPjamlj/YptuwK3EiF3eWUPnLAkp5I4HVyeOsW4Q3X9LpiVFSfbLqHUSWV+Vo9QeB+Jwi/LcohOZsNKYfe0LQJnZosHvRxrrdtMPmZZbU6CnB/V3TRU//1rCrN9fwwctb2LrxQTi1RiD7XQU0OfkF67/Unj5zMf2aaknBRnKi1XIJxIbns8DZ67h9SzZUA8v4oFlO9FvmrI+cZL6x8X/3MfRJMjKDlNqcaVWrM90edo+l5mtSnV6wtMSomMp0d7AmpxLk/AzD4U9KeF5fQqc1DnG5BBVKvByK4X838cfPY8hnlCc+woN0UxSxZ0sKhpSsFp21H0pnuWwuTh8KSTeGTdWTYLc1kwpnT8IMy+9sbGcoHRmbj8TXAfTB+CzfVG9KhTI3OZUzkT+edlI651Vf0dseAlVYnWC3d/ZF+vV06LkbkfzfYDr1T8Ab4zCMy3DBq8WWpNYpobauUbTPRgXDZP7Uu+aN6I2fDjxHRyPuoJg0Gpt5/UYB6o+NmE91Mtu65hOb7RtGciMvijz9+9NK5618zuHeNO3HdeYSMp2bJinxjuw8bLXoZOVb9Mg1O4S/jynAA+cTXP2yBOEiZZIXwjDw4XRaPlmO9BOD4WakTUEGYgp6lAhP64EsY1Y6JvVaIOrX2MIb0K1hwdtA5Jxww0uzm2yFsoCU3WH0oyaG7JNvsd0YR9njDGlxLzPyVtCjKWY2vF+xHPz8+8HZWUD4ocnsdexghC0SYxC3oz2Xr7Hjd8LwJatT9Pi6K+sa2INEZ9rY1fbF/LK9OSWruLObZ76JDEZmkoOZCrxjLrKvBse5/eJUGH+JQuWW/vRWLpyerHBE6Y7DojP3cvHx52/mZZ6HIosIPF4bzv9q2pCZjDu3rvnAan5pkoZNqNQmuT92JQ3lYxtKIW8VjrSzBfh5cldDeHq2NPW8IgUWTsCT1PV8k1c2+3hdTPfarKnbd9l8+2hHxv/KWJzee+n8rFJa+3wPkw8Lp94Oe7iivDw3/VFC0SeOMEFbhf1dLaGbuWW8j3w4Hf3qzlc0/2Up7SH0n4YKjRl+lXXZfuF+I4ZT+7xw/GnvSyoTsnib71W+SOEK0/NOpKWrmqUBh5ew59846xfUwKdflUfW3CzameNKKYu9aKLaSOzuU4Qjh8Op/YFAnWWvecw5Q3y80coOygfTjSs52Hp6Df8TspjrzfGm4LMCSmRt8fXIIM2HEqq/kkS7ei9gxo1RKH+kSI/dVKBdnEjHm8rYUeVl9uuc/7DvzzazoghPLG8U8C7rBVd1EijroEA//yRQS0ud6MsGWwp+E0UOty/xDzuH4H3+JLbnTTVbdC0QtJEo+f1u3l/mgm/2hNHMFRqUeD6P6pUKyIBpUMCDQnRXzkW2VjSahBLAcBxCZOdoe+Aws2FJkKycij8XjjBvrUlwNGrjvY5KKFopmsysAmnjn5WiHyq9ydHKlmU5RJPKNxtYLQlFn87FzP7BIOb3XpsPfHiLb9sP7uQay+9PaZWunnyBGQkudHTaICre+5w3zR5EBuqTKCAuGqaPiH2fk4gpeyPoerAvC+jhRLtrishkxgXmdvkOO0dD6NCGAdi+xJDcq4+wDT8C8LUsExtHpWLg0mhye51MbxuzMUzFEha1M9mGy3nk3D7LwXXATb5+fgxFPMpBn3OlmFDbykJPTxSF7hVw59D+hmUzHomuHY1GCQXTuCHFMCpRQeSnKFGlzQqW51OCeScfspOHSqjmTz7d+7WHvxLPYJmr5/Kjv6OxYXsubcjwodgn0fRLX0Gk+1XAELks+j7llmhksAmGfzrLUj1e8k4bc+q+0Rz9h5jR5WomehWigvgLvTDYpi+uesbiSa8YKL7wRPe3YrxS0eR7lxqwNRliHAj2pU2bPjCzrgWiD0EDmGdHEYUdM+J5ta5IjXnZcOCWlqzf1ERkFUNFClnY+KIQU+vS8L7dSSqndIu/N7TDrx8/+d6XanAZ38w2LljONJWP89DkDCRLvzMLSR5t+y+cZgfHICl+Btf5mAntodrUp+aN3Sp/JVqxXQEd/bqTi24Onj4PxG4NG5w8pYTaIwJu/wil5tRh0iVWoTgwI4buXjOl/MhAzDc3pdXLvEX/22/KzjKLNu5SgL5zFv78E4g556NHbxvUf8nHtb8CNV9IR8yvcOnJnTFYqSkh1y8ToaIfg9fv9XjHVCUaYt6b9RugRCWDldB6LBD9bBK42foAjOoWiJ/zwumP91BSmxlPriEJtP1NOyuxHg2L/vm8rvtVabTeLb7VW0DXUjV8FI6Jfu+4ziqXMjrVJx1FOoOwZnAXM38qTzmt3tgnJGCH8je+JsAQ72Xn9U63kh/JFKP87BJm9EmTdkaNJNFyQ/amMYsmBpQiUO8wv7WlrGHb9yKEJPXgv7YmkEeTBK+/3uZDTyTzH+cEPtrXBCP1o2n4WF08qC2l1nvmeLhwOJvXLPNUHyvpX1MFGH98JupSS6fhFxNg32sozIJreI+sWHxs9ORwXsHK+gzFpgJ92iKrDRGaBVQyzg2xEYU4P3wB01oYhdRTCfS8dQImfEigzYsXM+sNobynvA8tCLrAb17NoxlzDfFux0D8ir3C3M7kwDktFeZz4rGnSJUmXhcQtWkU3m2cwkXaiejSyCGvdw+kf8dK2cyfB/nQP44yNosjzzojGmFfzDf2GEJHKp9Kiw9NRt0MNyrcFoQf89/w7v65NMA2kR4betAnT7HdRlm/pVus5VWqOtxXZwYbcs4EwWvdsejuU2noKB86PGYYhrRmkL/1a750rJhmjxiL5+GJ8JcmSQd9S2MNVqdF5UU/eb5TEg41CMhptKBMWY1eeKCNfx8RTzYO3/jAMwLVHM4X7Ts3jrc+WsgXzUqkSbLsmWm/h+3/nUVeI59zdikQ4nBjNny9JZ59nIyOljCy3OQrSluxkodU5tPhA2pkWxUJmBZT/x2JmBWQiOC8J6y4pxJGB13iI/+V22cdTaeYrNFkUTCTa3ABkbnT4Bm5T9oxIYAW746nnRqMskpMseRSIg3TNOF3eypSm8UobBQ08UExmFbtVcaVUeMwMeYCGxfkTb1YJOx+/xQ9UhxILYMF8h9+kS2yOMrbPiTiT4gt7Z47lr6PC6eH3RVxdkoibTIZA42bdmR1J5L6N9VzF/c2+89FW1lE7iFmHORDt2wF3CwS0xDtJEqjYpr40o1M15Qzuem/uOU3ZZbrFUhtyQxRPcS0xNyDX9eNov0BPWjPfXPRz6s3WfFBMat42g9zElIhtT3JIl8Hk4tGCeLHm6BYPAHX6k6wYx1OdEHeigINR9PXhUFk+nA4BdxawJ2W3JKmBQfSwat3ee8FqfRyqBpOhMVTieEdVicZgN6meWhdWQqj0Il49/y8aMalIpx4N0eq9M+Llp5ShcYnXUyocMfBVIGya1OxSsZrG78U0bFlbrQ3Rgvr7y/ms75Zi+7tdCOHIEUc08ykxpYE3BElk/YxNxycloLnN3bykj69gEupNP+8BiYXSSimwR1RlnYoVTWj3d8LaP0bD/yekwkFnX4o0vdCWFEx2YVkouWLGD/+9377pF/847tUKCgdZ//bW/lM7U3+1W8UTPTFtKfuOX+m40UGF3IgWT5XdFf5n+jH/65D3xBgtmkyelpu5lofBeqplkaFbkWkXyahvPQSeqFkirevr7MaF3eYDvbFmFPq2HWkk6mcSyPn/cXkYJ+Po93Bq5SnoCWvg3ezTINOuANZ2GZR8B5n6eJRHphsr4JRk1xol2EAuRpki5yDozHt6kRauyOZzGb743utLrto4E/K1qp47JVG2/bqcKvZ8RR+IA/WU5wRL+vPe2c/su3ywfixwpQ+/JZQ+XY7blTpTUa/B6CgbRAFbJ/LxtyS0ObMNFTnxCNLbxrRqH9M770dEmIV0fokD8LFEApwLSHdpe6QtGdDZcMIuvthP7coz6A/MwWsm32Ju61Io+nnwxFl6E6DUhXJRCMEw8bcY6/OVkovzQhHrBBLurO2sx+d9tAZL0dKCTrSB5NKqbH+FAuoGYibB9NwuFul1OTsWNpQmS3DED/8hZgKZQ6y3XA6tW95wOQiC5j60DwWl7qdFSQvZo/KnGkjHMkxpJQm3Eij3CQ3KvdTpGj1/6RPb6ZAUmVMJx/X8oB10VA+Fcr/KtrD2q2AWkf8EWk9VaOvLbtY3+pwjC7eyU3PZ8I7OZcC9J4yF3Mxjdoej3NrK/jThSE4qRhH1ttcEN+rlFxmjiJv01rpqkXulO/swq9vceXBL0eguS2Tev9dxQxf7mHdpSd5W/t0JKXn47DYktpCExB7R8ClVymoiVvDmypX8MmFmZSywAzPeklo06xCipDLguKmQVgVkMFetHtB9UIutCSE3Y6aVFaRRE/Xe6H2fh8qrytGYNoX/vOWQA99c7Bf+oTPWFHErw3KIgMzAa4f+tnn/nOi5quZOCJnyBwbUjBgfwoWLF7OO7QONCSmDYNamjGYUT6d1C+G87sx0LG1x1CWhdsNXbxC/560sKadHaRUmIxMptXZLtz5SBp5XKhjlzf0RMPw/5iNxmQE+PnTdQUH1A+Oo4eq2RQWmkVpqs+ZsUMI73+/EBctSlC6JIBYp0AjN2fA+mMoDUpOoPM+YllOxmPC0QHk7reKd3UoMuUuS15YmIVOiZim79nCLxSI2ZGMQiwvC2n4MC2KB6VlUb6OhH6cEEOroj/ligpR7LGf3S7KZFHWJTBe4IrCiizK1BtOVZ69YGAbTXL98/BXNA16pxIhGZqDuRvG0NvICdT34TXucVmA3sc+onV3R1Fs2D82xS2Fwg5lYfHxDB6iZoi7a8zwRubY+wIT0MNOj5pP5lLKZTlW9W45XyZzLfOZpaj+u5S/rMqnX22m9OzlRenQa1m0eUYG6j5Zo3q1JgJzxRB19uB3lcypSZSP8kXdUbdoBZ8YrwaDzHSs6eqDypwCersoFXryAbj2Ngs8xwdnT0zEr9R0slkk4PhXZ/rrEQGHrQVkVJ+GpxUTSHrrDXswIhC7IibjlJkNbUork+4UydhWMZv+u1vssPlPHJLnKbHgLgELJLlk/nEdj/5QiNQ++fjcJIbJa9m8OW0D545MtKuJ2ZkhBfTgm3FD89BseD0MoITmTPQ5IMHYawKuiuezlvUS2qFfTCHGuqialEB/6rrT2Y3KiHwm4I/+Uv7ikRqKH0Wg5Xgu2W5MpdBrhch/u4u9lhMQD3O8Wn9T2seqBIV/2h2sg3uiz8weqOh1gn89nYdnV20hVtGHSuc0yr2Ri12DfFjSEzFGXAqkjrVibG0sxbUTOWg0LaX3VUMoPruuYfkTASeuGPOD5X7YGlcE76vZFLlzpujn21ya1DwKUz/LOKE6EoWlOdhy+jLPO/KHWcmJ4VRbRBMHqtN6h3buaz5SZLTDACe8cmHoU1f/cVYCIhXysNqtFFljilHos4Av0yrAqIFqdC6HqKhJTHKdCfSjVyrGhfShlf9KWcKZPXzjFwsMepwFZjZVltF5tODZTb7pcTY+3BXomUYxdn0TaFbdVDQmSWDb1s4u/iwhrx8SUrjxjAV8/9pQ99ObfwoZQBb7lNmdcXlwG6RA13Ynonv7VL46rxRDC3uSUtwt/vOzhHy1RXzqdTtYn7rClvEBtGb7Mtbjvxo2MEuFPnjLxtI7j+7uEai+PhSv9yRglEiCpVMf8WGHiuh9hID7CwmvDPKZgqCP97ecqc1FnSjRiXSfFeBcuBM9SspDYHZ/UogYh+XOg2htggBLJQl6Zec5WBwQyClbwG/DHczSSozJS6No9KrJdOC/ErLZtVZkdi6AHZmVR0vf1vCH74fgfooBdblGoOCREb66lMC6TYBVP3+MaXTHC4VDPHlLHl1QyqJDBzNJKSWF+hnnI7M6D6c352PTKhX8dzQJgmMsrryvYs+TzrGHshrtKWzkLnICvdY3oczhAv1pFWh2ghHZalnAeJ4N9fkvjz5kOiPCuJ33PWrHn+5OgmrvAvId7UsNvwT0HrSH5Y6KRrmkmDw+i+ljdB6pFw9lk6b84u0Ka9iKy3koyr8tGrclm5tMzqVzc9Nh8VtErR52iJR5/bTTQSQd2sl+7a1g7FEWLb/uSyN/Heft0hIasa0Y31/k0cc1UZSQosD3NMph9NVsbP8+jO9JV8DWTQJq30vo+rAFomPDrCHJCqW3q/Xp7SY3/MzT5n2PDaM5OoUweR9NJ7WONLxLGQuJkA/jvikwm5hFw90FuqxiT+KgfhD/zkf8VnsawGpZ6xyZnzTHYPD8OqZqUUjFlh2iyFsHRC7J02j4KUKWhz+ShymyhbckMD4/hBVbZlL29iz+tVmLGpUuOvhXp6Li2zj6clGMIsdCeFc4wUh9M9u6axPr9dAVKxuPONzwEFDjKqGugm/c40QR1Bod6GHoJVZoW86PLugPX/F9Jt+kTecCCuldVAHyzrSwU0MEeBV74VB6COW7LGVm68SIO5YMhVZrKtuUTd+apjCj6Rrw9/0g/RApwSbFR+xnciG0/HZxdPUgyfMBzKsyEF9C4ymtK4oOt5TS/f6FGN/mzq8NHssqtpXUS+I88Lj1Cjv0KISFnxGgMqOQMk31qFO5P/2tSqWNo8IR8O2L6ILTUAR/KoFf7ABkRzTwgvmTsCs1kG8WZ8E1z4vs9UNpr4YCyVUUovqKQILlRFptMBErP6ihV+JE9PYzRMi3Ynq+x6g+Zp435WSlwfTpNT64Q0DAXQm1f3JGXdBf9mhXIfVe1otWHg2gDT2e8RXW/TDkeg7ERXF0634RtkcZkLRRlk0d36Trburgi3seatUSyWT9CW7yVMBaXojhtVf5/BMxVPOxiG73NYKz4WzeqiowbC2k1OYCaq8No8AtXrTWS4RnG3yxNUKDnp4W6KtnO7O7WUhh50swRC2D7mRVN7isHo7WtOG0BQJCgwpwcHgKWRq08bwgWYYfDkOijz6ejq8QPe63nA/P1CTpq0LUPxwN+/ICOrKzAH9qx6BPTCnpTh/GHr3Jo8YjP5hLaqu0c4ZYJNqVDdWqXXxd/HP+ytcHSVUSGHUV0uF3E5nQIVDBlM8s2ouBrT3P7lanE4vJRbCtIb81YQgpmAXjn6UEz/TR8OZZAMZ3elLOTbBU9SKIZqfQAm2BlpxhpB7ZyFk3dRS8UEbDmWyqnWCH8sc5GL1oOvr3EDD4WTkL2y2gtWUNy80bTGMeFpNcShCdtSuhlXW++JJ2l198LcC70wZ7FbPQdU2Cqq58SloVW7+vQ5b7OzWoUzUG/14JePm5BJUdxehx/67D9ZRcit16jMv/eM9HLpDH5oJA6jkqg25nB6PPOnMEX4kkxajpCBszhVrf9MJh9UPMKG8de/y3mJoueaHNPIzKTSRknt8Dc6OesX2OJ9hflzRoz9/HfdwGsF2/gqmnaALOvs+iCX5vG1pmT6Lt1ZbUU70ElkImBh6fQqZ/D3HTQapw95LQ/A2l0LZNodVT17NiamaBnbFS8SV9FH0VyKzwOzv2JxiV7fnQGFxCR5qLuf9DOwod7I+2bF0MMxxC1YvHicZqJYFy/KhnWyCU9sXDJvoRP/Myl9a3htLc9cncR5zJ/o0qQd+fZjTtyjB63V9Ep+bZck4beMjLAlp8P56+zRXof26Uu8cbbyPyeNt9H1r2VyDDyxIad82Brk0ooW7PxPRyxxv2X+13lns+mUs3FGCQVSu/NDMWnp6O1HAkEK9OtIu216tJr1N/Ol0dTo9Nb/I8tyzM8ivBVvNXbNnPY6xavgB/65fw3FwHjNB1x80+0fRmnRM7vek60z2TSuXzLNkE9R0sNL6LzeZ20GkaxnRjSyjo5A8uNSxjhT/eNAxs2MbZS0tmn+lCb2dHou+GV/z+RkN2WnClTRukbMXIVQ6T48QNTz9H0bUsD1qeW4K77RLk5fRjzgXFsjyT8tTOFBrp95FltIfRtbQEtDw0R7KOFRftV6Vj2RWc9VLAy5Sh7Ou0ZLydXUID7wzBXI1a6aH5D3lEcQsfquBHvl/zcG5ECFo+FSFBq4C6BmejbsZY0j6liciCKGzKKsWe2EqRaUUJWsqWsZl6zvxtuR5m76l3OCH62zC+YbVUJSSINouNoNWUg/MXROg/J4i0kqegqVnGYOO/sPqpgeS2rYRuLc4hw/Q1fNjWFDJ+d7tBvjoUp46IqbTAH/2ze0mX/xtM3w8ns1nGGeSrnI2TdUHYrzaSdppEY4yyGFkjJtOqeZf44sMm7DetlVaFxcEvaTdvyfWhWoO+NOjEU+nb8nwaeqOU1L4L4IWv+YwrSXh+e5Zdop6YmkfNlK6z/8Pf+QlYEvdX6vHlmbQuMQTqoZ5wk+bR2mNt/HXaaLKU8YT9lVxUOeSKgu7VSZd2n04SYzFygsLgt2ck3D9UszffrXj0nUk0fa8lOdt4kK5pOnXcDiNfaTl32lRA0p0+eLFMjQoeFNKoikDkW4tp1r4EfH/rhKpjIbRlajS/Y5BDXvKaVN7hihnbetALqQDf4cVY9SuWhozoh4S18VSq0hvnvVdJfUiMQb8G8YGtvnDZbYOhBQlsaKHAmm6l4OZyF+rK/iN96+9NVnmmGLvIjTzmbuUaxgKNmTMII4yTSXWKmHYk29HeiCjse6aAYyPGsrd6IxGR5cxbTZ3AjtngZEM9bzbaz7wXGFHaDN+GZ2k+dLoskwY1emBvsBi/3hRRZUgBPs/PoW058izpn8xtfgoUu1lEDR/FULbz4EpvS+k/rQ/MUa6Sverbwr417Xfo9lGbj4wTU1aYlFXLxlG/w4qHXdsqupnaxBdNi+ad8nZomhsGs8GjadNRA9p2Ywnr/kCLnvu2SAeq5WK9QRatzBLjTmoffJbVqBWyWq1e002U+FiDLmxPxITIsTTJXMrDLxdh/KeLvEi8WmQ0fio5N+tCv1mXFt0k3BOL6cI/Adl5AsbLOLIrt4NdvC2Qbp4eZay1hjhgKQvWyKTbN2RuEanIA4qzyWhyCrqP1qSiGem8x0IxNMvUobknF/dHDaHk57fZ8jU96N49M3rZOJIWffSk+xr94LFRHenau7nRb4Gu/hRjomo+3pzNJ9XVYlr2po4t0kzB9Op4/L4I9jNTVu9SI9D3tjn0nsvqZOZckZ3qS1bur4AXCbk05fhkbAidJDr9yxK/N4thu7mI5vwXgXc1Qfy7w07mvkwCq9h1fGnrMFmeT6RlDZ1M/oaMQVMs8fK5Nyl4gFf2SaeXJdXcZrOEtnrbk8sxLzS/FejtpOWsInQQqrZK6NCNIXA4OZ0ZhERjYW952iTni26v9GiHmi57fscYEwxKyLJWgrjz+rwjywR8zmwe+CGDzVqyhVnvuCV9st+QTi6+xyt+Syg3oRt6ZqRg2dMaVnd2EHWZyeHV/cFQPSyhNYYJqC7WwO6r0zEvfQjlWRaSW00Opszvj42dMdi5yB2rPsm8TKsExZsdHOxnZ1D/mGJqeMr5lXoJuvcIx9aqLzw8PYC2l+nR13Y7cjD5ywY49yWDOSXYULydodt7Pknmlp61lXzZ1FK6c/2TdN0iCWU0SejR/kXSx7K6/atqHyuyUyPPG+P5BDei0A4NXM0ww7T0AvbCUJU6vhQhdog2wtY7U8yreKQ80MSXyxKERrjTxCeLeU2qBCFVcvTzTirsXk5FyypVkrjMkh5QtkfKKIEiHZIQlzoVkoG1zK3PaLojesZDbkmo9ots3XSNl/Ld3VjD1E9sXZ0PXne08PkDe6PxcB/SUyylul032MLFnghcnY2gnc78VedL9knG9zWPJPDvXc40j80UXfPJoBU377OYECPRipNFpF8tT6mxOTThZQo6Q/qhQNHi//a3uJZihjuCGL005HC1TUIfbwxGo7UlM1qkjhI/V3p/Koxma8Q4vNbpBq9vmRR/pw+FLRBjTk033OqWi/JJOcAWYrc2y+NhhwRfewn40bCEFWdbIHMU0bbgWIoeX4Kz7d/ZQ+/eNKdif0OH8jFejE28Ut8b5kOLob5iOjVdFOjSN9kYqPziVY2/2WyVKBpRNZaOv7smWvT8HJMs7uBLR0NquqUnumsYU05Mm/TnDV1sGOSAq8cdsOTrI77+j2z+Nb9gb911KG9RGVNYPoIwwQ1+vX2pm84LNr+vJa6nJNBFRZnvKJwSGQds5u0bVvHld6/w/GnaiKyUzfn/xNT3kzmlpv7v/fShtCc2G1cK7eC2+T4/2lKKzBNiDLy6ltW6VTK5uVfqbZ5msO7qAo0IlpD9ehnTfTEn5QpGVjZtbK1T///H0VmGRfV9bVglRBoFAaWkBERAQiVmL0BAQFKku7t76ohYqNgFdmFhJzF7YSOKgoDYYoM/USwUFHnP//0w13yas88+e+1n3ffUgX0niyidLYF3freR/sQYDI10BsNEC9xZOUQY2x3ErHEWmPQX0npeAx3ax8DApULq+ysG/HfOg5zn8VDYZA7nInMBbG9Ryz13qMS79fR2bTE+m2QEYnYqMH9dEYxmBaO2ix5W1zAoL5ODn3oyQKvAB8ueRsLFrTw4qjue5JWJyLFhAfHZtIGOKPHp0VlecGiuD1wUsH0o2B2OIxc0jjOwfZ0fbhtTBFcDb9BEv2AosW8iX9wYnLftHD1j5Am5253B5rMd+EUlg41YPrS7WkNYVS15MbqNlp1i8HjrSYcHdny40cwHI0c/OFg3CU+2ryXOVUf//78Hi28I0L0kH7J+lMBvlr2kjwjIp18MephYY/M5Bvjng8CmnAFd2CsyOuUOh1n23/08Fn7U7KE1oZ7wwSUZRk85Euf3rE+/zweltkp6LWYX6Xcvgq+XGAwYnwGmXnx4H5OPz1+XwiK2TsUL9PHh0i1E2YIhXYsk8XmHC97/wIPtvb5w5pwtrgZvGJOQTP82MLBJeQGp0ygCMT9J0HtSBO28PsJR10K7hatp+Pw7tHWgvnFfcBbe0s8E4RpVuBhRioZ/F+KX8U7Yx57rqU+WWDY+E67QsdA8Nx/MWoSweGsi2JiXkR6fGKzQVYT6XRL0IlkE0qf2kvLwAvRbsor2CnLgMutBgeJ8fDM9FnYNF6D02Gxo3aIFcopC4kEzqdUzPjITMrBEzgcmulkCo7XC7o2iKV7dw8D683I07RaDZkrtVOYxgW/x6aLlNmkg/qiGPr7KYL9YKLnqZ46qv3mQx02Bq1MmwPZbu2j2Jn9qIIn01RMvHGH3oEGuJHx0mAEab3gw63ECWDWUYE3cdJKT40J3n7oiqmq1Q7+Zmznns86RLX6TcVrqAZKt4Iw7klJBlc2TSmTg5E4ViJp2n54cYuBCuC0qL82C+pXmpNBoAqyTKyLdLENf/a1O8sR8MczHCUTbI+BT4zLRObYels1WB+n8v+Si0wLsEjBwtN6JiOnKYk3lFOr5JAfyDQ8T2QexOF0sSRSdlY8lJ3hgxPclLRuDcRbbu1O7a+iDxWtJ59j5MC5BCNfPZML3FmtcViZBpgny4cA1D7TrT2hc7bzsf/eiRX3Mge/jhdjJ5s0Gtj6XSRdjxnULcnf3U/K+lg8v0/0gZkcQrPF6JQp/qgC7xNga4UdBzsReqibIx0/L9OBY40vCd5Rw6GSP9zDHGZ+/Xcr5tIGL60K5cGDjfNGWS5eJx7+LnOibi0XV5nHonBAGSTwdzNf2RdnJBuToJxvwHBuPv9nxvzW+5vjtsRT9aq8kmenFkP4+EjWO8vHhjjWi/xYY40PNITJOxU4ETa6oUzwZBacvU+60RDyuEAD/m89h/ziwm7qGriqMgq2BPND0i8fmA/ac/ABXjkURD8PcKjidixPxyw+Wl3V306iV7mg/IEXHD9Zwetnx7WbHgfSBJpoRZY730gUw/Eok+tVRDNrnexr7NBPwMNs7xCYtRPmiIjySu8c2cbEABZY82KWcjmfZ8e9+7OE4vv9GUUMWA9l+onbMA5YdekOt9j+xj3/zmD6yyIHY2E7OzCwGJbN2k+RjGZgaPkrHPZ0P0ez437c642lDJZxszLrlFQYOHs6FHZ/d0fpBnEPFymiScbkUSxrTwZAnwKhtk8H66BRMfm+ER8y0yGe2/rT/FOFlMbbFDz/kyLLHM+3JJMF7XEDzkpzIe54rWPZYwroZHuBzNgdfzPYFzpoB0bnVGaD/Jhdj2PpblVJD9/B8MfCWANftZqAkYRbmv/xCZI7Fcfb3MFDX5UF9GAO6ek8oJlTkQ9mOVmKQV86Zu9Eezo0wWJKoDmZLolBj62USZ5QBv414WCRZhKHOO0QXZH3hTmIjFUIu7txm+//rrR3lC//pFiOOTyHitxnob2Rg6sc07M9ZCPPrc8BZspEey90sCtt9g5N5Zi05WeEHzpsBCyKVUCIuHzZ1CiHjuTwUxRdjzA0GOSAJ/ip8lPaNx0WXCuDjBhX4UyuAyrdfRO1bGmkfy8Zic2KJyeMb9MkiHwjSToEB2ang5mYGu9nrbcFvp5Rdm8/xBXSiZgkkHGDg5tvjVF9KjsyIisDeIxFw/mEhNIs70fMxKmDEKIHuvTwcV36YUNZNdsrz0fsmg9z/3ZO3jwuSoSMk2DkMHe9PozcHBail8o4ygTZQPX+dw7zNO4hmwVbOfql6+z+VPGyqY6DxqiX+niLAlJwM+u7CAhgu66fjnpvD+2uzyc0v1nhy9RvRnq2V9OzEBLL9woWGOu3X1O9FLlRM0IGDF1guHF5Abhew7s1JwAHlABgcFSL3nhLEHlxA+z6Nit666sL4baUYSB6S+uG9dNbdQBSGaWLC5haSfoaBsVoZ8HNtLF5wN4Tir6HQfVcDF31Lx2/mMSSQPw5uvszHQncTzD6kBrtqVTH4KIPbc2XxS4QQJ51gsGJzEDzSCEBX48nAmR8DptrHiXqOM138qoAum5wF6/4UwW+VYyTMD6B71Bdn68Tj955iOPVFEXYfYUCWe9ghz8wZA+9kYPDEJPjhVArqvxiQaFxKTrYUgmaVCc4U8GGTdRSkfMvHSFkpPHWwWtRT+YnIHGRw4xk/bBk0w6Wvp1IL33TA23NgQUk8/fh4LV0UORnOzdUkNzq0wK4qHxrZ/TJx3kDjvPR0PMnWbhjbR44MZuOlTyP0+sNccH0+HXT3eNKM1Sn4Q7KKjCkcJzrt1k4OjblJrTcL4ehgHr6xHCStE2fAFrmxsLqaQefOArrF+S4tSpTGIKNZcGd5Fp47qgZT9Q7QlhA9NP0QhPmVQgw9IQTxaT4YfT8c5nkPk3l7JWH3FgaatGZAu8V2Gv3EAl9K2MF3NdZF9huDocIJUt5Tiv97/wY6J+LWN/mwW10FPdg9e0/E+9/33XH+kBAVTwjgwhEXejwnEuJynOHVzXTOizVRmCl5ifauS8Yt54LQ+MwrMlQbBcv6q+nKU5FQn69MQr71kMSvQuClvCUW8gacZ3QDORHnCYUS0ZD0YQct72siOd8/0DVVbeTkefYc3BxhasEre/Xjf0lXezxsgcmQ2CfE+rxhOr9HlWw5wmBntz9IzosDrVoh9Fe10L4eR1J8SovELJKEnBR1uLt3H3mzNL/R4ucclFwjwPmvhCAhx+C+0CUk+ZwDWTE4RLI6jnOyT3eL9vl3knNmjnDjpQnGGv6l9351k5YmNeC+E8cQ4RP6cN8pIvNYiMyhaehod4k0lKXC/nvjIVFuAeqty8cPX59T+SUM5CgXQwLrPg9vJpOwW35QkxCLL6OF4KtgAGfbWM8r3UiW5Suy3qOBE3wmwrV3OdhV84+cKf1A7PYGwIur+Zjn/Y2KnTzhcD60ADgdDPKehtEUFy7aNwsxTaIAhhhDmPX8AR3K04Bvson0jAbAuTsD9NiRXaTs7zTY/lIaPrhz6DnWf8XagnH51HA89Xg1qUEh6Nl5QNVWO9hazgOzOYbQL24FunI+eFp7mLz9eoJeqOoRhSQwoKl2nW6ryAN97KVrvs6DP8OW8P2iEK+XV4qecr3g/CVP0a7N5mB+kQtGKZ1UunYsGrqH4DH6jgau7iauMxSxapE3rLqRy5m9u46MPVCIuieFsFE7EtsHw2HD+kLUuVxF5rdeIp/ulYKXqSRs7BTip0cRsMNSFfW3zcWescqglC0PT66xjLrXh1gfYv1PIhmzvAvg3o9RorPiDJmVrIpvTy90qDwujfx4O7zEZuSbtwJU0A5F66jtxKEiCZrV1pMoXVcw2imEHT1TqMZ0HqwosYXw3mYy+r//hjPh4ZKpCsAd1KJ7FvrBfdlGKh2dg8nb3RrdjRiwD/CHuSezMXKHALeU2EOHDNvHlIIxqvUl8UsaoobhzWRb1iT8ZPyLJF1ooLctWWYJEaCy6UG6bak/vPntg7E54iRqnQCsNpdC9xd2XX4doaPTB0ls+VfS/cYfTlBVUJNLh5UeC+jX1Tz4KmBwUFId26M6ye+ubfTCNj+YVy7AWEE9Ua8tBikRO5flvpx9FgzoH7dDiX9TMeV5MfzVs4Levj0k+VIRfhnyQ4Fah0isl4H5WSk4oVgAeecUcYVlNkisvULX69zlNIyZCrf+ieNxbR14vFcD7FtL8WSOGTbtiUdRfgHmnlHFAwpn6V79JM7ZNAHuYBlkfG88DB/2xXujquR93Cm6OD8Xghz0UW7f4fohhRzQCC5By+NueHCYwbV26Wi+zA0ffQoDEiWAeQ4KuPdzIIzzcIbQDR7k4jWW70clyGs3I9hANPGUvDqyq0H3z52GqzLz8MqIANNGQqBjIBxr/QUYcVtEDLaYw/onvWReDZeoWc/GP04MPiemKK1Qgqn+yzlqk23hxLU+6qXjjWltnli/J40IMyob/7oI4PCvUgiOnAL8HyWovn4LWc0EEKHvOmKvZgH8ljQsvDZErDwKQEptBS2sn4S+m5Ga3hXi4kkJYD5XgAXOduCxSAx+RH6z2yXQpBOCs2G4PhbGfLRCDfFB+iM8Dq/NXUBWWhSK2lu30MC8yaRrege5Uu+PLjMEcO7VNPpA2EW2NJTAh5ndoldXvMG6Rh8/HZ4DjWumEfs/QrisPg83zJ5MTtitdRg6rAMfqB1c3FYnstcSYOatZPQemU+eVnykipcSRVZ/VojOvNPEWjd71PhkB3MrPFDeNxubas4Q64s1xLUoG3Z+i8UJo+mgoiSAzbYRSMee5zw65oQeum8bJHRi8O/FROh7SmDRByEY5NeQT46K9GTPeGhRnwIOi/igm6VPe+Pc8IGYAAsbhKJdQUcah7dkgXaTKSdhFusKEwVkcoQzenwOhlOCYuJV5QXm22ZBNfrDw4vz4HIuHwZSp9NFO3kQ5+8JU7rFqF6zFHl2t56TdaVKdIjlqiONrmDKrSXx8vb4dmsqBF4MgNePCkA87Rj5N+c7XXKQzaI1PERuPkhMzqWVB4WYOu083ZvhBQ1LjUinkgcG6e+ht6X4MLTlKKexOQNCfzJgJTePTlEww5+/bfEhjwcuk9aRO9m9dEHTMdIWuZPKa+eAU0YWtPl5QfS/AKx7m467HQJRqpEHi/7lgpmPMbq0pkKh3Eqans7DYHE9TABZPFcZBCttl1JtmwjSESKLPdm+WFDM4LROBVg9MR73fmJg4kIvcLbLxyfP6zkNw8XwOowHGW+FaL5XF+dp2+Hf5cG0ZZ81vtxqjWfLFsJdBNRUPdUoE9LUeMy0FFL3TYR1plyselqCimtN0cGThxdCf9GuyXNwld8/eljFkEYTLiphEHjyA7GwjUOXHpLE1QaR4PQ3GQoNt5AUewccJzsBrD8dp6W2POBs+kAiVoTjvs5CcOL1inauP0fDOJfJraAQUA/uI2WZyXDhVjiIXfeCD8ZVjcs/bqY1y3QgcJj1eZa1y3oFYFuWgg5b7Tllf6Qw01Ydes/z0U43HC26k+HjZQHmzlAQSd8xgp3hh+kat2yyoTwHr1hrYbUqD+zvGkMAzcP7/GScHveX8sIZ6K0HDtMVCe3nS+FWmDMYeyfgoM0fsjt6CjbumQWTuk81pj6/RxlJHkpf4NF72lxkduiA06z3dHvWT1J+rRg8c2LwNdWF3Ofr6dXtAfhSrZpc/OqPB5YWgad8AWxcyGbgTy44bspBnx8MFk3UAMm8+/QL85OaG7+lHz/GQezvLw6vqjfQV+OQI7v1NsfboRBbVxRDTZsqPhujgF/fcLHsqC/mfRLgsbepbI5epmYRDP694Y5WixLxxdvJOG6HM1y9mQaLup6LnD6zuRRrDSVWE7H19AAVPuBC+MweB9cPJWgzPJdmDNlh+Sx1dL3tAmYHkyF4qJA9lgDPtnvCy5FdNL0yFw2chERTruj/P1czkijFY6uCofd7Du7ewcBCI0vUOHCOlEd+IWqdqdhzNxNt5iVDxdspVOb6P5q8ZAFOW1pJNy50E12X/0fvD5SAQ2Y6HE4MQo91m6hSnxEWOHJhlVoRSnxNh6y2sVjiL4k2QTloGmCCgSuUkL/dFu8pZOGDHVIY+qwEqyRdSdGwKx4fjMBoJ21sqbaG90kbRR+/Z+LOdeOIdXZDY0ioE5b89camzRvptYBSLOhm2WcB6zm3SiBL0ho32FmhV8UsmD1HBS+ER1D5AwIYfJgNzXwbaDVQhI31QeRUdwpOUr3u4J5cgKUmYiBpeY9OPleCjSyLCqK00LNjD2lpk0aubg7OjKml+YdysVpBCONlMnDjpELop6U47+1Romtmin/j2HrRVUOP3SXQQ+7R4gndtH8Gg8bHw/BkkhdO0glDt9B8qJWKgMRfPPgzZw7YDTGokjsVtM0W0uC7XzknZgsgqqIErzSNh72ZB2jKXm3i8zwA7RWrOckTLWB7fwH+jmwg71rs8Ev6StoVxsMVFxaCvdYBsrY6BRrsj9PgghJ4dZ+BvJZw2paVAflrvFB1dizs2K4Ez9KLwP37Ojr3XyG5OFyCj0fTMTSwEHKtnaB0pgUOPTREq+gSdBtShQS3ZyLv/smoI3JBmBoDl9g5D9wrxudlvvjfzMPEYL4phq1diMZsBtnV8CBG9TNtvVYK3zxK4OKVo9RorJ+d/noDNCpwwLDaFZwwXyd6YnIpTH/C+spNNkfFDpGrGhYYn5MLG+7mgEIfH56GrKCbrEvwT04xKt7OJOHHE+DYRSu8cssb/7I5uw64uOCRC/4UY710Ep/lEEmc770A/A/qgfwbc+pqa4dq2iXwOcsJTQ1rSV1TCgm0zMUVAdl4qziBLvfjgZgggl6ZGYc5n3TgyItauhCUQPysAQnaEIPdvwpg8YQS9Bn7hTMY85Iosm7VeDoDX4cE0IMZKXjVlY9ZXk/J58lDhJG+Qj30ZKli/wZyK+kyzbjsAD8KXWlqdyHMmxAPjiPjwWLSd3JnZzJK2c8GtxdqcF1DAHdTE+BNYgXnvaMQl/Su5fRJ3G+8m+2OVY9bicxuN/xBC9GKHwt1wtlgzyVwaCgWV18phYZcY9j4RIBrg3jQHqmOShIquFdqNbmcXkt7uAL8qcBg+uMsiKspBMmDnRzHl+6wYXgh3mwJx2C/U8TkTBRG8oUw4Z4RzLbMgYxnz8jkxH5SWKaBBXUZ+OtjIcGFg6JzlYXYvHUehr8OgZr2i3SsciBuk5qKG9N20ONjGDyrVsipvFSKBusZeNWqC8+LA/Bk8xT0yQ6GpAF/7C0oBImNRdgcnAT4XgAWuwT4yJxBPCuE//0W2W+xIkY9twTPRCk4/n0+7LjIMvrNr6K4YHfsY5lvTEQhFvIPUNOvpfCo7IXIZjsPf+Z+pU5Dj0U+lxk8fTof7W960KtFv2hyZQLcf82gTkArSa+upYtbHEV/nQth0r4pIF8ihJDuHDz2qwTXFw+Rz9x8TOpjIHxrDupnNtIxYxajRlwxhI7k4Fc3X+hj/fyMTgQ+NS7E50cZGGazMOSDLFiICnEn61xctVEyjuXSRkYKf6/2g9Piv2jDAQbaeZ7YwjrXqehdlDSGwS7FQpj1WRwClgqh8oQZXP+Yi7bRqtD0H4Ho3+xx3t/kqFxlMGGnFGTJsJmkp4gj5inwCryxzuaRg/OvAnTy6qQ/HXJBX7cYwV6Il15eon+tfFDpIYPXW8zhQ00E+K5hYMzRbFAKW0eP7FGCzj8L4ZFOIt55XgBLF/JRk3WSV8+fNZYkMWh2mYu9Ax3UcxsDxoN8kJV/R18mPSOj40PA5+uQQ/CpbZyU8lUkItYP5lwvQOXDszBPKxQ6o7lw6zvbX0Ys8fuLUvB3YDDpczSoHn8jGmHr76/JHMhVPEHaH72iLXsYbH0jQVYcKwAryYUk3sADPBc9p13s+hRYJZLZfs4OM28IofbRbfLNXRfcbIW4pVQejtzUgPyLgagmgSTjbQaKqeThxgsZoHJvNkikuuFxKwan9uUAo8nDDish7nq2mA71sGspdYVaxCHRPhoAXofZsRWcYeN7Fyj7nQuKjYFgFCkF5zsXwMX3rM+MXwAZr5vJs6UCONPhicuGi2CS9jR45GpF1O8XQpZlPOZvj8AiBT/y+mkuKszKcDhn2kMuJLaSr815aKpbK/Kh/vCpjo+72RqYsdUExVgGrFD50qjfw0DHTCmMFvrSszH5OB1zIWuLP+b8qSUDcnwU4xXhwflx+L3bDoe7eHDiigeqGGuSMSsPEsvGbLpzZQ48s/zZKCElhPrwueC7Pxffp2fj3rJsMrKEyzm9uRR3jInGryiOZzu4+CaHRz94txPvYlOcfvwp/XLEA7b1/ySHGl7Szso1JGJZLlR0ZtFLzgb2PfwwkFHmo7pxuohU5ILoQinse3CX+PcF4Yr/WH+SnopbxyvA1IZwUOi3wfNrStArJRcfzDMF/pVHIoM/1rjqlwCvavnCwj/iRHlxCYpuxMAfdm9Efl5BHXY7ocLNNWRkhAHbaYkw44EJ6CzIhetFAnilEEbzyyRwLGHdtzwLBtna4c4oBrULfMg+E4Q3586BFf9FY0nmRJH7tUjgfpMi9f6nyaOZuWh58BV5t3U/PXA8H9yMIzG5xJO4tK4j1Q2FqLbTDAI3thHC9l9+TQHGbD1F2778JVWWPHy7nXUnxVxo1h2hf13G4/Mz2zkVdvG4p20OyuyMhTSbAnAbucxRRi2SfmURSekRYk2XJhYfTG3MChsl2yM04Pu3HOyYyWCuhBZK/e8eH5iC56EUwx7pY+u6PJw5NAEvvjbBcUudUC5ciNtSF+H5a+LY+d4IxHQ7SHBXDkTl6eC+Pks8q3yLmJzPxLWyx+mUs5qYcT8HLrPzK1UpBgosIw7moYVZEX7wj0X+6gwM4zG4+1IOWmVtIZvPumLeW2+Qm5yHhzo14GVHInB/ZOE8i3w8eiUf5Xum0Cdmkbiyk8FT7ONYUbnoj7wcODRlQj2vAI78TsSAuS7o9LoQVUeFYK7MJ7MlMqHXSAHzhqZB/ANPEPvOwR1pOej0LhgV9Yph670fhO7JxJ8u7rCuNQe3fQykni4EuTn9JKaZQfnBNNQVruR4cntExpvSYPP2yfgvzwOXSvTRH8/kseMvg3pMJiR2VonO3C3B/3oZ2J/vhrPSRqjlCmMydDcFnCRMoP7GO6r4t5ETu7Od/qPyWOiTzUmQm4rnP/xHMqMycf+7SPzxXICno5dRNyMf/HiAwchFWWC8IQm1D/Lg0xOWoUQBeFYvjN7RXEPtfeThR2seFG2Whh2cTJALS0KlbwxWbY/B/MhAfF+hglXrZNHZJgEqdsdDhYjBGq8E3Hx5amPcW1WOTEkSGDbVNezUYPDPuBw8dEOZPpzDRRNDUzgnH4H71RpJWoo1mjbF4kS9RyTAzw8WqSuIXGQ8yHm706R0BgPMi1xc2HubHPyTAdeXOoD16zx8fFUBFk6Uw1cmPDj/OgjE7KJhtk88lT3ZQJX7wkFq123CTPnfb0j9QTZME4zHTYE351Jxin0prDydgnmHs3F5oiqGRrJ+cu0SObkpAk/2uKJk6AJ6XjwS1GUmwXqjQLhQ00mU8mThv6d8JHUZsGXnJbLqSjiWdgyJNMOnYbVkFg1FNou6QuHbewbvGVrBaV5zY9U/O1i9ms3CF+2i+z3FONReQxwzUvHCbyX8lDgHQ/yFMPDJBN8fy8U/7fM4/ySD8RbXB9XulaLSpHjs/h4GOrcY2LFBFWfsreZ0rDKDtKIMkB1hcPsvXZzbq4jKKtY47qAH/jtbDI/UFkFr1xpaLZ8DI/aBOKUyF/B7Nvw0SUd+ejo855SihX4q6vyQQ9XdsrhLawb2tDrgUQvKKdj9li6Y5I+HYylZKqOOtruOcwKuCmDRbnc4NU6IsRNnoF7XEjLdKgMSa5pIhW0fneMcC4OzRfTIonjw2+WOM8f4wNv/IuDlrOWc8jOpYOnBQMQ+OVja64mOPsO0t90ezj1lOW8BF+zv5VGvM0vIrIhOKnwVCYbfXUC91RONuwUQ3jJEtPO8YPH7QlizsoJseY70vIgHefMLsOhLOlRY2sLC7RI07iuDWv8+0ls/3DiH+z6Tg+XzIcLRGloj4zDFVZF+HRMHr4Nnika7VIggV5lOnO9Abm9MQeHQdKoqvNjoGhxOYlaMwR25frjWpAhn6rugrONvzqthIRT6ZiPkuYAKnqXWFTqg+iAeeSruEHQ8HaZmpeAU3mlOw6wkmOsog2onM1E9Yr2oqs4R1tiIYfNyD1z+xRmfCbRAzUgbPVOy4e6POcD9kob5Huxrs8Nx//BeotKvi7/2q+DzTU5Uv1oAXHMOSuVmYW16DVGZ60Uclr8hv0IC8W0YH6KYJ6QE33BuJ6XDA+Uk0ZPTneToaXXcfMGCLouzhYqntZS33haklxWiRGUxWeVRACKfMiJzoAi92ubB/Wi2t7sEwOnRZFzpvADEv4+Dsid8+CobR59fLoGmkVB07bXBK5tVUEXJHv1UbEEvxJWTcY3B5eePEdfHYWSgKw4rQhNg/Wge9LN7Se7aAZFLRgUNulFDBpws4LeOJTh/nMThTuCDVl059ZDTpdZd2Sid50zbp4aDU8ZUTtmFZHwxdQ1xM3KFNk4+qgv20bW7tPBcnBKIuZmhnKo+WLxJx7dJJXjI6RL9kuyOFRbGaP51Hj7yjQQ76QSwtdbH4NOBMP60JEwyqaNzbwtRxiADb8SbwHMVLjxrU4ARRzM81CGN9Utk8dPMfHTZU0f/LQnFd2nJuIwKMfx8PJBRK9iQ0EYd0vrosLcjHRdviD8NkuHC9pONCp/2EG9Vawz6tpx+tuZi5zUGBEdWisrq4+HEwR9URTobZtiEo1PtWM6L42Pg3g8G3vrqwvu89+TVEUk0XMMDzpFAnNHizJkZ5IBnVddTL+MkeK6djP0LP5LQdcXgM62aXuCsa1RtYsCmOJHmztHCu7UudFZ2Msy+rwuGqllo43yBaM/eQtcG+OMyMV+MTIuHBnUh2PUzMD6Rge6olaJNZybBdP8UNBk7BR63O2L8ZQF+W9ZIR5z5uNFaB17UZpHK3z6wbLoCTetMwqYDRiBUYWBf9gz6peqTqNm7id6bqA7v6lSwfz3LEyHOMKtfgGVdDCpMDgK38lkg172NNF3PhPvq8fChezH97sADqRFPjJpmRFexDlHlaAxu2UpQPC4An3Wtp2dnqWJBSQna/iuCodIiGN/L4PikebhqZRLaqOeiWXYBNLrPB1HSeuJWZ4757VG4b7wc+7yDFuzfQOMru0kaPwmvnGPgxYZi2CR/loyzCaUPm+OAd8UHT1hEwKMHT8gD8xwy3FlEzvVX08z1UqBx7RTZvMsZ7qYwYFnjifYR2fBWYA2rl7nBadUCYO5EIe+GFOcT9YJvR7h4ZISQx8l54FcnhOnS4mhyNwTKbgrw2RFpSDY0xMUb5oPdEyHRGgnB9fWzsaA4Dh5GhcDEt3bQukeckxwoR7Y5uEP0uGeiaMlRcgiFUO6UDDOavtH3aj/peltZMJtbSb/sS6OWMhuoRmEU/mhLg4EuQ6i4XgTLj1/kBNq3iY4m5eObpp9UWGcLU3wkMZnt9WNjNtNyj+VknsAW7e4KgfuAC0cM4qBlkBCpxo9EOuA3rSrS52hEJqDcqlHiEfOJ2OtPJTv6GxpT9f9Sx4YzolhHd1GLZym2GnbQfRON0V8rChdr2eA5ucvEptgB/6Z9JZ9FEajcRMBz3Bv648lvqmSsCPdWS0LK/p+ctncX6Pj8AvREO+SfO0/zOmLhTgEPD1iXk6NHFsGar2wuNeqKbof54POsx2Q+L53lsQxcfoatz70Hyc0SHYxcYIoh32Lh7wcBaN2KxMNpdyjsVSdGJR0c0cA14tbkD9WlHTTv2n2qsKcYzw7ywMbhFRlJGgd/XIOwpXkhjcrRJ2plU7BjcSyoK0wAu0tHRclsnd8qqiVzOzNgRn8p7NK6SVqfTUXLZjs07ZqIfz+agqlKMS7s2E/acvmsj3bTpdmR2PmcPbdPznRNSh0x27iZ5B6fS3hfZjmccGukbTWH7Hs+F5KdjBCvSvvBgv0Mlk9xgn+zv5OTuxm4MzMWdtVPBm7INrrVYiHkHSgh5wfssF6Rh79Cz5LEXk2ItzlMBCtvUPuJabDxVRYO7ubBUgUzOOsvjktUIzH7WC29k3eHPmQV3tVIHybMLMFPYbfJneIjtGlBCSxu9sBOSW0YiuPCsTo3/PU4Bya0pqLhh9809EkM6HcV43/TR6jEvgHa5TMFbnL300vb/UEmcTfJSE4DH3EGRvh8uM56xBl1GTxzVQ8iX9eLrNiMUscItE13RJmDZnjgXTGMqCiCe7wOXBS3xzbHzXSD/w/yYUYcPpx0jJgf54Lm2XL629uQNDwtgW/0A32SHAI58/5wNrx1whIfroP8UgngrBVC7wpxtJNdRfyeGdDxikPkzhETXLcnDf47HcmR07xMV8hMQINVsuiRFIFXaTyce7YQbzWUoj78JL7R78in3bnQco+hLzbbY+ayCk75ylLkrvGDoOTLZHawO6ou10HlXiGYSofApKYY8AmNxdzdhrDyxQuisHccFv0SI24z0gmOCPFPjDqmq1dS13Uzobt8GgQWCjC6PAd8q29QJbkIdAtv5tQGF+ODVhXIr3MHvW4GneYweD8ziH5PCcbNVjmwIMgGTmwWA/OxwcBvzsBZfacc0vV0Ub02GFo2u+CbCQLUGczAf7WO4PFQCccuWE9WdjsRg7fH6KmLpZjwsBCS5x4mplnFYNA7BYvkC/A6jw/558Jxr2ERFlxhMER/Ck02ngsTmetEThALHv2mdAnLBaueWII4RBPjeWPIul3sWo5+E81qU4XDCfvpfu9geLJ2LzUrYjBNnoFvVuYwfSwfNioa4M8vqqRu5SJovOhBcxBwyt0akcLVLKiXbiVfrCZCA7HA7NBwJE+nQvKpAtxRuJ8aPzCAE0pmUCrUxJBpYlQime1zBY30uDHL27iBFi9xA2ruCz9li3Dn2RK4+TkIMjez+0c1Hfv7QnCOQxok72bouYFEGJjwjDNTEmDeWj/o4hrQS+u/09lt0vCHzS7LYEmO0y4+FRsNw74dYjClPgoPqcyBvT0J8EU7H3V/ccnHy2dEvmJ2xOkqg86mHpDEN8ARqXKy2zEFNMuyYWUSB5evC4LQKd30jcgXD34dB0t/RYFxuxvK3GFwtGMFx1nhP/r+YATIHEiHNZqe+F47X2TQpAS3b+ii8kgOWB4Mw4mb+SjkTkffkwyuWRoCXU6vOXsqTIjRFTvRZUjBP5LvqHr4A86V14m47kM9rfy4nWO5VRJNla3oK/MguFFhgf3lSjiURciF1f7wNyYR+LZZkLH6jsPfpaVY1vJGNHvuQhyrV4LXK3XxzYZXtHlZKdSs9MDQMn+UVQwh6098p/EBqeCt5AkDW8Ih6KYsev2ubFgzOg3HeOnCosWJ+N8tBl/WB+PpvYHYO9QuchxNhcf3A2HAMxO6C67QnBmaKBHJhZaxIw69ttYotWl54wmZ9saZr/Pxwzg70WMBH7NuFKP9IQZbqlMwaHq7qFrDH6tfLQKZn2uox3JtPJldDI3SAbhPOhgOeapxltWqAneoCGYvD4OWlCzsrmBwn0s8Ks60AL/NizAjOxA60ic4nDhjK9o9KQVC+/Jh+890bHK/RH7bZYrmuhWB7SYTXBQRBd26QejlmoXzfaUwRu0z0aGxUHrJD4OLF+KhiDLOsKUnsbmbDf5oTutv8/HHyr2cOzmZUKyuSS7YJDbuuWCNr4644j7d4cahXj6+VtK23ykfCH0vslBL0pxotzF4d0o66PEc4GmKO2fx6xbR6+ExoBnYTrY8iEWjYRnUOjUBXY/8JB095sSVG4WD/n649UEKtTpcSaz1VxPJR4lgv7MY0vYXs731JwcSJWnxf0GYmh2El9vqaFbOYtpRE846WAyou4ZAyolF4Jk8E16e+ENOmrCMzGbsD9Pd5Pikd3RpsgJdojwb//fe5oj1Vs6q4ul07pYSzrERBqQvO+BV3x2ciZ98UYMI4LKjChzzCcbCPWxPMdDFndnu+Cldn6yaKUT3U0F4eksKNKWbczJWNRCx7kg48KSVPjRNxvYFi6Dt+QtypNcEeqcdoQ37WG7xEeKwhiuYR9nSmxXhqLm6jdyV9YEl3gLi364LS6xHSJ8CA/P2eUPrMl/kto1SXiUB7WYhrJMUQLPvK9pQ+Jn4jHiTv+vr6Jl7WmTOfgmqMfiK9PcFg8T16MbrvYXUIkWNPOwLAGcvBj98jYG6+fL02z0u7N8gDmdJER68HkdjZTaSw+9M8ODyLEy8pwmxT4vh4z5xdn7BeCY5Fw21fZFytHHvswzwVc5Cp7/FIN/HgLP/WtHPvGKSMdEPyiYXw6xjLhi9fh6sWMqAHsSiVLA7+L9yghkanuAbu50UjhSCbOQk0HzPhUm/FcBkgwCmvF1G3doZKGjKx8ennIhZeCw4GmSBxkMG+TtqyX+BcUTnsg9e+1IAr+fy4d8rI9BQE0B6bTPtTTxBx2ivJ+dtXSBucBrsL8mHgfJCeN/qAtzhYPzwi82kccW47bQHfPSeDx/KGTA5Foe7TwjhYgIfo5+G4hjPavrbMpS8DOsRZffbgawZA/KVUvBiQR/dEbWLMCo2cNnPB6v0akSqPnxolkyzP3KIga1JppiVawHD8QeJj8xzevvuO1oqL6BNXwQwcIJHwrszOHMdvaFk2RZy0MUD9h+Nwm7tIriztwT2sbUWOmspmWGoBPXeJ6ne4wT871kE2ImKseZhLoiPXSGKnSkPS/4sxDXzeTjQ643VtYnYLJMJcQv/o0oySWDQkQ9+JAMrnS8T5ek8DBcx+MnGAnWTw2FUitKupCRglq6iPN/pEPDIHbyOytO1Dn4wFOP0v3u3gSjWFdZ6AZXVu0qX/5mO4338YH5bNanvt4f7j/XwIHsdHrB1Hq9UR5oXe6PlOg5s6yHwucgPrv/IhpZ5P0Qp3xm40XaHlB9OFJ2taaBiE3jwz1gZLu4LwZ9nWLYRR3p4lhAPpbvDqZclEOg6A7b/oaRctgCcY5JwU1ES3bWsi4Z4KUF5ygJq+VIfOqbdJfkWJbjlcycxVXTGhXoTwVHNG5Xrz5MXh1TgjrcAIyuLYXVqGFbZpOB73ZdE6W8+7DO2gnXLLlFRug9Zm8XgoQNtIuPtEbBz11Oy/d982OkwCXH5XyLc0cCZXsiFDcvON17wVgeF1b001isHft4rxeCNAlzXs6IhriYTr+aq4rbFvqSyke35p7xQ4zeDm7euJXc982BpiCcM+i2CDAVjiBz8SjRPjgc7uRxQClNHn4Yg+mGLCwZOTUdVKSH+8x0H5qyPHDsmi7JfLUhthh0uuucPDhPT4eTXKPzt9oce335PFCqljneOPyAfxNuoxDwp3DpeiPvqX5Kk4b+k2cEZD5RdI/NhGWfDmhGafjUEjt03JBWx1XS6xTgIF1pg4MTlnMrZDPS9UcEjYVxaNWYBFvbbsHuNwRS9+VCxiwuuH2eJJH8UAt8imu0/BNtY70lslMQyHQE6tQ+Rlb2y0CfJwbiGCs5DwyY6pJOIZMMbsqfFCd/qzIUciTV0/SprTqVyIhRKFmOOzAvRDVsZkC+KxkPRcdg07jMlggjMPk5Jja0KqfshRWcpqoDLBTeoVzCjcmubaLYjD4fT0mDwaRV95JuPq6oVME2pmS4fFkL1Vwabl+eh4St9SE/TgQuWPNA4KUSNECfcMzEVfR98ozfiu8gGE0OQe6cH6/+Mkne/J0FGwxISstQDw76M0jhLIQ4Fh4JHWDbkhY5SZfcr5JqbG2zdH4bhmkrYYegBq1eawy8pIURsJvDJTQ3fBC0Aq4QaEuAkDoFjGLxVXgLLY/ng9MEIWwb0AN45onSGt2jaXw6OGHjgv9K5ILJ/TpyX+6DhUg3w/cdA+M5isvo+A50fCvB2EAOivnngsjoDbCfz8V23K9ye5QUjBb5on9RFY4YcYfViSSTjOmjNAR0cMvKA5kn2aC3znoQGx+Ajg0wQHz1GLHjlnBffd5HxeY6otSQPNvOjcP4WBnLHesDDnwwG/CmFyVQfmk8XE1NpPtjtVUSHcme0qJsHzx860dl6xfjy8yyoTXGFGXoVZMe7NAxO3kztKv1geLI8NqzVdgi9aYQFD+7Rw6/SMeSmEGWmaWCT9njoYb2T90wevGWKsOgf4JY4AwyexMdq1js/Sg8SUddd8k6MhzNPm8Jt+1jc3qoAVaUt9Jz+C7rUoJCsn5aP3XJT4eGqfDJ5yAVgqxD/ZjK48sksUPkgATbX+BDVcZv83WiBSpV89J5yslHjowZ0KhbSrTmbRW4cLt5szUOtXwRWrwKM7v9Kb91kUNNuAvxXqAgTLptASZY/+MZZQ8v7mciplcTnH/mgzZUTLTK6Qj/nOODR4jr7Kosc/PrBFrIU+8kxDQFGeOZh1vEp8Hj9Spo5yx53TZyLhu5nRa9ykuH0tFPk/kNzIq1vgOfLttBTL3Ih/7kxcZsMuM1dCKeKi9GzpkHkV6YPRWUF+NBXHAfr7CHxtjQ8uyRAmxoj/HNNRKYvD0WTSVmkOlQTy395AjfbFs6XTmdZLBZHvAXwLZFLIj7PQ9NLuVDlBeiyOBf2BzpDiRYXFwXKQd32ErybNAtW5cniBOlUPLqPoEpnOXXUC0DlIUpjc7aQTB0xsuH7OELRGcpe54H7vfW0zHEjHWMzG/L8GHTPLoLjPcUwv1TAqR2cC//tzMUOG2ccZ2eAcf4nSF9YMqhZMjjByA1fOsqD1LEN9FKWFbR6BsND8UyW94uhZke1iPcyEM6aEqzo9kaFUTMc8yIB7G5dJ9fnhoL8o/WkiMsyv816kcFeZ9i1sQhOmmZD0+J5+KHKGi6kp6NwXwyuEMyFcfnTOHFj/tFJE5JJ8rWnJE8mxGFsVCwcveiFv0QmKHlLgEEXU4lXFx9SuyTRI58hP5VmgHY0pzHuEgdHLUIhLPguVR9h8Nycb+TzxkDc62yAnx/4gn47Dx9AMpT85KDFKQaOfM6mcz6toRzfYfJtTzTIic8FvmUa5Lltpy03S8mESzM4xiZpKHDRxFevFmK1fTyKPCXxviybPey8qf4ejmScLVvrPNw6LhjrnTho9W4umdHlQu8ci4Mm93rO8WeatEQuCdY9DoTB2Id0bEZj4w9FC0pNEuHZgWukxakUIVuZXPp+QuSwag68tp6NWqH6nOEeI0yepkD2LwUYP1RKdM6EoPykZBJoqghb2+aD1T13+GRsCE3CAgx9EA/6melQd8cBlxrzsPScKjGKmoSfbziShvQi+NnK9qTScAgWuMFk+QxcUJsB857pAWMXBptTTDH5xxzcZeeFkxXnwKrNzbRu3hJyf1IWuHzPJa9XbCFj1swgD4yjcNJnBkZ/8mDO+Q8c/YHP5FlfCeTEB9CExU9o1aFp9IzuLDSNkAZ1g0vEROy1aLHrBiK5zwBLpmXB6zqWsXu9QXjDDjVO+OGY2kqSwfahhJX7CY0WgodpHlRvmQ2JLC/OSFOEMWdycEqwIT1FBXhJTA6n2sbjkaFVpOJlIVlzIgntnlRxbJZkwPoxTiDfHUaLQxzR+qAFruWqQFSxIZS5yoH8aK/IWe8xDZexRuc9iXDHtIk+dThMAj9GiPpPSlHf/c5wYhvbOzTCcUxeOr09ZTZUbjhFH621g5EwM3iES0XXMqTA0CwYvN4lY1FIJH5p8UBLzVAYcT9Gf5yWgJAPOdA+MA86Wc9IMLDAf7NLULrAC7j60TinfnEjn/WDLLxIOtTSYPdFAdaIM3DmRDQ0JkpgXHUWkX+lBwv31pGN+2bi/j02UJcIKPk1HNoMV1PXfinOeCtZkO7k43JLdt3WW6GcSRyGXUxscNE0x/IX+0Qf+gzJvWus6/6tos415rj7ihiR8SwApTYGhF+PcH4Mt9EOxovjZZoF8zcOiLQnDBG6Oga1FRbiMtfr1KfjEt05eT1xusSD6fo2oJeXAB/1eXBspict8b9M97NZ2LiuGH4p5aC9gTgMHqzgNLwJxv+K0jE3wxCriTu+DvAHlTBtbDU0x6djY+CRNAMWyil4YO9eqiRpjB0r31GD57lQ25IFBgnquLJ5McfxHxctUsLwerYA5/72weB0EQ05Yg1HFBs5qgMMvFfzAv6alXRH/xrS6eaOByrzsYwUwjfrHFiYnwzyfxkktqUo7M5A3Ufb6PtCIYw/bIa/17vi4MlieL/1Lrk/P4a+MimEpwtc4ZBWIey/owxtF0qxdp8vTJfh4o4pDG4fPxVlehnImK2IVqbW8PlKDq6fnQ2bp/Fx+D8zKl/sDDj2M6moKsID0dNEl7otoWLwp0jZOhX7RtKxy/aHqFzuLM3ofEkHDM2w/ZaAmvfHQ9nNxZxTLioYoCFLN/sV4dyhYrjUqY+cwx70ekUm2tz1wcVXndD36l3yYZkb/upkIOjbdDiaaAyPfwTC9o0FUPdCGj/5peBKyTWiN+xcvTy5+JDXSBdGu+HiIRNcu1ocV6j5QvvfEChfMAaX18zEriEBxO8yB+7AOHynOwb3i4Wg9koBTIjngklDEq5d7wcmunOI7sYxmBqbSg1vC0HvYBoZEz0etbOmQ/iKXhKTNAW2TZ6DJ80/0kfyaxtMX52ga3N4aMO8pRlXGezZmwcL1uyjThtqOXb+KeB+V4jrHzBw12Am3k4dpOOixMCmLwRuruygu9uCoOF8KB6O5sOvYnfybn8ELGhygKprb0UBeJPs6VGCGwYd5FTUS1rdZwhru4WYuqmLzCzZT5SjG2kCJxVe7bWAPhsBLihwgo+jb+nWA3n06VHKeZYzHdCgmnMC7UDz52S8ctAUOzP1sMHYnZxdx8cYxgGXvFcmZgOKYD8ggOe7GfjY90a0XbYYFbK3kx1TwmHA8xVV/haL57KEcCvBEOxl15LR8POcNW9vkdKrVuz1c8LtYRn4cI0QjVcuBHlfXfhgYIlT238Tc+9S6F0ViDvy9Gnd10aaqGeKfn754PypptGX9fGMrSaoeLcQVRYQGsc6+rGhKhLzJh/1T28hknpG4DCJAf0DDPbm8uFIvQHWvDQA00OeMPOYGK1b7ILvxkzDexYbqOo3Bm4XMnhm0xma87sIJldxoXxgAXx/mg73guJx1OY7bVHmQt2+GXjUokiktO4CVbr8mj4Yr4q1odNBfksyHdvEQJxvKLqtN8HVHw1B3iIZ3Cqc4FKOFH5WMMOLTBWdE2YAB7+H487wXsprKIXIM7Loly2AEfUU3NfPYP4BIT6eoEnGLT5LNYdKoNZRHKInjjbeu5sKc3dYoUh7BmaPTcGPNrKIv0ob3TrCMV2yixgtUYc61mU75G0xIKKdhFkI0PANAztkU0h/3E/S876ekxrP+kanPuT906M523Rx+F0xztIMRE6QNM56ZAQbWY5N91elO/uDUFs4FW3f8uCQ5ynRso3h8PxpCdrmp9GsCC1MfWsLH+Xn4L7dAyTkjTf+77tqT5SisYVlbuXVw2SIPY+tJ++RIO9MaJxzmy5k81lOZgLsLXfEfB99UFThQkVZOGYkKoCliRu6eEnj1rdVtKySQf79dJh2Igh1RoRQbLUI1AuM8PLHSFixXAdGorKh9YEm7rhFCY+fglL3s3HZIAeHp3YRqVtCkDBn4H40m0vlbcRBXwFyNlkC7044mpWOkD6NbLz+cwzVbNaDMy3yWFefh9EBg6IEbxv8myOAXOUe0cgFIZqrTQWfF1pkP+mjHYUTIKq6FCvfBDYWDqqKZOO88I27Jv77x2CCJhcDXYRgMZCL5/2mQ659Pm7SFkLsWcXG7b0mWPWS5TmPc8TkpxA1F4ujhkMhdCukAM9JDwL0lXDpDwan25yhP1ZmYsX8DTSLMwZ0MgUodl0LV6kWwziPcRh1YCrZFJiOYaaxuGSjMurohohOUQ20E9aR1/0CTDu6CJ+sTcGY+4W4PYWA9z4+rF1RgoD5qP19LES5JIuuHnbE8m4Go0pUUNkwGCvrdWFLYSm7R0twQZED/BSPx0d/HPFbrQ+GXOJhmVgaBnyfBpHvGTDMW0qnThHDkwohmPqxABQXR4OltQbWddlD468cTA38Tc59jsA0FRUC/Ad01xkuTNv9jRbY94jqOz4Sj9m9dHN7HLXc2Ef1fSRFL7hHHQKsdEH9mhrtTg/CmdeLcMiaQTfnVEibwIXbm0rxqJkxedzxjtoky6P7dB188qeOYxeTxvEry0bN0gTMODoV7VuS8OBYN5alIklnrwBntARB30wFB7PgEvjm4AD/vYiAXgkGj/Jcceuj22SprDyE3tAFQUMgtB6fBnkJkXjY2QpdxyTBmVEeyv7d1JD+k4szR4vQns0LMxGDYon3qcO/OBy4YATVgUlwYqsk/NzUxInUmopLmzeI4pO1UNFFDym3FP9NCMHrC1rIf6sLofM1m8UL/GCo+f84OsuwqLq3iysioaiUkgYlndLM2TeiIoiUdEpII93DzFFBsRVUMLBb7ABRZt8Yj2ILKgYpggIqJkr7zv/9Ol/O3nvutdZvXXPNPsrwyD8LnQeChH1TyMFrc/GcTyLejZaHsC4nmDjrNQ3yVcJNQVn4wyUW7zp5QqxIBr54eYr4H7xJD3jyoCCZRTyXC/vrPCG6v57Z92wJiO1QxnNGGaBYcJTmL0wA0WNpGPx1Mt0UaYcZAWmg8PwIHZfjQg/oXSW6G4W5/5UPyZaviPi+WNjWpkemvJED8VU7yKU/QXQwQ5+k9/PwDt8BbiqI4u1dKbhU1gfTDeZB+4G5mFGRgpuexINp3DPBSQMTrPychtHiymhjp4a5h5oFs6NYvMFhMVs7E64/TgUXQRKc9GGx4VkurjiXh0veBGDHXWFl91TAcKU+GhtqAzLOcnBakcXlV9Rtq2ftJaYfWXzeuJl8fDaeiP2XiBlZBJsmp0BuZBnNNbXBAz/Hw8XJCdjRnQ9TN2QQlquE6eHD9F/5StJqEQTJP1j8O1UX2w1YPHw5HvItLKnxRCWcvsYUPi6Vwf6dISRHio9lrcZU1SwHn1fLwvz0D6RR+ixZniz0oFAWm5R5WOlTTCS2xOK2e1/Iw59rmdNzs2H2oXv0VsVdwWlFF3xfHIaDn/TB/ZsiFuzlwc6lbURiyR661pCP7l3PqZJdOEQsjoaYk7Gg/WaQyGtnkIfX7Km1+2Mqm1JLT1VxwD/lAFmiJwurCvXh5C9xuJgonAvpNPzBkYJjGVro0x2JvYtygR8egTpzOVj0aDXnYpUuOu+XJ2ebnpAyIS+9jVBE59o0WpdlCcUGhvRoSyaGC/31nvRMrMwOhwOgBgZDQt7oSUPl8BXE5mQwHnSZA83TWBQrnAmr98pAw71ktH3jBGmKCbhHNhdvRkjABcNo+PoxFC1mXeb0rHbGWUds6aK8V+TmQC46LE2G4o+Z5En2c6LSqIC3x3mib7M/7LigBNvOcfGG9VvaMpRDtjDBcCBdEauDTxCbOY6w45oifBJyi3psPvQm+0HlVT6mT5IB+ZBSTuOyaFDrE+b8WR7a1/Iw/zmLbdwArFyVhZcqsknH+ZVQGQJw5Ew86k1YCOd9nTDNcSosV5iEAmdfqO7LBU3TaIwR7mukTAs7zhmQz2d84Y3cSsybYYfPxEaZL2Mh0LERkO9cQbaUn6VVnJ9ELU8a6pYlQn0WH4qDg6iNWDBuPbCLrJBMgtfVXtiiOQ5bJuWDhqo7xh5MB4uT47HCbj7t+svC508sfiqQRAk5N/K/u63jTmTBbJMI/JGUBeG2UzDmkgdE204is9oT0PFlDN5ZzwcGgum8fh0sCtlPd64fIZt7psGBMDPcuJYPHoqDVO9pDN7NB5B5PQ9z9rniljeW4PVwGtzpLBC4VvAhasV9xnh6Om6zd8WcAElh3khg3Nx8RPtU+E+Pgx+aE7Fw8iy639MPSqOWwJYlfOjad65G9rywT4lmwIFvj8meRcIeMLwMtlSzeNV/GvzJbKCSW4Wz+skbtANS8ZdRPIosqiTLlRaj7KQg2HNBDO8MhMGUo6Ewd1AXzmziYOnajSRIu4nYBEtgycOJkKYZAF2RF4mLpDlGbfbBDW35uPrWAlh2p4qUpMXA0uuddgPW9rDPOwQWP9lF3x38n84VYff9qVBWKeykBs5g4snHc7c5ODI+iENLPZghZ3u0P7WD+lTxcBdZjrepEvyenwepuSvJmAiSBdN5OL5OHCe8UIIJjZZQWTfIBIgvQkkIgIlsNjACDtgc8UBpLwcY9AxAvxevyaMzfHBbYgqbpB3At+w6sbWcChlRx6kgWgK+nEwGs4al2OgTA1ZSH2n1XBvMvi88o3vb6ceLvszshDiyfk48lKzIgvayYBxdpQ2b5oljf2MWSs9vJzduTcOYWd4Y80OE/JvphIMpFhC9dwHuu1RMU5ZEwzW9Co7nCQLfjmfDszXu1LgkF1WlxkH9Nwe8LX2OVL/9RjM/BeJo/wLckbkI5p00xR9fvWmlsDNfNfGEq4fs6A7+OOi6Zg66EnxwlNhFXh0Qw7enm5mz55JJ5fgccDObjL94Gbie/Up2vDKEsAetZNN9Hibu6xQsYO7SM+P8iK7YKlJe3UZvLLCAX91j5F9/OGiXmtgRpVBOwa7pOC5vPY2KyEL7Ll08tjgK5pAYsFiXiB120rhZtkFADm2mTV8t0KkyA9slxXBdcQRExrwRtB7JRdvy2dhaqwf7hzcIjDq04OKfPAhbLIatU51w2VJrTF3xjN51tMHrc6Jgi6gDsaoeJetHPzIHPgTQbvYDcbDXRe9b+dDF48Gzx2p43l8bYntrary2GZBrugEYI+xg/1nn4t8/EkRMygHOZUzE0woOGL/6CDWzWQjHuGY4ofYxiQo+T00uzwS+hS+zx1ganG+nQqNNCq5dFYLpGzOw3oyLjuuTUHTDCFlilI2hduJ4TMiuFU522DhnKnJKAvC/XCV8rSCL+6cl4iJrApuDeOjlnYf31xpgtv8/wgZ9Yzhtomj37iRdPzITzRTuU+3J1fSQkJulG00gwWQ6bHHMwKX9XGgzjaUnClmsP8xH2TEvev6TDljVuUKQ/zCpqVOEovdm+FzIa4scn9PMLDHUvSENTz5IY9rnZHx42xbH+2ShxOUsVBiNw3TmCJmyKREvS8fgZmdR5P4n7BkVi7DE5SjdP+4TneXZQmrSE9E0TwrGmsTQ9HkmaZE2x82lkfghgeDuj/ZwjS0QhNwzYxjtIbKnSwzO/4rC3nP+OMdglCZN4oOqHkNnjJdA1SWNjNfCk+TRpu3E4Jsjyp0ZoyHbuRCskQ1/S4T+fmUCyio8p5UPUjBthRVsCZHEn00acO07C1fWTQA3U1MIeeKMmkKem6CrhrtqPemV3ykwoXkqtgaE4un+AaL+hYtiD3Pw8HYRUHkuhyU/91DXnbGU3z5KA+x58E+UBZMfOrCjoJdaYzMT808DXKaq4FaH3zWJ9r200G0e8pt5mHIlA7+1adMSyWz8NhSDO8t+k66RUJinE4FJmeco82Mdffa6nrxYq012P0mDg4VRcOTOX7J/2Imw5Xw8sjtP8G9pAH0zyGBxpRIoeXymic73iLXIIGHu5+Phz1xBxoA+bNpQRQ2Nq2++SPZGMckeatCQALa2eWgwNR/eKq2laWNzyZcTOrCp4T25vaeQrrq7gXmyfybOWTeD/NIPhSEtJ2xtTcUZJi8E2sV/yPcxX6jrSkObyCf0k+1++nVGAnywWI5ynDfUatxSdI9RQt/bD4h780Xity8PjOJ5qB0wE/RtVoJ5/Sca0ouc+ddicd0rF6RDV6nvEm/YHbOHilU8JdIjLLpbpsDaGj6YrRWHLB4fuB8TUTRdCp75LkTr0H4ia++BAfeDUHdBLJwyrmLuj6usiXrCB/2Pd+n6bU7442ou9gTJwJnjpvA1JQ7EtVWwqD0b26eb0/7dH2mBbQq+WGmF8TlO5CqrxVQnB+K4kA5B6eSb5O63dPrhpRn8Su+ke3O94NhaBk6m9wvUy/Yyv46nw2u932Tto2zqPU4DKx+x+EVxWs2MoGgcmpuOIaqXaELwc4JlznTFPRbDnieClewYact9TPoSEuAyWuGSsC6qskUPtpyUQv/RTUTqT6LgWNcvwayscfDW6iQRuRQOxgU19FypKPqc4sJYnAdZMt4dNsjo44e9PCr69yeRvs+DK649tPZrKJx78E7gHLcIEq0INKvvp7NW5MObEg9Y5/BPOKcsTMhuE7j8xwdbjyGqeDsHzLU7qcO9LkKG0ui2s0a4zmk2Xf8sHXa9cMMNIyXkrYkh7L7N4nA/C15ClgvNbqDPw2OBFXCFeaCHtbd/EI+Zg5T5byLN65bCz5d2kjTlQnJPtIG+7FxP3XaeZjoPBcPRXd/IY91YsLExwOnjZWH8Snk69OcSvfq1g3YP8zGpu6rmQ2MKPPyQQwKSDTBVIQ9uEB5R3jMZ3wx0UttdUvijaimE+IViX9g+Zl9DJG5alw/D716Q3xx1fDXnInNwcyvT9GAxuRGRjyp7R+2aymLpt/pU9JbsFCwhLCqp6oKWkOMHJrynIr+toPyMCj648p7eESkmal2HiKB3Ja4wn0H2ldbRtO48/D7dm7TsSMOZy9ShdMlL2is7TDz9+KiS6SP063ek+sZLalreQEQfpgOUvyQLquVApbCOaR8vDap3HxIXArRwyxTc2ZmO3KOKsOfjXjLrQyyKH2bx/HI+KvTxISu3nT667AxOPSIwZ4sOdDs6MYJuFkQ0PGGzpx7R+FLGRP/NAnKJB3/76xjpAxz8EyvsuC8DSZKXFE4MqScf4osE7weVoKg1Au2D+wm/SBR0ttuhV/F0Oi1LH7y36GElXCem3HeCrpphqlkigdGqQbDF/w+tJm30fk0wSqktAqsZOTRypJXIRr2nu9smIG/1OMKMcsFuSJU0lWjidpP9VEbHhUqNFxFkfF+AccKZyJr5nAQExKDyGR+YMsbCSdk6UuDOR4uOVOi/1ygIvBEBGyLryXwLLn75OEZvqPUy22b2k7DD1RQzBujOCa30VNdManQhEtJuxlLLcxfJQVV1/PZoHOGkn2VkD70kt7/5otGacrpdRA/fd78kZvwgqLsjPCPPyVjU8pTM+2UL4mLJ8LdiBa5/t5usWnCAlGqzaLt5jWDa63j6vzujMrYbQ6yEG96yNIRfIyysnb6dbpJkQbyumaZgLmRvzgaVbVrgd3oVYQ5mg++LYrLRyIRZLGTIoTO+6NyYDXfOxGLjuOUQK+RwC29PDPjcSK32PyHxPTfJb6G2ZutR8qhVC2amc+CrSjiER1TW3Pkp/K7zn5Of2/nE704u/vqRBy8Eo+TDXzewv6aMcpub6B8ZaUyfIVxLKA/HZqiCWJ8x3X9CC59Yd9me8XbFz19nkc2etlj6l0UPVz7ITFhaY/W2jHCv50NR8GMy9o/F2cCF5bOQudAnAyZDCTixRRVv1uzjiMzcQDd81sNDNB13Ei7W0FhwDRXFad0sithcpSn9NdSodRpuTssAx7lpoJkoDmKiXugt5L3eFnWB3d9bpEQxG8SNFtGpAwnYEsGBTUrhmC51mdAFupit8oh0htSQdMtgMNotiTX6g6Qnf4xzZCSb9D/KYkI6Q2CS8NnO8Qvh+KNlmKM5TD68YdFhnSPYT8oFkYs3qd6TXLj+31Jw3GiKdrs8QaI8CH42sOj87Lhg7i4eLPyuBg/OJIHuVBMUrXcmzZr+WDjij63qq6nkuwckus4Wnn22heH+YOCqLIHrQVGgf9GQLJ37nCluMwMJ1XaBpq4kecWK4eOHvwWr3/RQ/xPxdHm3JTLxN+jsllm0qHUuvC/eQf7TdIBdH94z9j9WguunQcHhD91Mz7su6rvWFZ9cP0u7/F9T+Z4YxvcxH/omZABndx1Ry4hB3ee9ZNM+FkUrbeDQG0ccnzoFJcVkSXmLKPa2B0Pe71jEp1Oo7DwjPPp5Cgg0X1DNl9ZUTbOaSu4Pxkb16+S1SSNJeWkCNqrpGH9qHrpLaNM0jWQcJ2DxmMceweD6bUzNqeUYsDAapGtt0Wn7ApT1vU9EstcK1i8pIL/anaBorTbwUlg6MdMPmGo7Ivs3A5OdPaDN2he8jTeRUXMudpuz8PxHOGSfXQkXWqqoz1tnmPREiVQmOGOwWgKEKBvBhofXyN6dHnRgvDyOlt2kq5gIqD/+hXzR5CO0ecLrDE0SeK9ZsFHuHnlfmQ59wacF4/a8pF5tUSD9jwuVPfn4yz+KHHhhxvCKXKiPzDZOVIs6tBnF4iL+KzI+LR/l7njhx9JKemTjdrIFHWiVDxd0poWBYdxdsvecNzM6IY/eK9YCR+N58EYvFG0uLoZDuXZYbfRUMM7iFzkgFQ0nnt8ly1eqY6NDGd2xyk6QeTUAxNtlkCRlQ7NeEWng5cGF57modCMQM0tXQN/jATrRTAFx0ATqoqcI6tyvUbc/fBwf+oiKvMrEuS+WwfVGFmH1R/omspgWuUcDd1IKXHqozyx0zAbXLQvpxJF4NNCMxYDaEDy6/zhpvTBA7eRHqYnbGMlPdoHWXSI46/tifLB5H7EY7SCiIkoopxIFj46xcHz1X8ZLho8T77wTfDEKAV+Jq1Tb8iO5s88IzebYwsKGPFiV0UV8ji0C53XHabTDUmr/Yi3ztnoZhEvwYPff1+SqiQuG382FgjWzGb9lx8iiZzzQb5mPV5LScPm4HOBv48Mepa9EZPMFIjHDAaXaBknc/GeCt0+TQTPdCJZU19KY4xORsykat9VeppGherC4cRku1LguUN+VAZMD54B2RhaG8KtpSwWLvpEROMZNQPNRP7B/sRy6x8mTsoMaqMW6ARU7SopoDu3zCsdtDA/OlqTA/OIj9Mz3IsFZcoec7vbEhkE+PLjujC5ZVlC3iAtPDeZApWoyVN7vYjJzLtHZHilY652NsQ+RZg0nwA/jHKzKEOpe/Aldprab5rLOePivGmmPnQoBynywIMlQ8E+B2a2QhG9vHCaHVDxQX4KHWpYe6N8QDfu6bbH843nKqW4kByOvEHufE6SrOQUW6tWSAclYuJ6UD46cTLzy0gEkTC7Sw4L7nP5aFvtPz4cSYVbIv7MmmQkBiJbt9Hm9cJ5uZpPM8fZw8o8Zkd5mCyWqC0GqRB78nRhyOukQ6Z7jB71ZLC7c3Evkali4/iYRjvwzgR3Xe8inzXwwVLLDQ9Uh9OruHzXH0/+Q3zsqyMuc6TAamIFOn87Te9UJcPNyBi64mYOZ7/iQV+0JSoXSsH/OL5oXaQ1bJ+cDtzQTNyomUXMnIJmH7Cn0ZqD0TEswVD9IRiY6kQKVBKyX67+xIIgHA7kaguebE4VMO0Km8OXIrtYEvPPHAvOeNdFF8uPpmtc6OGORGLO9uoRYHThHTbjz8AKGokNWBMQLc6TvTjBu0OHQxlsiuMR2hE69Pw3CIjeRJVeV8VrUO3pfIh0LVyfDf0tzMebVfjLYzsXVd93xuoUuamzIhlq/FSj/jQWHpeJQbZeIvJnnavTSciGyYj6qXzHCj6552HNXHZ/F6cDl91vogWPP6LQCbXw5Xg6vp+bBv+ZngqOX42jt/cl4cnY29l8Ww+m/9ECnajmeGV2OESc4qFsmhq5+lcR/fTl5oiwOZIsMzmOSIB3zYWiBE5R1xOCeudIwzksZHTJiYDDqFI05m4FZvlx8pVhKi9bmgsrHfHQv3kwV61jIv/WLTt4qRbpT+CC5Ig2+9yrBzzXy+GK7G+dZJQ9F/PLRuYOPdU95eLktkXR+rxe0PlGCf7b7SE6pIjzfcIvqCfg4eS0Lf05kkf2ZQpZ7qgiNibOB5+EA371no0EMH+0nxeAjQ1NYuC8WVQbfkAr7TbTI4xQV0ZlJM36fIqbDLHTeMsbvPstx+agqZggZfTZup/FCnxv4loamR23wdYCQBw1MoOgMi6Ff95CtJBuvRtsKHGX9YO6UKAgo5aJL024apK0GNQfiwLi6mNZm8EEpKgQPKv6mfReyIUK5jySGTIJ73zbQp33z0UpkNbM72wY9VsTDM6tamvmSD1tBC5/1fCeLjzsIPUUO1ncxeGe8LW2vN4fH3fmoljxEMg/uJgu/dTPct0ak2HUj3aKXDPNvyoHz3E5BoY0uNOcbUuk6HjYeEXLmtxn4PFqMnFm2igQ3G5Ot31g8fGA9PXE+ApKKtxDnlyz0BqdDMQp5TTYdJQwN0beTg0mcGCg5y+KF7Hrq1NBM7L0307klYVAe9Jss8iwjG96Eg9beYSLX7UKfDmTBwcPj4L/z4yBliinUCTlwwkIxLG8XQ6cHQTQzxgT4+jboWMXBkkVTwM61iNZVvWMKraZDTmcMjg3ngg/bQ7dMJjD81By/BPnis1hBzYGyIeJ5ybKmaGY43DDOxcrmJ3SggY9PRUpJpOICtGnXg8SW2WDw1wmWP+Rj25gbqq+3huvMSdpsJg1RQjZWC3Qmp/7jglppBj6fxsKtMw2kbHgt9b6XhZ/XEgh7oQrlBR5gv3MWHufX068GHJQ/foxoyiTiM4/fpLyxjiy9w4M/RwwwtDOTRm5XhS8HE0hP3zHKfhHm5uo4nL3ND2qbTpAYrTzorwFQ3OcDe35zYcR7Em5VkYPCzmg4v9ubWqf4oZtvPl49XUilJiqDXkco3LzOoUHTQ6HiZg54vWm24xouwKAFLMwQsgY/mcW3V22hO9oG+D8PE29fJ9h/4xAp/hNL9lxh4cXHfDhxioVFsisgINAG4pvyUHaHIyy4ycC72kxirPeWzh8LhFr+L1L+2B4O7D9LPojoww6/Aqq/ayLIVrEQ5TkBDdbEQeLt+XSGyyNyq8MZx6LNyCOXk+RS7gxwjkuFnaetifgGLnQIWXgcLwcfLo0mKt0NdE+hsHvct8R9GZrAXl6JOwc8YJesK/zL76OPnjnjDUk+jF7hQeUtcSopkgrpwftJhWUuMZ+4hs4zzUcRj3yYqhUA7l2GUB3vgbq3hZ5t4YGcqXHoN5EFXxCQS4tZUE5poTdnakKXvSuobLCGsiOR5Pw8M/ytngWKRw4TttcSLC4vZhJPTsDL872AL5+LAXMjcN4adUwQzYKGEyZ4U02HtA4tw9Cl6bjn8ioaIuFOpkmvhA4FFnfxAWI+hcDhv6lw2dAXldg56DE8QKZUcpH/NxzUF4fjXaii59ZtpA8OGpOhv2HkZXwinDYLB50WFbKofDGYu0dC27Z/jPLkAFA02VEj2bmByS8/TIOrFkLyKy76THHCjoMs3JddAgXIpxnavlB0fhm8fZQIacebCGkQMDvUWNz2PAh1bJUgJ14JP+wxBJmkWXCliofFK3j4YzzSXY8S8dfSULJtw1ZO7IX5YKAzA/OUfxL3CmecH1ZC3maGwrZHWXDOKgW+8XMge3EnOU+jMbU/EXukHZA1eMiEKuXTLk1PdIB+InCei90jE2D0RCp2DITBo0nhKF6ZCLJXc/FFbiGZZLKKrNezwXRHFSwtCwZ5GR9YeziINF1MxgaVO8T8ZS4M7J0KvDcr6aY3WtizLhJulY+RaQ1mcDnbGh/MWMj81/mTpp/7IxBf604OT4jC4a+2tPdxKvU5u5VEzdhU09SuCF1RpmBzWhW/9ETh5GlTqf16Z+qwJQOVIudQT08rmmD6mNhJ8DFE4jnnYr8/+XtZHypmmpLrD7LRqk4dSnW40BsdDS8NYmFWshWu31hD+6grNYupoltVJYlkrxuUXW8jY92hcJKxpt+KeXB5z1qmNuoHeXY5lZSK1BDntiyiExKPR6P5qL7LAxrOO8DPnEnYMKWVxN/iw/w+Czj12A+znH2J2IQPxNbCVHDjf3eGahWTtTum4+FXLB5KT4SSDYF49zaLfw8mQ5mWOSp/NIHwtFjInhOF/u0LBSObLei6p8P023tlqp2yDEHxNHE+yuKPeiNiGJ+EI7+v0Q8HgkH0kRy5meiDx1LCYb2aLFgOStCvm2PgTbkX+TCTxfb6Anp093VKjtSSMp9ROuXtSti2NAU0cBe5/6OT1na74IZZSajUxQWl8nLmqxkXOCquONpkRosnqeHddfKofZ8H6tdek7y1X8nuk1MwST4NHUw9QaDTKdB9F4exjvk4UsGDoZp2mu/3jyYFDghef/MgP/7uIOZED62/NQpOOPQRjXFCBsudh5dr0uG6kMf0fdTB8vp5wdK3LO64HgVOy31QuVsHv39PgjN6JjRjfgZs+M1gQbw3KnaKM4sfqcLFcj9Y7J6J4UMOcL8rHY8LvKE8PQfLNlrBhwMsvtqYCElu8zHDwY1spE4QmOSOZbc54LTyFPNRvYJOm3+NPLyXBeuPeJKgP1nQxo2EJRXRWHm7jyhyI9GusIDxvhVFJ+YaUpOaHYKhrFD8ve0vMbEXI7EOuTi1Px81Z+fgkXXtNGKjHm7/3snp3rME9RSNyJYrErhGIxufT8qESztdSHNBCCq/j8dZHVl4848VWV5nhXF3/ZnlYbngfi0K9/eqEs+GYFyZMxcTImVpV46oYFPTVNiUZY7T7utRlQvR6F6UgbWewzVFdnHk/YklJCYkG2asy8Pw//IQ4+uJrGowlvb/pvcXdFCDN78JnaoCJ8zK6IRHkiTmkQItXMrFeSV5MHVgM1lTlATWfz5S0y1ceFA4Fz/98MUTbeOYApuddOMuM3wXFwTxT9JwTkMuPJDXoTCOA1EZ+RhY3kS9DY3pQwV3MLy/GBu5+XjxtqPgf79NLuiKBFmFa4KFGyNw2wgLlz5et/M8Phd9posTXkwu+IZm4XmRxXjs0ZAg5WOC4OeTRXDIjQdPO2Sh4ZQvTmO8QdnhD+PlKuydsqEwtS0Vaq7coBYRWjRXsprcDF+JnfYusPVwqWCl2HKcuuELyZrIR63tGZCV+ZwUv68WHH9xlOz152G6yBi55j8TO134ELtclHT6yWDFnnDsrtIlPnF1NUoOEfhmbRbuP8CHn0LdFY3OItoJ8bgkRwR6iqNwu7pDDS9dCt98VUCvC+p0/xiLJ3W9cFEIiznGDznn71QIznWtFZwW6ipCUgzy+vRQPMQFQ/YZwDZXK8zliGJEu7DfmrVQ5vR4cqV0KpozDtjRv5cMnXhB90x1gZwEHmxbxcLdoRbBPqVsCAtSILkzPKBVu49WiYXj/pK9gsB7LGiNzKZP7ZB4iRvjquFA6OXVUMWuFKgJv0D33mNx3wdTfFWTjtXfkyFUOR4c0IAy71jUTkzArBUSeKZrlElUz4PeW0qY7maO7u3HKJkYRd44BWDCDxb2xfLQ/KsakP/mQ8LSPKgVyBExW3+cJaMPIavG6DmnmdS32gZmnBNBfU8xvNO8mfqKzUO+ren/383/eU0o7h9pE2jYedH6//1/ub2DPJrVSy09h0hCm5AZcgaYs/2aNDTTnOaFpQNHNI4+uJaH492k4eRPFipX3aUD77uoVH0GjruVB8YvzaAnNYqJEnBxPvOddqtUCtpVYrFr+WbBD+dDTCu1omUPquir44ko/0C4/+x1pKcjGDa8tsKCsnj4ezkC8OEUuBMvT2ZZTcOKSb+YHS+CUOqYBjNSl48W0xWwLlYBeubaUacCFmatWQoNNSxONCoiww89YEZwPsjOi4aWhkjoL+VD2umLgnVOaYKKdQrkjqQyWKEUtq9mYetX4dmps/h+gAWNxR6g9zIH8lcZwrALD8ZZs7D/lxVKnEK6qvwrE/doCdQbriHiZeowT1YEmyMGya1L8fTxMT4E9y+Go9tvUt3vBrCyYAJoJkRBS78npK55R41iZ+FpywpBPScdfk6rIkkR0Vio20eb1/2g42tjsGsSH1sMU8hwqQu973mULhU9RVKlrWD6vVfkwHI+YB8Xugy7mc4dW4j3wx5SejSARux7TR+asLhotjL0e/NgfWADVTSbB+fbuHj94z+md/NXYvf7seBwQyDd4lROi+8cF4gc0MBJHVLwrY+Fb0W1VOefsLMt1IGra/MxVisCN9fnYlGiGt5LPipwjNYnarIKeEk1CMOpAbbSDmZHPR9r+tygcX0BPTKe4P7oC6Qvczk67eaCvW0+qkqmwK83SFq7iqmo4nTq8yUeFb4vgLTebND3OCQYHO6nupNDQXdKOCoFWKDW3VywCt1Nlbfl4eI8A3w7pojPtfjwouEnXcX7QHVn8DHpSDVxUk+hSe4tTM+fbhK0MR5yUvUhyd4Ar5lZwnlhj3+/MhfkWo8Irl1Zy3yl00GwSQtrZefhDScXvHWHRb/NgHnNewWnx/PhlGMGOGyWBgeBOjoWeVG38x8FTYY56BYrBT+7BknSo3fUockZtzZFIF/kFk2W3072b0uD0rbXzEVLXbiuzgWRUD7mvFLEGr9kbBlNw4yGLOHnaWAUFYEe61g0vxKFN+PzsXCFHNng7QnnY+bS1bG7BBk3+LRVkoVDTqOM4bapeLXWFVd8GgdhYZnoszANsgf5UDtXEn9q5yADPByaMge2/HXDiatc8GixFmPyOgX9vrOgsyoZ5P4G48Ck9cyRNgZGnqaDxzEpkOQ54+mufhIq3PPChVGoYZIMS9+V0RNdCeDwe5tgm7k7Jr7gweTZ0vjJ3xeH5oRBlJY7XlJPw2VHDgruu54ge4XaL8rLR/LCEhdn5oNcNwteW6sEiseUGS/VQU5+dQ7seqGP8yzdMcMkHKaJ11OT8BRYZ2iAwQ+zifLEQSKyPQmbfnyh/RYLIXLyZbo+0Bdr2Y0CuT9+YHzMCyY+DYPwHYvx7KU9jOu8PBhgkzBeJx/bVOzwnbQE1iv4op+OOo1urSAH1jki1yIMFlVOY3bOSoD0hw4QvL6Q6GYCPl+5FHkzauySCxNhyCcCQSofVu9hcXi8BT4KnEVsJ9pTqxF/qCZra0YnrRXI/Z1PDh03hrrPLEqFZeHyWSno8TQP7TLjMbalgU5uScCzE1ppqrcUGp37RJ5U6+DmU/GkQyoMjetEGYWd5hibrAwHD/gR19NpGGMYTJfdeUQkHWPBfG8SeXBnGpxxkQdSUU0ZgTkEjqThisd8TFjvC4P6rMBkDxdvxEyA40JNFAYnoelRc7jW4AFWo1GYf3UxTD94tiZjKR8YNV3a0xcJhr55eE3jBRm0iWcqr/zi+Nc/pv9OviKH5rphYH4c9jN5cLbPDkX3RULdXxY8K8XwqM1tYnjOhBl8lA/Tb3FwJIuBcxtikZ2fIbgXJwG7Rg4T08X3aMPkFdiz+QqxTp6Ay7TDMb92KfzMjIHBxDm4sIJL3nbmQ9iKUjpdMgx/XXaBfY0tHJvXLHys9yd/ruTDkqmhKGKnhGNSaZC1IxSaoYg4VfLwjnE+mip3EEWrFdC3KJHs4ZjTqapaZKdvnKDAXwGG9OSpnX65oHQ5i/5BLD4rGmUmfwrCP7V3aK+rAzCXT9CI3JlwytACFsqYApHngdF/yag8eJ8jaVtBV2wQ9rgN6cg35qNaQj+dq8pio14ArHMPxU1V26nXqD6cX7IAcuq6yXT3LHBU/0Il7wLs75jIaTqajR9yr9F17ROBE5GP0QX95HFLCeG6+eKtNB7eri2mu+fkwvUV4XD76RzSXZ0NOkMmuKkpidiqXOaMfrbHxMXd1PasOfQN5OKyYBbK3oZBXqAXmBiZ4ZuTDpB+cgNxicqENVdn08av5vBNIxZir2Xiq0OmnEKzfibFQAq7KgNRpj8L+4yVIGnSXEz390B18/eCXXU8PLHbEk8aC1nauZvqHlxFiu6/Y/Qfm4EEU84xjYqEi1vVceXWdMEa/3R8duIKjcxUxThLV5j3eCLU2MbAeB+hl4zyQGfYHO0lNtNVd3Nxzal95EZoCCd9Zjj8E7NC/yEunD3Dwzb7PEwtiIabo87ox02GE/ZiuM3en055ngoXZ6zAedI2+PnYRLgnzJQtoes4Y8lPmQo/Z3xv+pQOT2FxxR5z/NyeSd5ddwR6PgMsPglq3n+1hxMHgqDqbT6enpuLYoc0YGGLJqhuLeQsVViIq5wD0LDYBUtfsLgyMpj8fs2ilfcCvGslB3MspSF2awpIbLAD89f5WHc4A9eoJqORFJLj/8I4MsFZaDmQh9EXHEF0mEU9jRRQf2lMJN4BfF5sLzgrm4g1sbPIbiV5iJ62AtVqDbBYQlGgtJWP+xP3cqaKHKGX//BRc1MzMShkkRe1FB7WroRzSzjIcZuLH39xYaXjEjz+6TF5+s0cF83zoSmcTNg5aQbcn2/B6QMVuPFb6C9ZeeiXzMfR2XsFc1un4OFL1rDyYh5GCGwxvS8ef6u6kWKDbrru5GFSWiyPH2Lek8YlFzk+B1lY9o2Hsx/3cg56paK4VTgOL5mHIYPmOOVDFK65lEmale8K3jZr3FTJnE3LL9uDc+50rGpmcVeBJIddIArbv+TgH9FsSLHLRBflMBxGP1AzNIWj4u/pAq+TJJnjBT8l/ejcqWqkYjUXpvdmgKrsPyLQuccxqnpFtw6n4OiYER7alIO7yosEvHtXyewAQ1SQ8iC2Dc74+soKOA3P6JSubhIwLRUkOGKC6Y7j4WpSrOBzej4y5tFouVEKNRy42FfnCor/8jE1WhemVTnAxYksnFsxk7H8PR3HrC1AiqMJVauTsUHId45izRwTBVNUrPLHKaMp4BLCw5fLkkGvKZiRCdfCb1UssJoRuCM9CLPlGQwhK+CgQIfk3VWHgIct5HRAusBYzodkq6qgxuQmZtZ0b5zr6UXYpdnQ5qMGE1d6QtzvAfImOwT9HYPR91M+DPRX0mNlYlCuJQPxH35wnMevhH9qohj7hodNrYGYyLEUesgnesZgJiY4lZIdEzYy/e8rONZb0jD1QT6c/b0YY9bmYoEPHw0W8gVDrj6ww+w1bT5cRRaphaPqrVw0PrAYZ/5UhC6za/TISiWUPB0BBVp83PspEvpCeRj5pZ7R9rhJprZPYJrcbDnbdh+l/jeXQf6baDy89AIt+LsIqq7IY6pXIE62SQFBug9sHeNj2yVziBRdiav2xcJuzhxQLSsU/Ce+DL+rB9Pjdwn+HZeIepNUoF7+C5H3l4byJD7Ou5KLvPxjAl3xTNy2/RNpilLBV/om+HQ1F3WfTmUGNZNxy1Cz4J3bT7quLAUvCn0mrywLC75MxswqK7R7ZgaVXnF4yjUMw1VmEZtPfwURKn1U+0UZ6Zy8USD9OIEaPOq1eTHGQm2yGeYm/SCn3hUIZrDiMPfMNHorx5luGl6KulkOuHqpHv3R/Yhc686HXltzEKlRYuxeG0DYk1bmc1o8DWaEucUdpDoRPPDoF8Gd3aPkfXENbb4hRdgsVXSd+pXeaXADtXfGdO+ybExctU2QFsgDa4kQMjdX6E1lC/DmGhbrmy9Q4xsjZOXWRFhZ5gHb9ZMha1wrzW21xAIBHxJLwvA/iRiy47Qmc45pJ6lZk2FmlxYo57jgkRAFNJwbgO9W/qJ7WrPg+20WSwoWwH9XsmiMSxR+WRMLnZUccFeYDxd/7xSs4I3Rgx/VwHTHDJjXtAxjPlQRjDOFORq9ZP4WJegoD4arlU601FpKUDqfh3/nywK2PCEXnqSiSuJrhs8IsyrJEvL2JOKXmwG483oetCpIQ9yRdupgta6msreTVotmYJmhEZkSm4/zX+9n7kwTdhqDmeRC7F7B+cbZaF7vBG8/zaRb3SUxY8QMHtJE7JV4TbQWz0Ftp06BxhUb9O4RkFkvI/BvZjvV78gkn7uXwnBTM4OfS4lnoz+0qrOw/bYc6thFUUbLlo7PfELTRnLQxl8ddo+tJorRU8BJfB6mOvtgUYofXLsYg9HNBwX7KzOgyzUaZE4eopbvZuK9nhQc+MxCxLk7RMUmDoc+pqOucQ5o91hAbdgHeqOcxUEfJ+R8OcZkunQwCXWLwU/oKT3WprhguzYairng84vCmXpeTcvNv1HP/iwI6zaC8z5+UPBGnlpsicQLB87S+IN+0OB+XKAqdUhAhnKgwygErraa4GfTs4zmsRgsGLlAZj01IclFejhifYCqiSXDluP65NdLCWwXctEC2eVYeOMT0+EUiqsn8aGquoQ8QBssdoqA7V7KcODvcSq5iYF/71SJQzgX09/kg//pr+TmdzHB8qD99E9dj92P22cE10Wi8fusXLh0mMWqpqv06L9QaJHVgbj2fSTGjoX+1Hqyg9XAjKssnFKxgtslvwlnviuO6YeDju5XpuaCDN2wMBVuXbQlCw0e002SCUzrruV4vrGY2r30gSm/fXFG8w0yqTQbHNKiIN8nCDYPLYN94IO1FhcFmycz0F8SDQ5VcbBzrIN+nu2Oyu/LaPiBIjLB8Dj5I2T3G894cF8tClzdeKA5wsLdoo3k0Lb7gpCyn4z1lFxIFbLS/XhV3LhwkPLWJOHrdj7Ync+jObknaMUJX9TsVYApMs4QGZAP17VCoecgi1mr44A5dU3wTLuSRM1zhoML56BJjjieSllCZaq6BbEO0WSmhi9C0nPi39hInz/WgDUVESC1eIx0jkPiIczUORMHmOxyWfzeYgneZ2PAf0gG/9VbgVVPGs5hPekcYZfVnDKbNPuzOL74O7laZwaTHKxrnGQcoMLbiLpOm0nfDbHoHa4OKpsdBOPEZ1N7yIWh2ePhxjEO8dZjMOGHHorHTcbAtVvI5zsd5KiyKLqWBaPbzhRIThtlulynoPvwZFgp5LGH6z3oS4kKcuYKA4XXtOnbMiN6VDkb6gx+kHqbPQyHL0dafoUhFLnTzRYEH92eTb8r3CZB6cvJoMw/0nB3FbVoVMUWF3fU/CVLis2+k4M303HfIAtHxiToujEDWtLEoqsEH858z6bHkseTwMw8ML7xgTnRYAOH+nfTXqHPJ2x7Tl/dHhZYZ8SBVq8afHoxQnqSzlDHhX1Uz90fS6raKMbmwL9MdbrplDWU9tjZpTz0xeb7F+lQ4le6emEe7CuoZ5a/yAGzljZBZ91ENCoXxdv+Vjgtqpee7LPA4Zmm+FJLlFyQ0qfOU94IJvadpmoSD5hDK0fJXu99dq/OnhSMCWaDRE8XXVbD4jd9YxTZE4XXhyIYqzlRMFnlEP2ko0Wzb8ViQFUA/XKBDz3uzszEYyxo38rDnPV5zPs1sZyezmDoaeyg34s88VMgD9+cngKZwxJEwSkf7jWJERmJIPyWcIROXjOJFtXutlq6fjqssntMDu3TEmiMU8bjwmfnKMnCU34C7luSj/fUokFyHhevyGjgmiZteu9QgOC0bCON++2O/+k6C9xVa2jdWQ/4bH2UwxXNxtcLPeBvGR/MRbj4aE0Eygk1udr3H9kcdofmFUuSvTddQGHvBLzwaz7M6a8TLNDnoroKB/vbAm0D/RLRecNNmmN0n7wSsuWkjWaYJbUMirp1oO0hH55t0aO82jQwXOKE2+/0kG2JM+iXIFMkuhNQ9+PFmqa+EZpzwoXaec3CRrE8HJnbQd9PaSQHNibivk1T8KujFHm1byMpivDH1lc5aDS0myyZmkRc1NNAPdWA0f8gQ3rN50HYkXwctysOOzVkaXX1I0Ht+gKByu4u+hgM6ewnmnilLwoN5qhUpzxYRbZUpMIFdpiROr9KcMLUGuIactH1TgUNrHXDeD8jYvNzBp7SyIaDX1nYKelG8s7wsfW/FEx4lwPqJR7k4RFnOD2DRZ/pFczD1XyoEU+B76O68OCUPiq19ZGDzfEo1zQVj4wfpjk/jOmjRb9o4P0cnHa4hyZJzyZR9lTgFrSD0LhXAkeJQDg/rIjMzVzAVePw3i4rGFDmo+k0FbRxnoD/Kj0Ie+0zEW/ioe84B7zl/47Z7hOJzI0w2Kv9jxnVukr0d65l5tVvJWrRh+mVg9FQE+aCnCdpcOc8C/VHzOglUz7s/sGico87LJFLZNa0h2Cq01wcWSFPpYPK6JsvgyT7qjkKNDoFmud5cM7lFrU4cP3m/OK3VGKFF0mYogO3RVnsfDqJXHMaJBavHAUF81Ux/pgu2YCeKBYTgW+nZ6HoEmQ4HVyYWSdLmLZUPLdRCV97WVCRwDXUTYGLHZf42CTygdSmeQj9MRo+LyN00iUWZw7yQbkmgiacLSWSceGg/m02OPjPhKNT+XBB2peIBKei0f4ATH5wmhw3eUyCz60EfkcG6f2yjJwIcMA9ec64WhvAXsixq9+ZwGT1ZPiTLgWOvyvpBlFrWuzihmnVzjjy0heM2q8Slc3zieU7Flu+xNA7Qa7UzfEEkctKhZYUbfj4vYXUc/OBtGfjKj0d/FHjTwI505nqTCt0/88GO2bvJx2RNujUaEJKfPOJ7IIOcvltNhkbr0vOjLjAz2YJ4uW+EJS27mXYaVzoumpL7R74wzRJDdw97x9NGiskUddzcO2RlWC3fT3VUI+G4ll22PDVCS9Zx0DWdiv6UucsCUtMgILAHfRZbBAxXR0PopekcPhCJkiXaMPeWyfoGfmpmHKqhOjsyIPvUvkwZyQBH47mwMTxv+hQHNDL0vo4/N4Ce1YwNPijHWkJ76Hjby9gFJpngsrBWWCqOg9fXyynHe1asKYpAUez3wroBxb2W9igfLkOzls2RPOuZcEi+VDiXmGBS45U0qokLhbclQPxB3yw/uAHrZ0niI67FVNYOw0cr3nCkXYuHPovGj99S0MXARe8ekOoYxGfyrhzcWHNaspazIEST0moiu4hbS5XyeiDi5RzTg7PzTlbs7wrCh4tWgmN1TxcGZqHa2Plqc4BB/TBMKI78oCWLq1n5uSpQprcKNlxbhzKSPJwWvkN8vZNBp6rFsN/bvaMTJU3bGng4t0mDs5+EYnxlknAlq6i3CpJaI3yJX4X4nBL8zIwrGUx90c4M3vXXSrBJOOR7Bg4OTUQxdeaQQGE46cLpdT6lxWIfJWjw3GRZEB4LoeeHSKio7Z0S64L/Lkgh+vtsiGOeUqcLSfirWvCvvIkFJtXSILFPhPUOJtAsgvekVRXN5zTs5pqKs2AY0Uvyc0jLAh2L4BUz1oy0vaJ5j57TcvPdDH+Lg6w7Z8+J0TjCtHzaqHVv01gy3I+Vl6IBkv5aPJb7Tzd3CYC23aW0FFiB592c0EjczFiwXvisMtMuJ7tNCk6HFq9VIjiGnHqujALEp4HkkOOJYICuwL6ViwHX31XguPWIZidagaNVcYgpuYIF7t66et1PPCpK6YzvvjCjrWxgspua9J8JxtGVIPgUps/itNYEvYJUNPMGHr1TGGfhS4cb86ihYc+k2VGv4m6RCjEnXGAB7LbBGO7C+nw3l/kvqQFVH9bgIt7Q+D0lkJKq8eRhEgHOCg+CT6MT4T82DRcdi4LYecoXVx1myg/4OFVi1hMKv5IB1Jt4NjnVYR3tJp6aUsw8TEJ5KDSCqjipoCjGouFb6Mh2M4J9z5bLXD1mgB/vdfTSTYxsGSnGw5YzsG0z9HAqdlEb95hwcUxAkva1tKFYWFwZ0cOWO6bSWLVw2Gqy3Omzp8H3k/F8bauKzozYhg5Q4RkrF+E8f/xoPO3NQbIbafMTy8Ik04iznNvM+WUB+8PJYJieyCMD10BTX4XKc9HCu7+ZrE8itbozUuCIvFIrL3NhZKZudjs7Ymqeu8YO9MiGiG/CM9/YWFNwQzMmOENfTXBkD4hAGMvS2Nm4GJcYSMN0ZxF4BCahR3TwyH+XAZa/3WDBRnxuN6ijBR+y8RYBRbe/9LC8w+Wwodydeb3blOo+ScP0xMS6Tz5RDxoQOibFBZfBpnA4hYDDGsqI8GvnODEmvX0ktxB6jKfC9EpK2C3Axdib3li9stpcMRXCfU/vCDzfnFhvWk6hhXx0OtaM2l84kMVulks3mBFsGA3CclVhe7MDEig8cRscw7kvEjEi68TcWzNTBjeHQEXbthiwQY7zBpOQSNtMbIh+DA5WXiG5PamY/2LDfTSehZuPoiH7/97lzMnEyqOWtNCDRshf6th5rR80DuSSSaW5ZLqJwHYbm9Ly3bbA4g7w7ijHDhfv4fs/i4C3QGh0C2fRRK8U2HqT1vw7BVqqVYL5O7ow83Ak8SgIBskXtrgK/Pf1DmPC/deCbl61SpyU3MTXVD4hvpe1IAd9QRWvU6ELQY58OxCIk26rostOnuZUy+dsaLJBPw5MkjO2+AR1VSwt8yjgj856DN7H1lyNR+9hd3O5+Iy1KyLAfHsI8R6fQy+2m8IY5ZTUEJMuBaVPXQK9x7VlYnBgEBNKNFmYUKiMURlbqZSg8a4pLOXGGvJ4+gQCy9WzMGSncpQWmSKB9LS8Lh6BOr35+GjaHt6WIaPbIAuiWp4RPU2HiNXEsvJRNUgMmB1nXS9EmbzCxbEt/Ix8b02XHG0gC+/UnH11wEycasWdrkVcuK8uKhztpKGZDMoVpGPj//bQjfuWgkrwwrIIllZYjyVhftBXbT22nJ0+mmN075PRlWNjcyft9fIl7ow0tm/HEebFuMziTDQGVCGB9X7SWGaN7wcVCJlE1Jx7AwPbh8WgV+RZXTHQQ588TvJPE5RQjaNBw0HXpGYk0a4KICHL4k5wYv3yc/p22i1zAW7no2GDE4V9rAzXLB4zcK3j3xIULfH4E4DiLVNgekyyvDvrwLMyn1D3eRWorc8Dy1EWNwldpBc7l6GC5OKBBpTjCHldxaceSkDd7FXYHF8AZxdlg/el3Ox7Vc9lUkA+Ht9UABrVXCR2heSVTYZ/j06T5tLk3DiSANt/ZeF8QO5cH3NXZr8LA3/G3XE5oZwiH1sBjaaLPYJvexZxgIyOGdQsH3IBBKFvKrgdYhMeBBFj2mK4JnhzbQxPB9Wjc/H9f7jodN+CQwo1ROHOGcaeXIK7jyeDhM8Bkj6oUfkskYsThtk8T/5i7Rj0BDEFijhaF4eWBjzwVbaCE8UMeC+2hXzgtOo7Z4aOmvlF9K9kQ8/newg+6YbSDx+J3jy+Q9xPXiY7J3DA+4OIyxPdaQJy1mYeoIli9PccaBMyO1uTli15/8oOuuwKL42DCNSChIWHdIpICAKu+clBERQuru7Y6ndscUu4IeBYqEiGCAW7HlRVFQURcXAAJWwRRQVDL79/t3d65qZE89z33vtzHpCoiBfhq7wgMomQvlFHlRfLQTNaVKwMOwMZee/INe67bG62RmbvzKwWSMHLsrfo3qVPvhY4IbKbQw28IVg/r8ccFfhQdbORMg8LIqwhcGFHw6SkTIh7LnrjT7X1YA/NRbe33UGB+8i+LfDH7YYueOr3aFgyJ2MGvIRcE6QLzHR0yG/WgtX+3ST4CMN9GIwg1uWRWGkXyeJKE6CDV6v+VE6cbZNsUFoHb2GHn7TT0ceM3jayA7yOyIgdvp+9jKVDHiVoAxP5h8mHgfU0Ew3DW38GDwSkg7Po+LwU2MRTqiFwvj6a2RbQj//6qEhevKkIsQlzIXPc/vofjkFdsAVLup/aqQ6L3cROYt8vBCYzd5wLBs+loXggcbbxLcrHCufhoNLiCYYvJGBlycfE6bnMSm38MNdrDzklTUQw5VHyHKDPPAzYPBXazDER+TD7it/WjrXeoJ3SBRkenBh7tsc9HgucB2VJDJf+Cd7w24GNbbPhLcfDOCgZRPNvOcMO5OLsUtNGD9xC6FH0gc2u7BQ/E4MGpSYg4tgTTeo3iaeR26wltxeRN33xyAWzoLXAt8xvVNDNgbX8WXO5eK0r6L4ersLPPyeAj+GhXGeZTzcW/WUzV1tiFOsVPDaF1s6/vYXDfghgR/e5WLzyy3kevE5+kgyAv+pp2B0QiDcsfSA0vOm5BwrG37mJGLzKjGM6lAhfdO4aLr+Ct282ha3JE7hj80TYxussYZrv64RM984XHQtHIeu7WXvCvCHwhIW8jhj7FO7kyHsfgYaFXaReb2nKM9dFg8Ietxn6XfyfkUGnMQcrLhynp7bIUnVOzzwyy+BR42GwqU9HFSSY9D7bCqujcvFeT/88eqYHjgHL8CnujxUtA+BqVc1Ubvdl/jatpPZswicCSeYoXWdrJGMgXkG++j1JztJzdF0EEubjqMTDDxbVgTBlkE4YC7w3t8MqFeKYY/GIpCdcpEePF0AzFlD7ARNuFmWCD56mtC9OxIcV2Ri+CUv9urD/vj5x05ywy4TP1VHYOBCwZpYUwiNgWm4lnuD5C5qJJp3LlLDfkdMadaET0t5wCrSRXe3bJg1VQ/EE+6ReR3z0GchFz305uL+jf/Iq6Eu/uPaqcTC4xKt752B1iJraPb7AFpLjIHl/oq8E1LBRcM5qPa6CC6rqJPjAh5KXsVgu+1jujxqbgt/TyJur/eAELxJmgSMcd1XlWYKzikpwAq8d4qgV0s8fCvKg5a8OAj9boiO5mn08HA+Brp959/8NRlTjMzg1vE4nChrptcnS2Gzsi1/MHo1nSJEQPkdg5P6c8j+b/loubqXnFDmQGsIG/z//72hljOZ6IvEGJXPpO3yP1aXUAdZNHKV3Dq6nB2uKfBvC2fI+iGDCc8ZrFos8I1uH5p8KQe/+WfD6tfO2L1ijPQJ3tv/tgT9lYMht6eFfrtWBI/r5hKra0Yw7b/HZNrpGyTmiTl5urIQIwMccdbQHAGnW5AYxTko12oLrrGBOPjYlGzlh+Jb+9tkSoU1HC/eTMgUO3DT+UymxpWA+oY0aDpWBNMF2dXO7+Vb2bvgg4j3NDE4EZaeeE8bdkRC1DoT/tOsZ1RZ05wW/bhMlhVmoI6sMGjEGUN2oDQm1hVjR4cnKgwPUIPJaZh9ay11FbjyylWR7JXnWNgWEgPVK++QN7FJ2KD/nRjHCpGFfCkwslpNQz/Nwz9lJfAzvYy+HQwFobuf+atqi/i/NIrhc+k0uPnyDg2q9sbu+ud0jU0wlrnYg3kkDxbPkAd8kIMJYoEwFM/FVZPOE04Lg/P1PWFr33Tilh8BebLxcF6GgWH8yP91pZMI/bTju/l6g12JLH6WnAPFkU44v6uReCjwwOVhMGx08IR02QSQEj5DZNKNQZ0wMNabRy8mpUPKuxd08Lg7zB6KhAtmfjg8nkiKfb6zx6dzcafAx7P8efB5RTPlzEth21WIQ+XYfTJFyxu6MwNQPdIFh4XvkSH7bBB/mAofJkeB8ZNNRC0qHhQl4tkrjBi4NtUaRpvc6K5Dgah91hxUT71iH/V7RPb7LYaNl+xopXYvLTm4jsz9mQfsiTFy69lhsioxGHx1OTC8nsEL3ZKEb20Bx1Ui8fNqHzDy9qDzmFv81v11ZGtVLqx16CJnx3VQyKoEuJ8ZLCpsIWHOZ1ntvm+pQicDX8snqPODQozKC7IWfpIK6hV/6bL7s/HlTwbTpGwxPKKPCq/g4ckiAaN9nsE/cPABkfP2RUl3F7wm2FcXfqViqFgmFKtGgkhdMZT7sbH/djK6BVeRZNE19NKK+yRw8zANoQL+To6HT4vfk1y5FOQucYaXgtw5vDcfV/1SwJjwQFgt6NZtPkl4WZ2HO+2XQgpTCJd2vaIDWZ/Jf3kpkJO0DH+r3GC9bNelesYfyFHB9YxcV8Dq3DVspSU5fGpXCPV5PLxY4IY2495Y0qqHm1ofkJAIHgiZRYCdohKu2lLBVp1M4C7Nx8+XBb6uc4TF/zZGDAXc6T0nEd4cL8LHZZfps9IN9NroMVr/4DUtf60Ll1w94cO1EjhzYZz9VZQDa9fz+DWFDH77rAEvE6NxoZ4nvjjhDkke4dhgNJ183S+4Fgku6B//Snb4raGnvhDosDIlH7VMifmTXSRA0HsaMxhYZ1UIV37xIK57hC00zQCCTYzR+Z46PJ6gdK6SIlpP66cVL7PQ0N8IuPE5OJ+JJDsGtfDJ/dPUZCcPD9xPBIMCV5SxTIMXX8aJSvB9+nlGBtTIOeDJPx00/8UIqYlcinBhFlQtrSRPj5aSCxI8PNEXiLd+cmE97xN5SGpI/RdnWNS6myT+fMn3W1kCk/8LBSVLBvycB6juOxO23sk/xDrUEtSFFlCTBS+o2g1zCGVxsVfJioq/4xDTKgt6TOCFj455kvrERcBZ70ECqqegt9x3ErE7CGDkFqnxtMCjrmLUZqUYwD4ZqEsqganmBBfNsUUmIBs1lZ/xZ8z9R9g5dSRCcI6hP37TKcwg3f8qCWrvV5AnSnuo5/Jj/N5RBsp+pOLN4mJUWMbggWklECHCxm7/Kpq2jQ3nOgndfK6XTBGsL/6iUTL4wIoMXs7h+wwWg+NUCb7fLGlonbOAGqYWQZubH449TcGHBhnk9pQJupKTBAE8fZz2xQoa7o1Qr/ND9HHrfHT1NaLJKQY0fflKVujT23TXWwY2uRTi07JDFPgysOBePlTKGaIFw4OMN9n4Qz0OdwhJYxPnB9kjXIzPn26k9e5umH/xMV3TVoL0ZBqdIVEAsPQYcTp4ouX3zXkQMHMZylQWwbH4Imz8K0GdExkw/vKWKuq00+mxrTTmowLU1rTS7ecMkZ2agB5H8jGo1h+m8MXQdHwn1W1MxHGtcEj/wUIdi2KwkeuiNrU/ydBkSUitHaFR+Iy0qhynx8pKyfUbSjBgmgc/lBnwLkyAT0lFaLK1ELd/NIKU3xV0jd8EvWqvint031O/zwycmSqHs6/HoYvhdiq1MheeHtIDmeoc3JHGApsmLnb+p4dvBZlReqmL+E5PId/f6uNssxJwe/6LRD6ZCeKSerh+4BdrdksWlf/kCDu6I3D1hyyQu2FIri12gAv3T5P//1d7JTxnGz4xg4QNqTBgdpIaH52MIHGGDs6wRaUfDBhssaOO+pPplcEKytfNRPuhrzT62DaqKc0F++EUTJzwossX5oPT5lXsU8eN8O20cbI+oQAtV6TipPSf9NprBbwckoA/PvJg1uJ0CHqahG/vbqenLVTAttUb3137R2/d4cAdLQ4ecS4ANbYIjvUR3H0pDx8bSeG3vdJ4wToIE2985Gc6pmJ4dQGO1zkAV/IO3TZhiqYdbGzSsYLdPdIwELGBysJvIvFUjMTIlODpCF/4/m8yim4OaDlyKRunmieDxx813H+Ti4lvBbmVLoZBe5JQZcEKckRGGVK1F+KD+yL4qSMWClIZLFn/mH1KbYSKlvlBz75JkCiaiBcPG7YEWCfA5VcyeDj3JN1bykOrzI1U+k42GonlwJWMP2SiOgIutDP4yJKHV44/o2EGyVAu4JWahjjYek0JPiwUw9s1Q0SemUbV1hbh9qcL8KLfBduU/W50SaIoXu9pYktrcDE/9wIRqb9KXzxUJYcrPbDONwbdlnJAuJ+2THrCAJlbys76FYZ5MwsxrTcLwvY74cT+v+SQjDMWczlo9jcDjJYDfl0DOL7/Lh3ojIS7P1PgKEsW7k2bDOL+G8nZE0YodjEX7wWo4oe8VFj5VxTrl+fh7+F0fCADaPhxPs5LycdvEUVQBeEoYfebXJqVii0mk7Do5keyLvsOTR43xoxcKZxm3sa/dvcfeaq1l/qUxSA8M0bZIRNUWl1Ocw9o2cZUhkDbuVmUGSmGsI8Mvv2oCfOqbvJjEv2pUh8H7Kd4YckOMQw8owrBP43wyVUOZORrY+lBHYg6VIQlNwJRqn0B/j5vg0rJH2lniwuo6YeQUrkDpGz9bv6PfbEwc2KC1GUwsChxNs43TKVq1UqYSrnQZHSH1H31A5G9PLy/I48EzZCGPqNYqEiaBJt5drArNQXPjTe2WOuJY+HeSXBe7x9Nf1KI85Jl8aT6U3I90hPuTvbBuWuD0CL6KClWZKDAjQO5IQCvs4vBWdYIfL1DUN9QGvw8ntH+5bfpPxtd2DA7HNPTBX6Tw8JNHz3Az7+Bzn7sinPFu8jn/xiQ3JIMQwZZYOYzTuZsCoHGZnH0GCnBhqxKGtm9CJK2BeCKvwzWXBDGSV3u2JS5i0SKMZA5XQl9RLnw/RoPDhZogxm/GC+98WZvapYGp6dzcdZcJ1p0NRX/rfbECHMp7PXKhtj9riC51Bs6NKLRIVuQIe0ZoCxRBCGjBmRx/Ezi/TEBv3hI4H3NKPLT/ST/4+MbtObXYtz67x4xGf7BvufojOmHGdg1+IvEDJ2liSMBkNQRCr7R52jUgxj44eoF9e4yUKaXDksIslYJxkhq2A4/q3ChWJLBykEHsLpgD0ccNrBfDOuAU6I16H4ygCVtAhYescbznhPsj28kcEa5D6h1UZIuvZ2fMZaP5SIGYOlfRiJzASVklpDRHEVMe1YEtcYyYHn/Nqnw4SJ3dQ/Vs8/AjS9kQL/X4lK+0VfiYxqER8ME7Pp1Ew0vjoSOx7Zw8t8bet02ExiFXeSvSCs50H+dbamZhlwBs1QPOsFxzhQ82u6D+QGyUJUQBglGGZhtlId/TuiiNWsBamyJx7wzhTjw3Azj/5tPPnBD6f7DymjZGUwTh1yJTroslIqlo368LkyJngxOccl4Lc8BR24r461NliASWYRlgu75FMXBl8rfWySbJmFj8DjfRTUEU93yUEdyKj5oi6C+jA3UHGXQWjoO9bSlSHlvPFxuMMN9kjrYFuRKcT+HtsWn03XlgF8COkjoF0eQemwBJf9k4YOSKSTEu4P14Gp+qUEEXvZKhJyBbBLx3BhGe/T4O6pb6IkNttDx7RmVPJOMeoeXwcbDFwmuW0+i9k3F+P4SSBsIAfX1SWC2hsGdwoGw7iWDYd/0UYMzE4T3eoKQeQ54XFXC3E4eHmziQZPAa64LF+G2LXKAdT3kzJ1ECDncQY2zePhbbrNtVs88UterDcc08sBF8NmvOoTclrXHRaOF+PVQIvwXWUqOjuuAaYACCr2fhO6/smH7R3V8iCV4pssfa7PSYNF9DRQeToe+ylD4vcUJP7VG4Kz9odixcjpM3+sD18LqSYTadCjLYzD0HgO37ziBdXYR7mQl4ast0riuUQUao0Vgx/N+unZyOu6Nz8X6Ewa4emk1+0q5OzZGcvHEGwXMT1ZFIzsuzLJ0J7ieg6dfzqRijfMEWaCA+4I+8cfX9vOPyAhTg34e3jK/TfuGemlh8X46vEAePtRPB9W7e8mpsxw4Xp6HNhK5+FnEHtYZBMJlg5ng4GyBfuGa4Fa+DD7F8nDM6Cq/UdUPr/xmIKv9JenLUETl3XlQTTLARn8D+RAscK6OPFjh30gCn8rgpHouRr3Lwb//EqD6XTbuexBG/Cr//7vBWropkkFz9gzY+8gJmJ8xMLp6Idj28FDEZDu5r8LFynBJWL8pDFnf8uGY+Ea2UEgYZgkJQ/iZGNwl7oqLa8Sh87ciRh69wF88yR9mjJvgz2YGjUL0sFYtmb3sixhqfWqlG1cYYva8AJR8wsZVdwHOyU9BTrc/3BAWAovbM+D87QiUXesEMorHadkEgxL/f774LA4UZgnDA/MN5EqzCqmOi8LS3tk4/CAZwvok+TkLishmwWdz1inhukOxuOb8POhelQnvDjP4ULibTh58S7N2/qahtUshankXebTDqKVb5wHtfsSDl+LfSZUeg0mnP5Ph4JkQkytKR6U7ibW9FDWp4+FQwFQY1HTBoYgRcv4LAwG//FF3WgRcLAqhnq8L4YxxCIR0dZDxYRm4N10Zn6sAuMvuJntFefhse+b/n18MgznOYP/0LX1AneG7gHN33QqC5bXmLUezQ4ElmDvksEGS4WHCu5nQ7FUA/61cTL59nQ7pZbm4e8lUaD7/ngxk9ZLMo4kk7KA/3kpYzP9zzJWoJxvAN/dwKJWIxMJbt8lAlTKmR5wmK2Sb2KfG35L2CQ7uUO+mrHv5mD/STTf0dtO7EfeI2mAc7v59gyze3kFM7v0jMvaW1E1BHRT0NWHVnFm4Sf5wy/y0aLzVVYSOWVxclFPEd51zm1ySjMFwY3XiftYLJxvPAJucy+yvy7VbkmdyYeX1Yjyjr4Kmgm728pxMb7l12pyX4KKVhx6ucOTCotwrFN8U49KfhriHecHefcUGFhhE0E+PJqOjxkdafvM4CTylCUEXpuKH7Zn0gmBcOrq80cdtO+lYeoa6NVwgIGmEjSocWDw3FS6uCobiySKo1BmJY+NmeDvGBKwaVfDKrXZiw9TTW0nvqUVjEH66mwcP+YHoUH2S7s/aybe6mIPBVYthLCMLioPtUV+QTXlaiaDcwMGl87XAemYxdLXq4BzFOJwkHYmbxu2A9dMclM4fJoai0iDWPwcaJAxo/TkuaLWmYExhIK6/18OedDGVZLBVsWvFfGC9X4jPDT3B+WUcslRn0YkEGbBo2U0PXs6GkH29/MP5WTjXmQGNaQyu/fqetl4txBa0gdEWLXhtZU093rjhnwZ5SNZIQXnHJAyQSUObnG2k+mAW6P0coFpdBJ/9y4PosEJ0lgpnfz8kCoUu2RjRropaYsnYlxeCF2y4aNKVgQo3/fFn33x6NmAtbRSVhKG3oTDUFEumfo6CNRCCQonTYKBaCxK/ahIvR234mhyEE9mJ6Lqvlpy7b4bvfpXZdg0yEB9SQN5JneK7tgjmd3Y+cGsXQV+NHqpNxIGSSgYeVXQBMS81THfT569tzMZPudkQcXYhTpMMhH316fRRRgzdttsEwc4THBstQVFPHcYd2qmuPQP1Sgp8hS0pGHlTG5ZULIWtCkX4aroNztdyQFW3RLDwTMDvQ8uIcEkJlhU10/yHWyhHq5/QWyf5lQu9oY2VB2HbT7NjetRQ9nIW3NvIw00L56GI0hIsHZAkw3+U4LqrDZVKi8Lsa2700ONCrPi1gqi7JZLEqkekdttMLI70Bk0PHeg7upq8EmRJpWDdPN/phVLLHdAzWw8sFXXINb+7dKuNBfTtNcD0bgd20jIRqFl0k29VNAsdtsbg9woj5Ovq4a6NGRgVW07UNwXigG4e+o9GYJOvLH2hmk+0HhbiiusnyPoGDdq+0x4iGmZj67tcOBjhzfrvoQ4Y+/FQPDgW44OKUS1ZHEUebKLrkiqoz6Sf7HNvF8NfkUyoOVgC/E3nqdnrFNDiJ+MHAzG+bmEk1CgY4/H7n+j8An9ssLIiFz/KYY6sAXwa4cEg3Oa3/mUgcFQBWZ9U4dn6qVg5k4E12jxUXZmBIv7BSIN0YbrxV/JP4AD1HFdQmqWC25S48OfeZ/7u+IPsV+VuMMtvDdtf9QY9w7XA9a6FcHqbG+5u04SdtgX8KbuN8dgGHjzwtkDfF0k4/lwHz766RzqOZGPVz0z+rNWryKHxMZJ8hIfdHD9cNj0E/ry4TPdUisH8oUSofKkL5ud0YG6hYB59H1F3Yoq7iBT6PZ4EaRLKbCfvdnouORpnKEzHde1peCFWDxy3yhHdNYWo/MUDlZVNcNCjlA5dFseqvQw4WizAsL//v4+TDZ4yW1oCHCMw8RcPRp++JvddSxAPXCcJZ9Qh4k0XW/SzBNWO0YP34dn47vVLemOxMb/52GkqEyWB36kHekh5NpsbcfFinivOvsLAnicM+svVsW+JZsDYnkQonpEJz/eY4IMcd1z/oJl/fSIUHaRv0KWsp8RE/y51k7nOKr5phiHWNYQEi8KV9AIc2p9E75eWgLCRGUqYAkYN6YFvmTM7YLiUVVfzh8j/eklXtXKhLK0Ink225jfe7+KH/eKQgKa/dPvBeFy8SwSFVy6CWNXP9IeMD3VXmYslScGwtn+CXXpsCsgGjtLiCh0Yt7lqY5y/hj0/Sxx+aNli+RiDVw0EGXPbHg/x60jp2xKQ+p4Hnin68AqS4NmiNLJf3B+Ntomi38FyKlxRhPIhp/mr9DPBW7oEbhp8pbpJujhHJBXTLAit7TalbKN5mLx3LiKaEqPBE+T+BeQrZ93gF2XkY/OhO2SzSQ87STkXMp+kIL9bEjg7f9ARZQYrIvXx545I1BbfQqX3WcKJh9bYqdVLLgl8csRwNs0tBjxt4wnzc4Rp6wE5GL8oA0UeDIjMKKM7rQpx+FE2apxjQcfyIrBWMcW7jUWY1ycOqmKU8lpdqPcMNcKqZqGPgzyJNYngv9A60XLjfTd5lH+W/7c/GF+kF+H93jtksbk2Dihagan7VVpcrgYDVTq4rqeAomkyLF0ujFqTjKnPNT2MaxXDlR/ksePVD3aJph7OlGPjCRkGs+paacMeU5TwkIKNqRawcKo9BNWX0WUb/eBXfDZoGNmTgenFKFycACvucpGOHSJpQtfpB+f/6PUXeSS9kQsJ5Vbg9YOBI/KJMCaWRFZur6OJy0Nangx9Z6tf8aPl4nHYeJaLJe8vUrX6SdATcJzPK0gmWlkBwPGdhc+UzNCmYgbIpWeD/ooCPP68hz3LLgg7RBmc5ZBMbm56QS2cHcDtjRbMrzWG8u/JpKLZDJz8FmP9+GtaLjsfBj420q53hTB6i0OMVdLYf30TsPJBGbmmzdC5VQHkTvU2OhxZDBc3eYB8vwjUzc2Hruo6umEdAw27BH1xrQD9cnmgbVIIJ87IsLa/E6ZPQiLh1Pn15PcPRyh+vJ1ettpHVmgmQ6W3IzQ85sDznwz4HBfFtI3z4Y6AC4f/f38hTx83BenwHX86wukXuqjTuZOu7WRgwykH2PduLh70LILOd6mwz94KrIOq6FFxYZyvYI6yP6XIr3U8CPh2jh5SOc8P4+bC10pl3CK3j9xf5glat7hoK1OAzwXHmOvGwPbVK8g7+6Xo0crAsKg1KJXEQY9sEZS9SYPDm6dR8/xS8vx3PBwXO0bHynaQI1YJcGDyCvqtpQjo8QIw1N5I3455g/jG9zSxwhyP9IaD1LFsSBp5wc+GKhJwxRAfDWaR2ndnyNWdF+iW+WK47OxC2DqaDLdOBMFCugBH1m8gHi/lUHW5NWzffJwdvssZ+h7yME9tDZl8uQT/6xNw8Ggz9fINwaRobJkvkguB6Al7gnXAW7IQx/YzuM2WB9+s2Sg1xQWrP1mA6xMp8PEPIrEtt2nucwuidfoamTafhx9AFnovOpMak7lQXXeZ7L2ai7f+URLI3KCjY9awWSYPK7TUYXZFAAS/0yNFmtKgE5QGI4Lue5EzH+3lU9HisSMaZ0+GPbUV7FJ5ExSd7gBpHzTwUD0bN8/bRd8WSoB1fziqG3xk/zfKgMs1aYxUf0rahSTo7a/FYP43HZe9PEpEZHPpB+EAeqkiFLcEFAMnZwE8H1KDW0l9JL1TCNd+WsduzxUD+dXzcLDiNX3aIgQuTTY456wUVXOZYE97I4arr1eTUxs9qLZuBR3XJOjiy0CsbT35cosBZ7O/5KTvC7rbPRCEf3wgH1zToHhVHtn6xx3UZQ/zlTc6Ii/WDrIkeDBfyRQPflwAOi5CcEU0k1yxeEF7RYZI3VQetj5rIM9zRumNoHyoTTpKbNWSIFlqBx2cmoa5KcWwvqmXuihw4Nd/BMMUXlKvy8LEwy0RDy65R9J7tHC3LheL9v0jvoPysPqpKxplV7NtfozQfV4M6lhnQc4XK9xXFUM5cQthzISLiyv2szpuusEemYsk4McuYmyVzn4o4OTwGVdbLogxwGr2w9LH2thfJuiQCm1oUo8D2QAWVkkBvjhvgtXutbRlpyroRR8gs6dyQEqEA24pEvClfQ5+VYtGOBaN1UOueIYbhBbbDcnIhATZvSoLfrxYCOv0Ekji3hf0xtQSDObOgzW/2FAX/I4e6JbCkvhCPNP7i+w5caJlZksctsvrYLHCOWpvUYwFZxXpu+WACefTQVNfHMvz7eg0BwMgYnr07JgzdpjJwbKURDSN3sC+dSEMXhsX4O6pl+mOWhfc4DtBzmxdjPe+LIITTd5w9r0qal7MBYV96tAkn4KOVc7gWjEL27d8oOocRaw3CwHbqwxuHt7Jf7X0/06rD+wDyfD8O+CyKc+bvbvm4cW3W2n2f7OA5eqLdo7vybIuRfiXvIBosTPhMaPKt+CU4KOYPDLTPA1NN6Ti209V/LQIJzhy2Bf/7nXCsSxr9LCUhHE4x9r1Kh9bf6tiyCMCFacLcffZWDxXmo5RJuLwp00ZO7av4isqBGJuoh3+yUlD6Vn+qFSTA4rFQnA5MxhGQvP5wyaasCeMgdbr86DQ1gc7Kv1xXZsjJJb95rvoFUBDUTSc6nSCrS1B9I14LPpkHiJ8dV+wCY6HDwu5UGSli6uu+ILTEldarnWDfeWbJb42SoMEmQ6i2SON6bNymhuf2aGPjRlohfJw5HAJHnjgDt+6v7HP3DhDJzkaws2VB8iHoy302ZwUqHaUwuEABu5vWgb/FqphbkUUlrGd4cmREnAR8G+wogLplXCCjclOOLw7EHt05qJk/3Hqvs8Top2WQNuai/QTUwDHz/Cg73Y2sJWDoHKZPVq/ekx69vHQ4MtScL3Kgs+dOeihOw++jJrDHbMAzG4TeHqGGn3XpU95HkGQejYRtv7SYsm2b2Gr1gmOz56MwlCEq6Jn4d+plnBg9mraZC4DkTetcIYIg70VoXD0Uibu+mzO7nuoDWaHpsPe3izUEPCm1mV7lN7C4JeP2Vhz8A2NeG0InZnGIKmfhqzshZD5zBZXdPfT0ko7tBkrJrzEVlJdeoB90VAJkltcYbGSC+jOU8HTNBH/64yBjhepEBddAgq9VlRZnI3Hf+rSmJp+/kbbYLIuu4+kzt/OvlPdS9/mSkGb2XTiY+GA++l+8uN+CAbZBbXMvZsAv3bcID/7Bd241w6CHnwjMVGawPPLA9Z6Vdh4Pp/eEfdD1yf5eLIvE+VLXQScmw8Nnu5ofyAS2ebRsHKyLGbUZdBR8UVYzU6Gsvc5mPfffDgZ4Ag1HHFUrGBQds1utsh+NlTXuONwzyI4EG+Deee/kCzxUDj1lkFFywRUCHcB87ACMB/JB5sDG6hdehQ8/+2IKW0xgutKBWHLUFI66gLckat8ieFX9FqXDOz284M7a0RQ1lwZ3BuWoLmPGlTyDNGLFKJISh60uafhjIVTsEfGCFP/y8dtMktx1cZIlHx4kv7r98Jn6stgX+Ir0mKnB5XPl8LrjU9ttetUiHGVEdqZM6AdzqBhoDQ/4ZAQVhebwd7oxcAfjkEH7xxaHz3O/1dfDHwPHtxQicSmYU98cEgZ/1vXRXKbasi9cS5Yf+JgzJVvZDa/GBZdO0A0CpbikeP/+HLqYrS2hgu6MwpBXlkTHD6WUzshXziymoPccX+cZlYCs1WzwUU2EBcphMCdDSKUM1aEPncXQ4+mHajwL7WUwGm69GUeOLjsow8EzsAZ9MfE/1Lwm4Ab7otrgFdZiGCutBDHGTj0IAbjYzXBs28prhnmgGvqKfbroixMl8sG/n4Orng6zF/bFASGN8aoVbU/bjJoo24GLAiWaaHrI8Nhg7YVOExcJSdsXaGtoJ5sa3Yi+1RZwDLhQXsWC19vy8ZYbijeXa9I/gu8R+xDeGg2ZSYYT+Xyz86cIKvW9BAqxEONxcvwR9RsXHWukhg/XEliLybC3VFT0i4jDCdUIkC6dCGM26qTxUtm4q0rt4nZbYZEVCu1WPVzsTN1BtQcdQWXVwwWgypYH8jH6K8MjI6nwKpfLHh7KAoVGnhAxg2xRewlMepcSl5mT4FpTZPxX0cwkTMYIltPLcODKVOw0moe3JENIeqaYTAk7APzBzwweE4sLJkcBC2qHKiuZiDhl3VzdrETmDdH4kL9MNg8wqDDjCUQd+AWqZzuBKfaiuAdiwHFln8t3qc7qSonHr+6niYT53Pw40FxkM3woZpG6ZDcy+DEZ0dcfXwy7JvtgToCNt1f5QfGvZpY6ZcOklUhqF1cBFMEOVMruYf+6JsD0yXGaHrsXbpqDQMFWwIxaOZlilNE4MwrHxhKsYL7h2Pg0B4X3C5XAk+C4/B3nLBNVX8y3nzqjcO8Xv4uQT5et5yFoQJ2vWj9gZ5qY8DyEIPuXA/85sKm7/6mwzXZFBgIzoe+B6/4O0uLcMvnVJAtZ3D+vwF68vUoFdVk47JXASBaHsJeHlFKD0UOkrOOvmB1JBlPr+WAncCbTTj50PvUGSYWdJIfL9NRZrY9Dr4PhXllU6DCMxS/XNUEaZdpsMrPC9fdmAG733hg4K5ATNcROHsag84TxbBbOQc2z/SE1a2ZUHHdiSq3MLiYJ3CMndm4NbWZXP4UBynl7nB0hIs+R3zBfZ8M3/ImA6erxUDGyAMeJTJkxW0WVpdko/3yATLC9oTQGz3ko56gY8o4bN1FAiZ/8R+p9Wwia1098dryZaBlngMhhSFw8//PNlaeix9chHGPWi44j8XDwpxmurR1ForOYJDZupzO8fSGtt8MajRrwlYxPxhjZ8FjhSToMQpB3qdQOP9P4Ewl2bD4VB4a6RTBl4duVHU2D/0m5eGRlimoPfCIWEo0ENabIpQ75YkSdSuIkEMoJGmI44pvAo+Iu0pzekfYz5Q5UBuvDVFsCwh+S+lvGokzGp1w1tQb/G/iruBRt50kJvjBAi999BD05s0YBp5tToSjr+TgwD8erskrQJ14BfbEw0JU6ZoDs5Ic0MAtHdN2zEbP34H4+rYZ2Mz0wl+6PHT9bQM5r46yyxfyoOYRA988y4l0XSEU6M7CqsmZENlbDGXxKnhnA4Oh9cl4xKaImhTm44t2P7g66wM9+0UXNNxTMOlLFhiVCcMiViScaS/C7Ql5aDKigFuXHSILnr+gsZc56GDAwy23eRDBDSbChV44qf8b+fNRFtS3t5GzblYwpWCABssIPLWzGDQ2puNGqzXsJg8TbIrLp7YNAXjomBt63eugTaaO8FnPHyoaefA5eYRMeGvBr/ECOLuVhwOxyviuoQRn1E7G0knjZEtTASZ9V+KHr9BEVb0rtOqyDe4UdPKRvcHoWKoNmSVTiMuRIjyl7wwrAjVwpkICrC7hwhXxUfaW1Gi8YrGWPi2ZS07evES7RmSJg0QkQK4s2b3OH74v3kgT33SzukLCqPp+LgiVHyTfj2eSfXN5GDLNEqYJsvjCE2t4M6+F7FXP5q/aoQ43udpk2v1lwPwMxj/Ps9Hv2DG+303B3lH2gzSzAqh4zeDLVh5U3+SCa44rbr6XCz350+DxghXkwM90WD6TixGnkvDNoD8ornHDyngHuubMCvp8kjDYD9lAyktLUmvN4BxZAbea1pBY8CLuP60gVGwqLNlaAiLm38gMl43sEa8QrGVHsLfI/0eTV7HQUyganr5aQLfZpcHtMgbeDVLSIF5AhhQdcHNkAOx97wTeexZB9G8zOOwfBO1OAbB9vqBXLDuocYshDIjKQ8HTVDz8WxoHHjA4V3oTPaZri5rDGeg7LQNmpqSDiEQ9OdqRhO5RUShUF4I/9iTCk49/6PqgM0TnXCoEXvBCZb95mC5wbA/PZThZpgSO+spTlcYSGNvKgEYlm07t7uFn1J1pdjkcAN+PG5Ci7lkYNp6Oh8sMYcsuLbZBQSCwBJm39T2Dg49SUCbcDbiNDHie4ECXgz6ue14AIyejoUY0FG8NLsCvmUYYcXGU7XphE/l8NRS+RzaQkfeCrNV0xqZzMvBufjLs7eKAZJo/pN7KxoJDoiiS64PRwoFwMroIVe1j8LWNIBe4LHjuLxhtMy7ObGbQJD6W5h6pbTF80sS+qxAHyac1QIpThC2lGrgk7QB/MDwUs7Mu07On0jHUTwYPB7cQ5zt32fvcstlr0wS9Hf+YXG4TQ2Hii3tEXCE45RxpMmLjoHIGTOtMBoXIQHhWOQ2GlnHweesw/VPmDlZOAi8RMNNHUQaPcGOgoSQBuP0xGMrXgpnn3NmJO/6jA36nWU9dFyOeD8UPvxm4JMJFsaBIyDR6T+JjstH57jv6eCUPhluKwegsF4uvCvGXpw2TPbiGHnqZRD6y8jA6U5kmPoiEs97T4QcyuPsej7WxIRjc9ufTiOvOaNPHxQ5vQwG3O4BZeCicDd9Mrv2VxH/7zCDvjjzwj2WBmWwY9t49RyMLGHQZYPD886ng22YKO647wzlBJxg1b2Av3LWN/rctFBJ3VbIvxrrgA618OFE0C75PZeEb3Sj4NacQ784tQbfy10TMapz9u7wYznW+I+89StD+gjTK9G6nb3UHWrwDROkzhwwse8UBUal8PD5vJbVNDMPnnSw0+ZSHUaIJUOKuDjqWrWRgJB9fBhZDm0wOKjc4QE1JDHaX1VLn0wyuDLOCFxpi7PVWHPheGAU3ZswgwX5L8Ur5UuSRSPKyZQZuc87m67sX4WWnLDx2i4txeb4YWS+Cv2Q5OLlrBXGUSwe4MRdfYhjqGMZDylZbPJ6iRFfPmycwCQadzLhQqF0IWz2n4jeLBPh8uYFdaWOKFmx/VHmygRq2aeJMixq693YULFAJB9MyPVy7qQSU9laT9/uFcGNUA/36lIOOl8+xzS3E8J50MnB3eqFTqga+OLgA+0wmY3tqMSjKh+NNUsXu2y+H8T4HadJGMfp05TvylxOIciX5sD1hLtztw5aLQ25wem0K3mtvogbChSi2phCkyzSwRy0a9ggtRu933+i0hzk4vvEu2XaLB/5+8+DA5DxMPVwCpo6yQLtmUD2xYvwKLnzje7n4tO4r/+XSm/RTWjjq7M3F1sbLdMoUa8icMYXfuEMNon5Lw5v4HJB7Gwk3FFKReycD5f4JOC9xLTloaYLz9ybitlkMnMqMhodQQSPmb6Y7A63gWdEfUtJbQ0/ZpeG8i1mY2nGHKH4thhXBDrhauAj/OEtBWlUAlZSygIkVchh0NRwHVDSgTW0ufb4pF2/VrWZrtRSg8QtL6j6YAcEvC+nOBhuUbPYg1zYl4eY9LlBSXE1i1T6Ttd4fqNIVge/MZ2CmHJDZ+oTEXmulfwIX4s7XDMiLpGNeuStq5eaRK5tzgXd2GX6IyoCUJXbQZ1+CXwX7zmZlPNq9+kuMCsrJF+0SOLVUl8y9ZU9MT2RSlkQqmAoyTDPrKFmdvwBe7jLCdWcZEIothowHpmSwZhIqz4gBn94+uuzITWKhqoozl2ynto1p8FU+Ab8NJGH+xBIceeaKPrlraHCGEE4az4fwRVnQxokGsSliqNUZj64JXDyV/Zvs1nbHUd/VZGeSJ+jEKcOzcwkQ/WYFlRVn4E9VAf6EA9Rhmj8YNGkDbWLh4eUCR4yJgSdlxqisMwuEpBJh2V072t7lwleV0YOO9Dh8pHiTzFaNxkOVJtia/4Evra8B4iZGpMSkm0o1P6D7FRNwpmEaCb5rCEcdXpK/tmdJzNswTGFF4oKJaNjZEQE3n/wiTfsPEOP3Ar3UbiL7OpvoPhkGKk/Nwt7GGBB2yoCb+QQ67zF4NeQS331ZDAZ6ltOYNZH4eioXLqzYwA49Vww99yvIuZv27EfbXfCFWzz1OcyF07EJuPShFxj1ecC9gNssg2kiuOIKn3/sHw/CvoWBwWwzcC1SxCFVTYiRHyNucuuo7mKuILMDMOvLaXpgciz43fJsEX0cDr9NIsDH9oPtujpX+PFqhD+0IARlNW+yV5BMcDKjdMZyPdDeJoVfIA11u5wgeJ02+p5KwGOMH3q8KYG3iwLwfPsz6l+TBW+Ls/F6fiC81pPA86cKcXgKD+9JuIPrfRf0e6OMrw/Uka1JRTDmHQu2vak40cRAnd80fLzvMfvOvTRivlgYvKv88CY3Ez+2zQOvgtnoZZ8AvcszcHbnH36jwEEiTDfTzWMJ6NjoQ/8IvCLsOg/6/OqpV4I5ut1mQe1ubzhyPQev27jSPzN7yJbrBXDiAoM9128R7cAqEj9nAQ4eiAX5QHPo9imBgKrX7Ks/56OlfjG2zPTE7zs88Gu3LDqeaqFjHAZ8+xn4+DofC/KWweX9gvF+mAdBjok4HFsCcw3y4YtSOlpmFNF3/5pp8oJO2pfrDtL7MtkV1Z5wY2gKRE4tgQUv/PB4PQ8qxryRE+hFa4+kQP2MbrJHIQ3W9DQR7VXaqF4+Cw6VFsEfY1d8lqAP+9oY7PD5TtkkDdreqOOrq4lgY7ODPjq9CC9sSkTONhFMZ0fBpi/6UNMuT+ceYkC0dvKl+tdOcNmhGF6UCbpHkD2vTvgCd/lZur1tOkwS+GDyxwx48C4F9A8zGNfsDTfmEHzDkUV9779kRnERSq1ywLkbYiGjs5+6No3Rqvx5UBeswr/QdpBdv/E8HTETpye6EzHqlxKGCDp00vQi2nZoGX+89BMdOt1J9MYIDIs/J7ev9fNvLBOHfp8p4F+2kZwN6qUx0wV5ZuaJm01S4UjdQSKWMEpmHykAJZkf1EOIQbEBD+Bm2OLm2BCaJfD8wSMMSLw4S2pXScOrr774YV0wLHKOA3O1JMzLzYezTm1kFT+eVm3xxPM3VfDSLxYe6LKG8BWLsP1DDt7Y30vCRZRJn7MrPN4pYKrNHmSTTRb/aWYqZLxwhMePt5BbqfbI7rxB9+ueICtkhNF6iiVufMSg3td8eL1tBk66E8QX886E7O4YtGd4uGlbMC6JT8Krtl38t/nmZJ6RDnwdt8NnDoXwe1s2nHI0g74THlhTbIjx8Tx8hKtpfxcDqcFTcPXZXrLeTeBajamwYHEUrlm2li2jikSyaheduG4N/x4Ns/VTjLH75za6dq8KedrXRv2aBina58PIXVn+2ksLYCAjnfWjPgkf2sagZocp/69tKi7eYIrVJ5fQjl88/C5YO5f2XiK3t3SRiEvqUOOsgRnbfeF1yTeSpx6Hu67EoO6PVNBNHOGHlyrRe5LNbE/zDLQpTccXH8uJY7I2jrwJgYN//PGifwmQcifsm6YG9yeHwPsnCjRDwxusRpNQRs8eFGtW0VRBj7QkqOCB2R5YrRIJz4I1wEmFB18nGLgafYzo7o7GHSqnCWtcsFctikFO9ixbd0EaCG3mgFfuTKQGU3DfrDj8J5rKHrPTw3saKhhzykqwh/xxg48RXojJx3tvZrMVc8OhKPQX/XkxBffZZGLxtlriamKIQyf+0l1XxDFOPALuminjwGd5SKqbTCK87pFXvoX4OJLB46xN1CpognS+00fhB85woTANhA/IY9AAG9sbQ+F0WyAa20cDyzIB9tbNwM/5k/A/N3WiHLmRVqzmoudtabyJsi1j48lgYLyIvMzPxL8CT5ge5YE9tVbsiWwefm48wbY8kUH6QQbsG9LQ9K8hjgrcpe57JvaXuKLhzsnoWbGKfTglG3supcEhVQlsSwlHozIGfwctwTsybjjez+DrF1OxWy0fr6pxQKgiD/TbglAtMxMdjkViil4B9mXbQH1DJurk3yGznyTjuEM3qckrwrT+TPTvsyJ1XmJQu0Aeb57LQXEMIRbXF6JqN4M1fQzWPpWG9nNryZbfadByvBh2Lc9Fd6VkUPuzEDf8jaUf9dKgr2sSThIaYg29mAM6AXYY8k8ObxjkY+mhQMy0UYPITg5Kfs5Ed19LMF9cjA+VXJE1TcBC4hbwy1wGTUrHyfl0TVgd0cv3fp6J437X6cFIXxzifqCNt7NRS84E7tumg0OgI9UtKMEY4Tm4+b4exjQVwSzdeaiv+JWOHi4EZ9FB+mF4DmWLLqIiN1URVYPYt1W0WwKrasmYfhYqOqQgNzYfz7OKodI1BYeunSfKewLhd/sQOSKbCAMfQ2GjxBKwv7+WZVNwiirqToPbvVngYMjFo8XpcOlaAFolp2HXojJqJXCeyGR5VLxxhjwIe0k7ZN6TT4LXhFxTIFY+ivSnyvEjnseBk6ESLm5VA05MFtb0fmqJOByFCWN5GGsXiZdXMPhzIReZAw8JOceiNbaeILmwgx3e1k3e928iGi4MROdJ4dzKO0SqOR18kjwgb74PbrljDjeOMJgUMUaFovPZ78Q76KU4wN6UZnrvkjeaSCjAwEwZqIn3Bs0n+ZDsxODqqiw8fCETvrs54GihDQxJ+WDghq9k1at8yAm8TOozGUztd6NeSnGo2kmgq8EVnhk/JhuX7+bPLpMBT5ILo2Fc0iNviqWLBF6owMURMwbuOb6n8YHnaYWuL1rYWYDdGS3+8PRwUNqSCczd6/y+n6k4mv+FqD/OQq0iAxR7qIilF0/QdVtzseKvEshWOeMLpp5UvNtPn3cXYvTfYLgRmgMLnjDAyZ6N6TJG0Ctgw5XZuRD/g4vbWoWx0pMLOxUL0Cf5FB0cc4aDZYfo6MgJwhbOBLf3YSC1mQdNcvmwrSoZBw4JgbTTV7JENhtP+r+lmzst6ctmMVrQWYQ7rxaiSvA7cj54FznTHgCTvijgaZetLR9bedBi6gtjJjz801KMmaISYF+TCyd2/SCnZm7mm670xOymTPQ2n4+aIhz85L6VPhnmgdDsNWw3Eo0av3Jg1mpV2N7qhltKRUh1IIMirtmo08ODoYWa7A/dsVBygIeTtvoQNbN0/hnTNURT1RbSJMbJ5a9+6DAtAu48OElsMq/QwQcxMJ1zl2x9mgt3ArVAxkSePMj9QEL+xWNmYAZs0eXCj415VPiIFolPj0YIqWLHq9nB4KgkO9RflmhpWyFnpzy4r8jG83ZbqJDwSuL4l0ENwZiuw6UQ+LyBnlSMIr+XjtKGnzwwHU2EXfsUYWbsdro3Rx3Wr+2hZZ5cTJbKg9uBOXhw4By5lsxtHnsUh3ylUn66eyBqb3Wnj+uT8UmmgL9XL4XY4KekQEQOc96lg2Y/FzI31BHti9lYWLYERQ++I8aN0TC5X5B/cyLRa/U8OJRpTfbqF2Cifg3pM51CSzoTycM5bthSXgLtN4PpTw9juE/y4EM/h61xVwrefh6iER0ZuMo0CkWuSUNHsyadx1FFCfkCUvY0HWU0r7N2B2bhvrOLYKV+GG54X4jrh7Nx76RQiHu1DIZSZ+Ab72J8d2AX3zEoDZNGpxHRPSHNwQa2OEl+EXYTFg17z2CUdz05+NkRUk6VkhuZefB6NAFeT4TB+mAPkMvg4KQMd7B6ZkEd7L/yG5epwFmZEmiwdCJRJ6/QC0l52PmHReNnXSamuBC8T3KwYocWcV6RDKVl0iRBwQFPqWZCWR8DrOJ2dnIHB8K6U1AHc0ApTBanv/RGCR19PC5gOkujPHQ/mAdmPxdiH+SC9vlMNDlVgqd1Qonc2yx6fn8V/6pDCmQ2ykC0pw0sTFyAD44L2JmfjYnhpWS26WIS0MPBBTJF6K1WDM5/jxL/DV7432FLvPg7Hm0T4tiHen8SKb3allrH5VTnfCCKp9RSK6YIH/l5wVUTZwh9kAdDZZQOCRxpXuJcuMhjcAO7CM13KcOdXDn+tR5Vqtouhkd/cPCiSyZWiNxp8XJm48xyd9yonopvRQsw9JU0PBxmwFV1IQb+CEer9Y0054UudN7KY7mo2GA0LwEWbDVFZxYXH0juIK9e/Ef3v/yPruz+zR6fnA9aAj4Ie8SFta4bqUKvHFayFGEiJBKVTl+1rb/Dw8ONgnW6Zi8xXslgqbsctC1KItcE7lm6wQtEzQuwa5ocPKYFIJeSB/a2yWgZwsDf+2V00YKnLWvTA3GavQMoLysG/w/5KB7gBtErzUBbvIHuXRiLOpb5oH3mIv102ge+DrrSwHs6eFzAovdFGRgvd2BPbzpJD2/fRtNMtUF9dQxeyMuC7wr54C7hirdfubFPhhTgueJC1HnqAEp7i9GKnYMv147SkPff+WvYG/lTNP7H0XnHU/2+f1wJDdGULaUQEVLCuV/2SCHZJNl7b855K+0hlTQ+7b1VKhXnvrQV7aWpqaE9VdTvfH9/nP+c97jv63pdz+fj8Xbe69mWtmV8m8UJfv+2G5W/E5DRng/5P8G4eC4MD6PzYHzGjjSPmUM3Xw9kaEN54QI9jssiZf2jopqpU7HviBMKDsmc7Zo+nfHNRXShNe7aF7Fg9TB6PqeAhplrsIKZmvhZ28y+vCykEDMNWl7aHfM+tkoT5gswbRKTzhZ16m//lift80OWziy2rllChW2RUv0VeehzJw6jiuWR4JVIHbUeNGtPDVPYZI/Mmz2YRoQDHC3jMW/cbRa+bD4PiNFCn8ULufvVJhZzPQbbqgvoZKDMM6/fZjPHlYrs7z/mEx2L0GrrS1c3afCVBU7sprkiDfUQEJ1YKO08t5uVzLMhudO2iGj0pQqZ5yz8UCFaariFuVULGBAcRF2rxsJw62VOD8xYXPYrPm56Q/0ZI2UYjRnI6geftv/TUUCdSpGkO1yef24poMk/PCh7VAj7/aQ7KcxSIIPkcD6gqgiT5jjzpwHxlOXWD9tWf+SNhodYac1yXvsghxRKDfi9TSnw3leE/JE55Hd9IH91eBq9fh5N94Zn0L+HZ09e0Ahm/34Uko9mCsn/EMh7p5Ksx/vT8SmjqWqQGO+DXOlSt0I65rWCL5hWyje1BuPDJE2aHDOJylKLqHRgKu9dMZxiPxVj/oy/XLm3BYYv9qL28cG8x5ivPC1oMq+6WoQMX33M9TnKt97qjXTBnTeYrxWdMrrKPOtKmfhTb+njaltaqqAAx4qpqFueSRljNrGmP2MowjgbZqMKKVxNgLLea07KAuWsHU0d42fgLPfBl8eyfeqlhqaYXOZ3XoCaJAWbPgjkP8YBKU2JuHHHhhK+FOGexR9WrKFMgWcusPVLKnmgRiQOmIuodcxKtvxYLtYW7GSvM16ww9XFODw+l+aGF2PRvfuiHjaLuK5fIVlse87HiIdRXlcKntiCri2eJ1K92430rDfylfZpiHzsST7T+tOQzwJ2F/nRsimZ2P2wgPZZFuH8oGLYbxCTkcy7R+W6kP+7M1zr0mR6uT0LNka72aCT3ZEoy6a51RIytC/EOV0tirtpgK4tCggMNKE1hYU0PsOE9nt6kvH+kaQ8Ik001C+dCq98EPVxreF9arqk9lpRdOtmI7+rnoDHQ3fzu40mbPSCofgpX80PjCtG6qhsNkcUQmW1hTgf95kne07lL79IaOQAKZu4Yyw9vd7BDDbqwGbCZDSNrK2Xzj7Ka7TSKaJUjBJZXkdfS4f0axG9z0pjGb4G0quicRiWVMWeXjrPa96LyXHzQlHH+RK0dRlBO2wFG1ToTvIJB+t132iTfnAx/nzyxa5vEip6m09ubiq0v6FQNq+nY9uua8xwazhliTXIc+Y+Vr2wk+U/7I9YJzGpiXKh9fg692sqoV4X1tSP+C7z1Kgs9sWrL3/rUc3Kswzp89j7nAdK+WNkYImfMX1Ys5aNPOWFs96pJHd5CF1LF6BXWIwDwYE05ouYjmUDQuEwqu+jRdWdmlS/+zkrMyxndw8Ukvn7Qtq896pocWAW9nbXpqHFg+nwK0U63FZCbh4pdO9zAdU2ptPCkRl8vrlAgb7xqGXv+Zqr3lA/ZYkF8yt4/7oH0oO/BHClv9JHffLgXvWV71xTjH87J3NL+Uw6vPgEN1gVwrqSPvPofWnsWukPdu2ZAHcTTz5gwFhoitfz9vk58JRcYleGKEoD/gr4r5uYrHMsMYLFUYqbBOeDdvF8yS92s02gkF3/uHSNK9Ydq+crLPIwMaOLmwyfAo0IbxT0T6OWYe/5iRPFqNYvwUiZ9z18dEPatu+d6MEsAWb641l5bg9s0JnBnmzxRaFNHNs4dBSNc9DAtt0SnDw9DF//9iJVbTEZutxgJWZalFPjT2add3jv8CGYoJqKz+96UsqMe/yibNb/KmekPH8yxdZsY763ExDe2QtGJgIC7xXjwVMFirPuQd0O9MGAhUvth7yq5U4JqrQioC+MnkTTw+XhOHkqk+bVJVBqTrD9x+cD8HJ2Ie0xa+LK48T0pFagA2F3+becDlYw7BX7c7yIuv+0IM+L/aliQTFNu/CCb5pvyneuKKSzt1fx5r9bRY7/Dok+XNAjcUcxTKUaVHJmK++yiqClDtulvf9YkhwPxroBauAzjcn65QvpoekesOkQaOmFgVQ1oJVnrUtA6s8SrPCR3f+a7SylOJw/v60gnWd2gj1wimVWkw8xr1hNUpCfLRXih6HAPwWb44upzXgyfXrgTydEo+mU/i5+dYYElo/zcOTAI+mYnnFYFTuOZ77NwDNXMSUv0cXXjar4/S6bxlY+EcV7pdChrCxaukygWL0P/HBPM9qaICbtx7LZ63vHprH6Pn+csYD96uEFv4IiUd62YfQuLhvan/JQbTGFrh/xo5bfAmW/iqZGxxKYxxXCJlcCjbJHUjuNx6LmGA3qNXYx3+u4T2p4LQ/Op0ciuisDNQWj6LBnHK3YM4Z+peXQ6eJeNPa6OV9sOpuzmWK61T2KeHU4G/95Iv7wMUzsGUVex9/y6KOjSCW2F14v1mZ5q72l66Yp0p5PvnTcfaBUsSCKuoXY0aj5ElyeHUVZ0X2gXm/CXm0bjD1/I6gm2410qszQN08q7V51nSVVheCW9wGu8Vudwsd/ZeFH7bH2byZmrBFTrEGbdMBtfUQ0FVP9lTWizjdJUo3h7vhdZEkDFMzpwt9AahBNR6p5X36g7zYe+jEUZWa3WUY38O1rJcg3dsLniHHw61zDMhbIYZC+D742vmbW1uPw9VrJ/z8b3LDsZR1dmS8qXelUXyPr4eWqAuXvcCdVGZeMNsyH4RkPWbbmYm+WtP7czDR4D86nKw8m0M3SSHr1XyCt+BVJJ0oWsbslq9m2eRHY5RnHblknY/c+Cdbv2MncLgUj+7MVLYi8xd4kTGTrNJKkL8tE0Gpq4oYTrzH1hCDSf/+O/VQZhIPr/7GGzCCccn4jdf3fuxkeq5OcSxyyUtbw3IGumH5zPJ3XFqOimyO90RezsWF6TP/TPFH1uGEwi5gM81bF+kvXXGnQEz8axyXwOyvQMaVMnBmfDafN29llVkgaQTU8PckF3TQnQk/OhP7diseMi+44WZ4Fl0x5+rplHx9kFA2FVjHNuKJIHd8KkPfSlF9QGwsthb083DiYRtxwp07Z2jXr5WO4lg+6xsbgmYxbvVdGUnWrgDGOmvWTrkrQ6XGNTeorwfxHJWS2LImdu6sNhxpLXFP1wvPiiag8mk3Gc+X4Ht98vPXPwcOnAvULWs8Df4STX5FAnTK+MDQoxMjNGvgZrg/pXwk2lKtg9XhvurBYYHfv6+Pz41RSXSOgH/fF8uPBpLbJh4qehKC9VYIedlbQmpQJq9PeGH8nHkWJ71ljZipVhk1BqlYT13BolXZddiXn7mK8qVGH6tl27vzLD+N1y0ROcwSiOE/evC4OMI4j30F9oZrZxS3vWfK+gf7UZhVJAcptvHmXLTvdkI7n8VtZzo5AkY/dcjbpQSwNlPl188NkSh8XgvKhbbxdNRx62wSa81jAtglB+NBaQi8+hGPfmyxIM4IhPmZqr6/QF702CdSS7IO6pQJpfQqkOEdPXJLxxatHH9ihuYPpQFsWv6IWSk8nmhPVCaR9xQ6i0PFoGb6Gd1yKxYaFtexovDI7+08CSWl9/cX349DU0cEPGnujU6+eZbXG0YimcFilnZaGTvDFRd08/ldLFd6Bg2jwRAGnBzjhwKxUql8j0DcfX9zJGoZUrxl0zuUU22ZSDMvLWjgcP50cBvXC7Jg6HuyRT0dbiUUsn0wjxH7YNC+UNrc446+KgP4WGXBVUUb0I388d2UQQi3A947E+D8zMOVCBt589eIPSy3I0WMc27Eni5bk3GW6FT58iiSQuW0VSJiRz843TWdjO/qh/Vkufg1O43OKI+lsXAzNnp0NBWcrHI9bxZQ2/qh/+E+gl5XnpFMECb5a5FGDjoDCzaPILe2FSM9HQo2LTDF2XyatCajkR07EYeRHVZy4U0grLhThz5E0Hrkxm8R+aiTUtPIb48YjeJ9ABWfElFPhZ5dyvDtXPpqKFZs9aVYfAX26JdJaVYn0kVwGptqPRJznfX4p0of4kiR6e9KGPI8tZcqmAtJXvuSDQu5KnQ+kw8JQA8N8uejglx/SZP1k/JxgTHfa1al59BHud0uT9K6r0R5LCY3Vi8XBA0X09YhAvSb/YlcjQrjb5v18q100CgOjcW5KFuUYpdKB5GLK/TFb9GiKmOR0HenJ9w28uMWLjB6osyerjBHaISBA1i8/HbbwsrGmdKJNwJzWaWg/3h0R+uk4axRDU1V/sw0XNOnwpem0gI6dnHLiNC+0Kqb7cTVMTSrQdcNhmPnmNhd15WNZRSC2HnUTHcm2R+G3DDqscp+zjCga4HeNrUrJIY+169nzE6pMHNbB+l8XU9NvAYbjl/Fdc+VJf84cduenGG3ZvvQ90peaarMw5kAEC/opm1H9BSi1SchbpIZD/nqwe2uEtvWDcFqWXSE3smVu50GPdhSS0U13bN6XSvoml7lxYA71GeuKhDSZq8aqYH+UhNK+e4M9SAP9TqLFenfY2U4BHbL9Sg4MpvoNAcxmZSGCZzvyzK1FsLyfi+eynPo4dAejO2/5Aats+iebp4brSnBnbZn0TKVAP5sE+ugyXyR9EUvnXVwgzisEYicg/8vwOoVJ+VQ52gfTRhawRBmPvfSd9v95+PKyC8oNC6BV3Qeu/2O4S+Ew4RmU16RPNRez8L62AM+mF5Hn2gLEtJbLasSOfMq60Z9cRlmyej65bj+zuKdCtxu748gNgbZPjkf30wWUYnOBJ7SJ8Ui6g5VvusrOXC6kG9vqeUy/EqyS7afnVTXKlF1PwQx7/sxWl/x/CuQju57DhTpMo0VMu2YlYdCOROzapEu5q32w910ROqeE082bybRItg4W3W7ymIVqsFMyojnNWZig85P9ahHogYId6X4WyGTIaulpGa/ebpRQU5iIFr8ppsZfYjI5qIr/7N4wI9MwPtNnA/M7lEW9HQ3sl1cNwFPZ+dNPFFK9oizzaoPps0MErKLf8MoN3ci9qQRTh1jTtl276w8tVaUDW03rrSoLbILnFdGf1kxiHRK68FigVcfq+OXfcfQqQwFb4gR4VsuBTmbBuEpMqQFyvPCSIhV/llBFHzdW57CBv40U0Q1jTUztc5z9kZ1/gVE/aO0OIYNN3fF0ZipcGgS8OPdOdMBBAh7fHQNK42F06jKvGnqRac1Wo0Dj1fxBrjIurNHHugcytpUdI+aCJ53NCiW77BKEHBiIj+8kdPmqhD53S4NOjZgKBS00JMnu18iHHu1JYJ138+jJ6hLaIPvuyguqmDNnPO06XiydU58LzYAGPu1VFXvmKiDiei6KmIzvxogRes0WyrOy6XTsGCiVrBXtL6hki24LdLCins/VHE7jZbl8VzUfLmuKKfn2dEStF6hStl+qCyt40+JDzLI8FF2y/FNVycOQjhQclZrjkqz+ht0spE9nlMnM6TI/u6oYccfMacg2Qwq6J5vVP+Sk1h+W8eZJhrQrLxPxidnk9j0f3VVNaJbMYeWvClR71JbMF77hR+4m04CWyZjyIo/NadSkUTLWDHcbTqJaJ+RKiij0hBjdfnsTVlojapscNVpEMnGXAFVPPbau+Rg/G+6GkQoCdBOzEJkSD1/ZtQXcKqRDTWJ6uW8lnxUkYLWzGj1OnMWe1hfj5gd79GgUqFmIx5DXM7kxebDFF8Ogb+uBsv2p7Kms57TWxlP5mHjoGVrC81cu+u1ez39iIfeZ24NPasymkzIeMV0ajimrr4kuhhWRpSwPLoRek/73WiC/wwI0dF/zPFNFKnybh1XhUXAbt9N2SIoNGd2Kphx7NbakQaAv62aLnPsGsh33F7EL/8XjZUQM/RtmwzSjBRqt4MDWPWqoz9qbwdavd4HWqw3s/McCkg63xitNT0hkLKn/1pPWb9vErGQcObtIgvd14ZT3IwXhHRI813XAMc9+yDhlTyFD9NBySdZ/r3KoYt595hKdShuPC1R2IJuSI26x6RvNaIFFJiaLh0lnbVCljWkSKpe53B2tFArYlEW5/54z+3AfXEobTf1eiEnzY5Wo6YcAm8AKPqi5Oz5li+jnESssPD0FLy5YUGCzGAf3T8Xhj8Uo0hrPD16fybrWZSP6RjBfdiGU6Vn4Y8JBgYpnDUPdYw2cu5YB/ewCHL+RgjtPgnFURUxWd9azUfttaewhN2gunSJ6Iqv3yJcbWZPRNOBYLBkZyGr2qwQb9c3R8r2GralzQdt/NszhxkHmO6YEe+fv59kBucwzJBUd3fS5g3c2bNQc4PLBmTaeqpwQtlPGNqUdbL+/Eya8EdNmbzHiv9pStMzLVtkVU7BdMPUv3cl6v/ku6n7rKC847w3FgGJkqZ3io4ZFYO1gAV/i2viX7AzM3zMQPx9MRX/TAqosl4iaTYpw/46ErjzxpD1bfOjs2t60NXwQeEQmtpwR0O7qT0obBLp1Q0zB5QVYrDAVBQPk4JG9hZ/keVj3q4CuLrClBiUBjXbxpLjYivZarmEbPxtgyrT5POb8SulKRQH/PEfRf3VimNyJov2fIjEtbihqTdt5dE0+egXo8SiNKHpYWCj9MCuAws+Nqt8SY8pGBU6hSU/iYCubQWVGhWzoOwFTrL9yV+vBuL5EgokT3eiUZx59vPCD/brUwaYaB+PXlVQyDtnK79oe5dLKSXht4Ukastk6PyQDPhoC3sh6Me5eMljwS8YmuWPf5Rysk09GU8NC0fuFkZihX0L796uTd6Q7HWldxSbvHsx/LpD1fx9fbHYthrhjr73bKyPkbu1O4/1l/jQ+m6Kf52NdqQZlyMfYvbor0LITvrTBXkzn/wiUOT0NI19JUGjYVL+5JAcDjsRQe/x85i77u6SR+fSuOBMDL2vDzjEDe95NJ/edBbTGL4dWTEulQYqH2RslEdkLAh01D6JTN5LwedAT1vTTFr1v9qc+8nHSdQfTaeqC9JMd+wspoiWY7qyJow7ZeTv1tWlHT09s2jSd19+RQLc6mfRjI/CjUw0DzY6ziLln2YcBYhQ0peKAiQ65cEtEdY8VzdPxIutlWRTp+E967Hcw2a3MQa8cgW6sdecNp4YjK0xMZrZeuNetBFeu13BRQzJZdCugC1qe3PNAHB6HjKTrUZPpsPUl1gtFnLeY04NzEuwpt4J1YV8ovNfjXaZt7IKKGewUgylvZyIUbiXRd2k9H7rPF+MzfnDvwYPoqJs39jRLILao5Kmy2hv8sBgLlduZnmkWMDcER/YLvOq9JWblxtO379/5qO2+UKprl85+t4LvW7Sar94ugeiKzGsKC1F2RIL9WVeZKOo4yzqwkYt/9YSpfjaZBasgWDsW1mb67NsZgdY/SqE1d2qkM7Y52J98mYDo07a0/pg+JdfJuNJSicYtLmMuqyzII+G1aN8VNxItTCXT41F06oEtHvzv2U/BneQrPop+/N3KPHsMxDGlSLw5V8trtkhgc1Gg+KmaTEV1HNoX6AAznomG37Xgy+wiMeyZBJMaXvDtQ8awu3rb2OOv6ng0YbvIcZYBa/smY/itYlpQqk49vpuJko+sYDvFQUwaFYuYhwIuVk2j4X+CsKHHS+m0vlnI4d0w+p0vspse832DxKRydAClVkigN3sXyzoxVyoZlkPfK8fg9Yow2L3M5svfhMJ+5z62JEnmNhtEUDlhij0pOfh9zp8UzL+ylPD7vMdiMW1xzYf0xg3evt+GuYSsY0llnfZm9XF01iCY5E5u4KM/ZJNfdTKXyJjqs4x3HWXOZtU5BrWy7D0olqDjoSP0+3cn/XcemJ+iyjuUp5JxN238HBeAb5+mUv+PeRiwIo+2jEqCeW4WMn5Fk0VTIjW/7k4bssU0vfGN9MkYDRoir02OmU1M9Vsy/S0ciWBnP+pdLFBD7igqTbWiFv8iLDKbjMHre5Hy4/tS+3N/eEmSBIF1UfRmlhldO3GLu7eZSqnanHeujaT+E3yw4yrIU6zNTjsvZ19l8/jV5YEoosHSpocFeFol88BQMe1rmkG/zjvSvJx4xC7vYFfF9hjVsoIPV5hEc8wYt1l/nXVOKMKL40VY+2IVu/b5C5u5RYECrr7g1wMlEB7fk/477Uc+kbbSjxWxItv/8mXrIUDhigdmfX3N3skF0WIXIyx/nwTfssD6R/dDsWHBUFJ/05cyHMXUf5Qzmp1nUMQjgRzenuUZIzaw7kW/pM4zXanX5Xh8kM2cZwbVvKbGF9XbtnED2cwVtNLx6rYYE14VQ4XysFMnlZ6a/WVmOTrsvtEw+s8+i8J6O0HtZCFoTSANbS+hN7pm6JqgQaP+i4B+0UH7jxWn+U5DMZnXbWMPTPPoyoPp2FO5ktdMlpD95+54vZxRn7fD8NHpGltSpk3tkfI4qDuFRjz7xxxEuVQqy5ZrF4rxSFOTjE0E0i0LpC9/BXbQ4ykPdLfHO0U73Pj+xf5akh5riG9ivsd3sg1zc2hE16+6Oo/+sDlWhIS+YnI7LNDdtGL6efi49OhaTy6/sge65vpStu94im5Xo37yJvTihgTlXvKs+ZdAoT7yVFU8GLWuG/mXnbLzixXoaH0Olct6sbjnQWb0RkDfmmY+eJ0VjnzJpVtD83FzQz/kHdwm7RuSRf6zIqm0NpsaKk2o5ksJPdO4yb4YJdP3XcQrrJqlV7+poqumCNNumVP97VSKOJJNIpU2/te9go/+PYn6PJfVpIqiaJh9HjbMK8Zz6xKkXIsgufVp5PBTgQ7WNPKYvNi6rqcm2CjXnS7c1Uf8LYE2aXzl6XMH0urLwdT4IgNP7PP4vLsl1H+oJUJafMh4gyNOsisnp8eWkIWfjDe/GdKeMd3Z/qGtUusb8vS2zYC6Eldwi97v+MTTw2hfgzPdjivGtGwffieV0dXfxth9vZXviTalyerXWO+Hw///f7NiBrbxz/Fy6PzqSe5mx+yHm/uL7nT2orupGWirK6HJuSkUeW4U/T1WQMsunhN12sxm6lk+mHBJj472lDHcp3As+yzjjLOxNHbAbjbyhTLOzS6E5lpTXiWbmyXqgRTy8wTXLxD42kWHeMitbDycJyKLnVpoGxSOijqB+L/3bM2XfFqlroVFJ2Ih/X1I2jvQh65sLKGVeovqf5jP4XK/CmAwz4YmTHeFU3A3qgwZQirn61jf8b7g/n1J7ZxAojw/yE8WUL0ugcKmJuLZ8GJsGzsF3gd0uLmRMR0pn8lNo79JF1zNwoDBAxF6bikfNa2OO5sK1LK+hPKtc7FphR98RKNxtP9Z6ZHoQnLamY7GOfbSLQFO9NdmJP2ZEUcO0rei2gYVasn2oVj5Sfzwrhs8cWYaXfkgYEL9XRY39AP73BxCx7sVoz4qkz18KYhWj5PxwNahvDg+lF7JXF/ZqTeUHspq29UKqWEauLUkkCavzIRb6FlptWMJCT8i8dOukCZcMKGlU81Y+6FT7K/PADJWOlJfHrOKKV9VoFWvXGjl3ULSWi7G/Jdj6V6FF0ot1ejXFXNmP7O2XrO1CKPLxaTru4R95RKanp4lNZnqj9VHp2PL5G4ouR/Oxf/S4b3iGLtsqUzv1vbHK3ESTSyIwju1aLowsJD63njGFzefYrO/VIsO//7F9/ZLxu6nhrTX5Rf3f3Wf7X2tToFOI8hMVMejTi9nAeMl9PuIvYwFgqBYXwS1v39Yq9w31lc7CJb3ptDAhpFsSqUm5fb+xDojYtB+Z7Zof89iqpxlxG11TtVP/+RFi8xv8y9Dp4iGtRfQyKECTtg4wrhTiea1N/OMHozC5sdjxrE2riBjXRu136z35bV8jdVwkX/tbh7T/QyfoyrAwDCZxKuL4P99KGa4+8Gyjxx9zHSiFIV8qjyQwk6oPmKKT4yglhVFRuPHwMy9hBlHaFG8jirLdo/ji7dPQs/TBSQyreTODyOx6aaAB0/X85Rba/jUtwJFLbzJX1blibp+SJCyLh/7qp4wuV1+dNpaDypZQeSZzpgkpwibPHMopSlFlu9eItVRFqT2ejjOGExgDh8uMn2/ftSR7EnbhhSzwWt0sMg7l3pcTkPaNVfkfsmk4tUFVLXfg9o+5UL7XTxF/kmnIlnPLbubgl7dGnjo7mzK1djBVn+wpf0xzuDvBFqypQQ9tu1j8y474FVBHgyqKkU6FiW4d+IG8zTVpfVjH7GTt1Tp64RjbN6GDBmvFTCtgmwa8jcSkwsyyfCVC/a0ClT9oowVphXQ4mXTYF8oYPNTbUQPjKPXZYp0crUF2U7fx3tX9yRuZidzSfD6jFw0hHmR6qv9rDNgPROn51OUZx4ct8YhggSkizxoyc5e9Pq7QBvkQ2BnvYV9NDwvGty/BEO7ZO6oK+CDX3/SOW/Hkzf5QK/dCj1cCiho+hBWJEnGln/2vPZLCB0cqELOQQeZ0rjVfMgWc0y8kUypQWmY2CFGreZynvTGkD53+JHwYCdL0sjDxQH2NMRuOrx/FmFhtYT+PGxmt4LFlDKynBV/KUHbPlUU/6fAlqhlQU252V66LYti7ZfznVXFpPm/59VmFNLLQNlsXbOMB9V60e0zJdC/MVO062IZ//trOuxH7a6/0tebfi8Iw4zwPSx3aRH13yTz3n4jIfcmF/PvHOfn/jK43xpHT98UkfrRURDtz8M0m2yWi0vs9mVFipqeQLu07LF7qxbExSJauqCWR7+9xOrf59P0tP74G2eC0OYxyJhtR3mr5/OBm9u59ZkI7nWglJ+eGQ/dUculsRcHYc89PzTmrOZDBzrR2CgJDTueizQZ12y4NBhTzm5mSsdzaMusHLo51I3Kj3sytXMTqeqwmEYeC8T79c1sVnwuPvdLYP7zQnHYZwjaT+WTfKkKltzuZPHXMqnPLGO63u5Mgbnu0B5szS/+E+iVnRMar8+AQdIUVvFWgK7nGAQu92HtT96xG3NzUZRG/I5WE7u/Uot5G6eQxhh5ltz6ihlGDJUd01Pm1xW8qzHG/vvASXUWYZkwkM/DDEFCE68ItGNlPtkZFdH3rio2okws41SB1obHY71dPmkN6cPXTCnmjd+X8ZJ1EXRaOpUvHemFIf/yUXS4lR28+5WpBsjWf8cE0grOklZoqNEDvxm0f1wIkt/NkP6VvBOd2dHMLp9wwkKnULJccJU/7t0faoes4V1ng8GNirh0PYtmzhjBvv0exV18p9C0UwKtUQmyV+glRlbtKanHyUhYN4tJVddWpOM6mH5nLWPux2axr7qx9KlFoHvDchFrmAC3A4u4tXUMCn38SXtlAOVPOsKn3FslkucleGgWj0EzY1E/y57u/75T3/p1AY99pM4/6d9gI9Zm0fJzYRhs28CDbs+vN0gRU4VREvV9HkTbb4dJV+03g7qxIonG++FmaDiFdNvLD/ja0Eor2dyNVYPFpxzYVWaI5s76wv/3e45Vz7NJ5bIOT9a2guNTBdGHuhOi7uGn6jWi70gfuWfRd7EOLWorICG0g6vPkpBKVBYNejiRlDr60SfXz2zBtgJynAfED1LB5Tz3+r6XetGvSf2w4XYy+Y+SULSJP538kkO73hrRA89qpibNgdfYbHr4ayR9D4qGakgR7U7Kw9YtqdS2r2f9oRGZNE0nhaxNPMmnv0AVqnnUsmg0zf2dhCFWIzDPOIuipy/h6T3s6GmsH4XmZdLU0pWs1+0xvDTyk/0xx2z6+0HmhzNHsNI/OSSWrfmdlBAetrFM9F9TMS1YkAP5muGYV5BCLUsHEY+WUK9tI8jqvoBgo0lSbq9KeXtsqV9AJkKORtLpigyatHYjO2woy11hESv/lElfr0iAI7nUeFZWf0XBZMzFdOJ5DrdRqBDp0DyRplsuS0gE/oU50ZpaTxrSCZxoUUG2bI0XBuRApfk7S+lTQmHqg9gOGZ9qat/j5zfGUfXLk9KUB8ZYMG8nmzBhBt8VqE2BWqo0c3Ix/O10YXsin0bUZtISwzf8a5JAq8wzaWhLDGlm9MQZR23sOPxCdGJjMeyfe9KjO7nUZ04rv/91MW9LzkRQcxYZvZnM1PVzoKogoZ2NAiUcs8cDNYHmdAmw8hiJEislXrw7Guo9BagHWNOWIXn8wKd+otuqI+ARa1AfqOKAieJMGjPMmNKHi2lxijnM5dLpuLkyYk5GUqXMm3Lmt7B5Q6KoZkA5e3tYQ7o5sZolyY1mJk8yUembRfdWpcHdQGCuZfkU/CiPlkiLyaT7dV4TuoJ/HG/Bn3yK4v3PdrDQ7YXQaDZjL0/r4P2nIzzCWJO2H5Nn/5wyKfNJBrR/pZHdzyV85r08stUR00E1ARatruzyS1fyf7JQ1MtjFDyC65lXuD/K3N3JP0+ZWn94YVxwGsoMfHBoQzS9WFmE0k0OpCFvTJaz/khHfwiV+aBA58o0qPeQPXzdvL7YYZiH6+PEdOduHlSfxZBzz0xSnNtZr9/TlNrTXXi1bwmN/b6YWQzLoq74dPZuvjetdsqAq7iEMhTGIfa2gPCUVCrLXytdbRMtGq6eBmmPYFpdpEaKKu50lvlSD9NclBnLQ+7FTB7guIRr7C8k7c26VP0kEBezMjDRUZsmd6TQ6nOBULucQWcHJlNvn7+8crourIMFKslxwjtNeyxzXsxsrDlLr7PE0fzzbM6gdFTP9IREJMerVExwryWMNt9NxcVFjjx5TzO/5XKV+d6fRn9MlPnPfb6kG1jFm1vD4aHsyWctkcD/WAlO31DBkPcXWes2OexxW1FfuCyDsneOxed1q/nCCIF8FPvRSr8EemfbzAuCN7Hox2JMPlfPG28OQNlhAWu9FrPJ5t64UC/jueRIHKxKhUFlMf7OdeVjzZJEpVMSqaxbMI0/JfPlOXt46JSxyNnpCyc85RXN+VgXryadt1mCfrN7sOcbp1ByQAa5bGxmc7pVS+/2CMMxDQPSGby4Tkv8X92omCPMpPOXaOhZmT/UCnRbNRZ/+h7kuj8SkHQrGqeV66S+Aal4XtqTNibW2/c+0I265mXR2RuByHApIjlzzv8Nlad75eGI1VGkkR0eOKOkS5sMBiJouDVJHiTgvXoGmckyZeBOzuZEKtLMVePojF0i/i29xhyHNrIpCWlke+IFjzDsZMlZBtBa4k9vcnaIqubf55Nc3WjmwFQUOQ0hifxnFqofDPn4Qmq9M4jt9/LB/Ibr/EFlLikZvJRGdwr46/mJbUIeec1+zLuei3F0mBHfdjmR5qTuZadc+qHx3iZR5xR3slV0pCc1Iho78QEz3TWY1qYOg5n0C3szfCkzuyHQv2p/Cj0Xyg/0yoDz9RSc35OPT/4jofZeQg5X7/Mfl3LpiF43Kv3+gm9IHiCK/JhN1wf3oa6OVaIa/wx6v1igQcbT6GzLBDKem0hK9x1wNd0WvyzOsWa9QLI1XsanvMnCJO8PTOvdcCh+yMOj6QL136nA93t5UlyfGHr13hkn72TyqvIUFPT5IT0TPAnftw2D9yp52hNuiCVHXovSz/3kvF8RrmSNIu+Fdzmd3MN1F6pQDO9FwpBTrOBZIRxYItVnRtPNe2G4stSZwh9G0ak0Ma4+kdD6HDmIh8Vjb6o262c1BMt5D/LcuojbjFeXrjsjUF6yKRV7pSCj3wxyOZuAQW9DeN1MdbL/d4sdfbqK7evoQVvWt7HUbdfZqodiBJSbUeFjLRHb94VNCJ/PvHXX87ffE2hF1WXplzABfRMKoV3DeEVIb6pdOR2xC3rhvrcL7y8fRAcq61gvWb06HD7Iws6HQhw4BZZeYnxRSsF9IxfYXBfwO7aMj/1q8r/fLybNgUb07WNf6r3fgd7/E2BjNZweGifT0km6KO8QEFc5iT6o6pLR/gTa6p+HQmcx0p0KZJ5VwT979CafqZpk4z4A85YKtHpVIN1ZVEzqvYvpa0sA9hyIwMUnq/ioggb++0wybK9sZnEn8zEwxgR76kRkPPw2c98Wj7W71Ei+31SqtbvGbr/eyN90CfT4SR4mDpRD4x8B3wME/IpOoO0BWlTtlo5D0x1Qfp74okwxKquTmbhVE3/t1vHUIj32dqMZxvwtJKUTAtrs2up6KBzm8vdVqFJIRst6gZq7xeDdxHRy/DFfdGyyIX63C3RIQY+MqqpZN0UTincvwPzZibS+LB1po+VJrpcnaTQ+49M0E0jLtAcd7RuIrBOdolMZ77mOzzJePNKWJVkNR9yBYDzUz0cUn8V84UMGd90RkB5Jib+D8VG2rv3yYnDj9W1WUOmKxjQJXpTISYuv5dJenoLicYYkd1oCo5psKlGbQJ8XmtCWKmVYvxDI3rGIZUq7eMC1eMoqKEH9BUsM79ePeiUok8MvB9JX70fzDE3w4+IEHL+lj54yx7O060buQ+ex1c+C6XizhEI+dqPXI2OQK2+BVJsrrH9hAI1O6sH7jhjMZidaULGOOYV102aO3Vql/9lN5E0am3nFZEept+87bnW5iVl7CHgyJ576SgJ4qd1aFn0uEjvqh9Ir50QEjQzBudeW0DvUwbM123ipszO+3Xkr3T34KL9SO0V0/rQtPBIe8oC70fivTyplPPBgi5//ZndT43mPxkBMeFjNLm4aR88zk+nZl3BZHadj6sIG0V/9oWR4RhkZSpGksVWX5Ozi6TELoO7njooCZHv2JTCKXfi+sm5slJj6hdnCbr/M8eoE2s/6MfdfVexfz0DS2BULyWwDfs5cjKB50Wgosqq/5bS5PkxLWyrV3MZ1lIPJ7FGZqOWtiBo/atGpsb4YkT2JLhX9ZN9/5lG7j4BTg8RY/ewEr/kYR1FD/JD7oztvknmkUPBX5DExgWJW5cEozhFHqnfbXQ2o41klieS+1BBtRwQavMcPimZf+cnvpvRxXDRstNLRMiGDf9Dypk+vb/Ajj/rwyMHv+JCnLrT0tg6eDZjELVfslHYr8sK+mHR6Kt/C1nWOobIThfi9NY5Ozkhma/z38wMzNUlic9fef6YTfH+60n0LD7xzLcBuDyt8sA/AJ894JG1ypy16Z6Q/LiViUHgV12mLQmS7KS2pHkEDD5Xgpl478x+Zixkb3NE71Yuei5OQ1VJIHgOiMampACWTlUkYoEZLnt0XRYfY0p2gOJobKKZd9ydQ6QQLJpqxQjr+6mK2LOkVs9rujY3F35jO33QEuTbav3khoDB/Lu+dlkTvHxaQ7dscPK2MQrnyE/7fbS9qL8+m8aFS9rG/MVVOyqd9NVPovngEL32pTr3dptF9xRJcPuMm+m0nIYUtCtjSJ4YHVUbQpX1/mXWSjCWmrmLGy0dzF00xad8Jly7Z649LaXYUWTtbtHxHGBXJZv3EE0cZek6iroMjsaLOjeyco7AKAkoTE6jbibGobFjCNB5f433+leDc+iDa8EdCOjq/mTBWVVS9yR82cUOR7XWGe2ml05LGNBwWRVDQKT347suikV8tSfXPKDg4KGPQhSP8S1woDOcF0YfsKEqeHINMPSt8sQnEq1IVNtIsWNrTQJObfp6BL3qr+PGBRTRh0Do+YtkEFizj32sURKvUp5FT/718yw8JYnKm4p1TL3xUycdVNT0kinJh3sePtrVNoy08h+IWSGheVzGu5wxB9PSeGOtlhZ+npmPA5U1sfqInDVM4JU2oPcJ2HhRwjqfBxaM//ZB5as2GGQiodad2mXspxRvyD/v2ip7iHn/2qC80naNobqI/OvvvYJ3nk+jhEU22wTUd73eVQLlkML29skFqvnQa/Xy5SVS1Ip101nmRTUF3nG8Q0+uGVBKdjMGOBAE9XQtY6PWJ5KoSJl2T6I5YOVf412ZjpU04XfOagdTACDjHxFBImToM5ibX/6ocQc9vmPMzveJJSw3IsLCj822abF/bbO7aQxmRFgfYYQVF6dFDEZg6fhod/xGLytmBdGjvQ+aW1cK0d6xkn+8KiHVIhFerLTvfvwRJKRmwu9fG116Zw/goO/7rRQYtfp0mmt0RiUrn/ixY141cFsfSesEN8hl5GPkslxckJVP9i7e84VqyjMUckVWuR18mTJKubDKk+6eGodY6llRbw+mHjog6o6zIZtFre91Hu5lBEeBdFkf/UlOh45xAm3ar4mJ9IFfc4kp3wo9x/dVZNKWrF87e8Me1bZF4qVhIPxfqEmRzde8+a+hr6fKtudr4HJRO/gsL6ebeXfX1A3OpdGsURewZSotXFdG22YVUsl6RxcwNp4ZvR7nmbSWqnf+DW3tks8VjkmhM75G4aZmJgQOGkcV1RaoaP55ME/OoOSGQlKtEZPvjkMjoRwr1Mg3Gg8MDcLhPK/dco0z71Ydj/MIAkt6LpOmvZX7Us6U+syweUa/nM/sogS6PyKdxDat5+7oEnL/sCb/R4TTzt4DOefXSUjV/WF1IwO2/YfYWj5ZzXpYD0ZohUDkspsO9CnFpppi89gkklx/PXtiPprvnfZnmlSDojO+LSq259qvWPhCVf1aBvmMIRmsL+K9vHh1AHgImOOGkmymevM+g16lplGs6BvH6H7hVejbduxlG4omn+ckx/0RshIRuWk6DgX88/BZ2ScXZ+TgwJxVtRRU8scd2fp2F0sKPrjS2IxdVjiX4eUiEC+lB+FdURMfepbAxpbXM7sYXdvJpd/ZCOYuO8AJaotoNYV3LuE/PEgo/PoFeKfempz0LIG02562vFrNfhWG0caUNiev2sk2jMnC0WMbNHc4kTpdHrx+FyJ6nwCPqnTD25FC6874/PXGfyd82W2NJQRS9WlVA/foHIbHAhCn9ecw2HGeyPTrHLpTkUO4HO7TKeKlafhydbBbTwFmnWNuY83zB1kzRvu5lbPURe6hWjEGNXhjdbk6A02UlHLYbTQ1ngKis+fzFDV/a3aME1aMkZDw6Hj8dJPA65szH151kZ3cu4MvH3GZbdx9k3w4F4t2vUBwu00Xz1nwML83DtTOjID3dxJ1el9AcWW7WjVCk/3oMxvqABfYRxcPgYG1DhaoCDUssIcVzoRTwLkgUFW6Fg6ZJ/KqFGWoaSjDjShG69oixSnqCJ9w/Vf/S9xG/HzuDKR8IgtejAmKucez05cHI9gvE8GYvGqySgFvfCmn7vDScO97M/rL6umg/Cc18tIyJfPvh2Yr/vYPgFsugfFj9zaFpWUGYkn+TfUsMpau1WXQxLguRd0fA+7Mm9AxUyN6qiCKfSnD9sw9e6KbQ1ZieFKquDpdFAp7tGk0DrrnSxkcSWvolAKL4hfzVi0J8N3JFpWMMyr8LtFXpOnOYIpDpDQFeH4rRkP+TzXnAIOcty06lUD54+z5+xEgRPRynkOIzA7RkSlC1PIUMvgnQyFYk6xE+WL9dwJyxLji90/b/3wnySTsC3Y+7wtpnI9vRKuDm0O/MbXkAqndK8Op/7whpvMinXggCC37EWlpFlNUq0KyKEHZ6XC5bvOIHU4jOhsWZXvjyFHjtv57/+iyQ4kNfqh38lUUvlCDEQDZfRRK095ZA95Uc9ZQ527AnD3m3pTvZ7xs9yWypBAs2lLLvrrL72eRDyyvb2VqLAPx92c4PDy7C91+LpSfHe+JNy3D87MpCLxlvKzfE0J6XnuQuc65PXIJn6vlSpcJMbOzww8263tiR50vvx0rIqToLpTkhFDOgCFq/1/C1pm2ikqcCjpoUU0ZPAfdiH3Jz5EAzoo4rmo1AsWg5q9km0MJr/rjaYUi7drrh8ohpwNcC/Nc7n3bcl9D+PQJV9jWi/YOjKFtxMKLlp+MTH0aj141m6psEwvKL7GU/X7pTPovVd1lh0g15jBVnw/KniG5ErGZTggV0GCRKjaUdbIeFBJ0nHKD/KIj+7TjCZ86vZeWGmhiS7g/t2ExMtdbDCdkx164Vo7eKIbuQOx2SVxLaatMfj/4tFJ2raGCzyjWQdDufjpi5kzDQCeN+iOnMHh+KtvPGgCU98ev3fKl5eRIk6skQXIxImCaBdns2Ps7VoNl6+vTb+h7rnCNQ1xgx9ZwXSimFO5hDX9n5P6tJZy5/wtbzROheFaDv4Y8eepp0s0ZMm/0z8MM5A2+USsioNIH9GZ1GQ4tTqbzSh0/iRnBw8aFTd4NJIW4/G3LkEk8Imw4z4Vj9x/PxSPtSgvjTSpiwr5D05Lbz5g1n7KNLXakyUpsyBAnGIJ+U9k6FXV4S3Z2Wwar9hlKIrD8uZobS3INJ7MLgEhI/aJb+d9wS/3aYotdGG55c2Is878vx3EOtvKpoFgt/5E1fi+y4uZY8W1/kilm7UzAtJI4q3gtUP6QYCqlj6GKnB49tL8Dym6e521uB5227yGrqbOhl8jjk3/XEEm6Nbslb7P+M9WZfrpbA8fxXqaGRHbPpUUTvtcW0vbmeu76fx1TWqpJueYmodYg3anRjIXc0k4bkedMDz2K06e/h4pO5NFUjF5eTXXFsewrO3C6A37LpxHf7wmTCeDLJs6E2uXqWvl6CLa3q7Fu5HduZ7IlvFy+yiuxHXNt0Adsty8CpV7Lh4t2P/GfnU76okXudFWiQbP62nAqm9O7DIVZMxEGrYsrWcUaWijfNvNKLXgxSor6tE+BvUoxrA2azWjMLatDLg1LTLNayOBxW5M93eWeQzC3w1WUgZoV0sIcPk6jgnwfOyHyutUWHfq8ZTX1mToJLlinJe4Zgx6YcCvHwgtHhF3zGbEeMh5g82vPxuXqnaHC7EUZ7PROl7JhM8vJqpHXKkvr028PLrknQP6qEDGZUs6GyHi76KyDD+aV0MTIQViHQqlIBOqLHfNLvJEo55Y325R6YGnCArSx3pWeBGfDaHIrApY28LFlMDzwyabGXJ3ZlDMP1R5pMqS2PHvRNw781/vRka5nUpiGKlNhkWte3AJtlXlVh25v/752xBgr9MPb1HFFVUTp29S1gKz5lU/1QL3r2MZoeyrlgxUKB7hgWYKWgLpIb6YGJt4FVXdF0fLCEjLbHYctrAafn5KFqbirt3XGFd77Lw8flSWR+0Y1EEiUE68SQY4sK1bmHwOn2JBo+shtfESKho3MHYI6lHrRn98C/7e+49+xkmOnPoAfpo2jJz3rpowZ9qrmdxw629qLzdrrkmRRCx4+7o8fIGDKR8e/O52/Yh7QMjEp5yumyK21PSKTfFSWkHKHNjkmnwlv9LU/vqJB+PaApNW7OwovEQVKdlEmkPTWSLDdLqOeYKKrRtMRPHwlNv+COISPjcV3VjLbuvc4iBsRgbdJ8rh7awHfEfWFzFxjY394chYFe7pBOOiW94FxEjS2PTiS35aNCYxjt/N7GLtyNoYeKNdJxCkHk1k9VpNdvtvTDFQO6dSgUoU8y6ccrb2r96kjXRrhi4LcsGiLjsJ0TndHDawMrmZZPVzKi4HG0N2Rsj4g3YdQgDGTRXiH09aeAt0aaMHR+Iu3/yA3vX+VCtzGBfpS+4Sa/SsA25SOMgqXvnk6ng6mZyFgZSK9cplFLSjVTlRTQiNMRuJynjLf7MyCe6UinWzewZY/DKGlqrIxp/JCZYY9u30twz3Ma+r3NgjD2GruypNE+7LgSJMECSazlUH03j864O9OhNDf8KdemoIhJtOKxDwwcO9nqLnO+w/Eof78mlM76q6DXQj3m5x2NZeoWuGqWSpuLbtT9zVorOrXEljeoOFLveQIZ5tlSgLY6Nj2cjhN/kmnOuiCyPxWEvvnl0s89TWi/XQBW/DcVCxqdSdJHnmZ3pEAnPQ/f/7lC9153qlZ/xqMMxXRixkBc7DaVvhRa4dtTfxq2bxS9GJaP+s17pY1zUvDbSommhEeSiYoJnVozjnb/50Bnj7awjX8P8Pidu5lSRxI8//y199/ZF3XtU7D6RjF9PZRNz18mktHsYsjNyuXBLwWatl2OhnTO5I8qXLHzeAlc9mTzKxuysHqcIVQ3hMHkXir1avamjW7RtOu6PhozJ9HuTwKWHV4hWvEzmPbXF+PWgyLYGDnQ2JfmeFGgyF/K+PTJoNnsZ0gsDl8z45G7vJCy5A6/8/SxtMBpAFufWYjxnQ0stuodvz6zB8/9aEQJ+q5omBvGLyyT1ieusiSjtAkwOftUtOG2gPNpHjR4Qhwr02/ju3zSEdovAXf7j0C2g49o361oOtq1naceAR3dmEp1tQdEb/bk4mVXDdN54EZlT/L4yD6usJXt79Nv4TD66ATL0ZOxujAEI7Yo49Mwa3x+L+PDQy5wjfanXVGubG9EFYt+4IlOzWy6EBdHY3c4Utl7AWPrBHpr7s2bPhih1wpZ1lTFYt2C+6whTJM8HUCBiZvrXBetZppZxaQ06hkznyHwwt3a+KnG4NvLGzmWvjjqmUOH4zuZuZ+Ay7JP+3MxdWpc5HL/TYCpjGX+G62DUEsrdjAgGM46ZpjcMgJm823puOEiluNdx+Pqbagqey273TcVcxv8YHAmlI3fLKC5idG3xWkIGGGJqY2eWD5QCRF/i9HzVyR5vfw/0t76K8qwbdsGRAHpUkFCGgxaQOA6D7o7pLtBuoZhmEtRMbALQQXBQLEVmzkPlbCwFUzsFhGxFf3m/tZ633t9az3PT9/8ATMncx7Hvm/bWrMubGBiN9IEXT/ada6abJS+xmivcoX5vndJnEkcdH8ej8G6s2BxfBrRFXeHn61Z+GdDH/M0qYvsTFhD+ydaYb5fHFYKXV4igwNl6bcFrW/Hwa8XZwVLB12xcckY/Dqb4HtZU9RdHQ57nfupa9dk8HiuiA06LPD8zeDqySoMuFeBx5/NgLhtW6iFaQmp/KyKtZPOUt8RFvx228F8YceWNKTAwW0sbDgRQNb1sZi85p+AKZiG+iesUPAnH3QHm6j92lHaPGeDwCg+Gzd0srB17Vfa/ZTBX26P6KWvecDT0Bd8GjEFzhoxXPS9EIMyjSA1aFgQ9mwSvt5WiTvMpiJpPEvrnfioG7OYpgUo47CZHazaKwKv55bDCec4jFPeSFZXviSHW8eAVroenlYQhzt1CxiuhR5KrfPHfWEGqDbFD2/MDkbBcz7I+lnirlgW5HSr4Mm536TfZxJ9K+Tcz2cdIDhEG/jH8kBM/xcRWdRBXnVm4pHWKKxa10nl1/lB1sdbFI7ZwqNaLfC4wMK1CeOhNuQ8+T57CnT+CMKXypMxzr0UukuTMX20CnQ2VuCrVg6mvpAlA0/ryA5PXYz95ohOSg10pl0xtGM4njoaJDizsYXO87lMrz+aAIrdKuD5jQ+MxUQw+cLiOV8W733Xgku/WRQv4YGZtAvEWpSilW4WGF3uFMjv+ELemnOQK9ztijollH6qLZiY4Y0Lyu7SvpRKxA9zkD2YB1/GHCbJ+oeoR5AF2qMHLvWPFXrGHLjY6cy8eWmPt0UenT5lKCfkP32sHr+TjE1ksc11DjZGuKLHAR5ES3nD/kRzfHDWBnyt9zEvn/jBFyUBdZK+SJve2pHZBlx8c0YKFxEuxoVxSMqu8Zi6OwTdmsZho5o7aPyKxKpWDqyWckRGLQHOjneAzaKGWFenQuzuz4FZ7A3C+I2FRd3pqGvtgEUtI2TsZnP0uF9E91XuJ3muXNqQtpy+ODQVpL3S4ImIEZi0O0HQw3pqczwYDvwNgGX7hA7T9oKWKfEgS+gyfL9x+F5tNxO90p729PJhXA2DM0PNscxQmuz6pwDbxHPI14ByondST9ApTbDF7gdZvtmT2FzLQb7bbyJusp+qfe4mHfuM0KT3Ms2cXww6QhdPMiJ4e81nsjGJg8bJIfRjdyyy+0Uw5/dH2pjqDDE75SFUexyeuaNAGzSMYG5tEZp02uNopxLNnOyABm0cnNUnRZ5cLSU1N/aQ7+GpuFypGJxyXpLLm1jImbGrgz0k9JX8JczXJ9GwpYmLUyUSoXqHN5buNQFPxSN07Im7jhK+FRAjMSBYLrgm6NZ7wXz/9YB2rw+B5bfHoWn4E7ri2BLSa1UBdQcZPFWjT57I5cK9MDP8+VUZTo+0Ct5MNsZjAxJY4OEFubf5+LPoJgk0aiA6CzNg2x0lOK8wSOo0hHsrXU/7Varw7tBKgfIda7AUOmuLrAs9+fo47Ztu0tGVXAQvH2wkefUX6Nmph+iiQ1W4TOgLjIcu6EnMAalH6eTwrxckdb+QA6fZ4+sP0uD2YQOdfCQXbPuf06a6MnIhPAGaXlBy2iUKz9m4wOLiHjJyxxPkbzrDBXlzGDtqDtdD0uDiDmM4qN9N+8dfpvlP7wmyPx1hurrM8H6pMZLJR+nbHXwMLl5FPSS0UY2kwmCTODCDpeCTk44qL/0dbMJNMf8yF4u7/1DHp3y8KzhKG9dX4oF3aightoeUzJyFH8XX0B0ZPAxXLgdjmkV2KnFA/vV9Qc7qeDzy0Ar+VtuiRqcqfsiWgVdK9sgzP0NPPssAZ6VmOuHgRPqRukCXUhs1a2Ah69Yewck3M0GpkQuvo2JB6awOnlhuQcarheOKhxHg9nwhraTKkHkoi0xRrCNG9b8IuVKFtrunQq9BFeSu7aHygfPJ0uax6N69mXm+xQjUC7Jgv00KntoWj3XvjTAVWZgwxKL5uWV0ak8OlE/LgPvbK+DoyXzojzfCnTFLaM4hQ/zklIEBYTPQcXo0zG/NQw6I4txKE2rdKALnTRwJ01xNvqeVQ5zuOAybtJyIpEeCt1wMXlWzw4mT86HWPATEn1tCrgswMt7luKSbBXsDFYyreUsPLZsDuS8K6B63yXA18lzH14Mz0drPFro8K3DsbQ5a2kYyqTpqxOHQFAw45YceolW4zvMhTQ5KxLOX+Bi7QAGzh2LI82PjO+aGKoBDXymGSMuDkyqLO6VKcS1TisMH8/DcOkNQnFWEXyiLltWZmPx0lBqu1sLMf+aomeBJmUNamDNlDl5zTKCX1p0l1i//83y9GXjFUxF6XniATsh0vJK7iN6qYJHL9YU92qFo93wn2ZoTBb4OlkSby8G6wUp4lw0w+edM8nCjM+5rjqbzeKqwViaFjHtjAay8IbRb8vDKL024LH6DNnRwQPzVfnJznia1ls1GuR32WGlSCHMN7nUohIujungTMT5ejCb9JWi3VRvl+HwoHynFUyOVqPnSFZsiZsG5kipMPC5JtoiM0GyjUgKsHrEYk0+TszLphfMA2/6MZcJnmkHOUgPQjPtEniT74x6OC12j10KHbzhT85FFjBR9JXC6bkw4G1rJ5une2JN7n95atp9JIZUw9WUBSDxeTw4HTEPHlo/0TqMD5j4ugbgeS/x+JBcVS8pg+ucO5uR0R5j0xhtvtuRg+At1bLM+T/RqCiH7rC5O3FOBp9+bQ24yHwN2GGHR4Ap66kwB5mUGoYn6IB1RXCsokWPhgiQLb2KuCqymOaNosxFY+xnA5VEpjJTUgfErtPDRFUOcd1gRp12xxY5BbYwa9mR0t7vjHLcg5s7QbLipnIyL7ULgfEQCVjvEwKCtKVmwoAIOZ54kA3x9CF66kBgqfqE3qophh0wGJr6Ng6BPnpDa8KNDVGw3cT72gzh5psGqohLcGlMBMnHt9PjwIYGiYRWO8ZmKi18XwoFHZ2lA5nZydYk0pvinwP65r4ifro7j2PQWmq5dy9z/OYuh1/h4WZaFx6qeaLwlEx/9UQS7hIkQEeQP75KW008necjRjMHuwwImNK8c80RtHSXHhuOkpWpYXRONOYl8nNqZA/rTqrDCcQZYfH5A2rmaeHWLmUCGY0pXK08gE1/zke7zxmW7wgTJXaICpXAWQ90LhIwVj+kzinFNgjM8XpKLGbofiacri1pFenB2VxTKnyDkqmMAHFXwxwM7CnHOGx7kngpnCmsIRgvv6XNUmuNFjzjUGy8OD1gN5DvzcL+RGExwM8H5fVnY57eeGP1QQfMo4QxezKXTO47S8b3rBfvW2tDzRy1BQj4ZajMJzlXJIIrlorT6khm54jtKCtwmQHzmTLqr8iL5t+wFVZVikGZawPSKCFTeg0xMygfiM+hF37XPhtblE9HE45DgRsxFcoOTDNeEzmyXroHfb9mAne9v0rUmBcbFl+JNo0rYvssKSkrfCrK3ZMLfWx20etYGwbKHN+m4+ZfpHJEAsG13xNnMczI9VRccR7lwSEkVTi2267AxFc7z79NEUCALM5TGk1hhJ0ruDQKfy5nIdsdTv1QTDK7nQ4PFHXpymxwMOhpj1+cewqVT4d3jREwWz8VTG1QxfjgX1z/QotI5OlDWxWKZTAA+2GzU0Z4Xi30TsiBXnoXUE//5raIq9KlL484MAuOvfyQtPBaVtwn5d+UM+mStFbky+ahj39JYGLtVglREe5CtTyrQNU0ZpFzTadTwGbIxaDJ++8ji8M1A+MEowOvEZDwZMEodbrEw+5gX3esxCSfseUGdIvNgtftFsqafj9/e7WL2FEVg+UNNkA7XAcNzKvjocjx8rQ/GwX5hji0bJNcHyqmJexwBbhneS34puL3bDaBSCST6s/Hd10f02DV3KF7MxZLTssKdOkQuPObBycK/gqfLw5EnwoKFTzrqdxTSAY1cPFasCx27XKnagD3sug34a34YGu7cS3oatpAGaRXQzePA+3YW7q6rgnunqtDogga4P9KADJ/5xD1cBLuHi/EapxjbNkaDroQYWi1msfkxi2oO0+HOLHXsukogbpGQIbwP0IeWxRg5dQ9N2GkA6dnL6fIHUmT4fRmw6+3xlVEhfBmOo25mqXg7dAuJ/luKHW5TIMJdgrZG8MBmZyE2Ta3AANcKfK6dhEaVdaT3vA28mTMVc6dUknEdVaD5sZP5KTIeVZsD4eXtIjKvTgZn3E3FNLMyONvlAHSgApu6bVHx2WoafKKZqrjxIS9AgyQIykFhNgu1er2kzVBT4P8jBLv2zIIVg0pweG0CGujkwvQb80mNZREmHPMh3wxZ+LBTyA8TYuDu4Rtk7V0zODZSCg92mICkzmeymqyhSVMK8LOOGk4WPSqw8EoCUX4ZjtHh4/DiPDC/IkIl5x+lLu3HSYxEBMbeaKCWoclgsCkWvLZEw5StLHTomxAXK234NMUDnC6EgK9iFNHxEM6FaiCoPyvE4Rokc7hr6bt/Qh7QrcSOVW6wYaQCgr+zULToOH092QMWF0njR04pDKcqwfNHXDAScsNygzmCgcdjIMn+Io1eF4Rea2fjJicDrJPQBqtrfNh0Jx4eKbvgtm+ZWHBVEa4PbSIXVnwlmlxdvCT0+njJBGSXOkCR0i3SeZFFzxnXSWPNcTJvx0ti0J4NByNEQFATDagqYIYlJ+HDHk1cn99JKgc98ZAbi05OybhOKxg/bHxI7+x2xZ312uTZtiq40DaP7G8AiDhJT/dsCCQh3GAwuDwGHuSyGDlmJbn+MBuCzPPQbPsegd3NS7RZ4hVZ4xtOn0lPwy1RatDrH8QITovCsrdiOJfl49jaRFzqNx7fLRBF/esWZMr1G8RLj4eiir5Q/YYLWhuG6PbUAfJerxxkNtyiacE61HVzLPH0TsTCg4/J5OJNji77FeDw3AckcswiZkMZi+pQhueeF8Dmc6OnG2/8IH8d00BwsBiLprPY8PMoPb+zR1D2isWJHXZgw0+B2x8nYPU3HkgmmQr8LcbBTX3A/q+pJGEiF41+imC7aCVsvKoD8m9kcM+e8fi9s5cukYnCIq00NH47G3c0ikFQtyn0mqrTxo4YIdtWoomUFQ4r3ySvN4yFS9lGkH1uQOD76iOJabDCiNcmWPbwA33ydBI9r5cgWGoghZnLk2gwdz6d/CYatYYC8E87FzXuBkDj+PF4/0gEs8b4Fd0yzEKeSDhWno9FlYt+TK9VJSxkc2DscnnYaJGHD5NUcO5KCTwo/Dttzx2nLbYES/Pl4NVbRXzyLRYSPv4j2X25WDCOi9rzZGBB5A8ar+WHW26p4uGFoRirMRW3RrPw8WQs/siZCP2lIjj1sRKmQAmOO82i4e0x+FCfh8d7+Ki/MA2q3xuj1RExal4oCgGrV9kb9TH4+Mse5vBUOWyex4KKUTGU6U+ESyP5aGweBDXXWLQSZqPx6mzcFc6C8q9sKHYogqWjEvivLxQuhgbiajYHp6Vk4teYn+SJ+jOGEzUZa6QlUN3zNE00eEn5LRzckgg43BcC6bvmMytmOWFHthxw/uRCbWI8ajSdI48SdPH28mFa18vHrg3acKz2DvPQwo8e2qoA6qvT0btAFKesvUt+/uf512/8aMBxFbSKSiCK17Lx8HE9EL9aiOoWLH6MrCf5u/VQx9YC/AOswLuHDz6KvsjxiKH+bWcEF/rl4MiucrjkIgL9pmb4rKIEa5Zy8cCYmTjmjzqenpkA5W1GcM+xCqbdr8CNKZfJCxkjjE64yBTfVEU7Ry0w8rXCWqEbGj7h4UPXSuyPTqBT1HZRqbwDxKTVGKcfmoY7JBMg7nc+rHingd/8vgn78wqBgGz8fbkIT4jW0ixPWYiovE9fVgr35kMZTHzjjmLdFRB0TA2tl5vBkoM9xOTxJLzXx8NPB9ZTNJ4Klu587Nyn39GNlVBBP9HFomLQnF2Gju1z6U/so0NmNlDyNwOCZ1vhopwF1IezgNk65QCNsTkiOD4SC0O2frj0SSFIphngW8O/JKqchX1FufjogSOGPQ9DwYZQ+ro4k3B0beHssC++nvqLtOw0hi6zT8zCBVyoftRJy4zV0beml65qG4vvwjRB8UASroqeCmPaRFEvKh+XfmJRvssed/5hcUt5Mo6IceGj+3ZSsIQPA56yxKNcBsUq+SCrJwNaY7ZQZ9dQFAk8TOKuJ2E09ceYaQypCSaQ3uiFH77yIQfWkIvXRKGmMQse39WCJf1lkNIwAZ+GfqTXmkvxSLQLBl3Ow9ojwjyYupjZUjwHUh4747iLJbQ11RtDL1nh7vHTIe06A9YtecDNaxDU532kKT2/iVu3OyYam+H53cOM9POZ9JluIuScUsCvLu5wxPw6kRyzk5SvKUOX7YGQcusvaTjKg9M/5uCOPhY2s+IoNu8ko/NKDUsXhsB6r6m41jQJZK+bo+dqLzy2IwkOp3CITnsWfX8pC25ohpCblu6AstNgz79xuHd0PPy8lwBSt8Xw7hs5dPxSS43W1DONG6Ig4pEvlOTy4LaNA67yJSAbxYVrog8FX7v3k/s7RKCjRgJrX7+njnWp0GE0k55pHaan7RzRosAP72bvJyv/BSDHZQYYKfNgq0wRyAt7LHbvDfp5ZBZd01GJV66zMC71J5GVHEvKLy4XPFZLhLFbTlMRzQL8VCVkbOkQmBjaxuy7koMuE0zJrXbh57+aho9XGuEaVpzcDRukB0PHoIQZ4NlrU5j7nSsd/fsbmB+asVT/7xJmrlI4zsyWxgnb5OHWXE9c9zIbNKfE4z1+EX68nQIf1CfiDsEDMsGXg4W3JpDMp0L2K3tLR3dYQ2ByGXh+igDlS4UodqC1Q2XvHPzuEAwVG7i4q5WLS3VnYnUWH3gVb6ne9IPUtG8eKW57Snb6n+zQZrjgtmKQZp2PRq/VRfi2fxzOOPZcME3BFMbP5aPmZQZblt6nxcmXaf+/b+SYrSoEnz5GdjIZIL1oPFYvPEK8CtzxYmkclDTL4o2aDCjYFwTG08ThMDcLJSLq6d3nVTB7wACvGorhvPcstFe8Jbs9/bFJ+rtgBhHurpoX+Mkkot2MOubfwSrcX5oOQWtayY71DMq6ppPBM9H0aEsltLndIynN48G6QwY0CjXojj9aqBAjdHTpV2R1fTJU+00DM2sXsBxIZS7+FiOrHP7SiSNmsNMkDlOuLaEnpF5TudHzVOZ2APwq40HDwXhqkfqVqrPlOPw3Fcf8rISOnFWU1YzCK0E7BM7ng6jzs1KYPugKazus8H79F7JUrRLrv8SBQkwzDVTKIeFLv5IDq4ME230zoKw7GebJrKGy+Qm4Nm8VbT45wKRqcGCj2lniOLccfv0QwV15NmjGyYarv4uRd7+J1seywAzwYNGyLLw+9g5hz7pAqGe6gxh3mH72vUkWiNvAqk4WHSt8qNqlPpLUbEsmzCqBwn02sFxbknRNV4PlYcdo7c4ceFaRTc9dEbr8xUTw/aaPDUbTIW/TfDI+ejHhtjih+8YX1FoxG96qVcLD9pmw4sZVOnP/EdoVHoFTW3NR/ZsHdplkgIbxf/7PmTeu9I2HBfJLqH5EMPRKFIGK5WeyIDYKBseyIJleglGF8ky7Mxd9+oX+VJ8Pnp9ZFNEeh3JX5gquXUnDwF1ciPxri8fX+uGIgwjNXPeP0g2GTJCw32+caqSrJw2Rug3WqGkvC1Lphfgl2R9n9p/rMGjLwPrv5VhTxocTlzg48qyO2vP98Jv/bbJqpz/63auCfAchx7DzBA+LYkm1VB5umlQM3YuX0M7ZChDRHIhD3Sx2zs4CFdVSvPybhW9OOWBk8IwqnShEiy4OXDqqgebVZ8li/QKYuHw6TTpVgkOXzpJhnTnIjvxk1jfw8Ho0A2+fzcCXYofpl5nhgqUXhoQctoDuFoRCRbEfhp3WpxopfjBNmA3nbMsg924cPP9QCXrRuXCiLw+bto0SBccYekTKGwshAjXOdNJcByPoe+0Cm25nwHzZNaQaGgSyjzh0U0M5Xt5cBVEt9vg93gMeVM/G+oJA8vtEI7n2MxJUZ6aAp5AZZUP4sLbHDKJMHxOP7dPhjV0S1s5IR5d3HDjwyRKUJpUSGY1Z1GDZTOH9PBU07APY6s4ln8IOOKYcfEw9WwbJ4rXqsP0Oi3sSXEjypEQcp6gJ3ydwkb/tDfNpSyuRay7CXzoyWPD+Ju21qQTRYhad+xPRKW6QuMQO0/oVY+D5n9OOnc5qIF21SJBeYgCnjCvB6KI41rzzxsgsR8SJp2lw5nS0X1sI2pcvE++hKGi1O0tXLedj6K875FduApxbJGC8LvuDLCThNyFvvVLPxylSLDi1FxBiZUxflcVjXLYB7NF3gmeS5xiF4GskvtkAF4W0kK4JX2hhqjjkBudDl/tqevcfD3wlSvGHYjJSGaE/leozkqeE+XfLlLwYl4jZdln4Z9YjumLBYvK3OoqcOieLu9xOnD7oHEpcprNCkq5CvqoKRh74QXgXLUBnPY/MtOSjqnI73S7pRv8ecwf7T+9Ie0YR7C5XpkUn+dDNPiSWYINGN0RpUAYfJoVbMEOklrE/tJ5eO/WE7InLxHGjHjioUgWzNk4jFw5QmiScl30Wc+mtfa+Yp689gZypwMbeQqwZ4qN6hgE4L1dDq3oufhzQgKhuB3Q3EObhzxc0KqIAZ1/vJQNeXgIjg6P0uU8e7lDdT6zdD1DRSeLgKsyRpxwuHPQoAFVfI2wrdIX0lT9p3cZczBs1R3UyC76fvU9Pl0fj7m+vqeBlO/lHbCFQdjI8jXMA1QMsDmakA7etApXmHSb7JZJgmosH1WrSQOmcdHhPSknfjX1ki3UWLhwng73BJlD2q5IkmLGwwj8Y1V6wMNn4ORk5bAn2rlVQXFYK22W/MH3KYnTV+/uMzdQyrDHooGphZnR/ji7KXGLQ7IY+zMsYJs+iLtGloyzWv/Cllx950b1Vk6Fbm4X9M/qJsfJFYt7IRbfuaaARcInsey0882sbrJXl0aYjWfBu3k9a+ZULY4Q7vnTYGWc/6KBNB67Sc1+roFHHFL6+8Qe55rGQcrGRRHvdpR77ubDPwRsrvcph9LIC/Kpm0cRFvqP5obBH2lgM2R8EB4f5OEahCAy2JMCM9Xy4d8gQvu3i4n2yh5xU6yKLLaIwAlOFjhYPRdETMUqKi9+EjhzzLRibu1js0pPHz2KxkG5eCVONsuC5cK7VEzThRf1J+iRNBl2GuJh7zhl7z34j598dJ5m1N8k1HxY1l22ktxfHQo7eOyIu5wgXjPiwcqEygFOk4MPEcyTr0nly/huLmz7zUeu8Puy6GARJJ7kwflcAGNzjwyWbI0R8+wva+FiYzydV4JVdJnwzaGM6mFJIFvaCyZU79DWrjMu/snhF/gw5xRGyzk9LKE0hGObFR9m+KHCpfilY6CUG8hW9pG+JAxza/4iGeypitbYcNLXtIuOntBOZr1VYIDUHHzLOsNInjRozImjYVAWFK/nQVaYLrgtFMXNyMFnx7Qf5lBuG33mq6BP9gryWOU6jejkw5TsHFX4eYxZOKgeN9wW41KEYtt/noXKJHfz4XIxRQmedZ5Z4+uI7N9K7kMXfx4rB/p/Q1d27CaeHgdvjC3G5Ews5t82ge4wvDio+YRy+VkJujAp59a4MtIbe0mKZMLqw65tgqnMqrh8ZK1Cz/UMyg25RrxRZKu2VjtNOS+Kq8HzIDt7L2Ak9b/9PLrYo/KFfs6bi/gw5WH3vCs3s7aegOx5jXhbguoNiYKy6k1gvTkPLa5F4xywak05oQLZPMhR/qScTnlfAXMs5eI1okAfCfiwKV8WjSaYY16gt+HVGF8bZsriO306tbsYj6zRZmM2xIO2aBodH46FJNQG+7+agg2k5yp26QW5UXCWzVzniwfMJmPZgmLj/lYL2m/dJKz1HLh3ZJbjsJY7n+him9o/Qky9lM7l3jVEsrhwmpqvjXKkInHNRHeXSYnGWcyUeWxwNqQsqsJBVgjVv1tFZ6p5wfkE/XbKMxXvCXmF0A/H1PDVM/1qKteudHN8IuXeIVGF0QBEymnxs+8aC1eAhxkquCrn8bWSHRClkftxGX4XdJlp8oZPMy0Nt9UzoySoBjWht0Fk9G1eYtNPT0/k4TiUTpd/GQ+qWTPgJJ8jQx0M05+9G8vqWkHkDsyF6ow9MTfKhrSUZ5HBvEV7p4IC84XUS0GQIdfIskgAHHKgQATlHYQfP14FmmzPEfVQL7ycPCOhFL9y9lAt42xZchdxRrFoI649mgctubWK+lgsroBTr435QqR9tpzP1PlLBogqM81lD/UJYBA/2pFymDk7J4UA0qYBn0wm5a5ePUirD5JbcNExNriPH9yRi4FpXuuG8GKY582DP/nVk86utZJ6ZCKqmPWJK9SphaFoJJInuIRXRc8DePxcWas2gI/rl0DJggdZfXdC78idjZ5OInhfMqMVuM/gbd4Cu9x4g34teEcfdoeSmdxWE3dPAwk8KKO2Vg6upIy46UIxnPcuxw0MDpXLukuWvWVStSMYFu0pw5j4BGWNZBUVB0lCrWYYHC3Ng1V8+7txqjt4aWdBnwOLiyzrwRL+EPkx8TE82m0KFThTeOeIIz5oSycQ9K2nvthmwsFMHAvItoOwYCz/TBunkk1Eg8SId50fMxr8tAwKocoSVF0upzucEqNYdpC2Wd8hFL09Y5d9Ivq/OoNKBNrD4lj1UFSYCv2SULBL2wos1qTCS3UqbzV7Rr2+KIKdwkmAouBIOBvswicK7dCnPw57Le6nb2TycdNQLtNSLUHYPC/bbWZBbVCWcoWQ8V76dDPyIhYRddmTi1GlkvRIfJibKwG2ndKK8dwI5PZ9HDHeG4CwaCeMP7qX83hyYuFAV5KSWMZl1CeDEC4f+UyzmXnPHfVUdRHooA5qL0+BnUQRY1YZA9/jltPXQWwc1+3RISKmCI0ZVsEuvg/KDyqD+TyxOdWfhwYwg2Ls/F6+fk4GVSg7/+Q0YRAvfr+ZPOsYd2UTuHQkDZZFC6GsXp6GFJTBueiUGin2kzrOiIfQyAwinqdT4n4K1yjbAc/lCIgZDwLZ1L51+vY7xsWujPVm5cHCIA2OuBmGTJQeUoyxwhYc7rsiLQA9XKxLp7Ec5GoGw+f1syEuwJyfu3yVbhHl+oi0WrG3KyaXgKKKqIGSPJEX4Jsz+3Zp5RMzPC7YvC4PtIR+pdrAVbN2SAe2T8yFRchzdlI+CMyFb6A6XeMzJWkp1A8wxxqQC9OM+EFFDX3iwtxBWbnhJCucFo9L3TJR4UIGyA4uYJSd5END6kGYeVsUsIU+ftHOwl5hUT/KbebgntBDC/rL4ZAMfYrZ6w42V5VhdGADi6hU4+lP4ufsjcIWABcvZ5hhMsvH39xCIqk+DVdta6d4D/bR5ZzocNDhz+mxJFAwkUEEX10/wxNYXBaun4N8Hk9D/dCLOP18OZkpJ2OXNx1ZYQw6Eh+HjV4XEN3wM5LhEwYOhZFjwl4WkiBQcvcUDj9/C+ZzR0IGRC5jr+S9OX+SkQeTZSjwT6IfGW1kMeLlOIHd+Ed29koW0QG/IHfuajmlLgyUDcfjj6jE6Zbcb/jFUg16LXyStKgEOvfeC/J4qrDbsoik1x8lZn1xYU7mBzNXQgOZRG7jVr4zd+ZqguMFDsN0kAi9NdwaL7jIoFe6q3fooEE3Rh96P6Ri2R4Y4qrhApKs3bpHdQWs+N9GIBbqwV1Z4x635IFMaBC/HWoPuQQYDLs+B1V/58PnkbAHX8BMJLZ+FT09qQWFNNLbNUqU96+eQH/JleGvJDvKkQYF84pnDyz25UDxG6E/a9uDR85fYCL2NW52OfqMryI5dI/T52jz4uzGQBrQXQ7veQsq56Ibm+okkY5iHd8PKYQPPmWzQycO8FBsUM5OjK3/vIInmwbgnpInxkJhFX5/mwSqWD0PqDI4dWEv+Xe5mYnRfMu/HtpKnd5xhc/osPLt0GjVqs4Q58bYoUcwhg00vmGvfxlLfFAnhbLrDnAOPqX/dCjotxJtu3ZgIQbZZDm/PmwI28qDsOAdbrrDQ9MgUv6XwsWaWA0aEz4EDo7uokfsjYq9wgFR22KHeFQ6MuGQQvxcp8Gh7n+DSOTs8HhgMOqd2Uh81Ls19NRX274zGfmUeLJDzgs5r4rh+vTksjJxFLbTSgAyaYRJbTa9dH2Vcx0XRCPdlNC3yB5nZogX9Culo8cUQT+gco3q9OWi41JSO32KKNxyS4G+hEWq+mETOBxymKbCavLJ2wn+qLbTdvRTMu+Jh+wU+CNw04ceQLsStXk9uNMlDxblCXNsWLPSCSphRVIRvrCrRdgUP3n9xpcfelmPEmTu0/qUn9a5VhAMJb0mKiwFMPKuFIorB0GDe2vHCluBVixwcOFUF+w5wsSbiB6ns1cGCj0kkxn0fnVU8BjXaK1FGKRv+HhRBo/gkvFSvDr3CnXN7MA4DnNNJ+QMujl5Ih6aXDA69N4aX8neoVTtLFTgTYGSGF7YqaENLtQFEn2FR7cIqeidpAm7ScoW5izPAOLwMUMjjP+1nQXZLPb33MxujPvPhcNMKkv+EhQCfSBxt8YEqPks69irjWUsWUv4pwiyREMI5WIWpR61grTkXD8uNEJndaeT93vmCab1KyGwgdH2YFHTfT8MM1Sx89sQWuiLPkAl3xKhGqyxuiXlCN1m4wNNlq6gPLwslD9iS5D4z0DTmQEvbN+qxMZasunWJlh0rwA8uHMb80gHSUl8JSscLcQ1IQf3mNDROXUVdhTxebRWIhlSSkv5SCOqXQ0chx5zP4NLAtEps7eDgtqMsbOd4wfd2M5gcJA5P94mj+SUuKqusofeDTFCcNw1fPXtENQ050OaohVKiYzBDdDLZGjkTp96pwrwCMWzWHSB/3pdRGwsHeGMsAt82GKJ8lwuwZftIn4koPr9tjj5TbODK5SLw4kjjNJtZNOwIQ5o/sejwJQ6PekuePptVhM7FwQjTv9O5Tj4Cj6tVWPSAB40jW2l0mvBeX88nojf1HTkbWZjnFEGmHMyGdkkWf9VHgti0sULG90LVros04cggsb6mBErGGRCzRRMMbd8J7j2pxITOxfTkk0Kc9fU8XaRWSs8vjYTV4hm4yEUauGe7mLnZ15lksUr4FPmCLl5TAi2Tx2FS50Vqb3WSMTtQhY+e2eIYY3VYfl8ddetVSfVbWUbiaBRGi7K47LcaXEpIhT8t2h2vXt4nsbvyYVN4Z4fRRT62a9cR5W3pmH6Wg2fKhV16r0rIalb07aoADLvjjQ+fXhWUD7mTwVOx4Py7Ah9OuUG7NceBRLQCFMYo4t8t38nSv3ZYOaUUk/vLsO76dmryOpjcti7EwPW2eMc0F6oCZ8CJOF3mptgtYsieJ5+YhwKO3hz8EvWScBuNoKXkM331bwZ+zWpgHqWbYNXFfDqjfB79PU0PDXXckHtpLdnbGYyR8wLg03kBTbtpgWYfKiGnhwXrGl+QLDOl+X0x9POPOfDuNBd6zBRIb5oxPOiTwWwFI5xqqEN/3JiD7+oJxp45QqzTqnBHkz1abBCHFSEZoDdWidzNbSShstOQuTCXihWZ0gN6fGAPv6WrGyrgg1knmbLOjhaIi+DDJ7vpisZ4PClZSnYqiKDkdQ7IqD0jGWsBInaKoLaULXov8yP2L9+RwW/H6Mf8AqI6r0Rw9os9DAcWg9PhJvJ6bxfN2NFKfgj9rv0RCxULLMHUrBI4HZVw17UMjpRnUymVL/SNcjX9PZ6DHw5/IuunceC95ABzff96ulu6hnz54I2ehz/SkDE8mM4mg8/WQgitEqdvJtvR2O7xJOwYH8e/uMakphWQvs+azJL8FnIB+QibasmJYj/gSLGwa6oM/FmRCz7ePJi/2ADM76Qgb78risiZ0PBifWxtDyfsJYAA4yv0dcYUTCtsIMft51GHnyxoKiQjEBYq14eBeGwVJOSbET3dmVBTXgDvzDxJTVUt2VixlcTKGeGEE9Jk83I+1g3Gwx7lPOI2yQsKdv0g/MRbRFnRCgo3p4H6xSN0d+198iH4gUDz3hVB48sisJh5m6yae4qOTGmkx8Je0lYpE3TeGEXfLi0gnPhaZmjbJPxsIQWm8bZg/MIDe55XYZV7IH6MGksWlvjCphNiUBbOB62GPGK8QwxK/NXw/CI38tL0Js1ZpYYvm6rw/Op2IhHykSRqV+FxGk32RSTDuV16VOFzomBrrxp4914VVBrE0Kh+Fk7PzASbx2Z0hnQyynoVQPaMkzQiSosYas4gPq25OGQSB+Lc8YKJN53JXrcYVFM0gxl/wtCqrpVk31CEiNJ0clBvMplNeNiwuwJ91urD4sBYQVWCMo2jGnjtgytq0ih8/O2cYPmqTBx0dwEnJ2m43DckEF/aRdeMsLji7lgaVDIdtQ9aQoJGJfRrzaejR5/RjQfF6Hb1s7RwJBLkH/xjrIgJnTanAI5E/STzO6XQUoyHjrlAZH4/Zd5u3ibwvepF1edtJlNGX1H9TkusuHSBLOlkQS9flwlx0aL7ftrB+8UcePY5Dl08N3eEyZqiu/McaJl+QuATJgP7n3XQT8KuntaojS9bk5Fzwx9GIhfTF1IpeL/IkfyJ5cLGE1XQKSYn8L/7msYcZUmuoSKO+ixnlioUweaDQTDxWzle3e+OKpE8HLOaD5yIYCxJCQBr5ROCh+QsWf2ehWph/ncotzAi/z9f27MjILuaC3UFuzvy32rgDzdp/GuSiKMBIWRZeiM1bRbFiTFFcOhTIgRWlTs0HwyF599kiLdyOYjsUBccP58JD7Kv0sAME8ZPeO4Lx7KgqSYSPW5dITrNEzBrVQJG6mjgsQuz8QHv+2kdlTwIaaolv2aLYs/iBkHnPAXy9k8guovzBWOHsnH7EA8HnAuY6Q9VINIzEEWEXaPcI4tLbzcKVLbMcnxhnUhe2V4l2Z5WGDb/reCawBlVXjmCtctuRrHuG+1AAmclMsCkzFHYZ8V4/04U7i8rF7JFFshWlndcGGuGjWHjIMMxAf5OVKcKvFLcaCIGHfm3BJA0hfwLr6H2+kGgIxELz/6xeJEdoiYzlgtqwn1Rfm+DY9YrHl4R0QOygEuG7YAuk19J82qKUH3RKHO/1R70FDbTH/1DgkSbIkjzKcZdP6yZ0ElzUDlai0o0nOyw7LpPeWccQeQLixWjGdTD3BjCPg8L7p9RpucH+HBE0wPOquUiI3THk0cDsNVZGkrOmkC5biGELLrreGtAFMROhYC09Uoi7lJD13tUgd0fPvzcZ0xcJ0dCVT4fW8Z74t1JLAjuqqLlg1Rw442SC7sI3iz+3tFRwsL3I6lwc0k0XJy9kVqK3iIJjetp42KG6rvvI98fpkJ3vBWRglJYtXIzlSpgQTu1Eg+Hz6HHzKSZ4hhNGB9RBL8uz8KLNm30/8zL3P9lXpL2h4JF/3/nJWpkM/2bWAkFrz/Tmw9O01jhdzxsYgTRo+n0/qVQvGY7U1DvzEC0tCt8cdxKOO4PiF17Kkj2meMptalMRVMpvtvBwm6XCihQuUbvxkqhqu9keLy2klzjsTjUogLlFnnU744GfM+cCNmX2wV9y2wwv2gzvbN8lmDRJXeUGCf09g4NgefXp1R3Sj95qJ+BQ+MW03/DgHMm52FuTxnqPn5MbO4GwtzNKjhABqjK0kpwEfVmxOAkc3txBWjFl6BI+2datLEKNjgVUMmZa4hQ6+mfT1zAG5b4XrKCbHtWDCuOZOEjHT5s/KyDX4ojBV/uJMDKwHx4x5OCkJWi6FZhCBeeC3sjdDPdcOQN0RezxbGT1xHzES3mP8+gji3g42BlCDXac5ZKvMxivjekwtNa4XdbMx0666RQIXMt/VQ0k6ok7iQbj2SA+KFEQiK5MNZ5Gv30pwgUNf3RmIlFmXI+vJtdLnhjPJbMW+gPG7el4NICRTTQKsMEq1y8EX2QVvVWwHjeIeLRZQTPOadplF4A1GzpoV5cF/CUVEL51GpGMQzw3kYhE/rOpp8mqOERJyfkTJAH4nKKPJPWg4hJ/nh940HqtkcGr8qrglyiBHEbERDvLy+p7uRaAZ/hoMtMDbBpYcF/8RTMqlAjA2AJbqpdVLvwt2NPPgvpslW4om4mip1oIDL7NGFLK4uqAyysvLGW4XceEHbOH1IttpXKnzP6v3lz4+b/nDdKE/Ng8dz/5s2PVFXoNadk8FY4Hjlrigpms8Ff9TC9WdFPfV6U4hXOEBnh9pIN1/KwR6mFcoZ5ECbcN5854rgKmpkJ74QcblpCwtOdIOeDDaqOVex4OLMK7KWekq7hXEwef5Spfs4H3m4eOplPwpkGHnB7gT+uPc2CyZfdgiVF4yEqrY/56f6c/Etm8HVzJNpzSgR/2Hd0sbYYJh/TxIBv8jBOwGJwZhB6rDlE3Fo20xH6mH5vPcw0plygKC0vWBVYiT/qXPHPrQz0fFeGnx6NEHlXUXJYsgju7XpNNws9LXdGCR6fYE2XCd3h4zcVVIw7KYjXr8SXYy1pf+NmJrTTB+0WSdChMe+Iz5RRut9cyCklHPx8l8W589NQRehQ5/IMcYvtPuogw4edimeYFZ42uGruZqrQUAgbrANR0tYVio96QJneGHAdw4feN6aQ4DcGbWd748b9RVg7pggDbL3wi+hp6tp5QbCFZ0eOvxmgS3cq4oUj4Sh3pwRGH3tg9i4JrLWNhVR3a+o5VhSmrNZH21X+qPGFi5NrI8Bviz4OF99g0Cgb9u2URfaDAVocjsKp7UvJldq3tFpLBoZ+niZfV/TQxx9Y6F8xTEM9VNDjEsFvuqVE7h4XFhneFShAFEj+MRH6cgbM4iXg2+cmqPSnDJwXKiAmraTGe0MhJ+4D+XV0JX3+eCMlppupt5zQFerX04V3BxhRDT1mJMoFdTqtiJNYyv/bVw/d//e+ml2cSIJGr/zfvjp13h8P743AqR/4qC6fieMGrtLn06rwkq4aznzE4vlbUWCrzeLSd2OI6JttxPmRKQwI+6LKoAz2W31ijuwoRImETNy49SB5NDEH/afKwpO1h0lzpCb0ybqjue0Deib4Bl3Tv4v8/jYJxMtLIbw2ES/VxdO4598EJRILaJ9SKY54BUFtYj7O+sXC2uRIYKp0UEU3nsqulKAzJNVhAMSh7XsQeFVwwE/FnA5IOmP8mlEmpscIwv7ycMkJD4w0K8ZvhRLgX2CJJqsM4GfdfdLFNYSDBVUwVGwKoeUl8EXeFj4N8SD7WT7MniJGLzry4auQ72WW36AZ6aV4zv4BLR+Qw5FVJujplApNwj5ev/woKbMNBk/h2bR5NWTthUGa1qpADxVJkHWxHwh/Ah8d/vJAMpODdi+rsMMvF6ayM2BOeiXMrSyF+ecMcFAuB+ITimDArIZ+TjXH+yX++OLCeCq/9TNdPL8cxel9QcQkLr6j+tiSaU2Soi1QVMIEOssNsesZFxsHuDD7jx+EfrHB1dxMiHXOhHYtOXJ9Kx/DR3OwwSwPm7142CSoIz1PWZwmZPzLIZVMMitBkk3q6QFhdgfwlKEzn4NTEm4zbYfnCx7mKVMtVQO0fBiDZ/9OhAf5Veh7ugxUPAuJVp8dVlkrYBh5QATvzEHEmgszJerIrOpSVGqsQNGdkbitcwJZ9moJ0T/ohXsjtKDnMh//D+/cKv2feWfyL1NMmPVf3lncUgSf15mjo0EOhmSy+DtXl26bOxmfizih1ukCfDvEQgKPQy4HBsK2jh10kWwSfaOmB731aY4fcjTJ8i++UC+hivpysvTPIIvGzdl48ZshfihzBb3bkji0OxTHlTtgkcJ0MBKo4e9nm4hxYD5+3pWM2TiFNpwJFVTpDdJFfVUQOcyixVAYHvJJhOX9Hpgs3cXsXdRD3CyK8XOpKEBNMH44QOCdobwg9mkYdEsbEs/+aJxct4ey63rowX0sNo9Z7ViiuZ6k6/pA73Iz+BxmTUYuMdR8bwsJO1OCvh+4UKVcCjM+nmImxpvQ+jeZ2GMbTg+0+qLlfT5evx0FlmIsTo7wx4p3PDjw3hSV493B/vMgs/QbK/TkIvB4r4I80xz8fGA6ectOpw26KwV+tb7gIM3Hhgfp0Cr5hQktDRLORyJwSznAquaB2OcTAif5IuxeZgg92qo42ckbtvy0oL4ZdpD5dQVTuxCJVDIP3yiPMlcLiuAYLxST5l0imRJF6FgkRviD6+nRDD/0PmxNekVKIWLsdDLTZybpWsyFJp1wIhnAx0g7Ls7e64Y3U8SQlzsbou3nUvcjmmBUF4BN3t8pHlJCQ69DtFxLWrDdtwj9GuyoZs858irwAJljokGcV5ajrkkJtpxwRGuPKOxb7YOv3rULBowyUVLWCH/NWE9NZTxQPj0fNcERnr51IDH/ZLBEbZCcts2Ba21l+D/x8gT///Jy/Pv/Ly83/GTxn5gqTO3SRqf4YtRfNw1Ehq6QLwfiUf+XN85ekwD66hEw43M67l7VStYZ5OGq2nFQ/bUc5fhupP6tEt5dZQ6cRRPwwvp83LqWB86ruJjqlAS3j/Lo/HwWN+cmwY4jHOysVoepvZq4dnEFhk0whZOXPemV8qOkaa8XfFmRD+GhOdh+7BL5J7HI3vhECjZ79pKfqv5gf9HEsUCYxQoqIdD+R4JGx91kSl23kf5JviThuzAP78VAmeVDZvHXLLwiI48FJVz4NSEdXq1PhK0lh+iix/H4SpOHBZKlsD+sh6xZnwmZejy8EB1Ef/kwoPc7B+Yf5eGjiWn4ZYhF/VdXaUNsJh7r5EH+mUTa+iEKR7bk4qNhFnLG+oOOjTnOkNOGOTtDyXd7BTqhuhyWPbxNVnQnYPyCsTj3TqCQbbIh6p4pvBG3hZ/33Znnv8PQmRbCaA4fnMQ+0i0+Fwh3eDYNu5eGK6358NjHDYJFo3E06DHpVmbAZdMctIrsYLqHuVj+JA0ifaxw9k1f2NuZBT+f8GDoKh+HHKLJwvJ4nCQ8l15lCKpeDcXX0/mwbEgUbq/NA/kdUvhTLQ827wkG8SEp3PlKEdo6VOChqhn1VpKHSwvj6PNfLQKLfzyo/HGQPnwUjK6ieiC1uQjdiwrwsVwhnmhSwcGO+4K2lKPUOmwtcRl3jAlqS8R7Wh9I7rlkMne5J3BUSsFYlQVjbV/8j28NP/jffUv27306a/1/fcvyZglkJGXCubf2VOqnK7Kh5Wj2jQ9fHhdjR4Y8btguCmcOeWFnYCijnLqbGm4MxSG7qyS47A/xCskiTU11JM8gAnSD0lE33xE/2zN4SnsDvZFXCtFJC5mOgVS88NKY6N+oJtc8J2NmLosfxwpzyk0CPGrzqPpybXw0xhVuXzCAZnV79E2YRrhSiXjlWRmuPmMCsvl8COUXgl77S6LZHIIvPqTiDOMS4r6ExeWP88leNxbdRKaCnJEC3FOxxNs1OSAXZwTowIE96yrBarIH/hsHcDRNG1oDhTybLI3v9kWAhfs9kj+hlB6Z8o+m7ZGFQMsiTG83Qek9oeCtOhnmVVRg77UU2CK5mTjcTaEKU9zB5QOL1gW2Ahl5L1zWNhHqfLkkZt1jstrgEQkP4gmWjU7Bq/PudfQ6C/1pk3AH+u+SprgaUi5fhls5yZA4pxA/zhoir0r20W3rqvBpYyUtK6yCVdsaCKzgwqkvEzFmTAxK2ufT9y2VKP8miSacI/joiDWeMy4DW0dfnHA2Cvq+sKBzvp28zZxLDidMBysNSzL+8zNqZqGKdQNp2PGShYgVPCjf445On3jwQWQlWRDFwrjJiqgp7LPbkpvp4GIjcFeaR987C90wXZ5xtAQ81yWN/irvBTY8G6oeX4VXXwh7/GgWHj9fAat3VsGLitW0dVsC+O/xF3ohFze1zSfL4vPR165PEHJzNkxXFMP/+Pqw0Nc1/xdfJ52isH39f31dYU8wnv9VTkd2NJCTNp9oMKcYHALkwa+Nj4E1y2n1AM/ugLDf7oZHw2ObPURBWsv+96NasvjfTLj7roYodCc7LvLpoIeaWBxn/I7qfZ1HkuUtoc6QBZEeAfHyksar23i4eZY9+MhogDoriiv2sjiiW4qK85bQwNZwmHvlDz1UFUXmNd2hg04zMLItEj/VsBh5VMjfJ7Ige5YWhkxXgmj9UDCqSsBHICBStcEkZMIy8q5zDnSba2DwlWSwaztDl1nycOaGQ9TvMgtjX1SBpJCHLerOk/fPz5Of60vhoZCHt+13AUhwhTq6jJ6pDyHzP9pg7kZTvBuyhwbX9tAuqIJtIyzOn2cN6gIC1s18iN/iTlyF91r/LgptNWKx73Aebg5ZSYZqLTBJPQCp0zhktq+neR8UoKaNIxj5x8Lqe7+YhIAa+tZACSbsCuzYolYIjd1DVMI+lD6bM4Y0vlxNH0fycItTEuptLgHuxJV0xRMWou+WoZ3wnLlfxLBt4mZ6/tUzelA8geoc8IGJ8yyZCwZ8WLc2EAKXryF3V92mK0QL8FHMP8dJCkV00roxIMp7R0q7WGjxy/t/CDvveKr/9/+HIomIKCOjMpKVPc7zElkhZcvee+/jOC8tLVrapTS1zJLEeV5NGpQGpbSVJiqlIX7n/f3cPu++n9/t/fn9/vWHc5zX81zX4/68Ho8LGqnYw2zKoFTzXfp4oSRwVcTg9sbH9NO9KBRR3kp/to2FCv1cHHs/B0cz7WhT2EsqtM4eqjh8LbY3Ez8kc3HrpnH4pkAOPQs2EPa1CHC6YoKf5Bawwpu3kg/lDFT08Xt/1xAJyJ5K/33f8/G/3PccSmNAd/uf+x5FlAfFUwzqfjCHiP4J1MdaFXdJv6UqO7nooivOulIqgHU/GDCoCsQP707SbZc9wLlenHhuDUSl17uIkPFBOpOdh3nfliCqNPIKg8tIo3g2HPmdD+HXZ8BQljFPYtInkjgvHQ3bFfFJ0SPikJiHwlteNXUX7abPVmXhY980fFu3jj4VtuY1vc6GDJEElrx9PuoIskHjiAz83iSGI+lcfH3BA9dvPMWLvDdM9Awi4Lb8XiJXZYuvOBHY8ZuBxnLZxoBaS3igmo+K6Rzoc4qGZ4pJuM+DQT1+v4p8lInuHBk891sLtjW4o8mFvXR6zl1WjrY7qlQup6dMdFjarka0X+oUUVuRh7Y/XxOjATZkqXfSYJcMPP+Er/8OxqLVcDyKcu/zrrjuJ75xoTD1vCX6yAWi+b1wVnM0v28SX/iVy4Z3M21p6KgJ3n0nB0uDg/HHlRQ8ncHFQskNvCfsBDDsOkhPyIdD83FVVBA1hdDvrazyMHesKZuLvzxyMWarNe7cH0LyWvNB5IM19gktwFNGrjhteTbMemSPA9xD5NKcAdavDhGUEBCCWbekSftvSdaBIxKoKp0DI+sYTF9eQxSXniOhRB51l01Cv9fNdF+5OCaoW9D348uppfc8jF7XTrtvZOGDTmuyTCcAzKISofRMFqaAB56Wd8ULvBl46EcXTXxRTCMuCRMmWhF1lqTD2U3HyEE2n538iulXF8tzAZzEv+8Ld/+v+8IDkX/uC98WK0GYzZ/7wrvjj9Mx1yRoZU8WiFxik1+TuljVT/l1tyIdd8pWkfz1fE1jupe+Ue5oOvq9kqwdnwQDqkstFpxdTuXoSTLwUA7UTevObqgWxFXhJURunQNudE4j5WPSYN0+X7id2k/8x+jBBn0zsqZECdYNcWBMQhqWXM4B3flVdNFtvnbZI0EU9hpggfwxsvdiGh58FwW/zFJQtoULT7TqWHNVc0Bt6wVyysITTIk9JN5UAPkybSx6U0VOPxMBsC0mcD8EgvMo8Z6RR+GMAlb7JIHaiDkwDUqoHbaUTPSfBB+UsyCsu4YoSUvAypJjVDaxlj790Ubrs9rJt6ZYMj/vLmG904Vvl9Rxg38CmmxQJL9E06DuqDm6PjaFeR38vpdnSMeEZeOsZg5M4p/T52fyMLRVBG62MzhY1UViWvPAOD4EltnFQYzjd7o5PQcnCmbg/NW+kJ6aAucWqeN70XUsy0RFHLeljpqpGuHImZlw22sXKdYZJp/9LxKbPVvILtkYfLMoHt+90oUuY0fqOpQMCxgnKP+WinN0E+BV3E1aN+UsOZQ2n0SctYRP6v7gIi4AGtcmo/h4BpPaImHR2WzsMDaisRW2sHV2Pvw4LQFvfrFRtlQLnfj96dPZszQhKB7GzHKHCyfMse+3EIxrZfDJzG+sH/HhuHOWIrK2NdM7Ysn8GsCA4LV1ZE8Pwa7Tv2n5gok4xryRjOz1gWvZEdCV9Yl+tJSBf7pv/t/zifzc/7xvHvuyx8rUKw1kT+RCh4AcfEzuJbmlgtgQGIS1J5VAlt8HDoxzRWOxCEgrrqYqXcHEum4iTeH3F4sllGSUuOG4nDwouGrOe3B+Gnw/yYXytVYYV+MPIYY5cOq2H/TYxOIqJwdw8Z6OEed14dnHCNQ/8I0uuH+eqpSl4IZaBuKWp6PfHU3Q4yjDUudFINXni/ZFcTBR/xH1y+Zr5PFqUNDpidfiM+EhywRuLJuAPrkXiLYDh9b1c8GmZiLohGhDzYlD1CajjbyW8oZ7PYOkfq8IVT/mBVYP3pPi8Cy8uiwHbnx4QtrtGTg1eImOrdYCO8NcyO3XBAtncxiTmYNae9kYquyBNXNn0sv56hBwMgyKc9aQ6W0MbtI2BXVZLs4wvkurkprJwo/50LszCZpZUXjigCUIqVvgTxERy90yi+DlIkuk+ZnY03zMyn5gBUtcMQUDrxaSAuvFpJQnj58mtdCszl4yYhcK1e+X0KJPLEg0ZZHVUWwwHnTBtV+5GMrOBK+NgcjplKQJdxxx18uVVCwoAHYJ58Oc1ddIjdwQnaq6EE4dYsA0egH8tE2EdWG36OwmJ5i7zQcvZaij8EJ//PSrkpJecax8YILVS2wwZUUlHR65Qd8842LyNxOo3bCf2rFdQEx2MYhXuqFNsj2Kp5+gZ5NFyIE0BZaGmAiazF5LTaNyEPUoXSw2G5P2tZJI1RmI5orQfMcTL2/wgb/mFR+s//u8Iqb4P+cVz6+vJ3Of87932+dBc5gfEI9A/HhaAEJLAPO9POHjfT+M0PWAAFkPPLezlgzI6mF5yDrqdnU1TTh9i0xNTocdwU1kcUw36db2h/Od4Vhck4bTxllBTQIDd/l698G+KzxD7wTMm6oMteWhtPycO6Z5muFOjTs0J2YBdIemkC2CHGzeGAFuIi28gksfeVvCzfClK8CZcg9cKRGPb5bm4RdXLzi9fTeZbOYKF8ZkgOtOPnOGcniLyptZuyY2snZMiQL2oCOIDHAh76QRPF1nScILmkmp5wLU5POWw5EsKN5KYF1OKdHKnE+TzTroEvUAFD6hy7tYmgyFHxNBfHwa3CzbRgVf6qNUwju6NMKI1K2bTqYyubj8AgMZPWyY/p1BxeV2aHP0Pom/H40h4rPp/q5J+HN2MgT6pJGWN8FYP1YH4hRjcFOfLh7cmQIWUTHYciUZbJXS4M7Sk+SzYyK8sX9AN2Sp0aHiXXSWpDquDEwBsY36SDdowBcZLezZmotJWTlgcckaTCJcscDFEa4cV4cVwY6w0O8Dz3t5N9muaQEzpB7TwGfRUOiXBWaheXwukwctqVlwzUUNpwun4T6OH+3Zp4WjU8XhvqsxgRgfVFq+GI1UH9OW0Vz4tl8cdXfNIQ/nZUOlJqUNQdF4d1gFG1d0N2lVK0HFokzMh9/0yKTVRHCkmlYrZdHCWZHwgz4mgZMoaR3SwpfI10fTc+CveZe/8X+fd6l9sUNpqT/zrrFKk7Bw2AafXaknL3Q54M+vRXt6GdDZOxVr+cx1pbCfNrxeQ06+ussL1zlN4RUHKy60UUmbAITNL0j2KIOahr74ZsJCOIQc8PIZJR9XmtJGTxkQ3ZYItluCgaOXCw0id1lTHiqjpFESwgUrHDrgD4p6L+nqO5J41iAWe+SdcKs8Azk57+mhMYl4cngSai0QAOtkBtfVacNi1h4y0uuFdu+VkO0aAs/V9VlPbrwk2qbniNvBTPz6vJn2BjLg+4qLmwsngX6JGDyeOo7ukXaFbVuPUN0vw0ThTRb67BxPb0vMJdyqSNT/1Uvb1uWCUQUH2Uc4ULKVz4anlbDriQFMVRRBvafGaJJYQ4/ymUaA0YLz9QzeEWsmF0c4eP/WIlDZ/oawNlvCQ36PWOfUxfPnc+mGwaPE4aYvLt0iCFtUufjwVgh9Za+Co06uUJaUjPjRBbKVPpLQ0a+swblv6WZnefrKmItonYTCTxhwGsmBNXI19NiYYfrprD+eEDGFvTUXyU2PGPhV6oA7o7lQX6IPX5sSsdP/I1H4GoW/VmTjlIxp0LaUCx+3msFup3ywavhMdHTO0eLPbFzlwwXrzFyw4vfMy3djoSA4Bpua5sPdbx9p8dxAXDUaj2pvFeGIEhd6lOwpPuPw9Hze0qNWm8jKqUnwY4EMuH1h4GdyIb3dLQMPQjp5uuZO5NSmPrKpej1v2lN/vF+WgK8r8uGf5qWian/mpbLz6lkLDv+Zl95Tc4J5GQ54ouMXT9CXDawrbig4+I58X9BJK+4xUNBgxDLdw+C7kAQwMDtDSs88pAeVhFGDYwpjF34hrZIZMDpTD7cuyoeY3bmgk+THCgyxh6EDrYSbPB9GygXAt1UEdfU8QH63Crxe6oEChjNQU3kded/7hZ5/v55ciZPGWTva6NUtGbA2IwgkjR3wxbOLdIH/ErL0WQQeu/mdzH2nD1OCvOFCHF9nLUyG9tW5yBH6ToxkZqB2/QAVu+yMd4vS4fuNMPDd/YZqijcQHZ9uWprEhZ18rZCkLwLjtbJh40gEHtaVY8WoacGyPd/pshdc1I2soi+TnOFibhoc+nmaFaGcDUmqVfRjVQTWTrVHkZNfSaB+FRE9zkD/cjEsNMvEta3VJN74BzlpPET7XL1oWDfytXI8dPgtwKH2aJaj+mFiY5qHsgJ7yG83UXRcIcXXdMnQ2GvOq9ojA75L8tH05g/aX99Lbrht4B1w5OKquXGw3zEVG+Xy0WbKbnrHRRMhspB4L2Vwt9QK3nVZB5AY7CPmSVOAfWUaKL34TVTf54Pp1fWshSkK5Hh1KKwujKVxG08RyaGNZOh5PE/wCsG7rx/SXzPW0G9sMXA96QcT+/jPPr6DCLJH6L0mLcgW8yJhVbGwVYKv2ebOhfHyvqBxpZCeGSMF3QMcKMUp0JP4nrp4riBGfQwMTxFhfU1i8JrUKGkfy6Hz7KqIYqMa2ocFwV/zdpHbc7FS8p/n7U/OBMNrzp95u+OuFNTdOUDuJEjjejVP2PmpgRQnSeDYjAiaF5UC3kQMKkKQDOk44Knr9dTllQCM3FTD+IvfyfmHApBT6oabS2Xh5LMUGDF0h6ZcWZRhEkG12Y3UC46HpU+GzeCEKq4aCcc1/qooIbbXYmKRMKoPM6isLogF+MJq91Q9eHbkMeu+5SLYWMLgI8+5pPyHBe3i3eDtdBbD8Bd5eFUkGwO/eaNQoACmr84Fz0sDZFBzLFza6gO7Yh3gVxMHO/3mg170L7r6MYMbVdTJnnn6KHAuFjoNDtKgjQxKxGbj09hg1Aq6yfNYKA6P0seiemQ8RHb6Q8Cqe+T3DBtw5vSTW17pJFxHii6SzMeEPg6GjNcBGYN8PLyXwdpPDM65EYOTBRisoWOho24++fQ9Ht7HzIfRfiPYPZWBV21RkBbxnVV9JBizzxHo4bOWkEAKWl9LQvdRP7z6QA9eKbSRxzxhlJhihIcKsiEnxB2/FswFt3cKcPu9Ak6wvcmLOHeVon0h1UwvJCf2+eL+KHtMFBugN171kd4wJTAfLwKjDXn40pAL+64doy2i+iBqUUG/blDHr/KHWB7TlhEDZb7ebzXFQWEz1DkxAZeL6rN6OjloxXBwpTtf0027SU0/MbAuhIGSNFUoWZaLuutC4N3+dN4NTWc4rWYPyq0ZGMNMwpZTy6j76AQIkdMlGhZHyaCrKMy9IwJ9VRzw7HWi/S3T4d9+jdnSyv/h14h4/S+/Bq2PANa1P36N4+0MOBdkwI+9EbhBYRKEjnEj38UZFDuypaku1AkSuyJgZ/88WLjXjI7ebSDmmQysibVCNjsHlz8Xhfo3cmC4LhpmHJ4FL2wmwtS+JFAw/s4K9JzVOP5bDgncJYAlv5Lh9sFbVqEKatDykF8Hq27SMu9lRK1LDFe/OUbD+KxAXSXhVrgo2KUsBDe7NHhkRq1UxW9S5TPGPCFfOaL2Np7IxvsA3HtEno9jwHu/BYSpTYR5WXwWjbMGlRlpeMT6E9Ety+eFPRBEvQZDnsPEUAw6JgM20c4493gXkSzLhgqvWt7PWRl08Hsabo6fiOmh8/CYgAZMs2J4tw3fk5BNwXgg0AoVDVawhPdw0GN6Ei71iUMlTXm4u57PvwEyOPmLD/ZFGZKa0xOh+VA/q9t7AnRsZ0Om5FVysCkUz6Ut4n8PKnk2M3UxoZ8N0YH1vNm3rGEgPAnmlcRiI3X4n/+xM1ougZcnh8O94B4SfuktnXfhAvF7d4BX/62VTN31gOV16DRv77AJHtT2hKeLg2HOEzsQzuUiL6qSKAgl4J4BCchMCwV0OkDGHrRCk9hGotroDLUn2FjtLkDzMieAzuYQ8qwnHFiQAe9RFBa5uGLpFwb7HlxjaZlPwkVj7rG6pCxI5VNfcL68jQSuSIEP6xZYTnOXJRiaCo/X1pCyQ7+tQq9yYez2AqKh10JvF+8mq2M9YJmmFIgfssWd5/RYxUtKSJXZRvJvv4+a+T/7ferVhcnZ23/8PpyiLJyg0sybUcogc2QNGW03xFndt6jrFGuStUsCXFIYCD4WgHJu6eAZ2UgvJSlBb4o7tp6rtFp2VgoyvQ7SNuUUeqTZl8y54kAa8thYVOiGUd2e9M7UYt7GomDsF3tpNYMXCAMX1PD3j0TQW1JEzw4EQsdBacytkYdRaKLjXoaBv68FuXamjooms0C9KJYsz52NHXH9ZOfycJBsJrBdwBd3/ubAtpMM2PT68bWgNNw6ysCRS8NUO0sXK8Y+Yq1wnQob8wTg6mQuHb2Qh/5OdaygE54Qvl0OIl5no5GPPbQTHv3EieQdniuDe7rHgtMcRdwVI4x32pF1KJkBgZY03Gi/huQFheOeJiWcyqvm/Zxyj9zi98Fc9VoiUGWDt2ZEw7NpsnAq9R4tfmeCUXWZYLLlEGskdYhMPPieV5FRQvXk7lM7sTQoEIjDktRgsvx3IhY+nUszH/qD4OOFWB8ii7+9ODhObiH6w0qr/eXbLBflfaGf3nKhdHMpMamIJPvScvDtJ2ccLFtiFTLHDqvmqqOqgRd5wxpPOeWyYLvDAD+wQvH3tVx0iZzUdN6ewfuq2+mZ7eU0yNEeRF7qQp24DHGonU+nThDFprepOJWvN2UXJYNenyyitzvZOpqMngUjZPykCt5xRXlc3RKMoptrSPZ7BuzVjejMmmSQK2mhvu/zaYVQHgYrPiaXt9rg1fapEM1LAKfJbPSfaAHiTld5f/nFLl/8T79Y9Yo/frG+EYK5n//4xZqq+Uxpy+DwmyDsy33J6xZgw5Vj8jDH+QoJOL6O7Kz2gGALN5qjboGyiTzScHkPlWVH4MSyTBAOnAYxE6OtenIZHJ1URJ8VEloVZwxCnGhkxtyioZuQzuFr82XFGfiOLMEFO2fAKxJMJOolqN9MYwyzVECyfzG2nWLjej4nvx9wpvVPfOgJ3URacOQB8eZr47JbfiCYYIZy01Lxc4YvLDuoC1G5eXB09Sb6vkABSiOS0U40Ffd6LKenatfRH43JGP+WAWr0gj68lI9vNB9R/QcryL6yfNyozdf2IZkQe3eY7stmQ1izAhasSKLj7KPQQT4Un9pspppyrlhYYka1j78n1bcNMbRShLpvzgHGn8Hn/LNpU5QD6iNa6FeqjillSjDBYw6cTvFE/zcmENy6gxa93siS7U3AhpFBKj/nC+9ygTo0V+vADaMsvF6RCFoOfN08HA9RE+upzyslfJaeD1OzCNprCcHGcB5tOBYCmXQSLIMR8uCqEDzmZuK0vftp2cowPH5gPmyXWUVyJ5ewwm7lg/bs6WDDPCejl2ai3MUcfHH2Em05GgJFFQZgVCuEZUeSwCY3E8Y35eEkKzvs8RgPHttX0lsHs3G8nBw+GlXGIvvf9OpDEfzs0niOCvbSa50bWeqffdHIZDw0+2bCxVR1TAtQQ5f1ynimKpeM45lglpU4HMp+RJXmq0COHoO4tZ0uOMPnANmP9N9+ww/lf/yGRR1//IZHAnJw+64/fkOvMBXsDS+mC1YLYrLKF+rAf34WHvmwS1wKVh5U5O1xeUQeinAh13g5Nf9N6aYdmcjRSUODY8JY7KcG67duJUlPJejvXm20dpqAI50+NH1dAS84fAo6r56JYx76oUCuLKSIe/GeJ8/EC4Z6+EFVoHHL5hGqVaqGd5dmgLqpccP98yHgpy6OMgnGkHQ6C88fnQaDJmxs4p/LisguuuekEjkh6ocP5oXRswVpsOgjA7XP1GCGmSPUrZQgRbwVROD3JEzmf3904t4T1YdKmPA0CvMPe+CldH5t16sjCiE5GOuym/xsU8CRrnc0rWImrsrZxQswKyYb7zCoVaqMXjk15LixPUT8VgU561dUIbeZ9ElOwPHXRYGEMjg7j8BIvgTWSUti5MBM6OkPxKCEdrK2fC0ptFNDMl0eU4Md8KWEOgYP2JNJGUZE12SQJJkYQkxgIm7rZkObdTZYVXNhvuEsnLw4Cre0iMBOQR+QvqWPu/j1q23mG6otMRsUzeZBxiddXMbXUUO9ngD9tig7Ow1qRFbxtqpeoZJZs8A0Q4Yeq50JS9eZY5OtId7P/kKH9bOhe0gHYRcDodk5sDjoJ9m3NhEEo0Xw/FV1OBqcgL4CKtjI0sCqzVbwNYqAw6aXVHanNaZnDpG4XeGsrNsGYHPKHTrGpuEZORE8LseBxPgsGvrWmNRZacHzD3nQx2hAlEkuTIrwhLNluRAtY4Z/+VVP/4NfVf7pv/yqXaeqqdSWkb/9qtpZ9UTZKgp2ue3mKY63w4/OU+GBxGnS5GQOcb3VtO/Ead5U+pTOasqBWKELZPJDOyx2zMPrP3RRdqUW7BadjKNb0+HBpnS81LUI887V0ZtffWB7rxWOn+GHF4r7qW2pNEwbFw23ZGbBtKccvFE/geSVZ8OeZi18yOeQ37ocmKwpDd+y/VGVw8bdYwxxSpI1ZFdyUW9bMIQkcyE7yhkO5flBoU0SqvRFALs7nYqPnw1jXopjag4DhydrQcE5T9x+yxCTTCRwzgtb5HmY428+p5X7XiaOuhrwrSQHAiZMxwyWCSq+tsKwZm1c0XqexNkxML4oDI/xe8C5syFk345EaLZ2gN1NUylIuUJtmTJmsYeIsSv/HBiL0vaPT+iW16lgKjIHpBexQVWODeoOVvTkzWgUW5kCnQsNyMo1C/D+9x+kLbKGPvHmYPy8bUR7MAE+XbxCZGS5YE3U6XeHOeg50xyyRVJAo5yBSv0w7G9wBrdHDJ5ocIX+mniwfxpC1e2P0uQN51hOEbYg0ekCTmQxHap2wrUCulCxQJNWOYZC9eJqOrQvDuV/HuDJB2SQUx2LMDGXz/ETTCFJThuEOz7wCnKFQS+cC55BAThSGwcn5+liQ1YMZsxcCNc+eOKjulTczAvF8uoouMvX6QahCuD5hY0DwblwxekARZ00cv9lLCxeYgNFNlW8c0t1YWmEP0b+NIHxe+0h1T7xH/3O/ztfMTgs8x9+5xmtETxLoggdTd1k9rN0PFCQzzseoI79e/wgLHk6Ou6UwYAh/ll+p40emy/SV3lnWS8OM3hRLgxeZ/J5y80ZxN4Pk/D5+fhsci6mpC6ELSdCIMs/AIuSc9CXbcQyLM7EBHBDx4ZZqJH8kG4MXkdsnok0kTp9lP2RBipxTaR4ewPr++NMDPuYBLydJeS2QDCc1IzDqBMC0PUzgL44EYslEktwf7wHvnf2gG9SIbhsUB+cBteR59kp5GRwGgRvZOOh7wtJkzSD9RtCsdaAX89fROArE2uYqsvGuZ05KB3DoNJ7F5RS8+Uz0VwUnD8To+81szaBLL6/nY+FQqZo6vqTJS4UAWuFNIlOEheEX6RAxAiDn60YNJWMxBmJe+myiGJW2ou5kNfERWaRLu+KoTq28zW0Gp8fs/TSwC44CskPSzCQtUf7QVWi3cPGjjUJKHFYDId/MHDgewz+VjHEwoJBSs+a090y8bBrLxcXm1TQyMlCaHYgBoLE+DXjx24i0OaEM5TicR5fd0KwFe/ghhNU57kL+N0wgmXZ70jL9AP0ursOsXjN4LP3CvBSzwqP3IhDH0kfeJpQSFaFxaOf12L8cmEcalb2kZQqR+yLm0asRY3RQpELDqY9dOhRPs6oYzCFz+89OxfBra4E8C47Rg61EHzFa+XNXa+Lb3qTKLPdD06o+YC4WgpKORiD3RZN6PwoiRlJ9eTTKPOPfnlS88cvb8/vWxclc/72y58bLaBjP8zB7oVOYLiAwbK4XzSogY2zT6TAkqee+KkmFtdLR8JV6WP03bbLROp6MLxTYHCaVzjZqmUKyiqpaOhvjb2/HfGwYj5WlDD4RmSVpV9CGh7ia6ZxlmKwnqfFKjg6lr4vVgdvEUGY3tlK2Ouy8HiUKfrWLcSvY73x06MOWlHNxRk5sjhtARsapTJA5hjB50dY5pcfLUGpb6Ws2cwS2CT50UJwKgukxOfCZ2Fz4EpMYK1ZGo2fr+dD3mgeOnxtICt92un+0kxcTE3pu4ViePxnIJ4J3UDs5HLhrYUAGrmGYPa9nSTI1hzvxfpCSW4GinXc5y1+lI2hn9gwHOSG2ySyoXrGB7K3MRa2zSm2qov4RG6qM7DZOxgnSs2l1QNsrGPM4cLVOHA+ycGFjvHofZqNPfpzIWEPC1IjcjBsRQy0f+RgmF0YbEibCXKmCbD8M4MPRjggJjUTuuUtse8wIY18jbx94TT4KJGFFutjqes8AdDemwuhQblwr56A4nRvqDB0AokZNiAsswSzFD9S5/k8strbEiboGONGEy5mHpMFZz5HLHBKxfDuNMzjsXFqsSooQyEd8trN25EeCSITxoHp4gGqskUfQ825qLTZEvtT8jCxMBu/rk/AJetjcJ/vAmSvfctbcCEP1mznWdWr76Hre2PwSHc2wNE48m6TDmt+cAwobZKG0jZLeDn/Bm0QSEDd4zze8tX/ylvU/D/yFmpP/jNvwVYRhdHn/ujpkw9nSiPxZWsYdBzYTmMy88E4IQtvpHCxL16U6NTwuXNmM/11OALiD+diPjJ4p5QF+U8YuJntgM8LKH0ZEoAb04PgzJFg1FbnYrJ5Ipbfj4DkpiSQepQELaEaeDWbAU0lYdhzSwjWP2DhMP9zU9XSw6UD9jh7eRiWLvRj8b5qgHgjFxZnjMGv2wQw0mQ+bH3mDc6S/vhlwANSDGfAPW1BqI4BmHP7JI1KlsY0d2fg3vHFyGY3/JzOZ2pDBnX7xhOHr8a886fmU5MTxnDEMRd3KXSTCwuT0ecug/QAoPPDbLQ5/pb6OtSQgy9c8ZR8El8/R+OXjQyIFJvwWVsGXtJ03N46ET7IMHhVvoZnYaLME1R8Sc50AxwJtMbI1+XUyJaLqxW9cIasPX2vkUDrbzP4jXChRTkJhSwt0O9wI6l9GY8fLGVwcWAmkEPSsDZ6Hl5//Yu1tN6OZs+3IJPOO6HLdjPQ5HPn0W8MrDX3h0N9bNAuYkhapzaJT5iH0stj8aOhDO7jn7nbZfNA42o49E8L5nEuxsJWCxbe/Z0DM0rNaSG/FtHgs+ThDxPkvsuGwjoFXr/8WPTI5+Lazin46uZ5IvBwHpqd4evuZZUsx2pNXLtXD+8NHCXZbQno9oGBCXIl9OKyEJK3Ww9iV+bQ1yHR9E24K666mgF3dqgBO9gWsupbWMGOdkSuRBo/rTf7O69j9H/ldUJX/iuv8ym7hQ4qjDT9O6/zff987KkuJz9+JcLqrbNR6xUXA7gG8CWVAZMbDCYr88+uZR7KNilikcZVIrJ2HDE+noxt5fz60W4LP1Sy8HH2W7I2fB9rzH4F/Jbxnn6b4ocVExhMarfFDSn8cx86E0s/2uCOsZp8nakEp+LV4IXSXKgcPx+bo8uota80xF3mYETiLLywZJSIyVfTxFVcOFXiTQ+uDcawwbMk6GUIMow/xL1PBP8fwqC9MgtVVtvBRn6/yZXRg8ojL8iBWbKYXsvAx4B8uPImHzdPeUK6wvuJf00+CBpmok9uLpQJ5cOe+FwUuxJHfIQckJQwIB7lAAOF8+H+GGH02C8D3CPLeB06+UCnR8L9mDpW64GpcF6WwYphLuik2EO5303WMTUADwsHsF0rBNy7/pDR6I1P1kui3xxKgw7w35sPB3VE8uAsKmFg4T06IsvFjVnxUJxSS2JvxEDFtRQ0T3XAarhP8Vs8fFM9TrtVx+D20DyUePGe0Ig8ENmsAesnpgHLlt/XHNJw+Ik19NrOpUEP9NBbYhoROeoACds5aMBkw4Y2BrIGHtPS9fr4ZoI97PVho31KAOtEag7C/ImYMy0AzbWFoExGHW+87qMRXi5AnzjgE7Y+mhlyoUbPiY7xGaRiwWnk8hoHdNDNhdhWKZz5TBdmbu8n83ZY4WLlcNLWnYptnRz42BeDgVMcIT/El3h/YoCrF4OP196mf+W94k/+97xXolgezpyi+3fe66h7InRWc+B9oTb/s6yg1l8cYZlNNuCaLPgumYK3DjXTjaKjZLvWTTp46hq1mAa4/FoIVsv7QPX3BTCn0xO+8xk2NkMS9NaupqJyLbyb4cGs0UmZ2KovgndqPUDt5lhcc9Gdfj6iiZ8mlVKNWCuQXRaCX3WccYrDWKvS4mg4tyEDg/eW0fL74ZghlI8L2OmwOfkuy4bfm2paGDy1Se/cbYkAPHwoiMysvkl+Zu5unBHuDOLL3TFfxhveSS/lrTyQRa3dA9Gi+DR5fi0Vm/n6sKjND5LSeqjmm8Ukuz4XSwt1oWMcg67HQ8FshzMuu5+EzxOt4dmzHLAonU2tphrzCm/7wOeRZLAwzodLqo/o5ywp6FzdRua85aKNVya0zMgg4i1eaNLmDCnLYujgOA2QSHlLjwsFUD25xdClb4gaCkn4SCcYJlIRKnGGC0XXZMBrfDzWbc0BQTNf0PxdxVMSdMGELB0o858AHbamqLhzErWyTQS7EXF8pJAAJz83kyvL1UHR+RrpCrCCi+rW2BRmCmZJDmjUFwvCuxbCVMt8GIrtIdeafWHxmRW8Y3os0lCTCMcE4/DO0HI6zGSi9Uol3PbBH36nCuLNkFUk1reKJnYvwINtC/HS2Sck3/MA+f35ANn89Q0vSMMC783XJzduRYMgOvP1UCbIt3AwUjsNlncGU1HnlXR4gIGf5WpUS8ANlAdHqbKmBeF+4DOBsSLvr7zg2C4G7Vf+c17QZ78JXtqx6++8YIsrg8pT9FlbJ/mS7ZHyrC/lh6hnWCZ5rhwOa686YvgiEdKjqgVJCgzY1N0gkzxycPL0hXhtGQvfR7vBAR1l/PL5Cs9SKxjuSrxhnctXhiUlUVCyPwTdjTPpeY9HZMFgLui6ZMKdHC04fTATt2m4Q5JpBtXc7YY7Vu8gV0VTaI+HLhbfXM66cpIL97wVcf+EIHh++xaR6fbD9S0byZiRENjHCYD1i/QwanEx+aWVDe1K7kC/ZkC2cRP1DB+lquqTSFRCL/HQUseIQH/sWBoKR/oZyJEywlx1W0jkv4/vj8Lop1oG9QP1Ed3ccZTPuW0Nw5R1NBNgsz95cUAA394IAfltvhCfzoH5Bz1Q3CULdd8owt2bXFAQ2E1a3ZpIiONeunWpO5S3HOG1aihiSkts09nZs8m0VgYTppYQiXovXJOkA3dSm4mRcTR1nPGM7OAz58UQApGCtjDzUh64n3LHq6eCcNtrPo9/y8ZbBllkTctkTJzG4Kbz7tDWf5K1d7kRGZ9LYLQrl8jvADh1jw2zH/rj4TMqONTjDssPRuJR6YVoN2eAWEWXkV3BF8m8yGHWt1o3jLQTQ+9BXRS8mAoiBqqsiLMC8NIkDXwNvegNlyt0uawHtvlIUEvLULy43QjM3FeTS615UDwtDX5udAHeqVT0ysmBNvE5xJozm04u8SdnX6uDljEXdtZyYcjBA8bxNVjp72TsvcJnQ48K8k9504AZf/KmvStMyBKFx3/nTbsf5oC0FBujPuTSF163yejsPCoraIXB4+zheKs49h9TgqDEBCTLJ2KIdRutiK2mTz6b4LiMXtr70gtKSk4Te35/EPjNQJTKOQKX0nDMBT4/7bXBlltbWKlFp3kxO5/zepYbY6/TbPx93g7v5UaAby+/Jkt7Y4qrNM5jM3gjeDNVa3pMHvJrv8roDbJhFkCngTbwtvRTA4PFwLrojSoTArHvAQddGiTIi8lj8JGtN5Q85XMpX/dz7QOxYfwYqFXuYn1vZiPdYokxfXmguCMYplhFQWJjEh5QY+PQoiRcmpuHs1+Ow6p0bwxliaKuWCqmLLCC3ixJePg4AtqvHbVc0m8K3h/CoCGJwbpD7izxgxw83cDFFK3pmHhnmAyWMRB/0BvEhG8Qi4owPNL8nci/GQcFcwLghSUbrhob4R6vDvJcwhXMMvxwGmXQOCAOt9wRpSUjWvDhzRNq88MXuXF58DjdE9/WZqGShCyYP5Fkdd5pp37x+rAqNZaGB9XQO67BuPRnNi54QlCMfYVM2xGP69ZNwYMafmBqZwzPcwlqx/RZWfrMgpZwdwwydcOGtXOwuMcea/pCqPCsV5Y5uyJxZssY9KMuYC+znef1MR8+ePlhuDKhXldnYvDvLPi8RgHGRifgpWss0N6uCXP/yiOKm0LS+FB4E+uE3Y6+fHHBBfvL2bCanqUKK/xAvpH/Nz0VwT4DUXzOzP6fvLKAyH/PK7/6Egtatyf+nVdW5PiiXFk7XfrcAG+zzeD0KUuo+5SGszgq8KptDd07JQnevi3k5Y1+oiPAr0dKU8BsjBK2pLFB55kf9BpM5F1PKqZD4REQt0YPyOtseJDayytwV0SJgAtEapsMSrREo/mEjXTTu9lQfaaBVSOUBpf8N1FJiSU4a/9C6P4qQUu14omwBguu+JeRkMdcqJObCisrA0Hf0oh1UMENx5zaxyvMCoQKjYc0pKvAsv2aIbq7BECpdioEvl3VsDaWzxuVTvBsnipem7GDWrZK48UD+XCzUgC2io2wLG8IEfVFbPDx8sAbx9Jw7vNM6MwJwJ/zlhL7ZAmif3AnSZ3oBXbJXLy3MhgfpfM1tKohTOyJwE/NiSCcdJTMK+VCSCSDL6MVwKjLj1ofDoACtibm6hSTBT7+EHLbFeYGv6QnNDUhb7wYns9/z6KJXOh8YQdqqmPB8H0sdKlGYhZXAhaPzMNNtwMwWIWLwmHt5M19ghKXfOHaViGYQuThUbEwXN0zFqul7THzsAUNidSF+IUEZjeL4VHrbLzskAxXfwdA0Ol3tOWiCHxR+EwPZIZB6TMGiuuiMWesIOJoEo7/vJtEv01DN8nFUHJ8lPA6CW+O0CqW7HMR+korEAPnDxKPyplQQvjfVfcAaHsnCfKq8XTfs5+kQioTmYxC8mySFS75GQf7hDyJVUwfeRacAGcaXbFgcTBIN3BBO5zA9n5D3pONkfBX3v0un11ebf3nvPuJZX7gaPUn724TNBNeh3jCk4mpWJmfCC/mcvDUEQt6Qv8N0Z46hyXzmQOdkA0XRxhou9ZOF/N/t/jWETp9TAav60gwJNSuo4fNo+BX2EQwWRYDuaDDr6sJ+LPmCd1pNws8q2MxZ6kBHLFmw8W12kgeB0OqYx7cS/DDwNvB6NKXhvO2pIOkygzQ+5oMX78vJUq6kqCq1kHa49hQZyoGFnwer74VC8aCQRikJcj/fCdQh4gguPorGDrCnVFIlwPNMlH4VjgL+nLSocrVAE6l36DSGVGw9AnHfCK/PmqEuIH6WjaKiE23DO2LwHvlNVbT1EPQ9lgYSzqHgc1+AfS1dRqkK9pDvbkfDlZeIFNWviQW7mK4L1EdPaXnwPJCLrZOFUAJlglkXAxDNAmDR+7+cN+Or79VGJh4LxvGzHzPs9hdRYfu1VOFxgXUghcLzgOnyOkDAiCfG4vWtsHY0feEFOV5wtHgMJRYGQPeU9nw4XEZDfyZB5lPI3FDVD5491UQRT6fGn7kIG9xPsDAASJawkLXLwwK8lnm+56LrMzVYcC7rUmSRVNgA2cR8iq44C3ojwUXVvC0Ph6km8oZnFtHoK1SHpw29rHuGI3QOs0lOH+xKyk7Go9aNWFYOdEcQ5xDYe+SebBgKp/NbLqaWjZmoeDDnWShujsypoV006NbVLLjDE/s7CL6a9sAzXkWABV6jnChKwwIn3NuaSyEy6tukvGODPzTvoTVYn/2JbjKa7M+1877e1/CgNgF6j9iAmKf8/DHKP9sFcbDtGd2oLdsJdlWX0vqNOxBbqkEXpk6RE+33CH9TROgc+J56j4mHQ/NDoelzeJQb6MIGc1DLHqKgaU+2uRarQOO3ZBFGw/7gWUqF6063pFxYzWQPXsOGG6JgP7tDKT0WUK9WiQuHWagZ3CYHKPJMJ7fkx2sWBil30t1+0PJuy1ceNERAfb8nysZOOPDoiBYzGZgTeoamu6UTyqcIuHsrMlwNKOJKBdNwt37uTDsno1DHnweNHGlx4+aQXGJAFo2CoD50ygcvsgGeRFPYLl4oKJNDq5JiURiUk9fF6cQOV0VjNzHQMjEPVTW1Z+VFG1GUt/KkCQjE57PHkH8WJmD36K5MBDWTZrfucBjAx1YsD0SNujmoqmLECk9Gw01HflwTXAGdHzn4NOSGTTg6C/aViUDm58xaPiIwQiVWHj6tIpnkrSXrFl2hAzwIvHeMSv88VWHp5okQRZvY8OJOVzYIFZH9u+VIPKTE9BCNQmVWW/pC3UOfhVgwWkFGRz5waDccg5yeiLh3AdpskhGnkpd46L460xIcn1KbCp9gGP4hXfQPBu7P7L51WMSuJxIxAA6TJ6kxeLm9SeIysvpIHglGqfbRMF4mQP06MAVUjItl//8x2FOmCFOmHSNZbZHHuWGTNB7iAsfhL2wJ96V7FzKxetVjqCw+xpRfBcNn8cEQSbtoEk2C2H4TTr8e99G6uB/7tuQyfzXvo28PFcMUPuzb2P3mt0s7beSqMpzRe/FiaAlNgmFryeCyuJYFtdAGap21pBWrgL+kp6AZZfv0TPnW+jg5Bw6QVICRgxjoFXNEed+UMeC9S00DlPB89x10jNXj7Tsk+HdOp4DraquaKGfj/e211EZTR30/DiWfOOzTIxmP2nxjcGv6zKwodAOF23nQMmJOPhSwoHyTgaX5zBWZpdj4U4KA4k2tXTymulUbCgIE/SVYEpHK40cZjAjPwZE7m+lXQLeuHpwArUjYXByszY0hGQRC8/DlifzP5Ezq4OxRkEDeJvdWAJ9bLzdnQjhU1nYKDtMNO3j8LCDCkS1xuHnsCQcfhgNt/wK6fTMKLjz/E5Ty11t+jign5ivZEPTumVklxsX94vmQ79LCNx4KgIhsXGgP5ffa7Iz8DRXGfq+BcE5rQy0K1LEc9qE9abMCj1W17FWVm4lWza+IuNbYlB1jT04SM8lglwG84vicMKy30SkIx/lDJXg8HVPkD4xH7OGZuGDmDW8+SoMdlxUQg5rLm7QV4aAFCvcc66W9I9wMF/qPuFWxcHVec/pHv4z35TMZ4BCf9g+4zdrxscQeFp0ilW7xhvvi6fBD6FsFB9rB0YOv6jtUQOaqfCROIhNg4g7cTgxOR+V1fNxOm+Et+7dIsjojgLJ7ZvJDN5j3pi2clqfkoNF5al0b3wYzNJypEXfZuGaLfrwyzsfdb/GgV/CHHyyKhtzpnHgkLwd/LWv5cX0/9zXQk782ddy/1Yqqn+J/ntfi+doIrFti8H1H1ppiaYZJFzeaKVQPMKa8GYqTYnxAk0RI7qrLA3MvBjwdukk6tJ5qBP0hJdSqwEqLQmQ65sMr3v8sXLYGY1PWoPTgCdu5uu7MVd0yFEnBuRONtPnZSaEczIf1jfrwDJDwKbHuaAxmItebxLwxenFcDXSBb/8cIMMX0MIKovDnSuEsGGvDS0RIiBW5wELVtvjXb7O2xsSDLP8S+kSm3G4+0kRSRJNhPr8oCYbHT0cXnOeVA3PBN0oNfr8jiVOC/chcgOGxI//bDakJqJ4wBLwmpoHwROCyPXvM/B4lwkkzE7E56VemPd+Gm4VZ3B3tC5IXCAg/mYRPJiUQNMefqS5N/zg48dvvOowC1hjwoVTsbqg8SUJJhzMwIXOiZBwPQG9T7+lAiPTyKNTynDlXBa8L7xBdjsVkKXvY3Cf/COqfcANfBdNBlvTGPjmlAmOoMxa8tiRTElIxPPd7vTR9zP0aVYUCEyRhK6G7aT3HhvHrFxLv83KxTNtItQ+Mhraf7QQg1WWMGNfDuQFZuLZZbYgJJQMuxgDaGyzo81pavjCVR4Sz8/Ek+uuWxrVbCHtrSp4pVAduCvzWbGr7cnl/u/EhWsK2YYywE1JQhe1ZDzvzwGdTVW8VdPFsTJ0Agz15KOwLhumte2lR47tbZJfyn8tOV80qJrftPbRfCJYsYqMdk4BY5PfrNx5yTBcdY+YeFWyXp9PgUKPL+Tf+37Gtf/zvp/GkRU08GrG3/t+fHAszpprR/ZteMaS3zoWDqWLQO0kNxS5WkvvVGbBhXh1fNUd0RhbfZem371PD680wfuiI1YdU6KwgOH3TL7GaKs8Q+xj83C9SS+x/snA6a8sSDrZRL4/SoAGaxFynBcMexbJYXGELqY65WDM8WTYV63L8tubjLRVHq8lviW93XfJ6r5LpFxTHD5fHwNqnGYqYDVKepY8Ird++kLeOoIl14LxTmUm2sSr4e9Kf5DjJcNGmYOk9cY0sE8EkC7cRw4+v0m/CqejlEk7Gb04kWo88cPdX8N4rP2BODQvD70T9fD+D2EsMA/EyRapaLs8C3wl9UBc2x3vWNaTz/0cnBj7ivfhdhc1sjDFDf0MvKuIQp5WOqqKc1E6K5zmxefCuOThpoU+qaByzhbeuLqAu9tuupy9mSS/0MK2Ti4ERrwkCf5cFNvvgdK7FmFEIl+HdEfj1bpdZJbXTV4mLwKSMlLxXf8d3vZ2f/jix4DaJT/yvX4tqT20g6669YF6NyzBNXHNZHHDG9bTdi6SQQv0sidw+EYMJo53x7RNqWAZFopLPiWBOkZTU8+VZH9HBjyODManKt/IgU1nqE6JERm+nAAf3TKwPWuIXmtlw9f7enBM5RVvSWUqblILR8zPBvFrETj0nZALxrYwKVATSOQofTGwhZgN6eJwiQPw+KwQVjyPHrdNg9bePtLI75WzbqQCnh7g+Y5j4BIYAC+kh/X/2xd1cqwG1Hf+2Rcl5PyXx4cBp35R+CI2k9y5F40ht6Oxlo7H9vMMzBNMQ0YuDQ96yuN3vy4So2JK2KqzSdNXK1q2IB2EmGyY/TAfjo9Nw/RvFayuMl9MmRxHnYzFoVzfFu4cUYIf11ioNekhjQrWA45yNX1gGwojDelwIzIdlbSOkyPLM+CBxTieebmsVXmhZFNCHwNntkqh5w138jXhNK+7No5cIYl0R3IIvCy3Q9lGSyz6aoZdBemg2SaHet1x8O44GwJjVvNm6XuD31FLev7DFNC8+5Q27O6nVmuz4Pvuh009y/Kg6Q4H565sp3dFVtEbu9Px620Gd8h5w6ILD2lP+IamH3WxYLczAbvnyOPZokgU3RUMCkmGcPOiKWF9zofw4CTs7+PCiEMu7D+TDsb9Iti3zw7enjXAbW8pb2fqNPIiUhK0AqaDbicbs5IYzGI2063jL5BNBdHAOE7H/SSFtv/QwiW303HVwzZCJmiAdUcAbEmZQwVsrPGrogEEF6jh3k51XDHRFWzmsLHCZBI0h1iATvIEusjMF4s+HKcaCplw8VQe3BeVgh7xLeS7w1feLqUszAnywy1F6tDy24X3w7uGugRq4strpjDn91eiYnCFPE50gAcvOOBnlImhOkbAFe4kH7czaO/hQB0jSmigYhUtK5yNm7dPBsWREDqzt5J8WOoLtYSQfUdcINevipRnKWCSayak6c2nlsEHiNQ3U7IpaQ39a9+Y8v+1b2zk3Z99Yw5LuPBj4M++sWqfeJSskSc6Fv6wUyqMztmRDykfCnnzHOfi1nPJfDZgo3q7IkjMzAfjlkf0g1AcVG3fRTbxWceWyQTToz54OFmf5fvdENuKttEixftEjJuFFyRMICxIGGbLJMGpget0ZwkDM4f1UJ81BWKmLwT55AkouCMTC4V7mwYrBandWWGcY9hIlwWJ4r7bH8jxRnNcqfKI17VZBptucTFSNhs+dYZg9qdzLFsNd1R0/EIPXM2C5DVucOlLMr1ukQ4bNhykQaMMDp2yAw9PFriUG+D4YUNW4T5JjFKPgLPH+drhWRfluBfTT3zWuv80C4cOp8Cswes0PdOUJ/CohbrZiqH1Ewfc+dgaTf3y0aRWABomviZSkQ5wpDMfx4MHfq/lM1DBIO3/lgWFazjkS9Uwndibiawx7+mNsXdJZuNrmukyH0xOB+HtVxHY2s7AVSU1sJkZjeVhDNKFTfTZKjF8L56Nhge1cGmUGvIe/yaXIoXxqX0+PI3MhreHHdBi9hW6MJELV0XHwiqbD+RGqzlmGkbiJUFbPKvLwPWZ2cDuz8eyFyZ4wMUCyibx65uuJkR982bl9i2Ajo6VRF/KHlc6j0GFtjWkff8XatMrhs+6/MF69i26zjIbfw320jWmTrgr1ht3zJ6IvtIb6Pk+Dr5WWIh5ni6w9N1usu0TgwM3V7JGhsyo2cJ5vA3GXsT/rTzecc+Buh/fyFqzUNTI96fjRxXw/7evzlmtj1gY/9lXt/ApFxpkEpH/GvCgQwpXNM3DltU+kG3qiYd0HeFtK8FBIXFYsq6Saro8JncT/cBT/jHxtDxJVfJzYJvuDOiexMaivcvohABV3LZGkuhv4rOOig+cTU8lVZ85kGLAwKNXwhi5Qx+83zMw/4sJwJfZqLA5BwWOr6e5q7h4xz4Uawf1cO3VGJhTJI79X/3w6dbbtL6Sz2s9d0j3BX08zgqFqIFgODIuAqV+5cCPIzlwIzYdS9kMbmHmwa7d6lgS6wO8l4mQXBUAG7+FYoSRGOCSabh/kgc+vJ0HYc6CYFszmYYuW0HPNebg57QFeNeIg8Gj70jGIyNc9mS4/qS+LmnxDMYeJw5Kie39P4S9V1RUW9e1S1IEAZEkIKCIZJAkmZpDyRkEJOecc6y0EMWEmMUA5hwQQUxIzSGKGLYR3CrmjIqAYg7IX/6nfXuft533a+eWizWr1pqzj94bq/pj313LQwPIxsg2HpytdXOU3s8B5Qf2GHujHGwCOSwV2QJ8fbAc5Sc64/GsMNS9xeDJhwlgetIEa+aOxxKZE7S1goMBF1Lh0BcRPKn4k2q9L4KjVypws2Y0fPNOxVe+jqd/Gvjjy650XJdbRsb5J+Muh92COzQDLsnrQuInBu8+toPXZnGoSI1Q/AjQi48roPFRGpi8CsAH4hWQVOyOHyrk6YrwVOjflwpXlWRg/TsuakzgAGdZBWbpDZOJwus8XZ8FEZ7uiMMVaNA2jeznEexpm4BZOkGohrNB5YwGSvWl4z2JQljRQmD/tU1ka7kwK+61JnGsSLysup61qicbNo1hw20NB9TcqIVhpiaovjka/6fv8Oz4f/sO0e/fvsNTD97TrM5/+w5LxNmY8EISh9fEQ9XXudgxfgVJkUmAh2o5GPlbBjqtG+j9Q2V4ZZ8hMjce0UM7DFhHbsjAk64g/GbEBj0zSld+6qAZhdVksUg8nrMwQEZtNfkonQfmCxsE1kbuMGY9h67aO0INZ1rgqXppOCCiCRKyKbBbloM9chPgUEUtrT3Hx6uYjj7uPKyfzmDji3xc/MYYDxTwUfeuA4h3NdDOnQmYWpUMVeKFeChQStCqxYEUoY5pbngkMGLvIyqOSbhhpJuu8hYh7tsL4aWa8N7fSoa48iKoP3dLcEmOiz1cBh5ozzv9/HE07jHj4O4n8n/6Q07FWYeB5vws3OY+HswGCyDTpRgn8RJx+kpRvDljNrl48AbrSg0PHTVD4Pv+QqgXiwL2LA54TGqjFl9MyZC3PebnF2KE8PPYnRFHg6slMHtND60rshEc7nFFh7/qSY9FKs4+8IDkp03GBStdWBZzOPhjUQXWTasle6xEsGhyLppGSUHoHkc8al+GU0JyiMRRJUhPysVDZ3+RRd52OLzrsGCIL4e796biXwkceL7ZEVu2dZCybVywWlaMJ3Xu0Jeyc8C7phxEKz2gcVou9m4ZQ1VLDSHywHta/lMRq2aVgcyRQGg24uIzSXXyQyAJC5I3CD6trcATG3g4R/IvsuhcBfZ+ZkBxgA1+sv5gqecHWWGWdHdiBjaP4QtWHP3KOmnPhZMDqVBTogX1h4vQewYP//Rlfnv6n32Znxf925fZW83HwY7sf/oyjR1jIDGOgGnFGOCLc7A3aw3h+fU7XnrMwXlPF5H+pHiyxmwt60duBdiGPCHSa/ywkGMADp9dICeMC0b2K1in/gqAmrGaMFWdj81apQBT2PDiHBck9c5Rte3NZPauUFT4yEDedQsolbpInxiIgNQvPzI5lYssVS80tk2FjanC79XGYArMAv9LN2n+Mh7268ai4XQ/nNGdgHarOWDvlghbAiXIOxUOtkmU4KFiLnxRLsabI/kw5wSL/vWMQft5anT6G390d+CDoJKHygUMLB2Nbu8oyAaNmVxwnKgJ7PaxRH7IGqbM5+KJZ6uptKnwOgcZuPqRwboTGegxnYD3X3ycm2mJlxbG4cbsWHhlHQxNOTywupYDHfEp8GBxDTlgzoOfbG04XsaA0qxVtDaVwZ+moRiwWgQWzBLO6GvPBU/mDpG05EgYo20B9j0p0DCBB1WLrXHx1QpcwuLhiYkM+C/jw+p7Pphwlofz5090qp5chNGbGEzPGAPX9h5jiU60E7TMlYTWVluQ3+EOc9uf0Go7A/D25kFA9w+yTI0NLxPdQVSTwU87wiF51nPByTsMjIblwYfzmthamYJ3qprIFtMhslfiKLH8woNtikfJQCgPo8e+oFd1c6Ar5SF50cugbk8aLLnBB9/vDIZeKIapEwqhRDcIY49OJEljzUiDjAWVCfCjNgt5WJTEg0O7eSjvnwDXY9iY9LUc//StntH+//atHjs67//2rZbc1cWOJ9H/9K1Wehjj4mAv0HioTkbWC2eEnQs+c4lDg9cMBss4CuLuzYDo9ixIvb6Yfmp/Ru+fLcC1LbPh1rW/Sbq4MOeNFpFAMVvIfZENYzfzkH2rGJOXG0DYMQb8xPRw41crWg1e0BneT9+dt8R76hx8LnuG1BUy6CDHx5rnedDzUBlaPyrjieUFGH1kKb3VowYPhM8yeSsb4452UTO+Lk18o42t+xOxyw+wT+iVSek4GFLlg8/NQBD8/kU+s/XQLrcEk14awu3BNOz+yYenkqnoekK4b2YWYKKCB7aHczF0z2ZakL+E2D39Qap1+Lh827b2oUWNZMKtfFD+noChQ1zwVMyg1SV8rL/zlOYJvXLhWsDlb5PwoT8PlfJjyArfYJD+xuBvUz5sKw3FtzvzyfC1RyyvhjRMv3eNvBZmZnYPD9Sq/UnKpQiQOhVP9ksIvVZRCnb6zoC8t74Yf/cdibTjo6FVNh5OFCe/+njoeSQbh41lsfGUE12mJJwDlc7AUbxPt90ugZc1DMZp2eIuUgLeeVvoUd0JsMGZD7Y3fWlhvalgCR4nnPIQtPrBoF1lNk505MKnowzEP5SkLXOtYZktD+2PDNCafeUgIvRlNr/4uL6Nj7x6S+gfUaMTRILhvkMUeuVL4td6eVBy5+CMzXPg/BI9WBVeS93k81Gq3ZDe3mgL82oP0qVPLxNyjg/GqlzwltpBwScIm0oS/mtf79e6f/t6rb4v+o++XosnN4lx6980Z+IpGt3uihUveSAzLwJNxlWgUl0rfVlSAsvjjbBE1w7l4AXJallA5aaFAfhw0eIWH1bIj8XKF/LYQHnQXuOLaYLpYLP9MDXBMtihH4kd6XqYr3yA4M4J+DbOCgYzbFBfuZzsl9pF+A/4mCPUvL4+Pfzcf0uw40EwpnSspy8reCAeU4r+HD7KWUiStCoupP3qoMtlk+AalKO4BoN8/RnQ+IIPoavUUGu+Gx5vSkbZIUvsPl2K6V8XC0YeF8PbX7Y4LvE+UR7QgkcX7el+LhfenirB8beuEFZOOS5yZvCByB5Cf89FulgXMo47YhYlGJxdiufdinDCF3nq8CQCp4h00Vb9y04RZjyYvNQMz6uy4NJ3hmh5MXD4XSlMv5yJ2p9V4JiUK14smUbu+Q6QmDUF8CV0hJQL96L4Uj66kSxy/14yGMon0/2+8bi1xBHWBzC4rM0ZbEuLUS0rG/NnTcHh7ZkwZ6kbNHTlYVhuNnRvCRaeaQU8uLaXtJjaQFHmJlKlbEnn1hXix7kMfLN5LPC5JdTRrinU+LUBqg6GgJGkJ3Y8yIHhb6UwK6mLWJU9JUy/Grzz7ycvjAmMGRXq/eYZxDyawVqNROQKz1jeVjaU6VvibedV7QHwinbez0KvG2pwesdMcqCVAY+WWdCZoUfudbKhRH2Apo16QmAiA5CViF/0eBikf5+WvpT6r33P/2++RXbgf/Y9H8l/IQhcUobJxh64IVgJ8V4qrnocxCpekoZd76agE7scZgz+pI2OxbDwzEtaP00fLsemw0Gdiyz+AQaM/Zxhy/V05Kn5gESqFtpLjAh+PefiA8cUMH5Zjl9FCvHP/ypUOvmQ+d0KX5y2JdG5t1nOqiHw6jCDj1WFn7UvCu9tX0OmmL2imxpdQXHccbLyZiJuicrAYzs1Yc7CYfo3hw8X8pPQ4+8menB3BbYoJaGF8HxLwl5ikeGLyvUMKsX/oksNiiE1JQxUg+Lh/MUhWuHXKoh6LA2Xgorg+VYunl/uhhYaY0H+8SKn2BMMbpRWwyQLM5S/ZUeYH+PQLGghOXbWCo3aYvD+fDPSumOI6gnvs+TzGNj3k4vVV7gYOFMHdKTiIPc0A587GBR9r4b1rzPA0OA4rV9njCvnymL/mCh47sGCz/Vx8OL8HRL0nMC3imS8mJOLW+pycfPpcIxDBru2jsVM+oZekNZArcKr1H8NF9/+zIFLrHB8OJMBq1wGEywnIbZNAtk6a6xJ08K3rmYC/7fpAv8+BtQab5A88RPUc/FYlGG9oIpHL9OdWqZkb8RccBcLBT1pPxB/I8zIxrfI39ff0HPjx9MZe0rgyr402P2GwcM/OPDpuRuN8S0Ce9sFNF/ox/+8u700MQjNx+0hKwOaaahsO728MZ5ITdKhFq87yfcERZyl64+PhVnykaIVBNImQY7/DtbSqAb6py+8N7uccEP+e1/4k+W+cHqPyj994YdZ3lTB4DDLrIeL9X89FhyuGgOWDqkwauSLOp/n4gkFW/hdw4EN811p1Ow+8tdcPnRyi+GGIA9+CNdSHs7Ct5PdyHH4TXzOa9OakMdU8qsGfhjyBGrD4KHPbCx8u5WY8S7QCQMzYVAxE8x/PmxfoDMb7wlz/ZzuULTfdZz06zrAuVnBVMeUjytX6uL3LW54LpGFxxVyQSZiBtq9mIg7ZZPBM0YZtCTysWFa2J93zEF177H21qQ3JGNxKC5Z8vy0YHQaMqYJQKycwVrbnG5foYQzwsoxeJwDbrjNBXuTdta2Xdowr0n4DEYZ/CsvENwuqMGa7gN0dGeKoE+xgTisXUVyThBkbxkiT9qtWH0jVbTFLhKD7nFh34ZeurZHErQ8Z+Jn4frKWvlgI2MGy3fyIX36HYFoSBkMGjIYqe0IXXMyYMsxUXilbA9LRvOxWDUZWg2C0dKiAnc0raciIpUoWFZK+N6+sHv8fHqw8hxriz0LsmJFiTXXFpnBcljXl4zHoBh2b+Jh38hM6ItjcF9kF8viBBsGhOvPuWyER65GgYhJAj61ISzlNGOadacAJpyzguHxU+FRNl/ojX3I+lgGJwW+JjJuKfjOOgN2Fulhu/D727YLtWsoF6JmG4PEGvPTmxZlYfw3cWTem2FWKc+pSc4dl6i5oKxdKd47N4WEjZNBPU0zXGp6muYI18+zekl9uWVwpEMeJOu8yH/rm58f+W/ffON5F4FJ4/t/+ubPHvpOeiUY8D4lnEcSqkSko9fpl/Uc2DjJALMHSxHnLyTjneWpZLgbFu9+Q0MeDJAxh7lwhKOEecL9ZyTOwJoPDPx8InZa3ugKK0FCeC9WXCYRCtNhJy8ba64TfF7tCxvXT0GXamv8HB4BZIoRvbXvIU0X7v+UEW14eY8H9Es50D3rycmHKXBuiI3WP6fgssoR6vORC2LrM2Cx1Ctqn5eMc98yoOGagHrrGXAVnr/Z21bR4Opi8Ln0muY845NNEv2sBklJJ7crk8FvvTxRb8xAxw37WXOWzqdi37i4WDweWtscoeO8BIoJz7/k8yLMf5QJQQHO2NXrRz5quaPzRDZsWD0Jpz+1BUWL8YBGc/HM5TCB10kuJuWIgePCG+TBh990+ysGZlR64VH1UBBriofatxfJsbIiVPPspkmeivCxVOhB9wjsGxWSsJ6nSfyOJ2G5fI7js0E+Bj9kwIsymPFjCWvRTjcoHlNNYpY0k8AP86jpGX/k9smh3sxIuB8+Hm/Ed7ZXbm8ls1Nm4pJbIpjjlE/OHHlNA9sYCH4VDUvk9GH6WwbT2jaSRGdTmBJIoIYnDQ/OtpKiMxmQqhsD43ZLgUr/S1qjEYeb7cJB9JQC2h1nUMcmDarkJkLU9lAyMPEbsbbkwoxQEWgaGYMv2+roz34u5p+IAqVJVmBUqk6vf2KwKscDFVbyQbyFgQnbNchF+dvkikoYfD33lPw3XoGdvuw/vAKJoEhcGtn0D6/A1dgRv2fsIL0F9pj59jKZMCiPFbZPBaZnvlP1Y0Kfb7CQjvNLRsLOgS3T+0nBtA/0y3sGItu0kexnIKKxAosvqJMrYiIoHqAA/CcM3rzEB/Ws98RwpTP6sBrouwPBKCYv9BwaNiBprMeqr9tE1Z5zYL8wY6wPPE2NL5uzempLYPd1SUhdLY4pezbSC1J9dJF2AD1u4g5bRZDFfix81jeSgWFLA9Xyx/d/1dOt8QzcnCsPvA1idGeQD+tMnDl4pD4nV3Xi0GXFQ/JuxVOiVsjHp6cyQe5aOZxU5cHa/GT4oBcE8MAMyyKF3lN4hgUP86jP5iJcMF4fWvP4cOSsAYwd7KHxIykQ1p0Mj256w+ikNKhdx4VARnjegmuJYncpWIQwYL9pAuCbyzRg/xgYDdCEa0QbWA1qoO56mbjuYkP7MlHMfK5OJ8/zgYXBSRBuFQKjLQw+0smj1/wYXPelnart+UVV7s0EwwJncE/aRpdKp6NvyRl6ZKoeCFpsBEo++ehtqw99+VZQ++EBOeLXTgzHs9DXQ5i7H1fg1MRpaLMuEo+MdQCHvjLkSqTTmduOkyVLWSyn40qwKGI2+o5+JU0ZL4hIZqNAW9wFThgVQOMsBq1WOuD+RzZY3etErQwtQKLZETOEemJ14hCl6Yo4dXou+jmFkpB1deSe9iRif18R63qjsCRFl2Y+40OD9lQiNdMLlYV7YVlZ0P/lXTic/t95FylBGf/BuzBZnwa91gnofmgV5cj6wOL0bFiTko3ztmyg+ql5uPKDDfZo28LC0neCaTvfUUWhl/ax5oOuTSY8vMmHkKJoqFPIwcUlMfhQ1wUWJQSB/9/yYFq5nHBqxiATEk+Omi6mlV+2U4NWG1xzwA8D1B5S2epjgsxLfEzuq2LtkWNjTI853PCIh1TNiYKY0+Zw33kB9ZB/zqra10x2/yjHSRaiWGmeghZBF+n+jzboxg/F3jN8uDbeGx8t5KOoyEMyaTgCFsyIxsO7I/CTTDExjrCEhXp+OCbOBC2fGmOmDw9fHhUnQRXJ8GMkAYxO8rFFuO9+sBlcnsHGBuVgUH6QjHytwzQ6Mo4acXgQVsDAVbHd5LZCAAqiuThmWAMuCeTIF9/T9meBD9IDK+gVay6ubH3mdKU9C1xFegXS83nAP+xNztjHwoMFcdgpy4UPGzOwszURV2AOnKjg4DyTeORa8/HmoWnwW64Ah5TLoGRZKgQv8wDXaTWC6nfW1C3oHHnhP0T2v5sKXdPZ+PW2Ja5W5kHxjV+kJTkUVI358HCQDwV+qZilrYKrr8bD745CKI4oRY2vU0n00UXU+8Ix1nepDhrwm8G9fc/ojzZ3yBo1hKXK61nrp/IRpn4j7DcBqBSoh4dPFEPjZA4ZPDNAdk2QoekvfFHukiZOZfHR5iMbp15VpI3+R4mUVRaK7k7Hb8p8uHHxFS2cR3CjfTI0teX/V16K7xjlf3gp7/PNac6krf/wUt738nDFZi2cXpdN7hpkg+wVDi6w9MTtb88JRpd74PXccjTa8YBYRM/B31MHSWSbLlZIlMOCABWSIMIHOW0jnPFGEhcIvUTAVA54h5+lcPklPXB1I8upOYdy78wA7ZUMvKybgb3etnDvWAHq/RiP6Q4FmJXNQ+2RCjJ+tIMOGpwgiQnl0HVZAQoXl4HcmVjBQu8eIv3aitZXjoPeVRLYdjEFFk3joNhfmnhoA8DPOB4kZOfDigW1dNKTIHg2rgQ+/MXAucX+rKm+0qzyzETY19ZFXUq/0h2p+2lAJg/GpAMmNRXA9948khPMw7P9AaB3/pEgfyPBgpd8YG8Vha5eLnKmh7XLanGBeyoY/vi0g3ckqIUdF+Z/2EyftIo4cR+VYoc7D86JdLcPLXh26t2ZYfo9iwdFRX10dMlxMjLy/bS5qAlUC7OlE+srPb6hnhUdmAjn/o4mDj8LMWSzJXyz52GCXxg+fmxKJnTx4MRyYZ5r5ONVCIOCpX1ORzo3s+ZciAD9hnGQK7uCLHayhG6zGQD16qCj20AmmvLg6MYsbJlUSx4eui0Y/sXATvYUDPlghW8FnoKW5zIYKnOf2gmv6zHpAxlNfkp8rYS+N3gCsE05qMVw8cI5d2pT10HCTArw1RWhn2vaRdZNlsWklctZ+8IKUXpAkiamPSCFGdNhpEyeDDSXw8KRUhxfpw078rlw/JgVmJ6ShB1KMnDsEPcf3o5by3/ydhap/j+8nZhjtrDnF+8f3g7p5oBqgRZwU8zgSRIf9k13gGANE7Jl9BnL1kYGw+aXoL89H7XiUyCg8j1NcFtMdM9mQ6620MMEc6FA7jrZGEXgRGU2dvAYsAAWtb/OCAfmWCrptIt1vrgE3o3roId6y2DU3g4npS6gEhwD3LpaARhXoR+Ll8SsiwFwoU+LSuuUwUnXHFzFriCiqZPJ/D0mcCZFD+9raYP8SQbS9FPRyt8GYw1FMPThU1JrzYW8vxhsnpkK+gYMLNFiQ5/SHFygmQoR8ea0zJULydqSRH4cFyxPcmHeYh7q2pTjDRUO3HrEIKPHxcoQVXz9Mx/kTbbRBIYLSWHWrHNGmnhL343MkUoEk4z7pE10N7n7NheGFbj4s7MEHdvG0iOdYmAew4HN/fuIvAkXBiQ9cLNIHrBCYqA6SheVvkdQ7Rs3id+zCFQtM0GZvlAI6k/A+3vMUCwhBS8OSMAKXw5eDS6D8DKhj3D0gQy7Irj3dxLs8E0AwzW5pFmlgCYKPfZ8of7vfWINm7ZboP2BZKqx2Bq0LzC4/RsHDE7NhtfL+KBmdZ1095bC2q9drDqVVWTzknn0tLonbmgIxteDiyjrgwwcf/qYqk3ZRCIET4jhghdEJYaNw5x+QYc6G+KfctA/KhmOfvbEeWYMvr1cQ9q/M/g1qoussbMFia+niav+eJo3kcDULzzs2a4EXd5s6LyViEP6OaB4KZMl15kBf3hNsYn/yWsKVvyX1/SynQ09vn7/8Jp6mhPwTGAC7L2ZAW4NuXAkoJD2dRTAydGtRFJiIR35NQPftM8l+FVVUPTrAzGIKYTxT6Mh+nYbuarLhnlGdwWSYV7At9VADAgD9ipzeGclgRsmHKBeQ53EjFMOr4T+Q+l+GY29bgeLZumDdctsrFqsA9MV2Vjt5wLuY21BvWwqvr4XA3fjOZDk7YhtI0dJ1bZYODxagHcgH3/kPaYF1akQHWZH3sudoccb2RgnwobcI6WQ+UwZcGUavHiQBOK6lJyImgOqfQL6VwwPNmzTBNOn6lSuUgNHd/NAtqaJPhJmlxmh6wl3oAITg3eRunEjpGTZNpaUqCfU7FJCzwmXSdehC2ThSQv4ZdoqYH33Br2bzigywIEdJW4YmbeEkloL4N6rgM3u6rh+dSPxu8xFkc0B4Cz0di2/2Hg4+iblaz1gHdk/RJ8cyoTEa/H4wzMBPLq4+MQ3GLebl2LchQpc18Tg0Kl88sjlPNl4MhReB4/S4vRJTnYS90my01h8tjkONH5RVp1ZCe7Ybw4G3FxUfesL16RSiW5rBagEjEHdCnHSW+EPYRts4Cenn4SLscGj+gWNb8rDoRyh/gqCMUCewcllD8mahVoYm7+FsJxnw7VtFXia10P6Z5kK5OtccLc5wCLhDFiK3TRrYIBk84vxoqQfXPBNxhANPmrelCSh+8ZTsR8MpsblY0RtBTgDFxb2qdEZ6Qfo2EQ1+MP7WrDxf+d9sbZ2krxlB//hfTV9s4Blv7fTsP184LvqgPe3fbSOTyCCCDMQ480a1kig0y9GwdIp0dhf9pFqlPkC/4gnTJb3g/7yMhjnHUif7P+bRs2TouJnTYD+LscKTRGgNyQwheoBW9kWNIX5690xJ3RItkfPOXxA7TCc4JyB5Yll6Lk0G/9+LI9jqnMxZIM4rFhrh0fdirDNTwpSbrKh+AsbtxdebN+irox3nqViv04mREwqp2e+VrH6fMog88dcbHiuhx5ZxsCxl4Nduu7kiNxjAT0ti72bUsB5Yi4MCfNnNFyjl87yMIanDBM0GNhzNgwcrcvwh9uC9gWMG05fIw/NVaqgp5ELWef48HiqAnxTf0IylynhMdW5uLo/mCo1cfBLqoAl2nOF6gzHYoFWGdiEzIEdP+dirH8ehvTeJSqdARCrs4mm5algqH8QPcmzYtGG/awjq5ragp/Fo+acXqqvxsJFJZPbl0qW4dc1efDsQibadYvQq3FDRFAbIWgRjcPXR9XBSNsFx54ShZTmDPB3mkPyZM0xSiEYk3zjQcUnARa+L4UDtflkSXExHs/hg85+MVhQEouDHw1g4lMtTEMGI98ko0+hL8gWXSX2w/fptBgGBZMsSeLkuZjaW4p3h4xxeQ8PL5i+oa0O+4jsX4l4PU0NXv+aBifl5mBuNh+8XOXpuRolWMeWoJtDU3DyKBc92/YLDM+WwsaFPIz+8/61kTN2XkTy/8eLyzD8T17c2bxn9Jo+D73PBQA8uU2ko7zBekwa7ZIwhv7k46T4fj0xVnPByldRcGL4E9FiuQsOjrUEcy1vDF9TCtyYX+Ts0jLk2XYQnaYs8s2jAM4OM/DozQwU7IuEuJqF5LuNL1miXwiFgQ4QVjdAnlWnYVWoGe3klOLe6Qx8v5mOa/oZfGC6nNzmZ5DCK4Ru6bCEgTg+mHYTLPYqhmtj+ZBamgYKrhGwI/KVwESkEBapFUPL5anQ8jYKxfhhxMlkD2mzssQ032yUqrRBJtAW+j5yoUrZHhSnCLPqEA/83wg9gjIbBFIENUaLUH/dWrop2xfrtqbCm1wJsuE9GwPGTwRpcwe4ri1LeAcz8KHTfLouoARCFnFgt0o8LE2VwIWD5XTtiyL4HVGMC6zNMNJoGkoY2pC3A6eozBJzmNbFwp0So/Ss6XiYE1aOxtMSYDwnHhJWiMGgYDrOYpfB0KUiXFLjiYs+qeHHCnXkF6aRuRwV0L0YgUs8ZsP1K9nY7PzQ3nL2DHzs7AKYbwZrvZY6ZtblguGYGTijqQjyfOpYvItvaMfmEtgf/MipaA0DsLOR/jzpihEnStEkejxW928l96Q14XT2PZLUIoLf/MxYfuu30cC1RfjzUAyIy52gv654sA6oNrAEST/JoWU8OD3BE4yl1VD+bAZ8Ws+BA3ZPSJOOKDGQisOwPSVoLPReoRVFUFSTCjN+uNEHbA5+1BQh/8PnPhD433mDiz4zWJWg+A9vsGqhCinqbxKEWvaQSq9NgvOqDG7yKEWDSn9ob5wKXo0EkuXa6ZlBHZam31e6MzUSzfeqQGnvOSrrUgQfbZ1w8EcjS6dlJgQp3GmnryejpNMgcVs7BzdrVcCWqQupwQAPw/2M6PfnDjg9/QPdyC1Gc2F2pvpF2B9SgUfGuJEVLZHosDKVdrAaSfBMNxhZFwK5n9PhpmsDfWcxCe2PtFGd7jSMParDUnzz0ClnjzwGyRSBd/5JumvjcdIh2kCLXa2pxPwKCI7yxCFOGP5WGCQ2Zu7wSeQuqYmVQQtpPjpHSEOgRD4kGQ/Qy+8Lsax+ImyNeEPsghnYuFicnhOxhe9+L2midyTcDbKiSwr5mJbMwGCyNT6N5eDp2GQw36WNOkJfpvt3IUj+ZoCeVQOvWn2aQHdTrXEBdOrGMpAfE4MmgyxcdSMZnj2ShtADoairGI9d8xiYcEsS/z4jgtby+VgeJAdK48zAOiIGvWgtvTaYizcLvFllO+PgkQeDwUtE8RAO01UHc3F1zwzM8wqBiRYVcLl/LHZ+yIOpt5AqefiC7i9zUL+aSdXz/PHRNS5yDRNxInsuvvxgI0j7xeBwPRd3/r5De+weEr3QLkHFQQZse/LwnhUbL12OAl0iDd2lt2kb8lgpEidIwaZk2Ge1ixadUoLyYz+p7BkGedu+CsabHRIY2GTgZfVeWtuaBxcq7dFrIBcUXxWgYgzSP7zKAI0wODX4L69Sbe6/vMovbYoo0DD7h1d5116P9ArPYkTHGyfxq3tp6+kEODNtJk41zYCd3FCo6mfD0mahnjWm4PSub2SqcG6aHH5PWhge9JXmQQVJh0gJBvQly8CiTAw33NvBevVCBvf6Z+HLaQxcGbTBE05PycgCL/xZ6ggey/k42MhFWfZx+jYsDwv2R0HzBwaOb/xCp34fpMHtPlhjmgsLQnPg5WknUHSLJx9uKWAfnY5OpumgetYPxTSNiEmwLt60y4OrBjms7fd5UKDNQo38u/TI/HzhPjMmER55eHiOCelvaiaLF/Lxlu8wLZnOh9ebLtIEV6HftWBDg1oe8ke8UWRRMQxODgFpYRbx3x9P511hcNLkAigL/kCfG/ih0uh5Why7kgqshX5/kjiZ0+aIFg9XU58fQh3SKcIjTzJhSVsWMZeSw7RuK2gtZxPFO4VoeSIDNxUwUD92CkyYuF2w7UAcXO3WANdFN+jvCTb4+F4ujvmyjF7VDAX3UQbnp4pjIIcNY0dT4EVJMXi1luNQVRxK8bigqSlOpO+awssjOfD3IB8Gl5RA6Olc0HmoA9p73SBLrYTorTdB1aPHSEOCJsr8KEG+9xTs8xgiW+r3E83J+0hd2W0ybzIPHBzyaa2rH22uz8Xdl/jQo6APi+2ToGOSDqoHqAr3ri4WpJTBog3Q3vXqGOtBuRFGWkrAk/fvWU963KHfLhx3XHTGUyVZ8CTrMzFRnAgqSLDaSg3/G+/0dt2/vNOmDUeJfo7HP7zTKUveUIXJt0lNzDAVlwZcnPuVOq5aSdpeVYCLWRkcnVgIX9Rq6dcvgcAhP+isv83opqgmIiovTm+bZ8Gd8Xw8XryDzN5TBtvybPAy/yNpmcngNBEOFmplw72cclQNtYb+15lQLuWEjg26+FHoF/SL/dFRLgvHfTfAuw/VidS+BqeCZFN85S7UlVJxkv+LA1LdP8nGWzOgrrMEPG+woXlROvbsK8CF6zaROWtSYUF/Jqz+xpDm6Ras9uEKFOMEoMpkdRA9WABVj7iomKVEOXet6Hv7eCIxKjxDvnw0M+Kg0hR/qNwYL2i5mIk/hvNhTJUYvesiA10PPJAyzVQsWwI37uSBSK0dvu/oot1rAdOd2HBymI2zqgC36AVhtlgEnt+dCd3W3nDiQh6NczKD9LNJ6CXGxm/qjijpx8faVXw8djoYOg0LkLmQAVQsDg85baEHQmrpzksx0F6Viec7KwURjy/TPcti8a0gBn0yrLF5/hzg3udDcEsE1gvv1UeOLg3fGgBb7E0x1zGajD3KwMGFG5w2xGUCPzAaZzX8oobltXSOeRgWvAojD+ouESc/BnM+tlByPgJ2BXuBfpIJDA/dog8UZsBO33Z62TEZI5wycU5rGmLMNFyuw4DNolw80J4HkdIc1N3JwP41G4jlmPtUpz0Lai6KwsZZbwWD+iXAiLtg9dVg+K2aCUrm2vToPRt0EM7zmEMZ+IeXy7pk8L/ycr1O/ycv1yfBAbSagjAxLwKlZidj4JeVArtPFZD+k4Fr+xjoWqEHMnZBOGZgiKV9/Cd5fMMWftvPJena2fhFmJmHBipA3yoBp6TYQOblUjx0IB7tSA9NHmHwy+BsKF1QgsWVKehZckKQX+8ENasWk2hrPnYe8oTHvqnYXSIgRfI5+OuSOLjpFeGtvfGYn++Pcc8Z2BfmQ/YUlUDZOBuB6Jl1dJlKBpxRW0C9VB+SX0vtSINeKixMmIB5smzM6+fgpX6hnlxYKZh7gEBhI4OPfJ5TTNTDo8mR0Bz1hfqn8KHY0gY7wu3g/JlCBJFURKHuOAv1cY5/LutiCRer1fyg5uwoUYwSZpSMVDw1UZKUR8eCZYsenr/AhlKTcpyzMgkfnbaHC70pELZcHkM5DEoNFIOOAhvr2vIgIKoIXQb4mF/AxsD8+6ROJ6U9TcUHt2yIBckbJfjygAp9+ruSzDyaglNvtpFOGy5K9oniY8UydL0eRvaOeSjYk8QHlWRdjH8RgUXdCQhXUvH4JhNYf84Mj9zhwIcxFbhoWQpYbKmAV/IFOMVnKiqwGazIscbl7XxgCXi4UWWGwEfoE5jVocjfVY7H03qIi2sKlQ7+TtN8hVkxJQU/RjjC24BUvLgvDSJi+CgmysaBvZuo3C8u3LRUgrhNwThp5lrWoz+drCues9p7NpJgnjGOH7ub3CcpkLp6Mr34LgB/y/mTyJ1c/MNb5j/933nLn76w4Iblv7zlE6pp6FBN0K2XQaPrDN6cog7TDAyBrcuB8XJ5cLdjJvkq4w/dO3OwfPxv+raVDR8mb2U9HdSEzvfJ8GpcDIru0MLLZAmJ/vObXKF+6ZirwcGJbJx0UgIej87A51vG0sPOc1FyPgsfZxTCZolyfMpcJysvJeNjjWWCHwOSWBHuDx41DMYUfyRFLul48kk5rBYqwjRuObjl5WLsWisMycvA3Fh9aPw+HgYlhftmRzK8m+sFi6TO0sI2FzQu5uGb8q/U80sKDa8rwyRpK0z+XYBBPa44ZCeN0jw+3hi2I8V/aUJ9/HjoZSfjta9FULCiltY/eUGvveXja+H8mPKJgQszSmHuZh5yNmtC++HZ5OWbY3RaAxtvzGqiM5sKcLnfB3J2QzxU6awmZsOPBIOlFdBzjsEwb1V8eRVo2uFi5NAYrJRrFdSUF8ECmyn0qWkssqPdUKtXpP3ndwZP5cZj1EVtjH4j2TYSL02z8srRlB4j2yyyMUKxCGIXd9BDR4bokw4HGH8rkDX3pzFKnOFi+s9C0Kv4Qta5xoOOFIPLLpmSqvxcXGHMwaVcNqx2nQhaUQWYtHIJubY1DjyVuXRrSxV58egmrVqai1c2qePbV+MgXD0e3ab/oNdlasm+5QrChJqNZtesYE2EOSjX5EFGuR8cymFQpLEMt15/QX48vytYv1sL3UImoKyb8O/v4+DzyRekraaDTN5qCJnX8vF/eN0D2/47r/ulqwyNHyn9h9etepQH8a2SsHpmIrZeLMaT1fmYO7WRnt6YBZs/ukP7rGYquL6JiL92Bf95o8Q1tJPUf+QJ+jJUQaw+DnxGjGBPthZsfTEqCHkTj+0nfbBpHQ+6ORn4dCCDxJgk0HJlDkwJS4RdigSC43wh+mw21hjyMCo/Dl9VldAbYUK9SuPDod2peF9qJ2s4bIVgzOw0EJH9m7JUbSFSXhcy1UqhXDjD0JcP69kGcFethdS7Cb3u2HzUOB8ANz8cphk9WVhJE2D9zFL0fJ+Myx4mouNXNo6rPkEPhjKg8KdPwSoT7huKwA0NIzijHofz1weirWUqLAuZiwU2wVi0Ig5+NA2SdQsTYe8wB69PyAW/1Ty8sJ8HV/LYkLtZGVpVOOh8pAKvDMXCcdVGp9/f88HpRQAsko9E5QQe65ukGwSIxONqL2O0XKKEPpFqOBAg9JNVMdC0nbJ+nRhLOI0LiaAzFmUaQ0HFmAvfrp0hl5Lisb7QH1ct9MSwlBiIlasTrCRWrLdTeojP2kjw+WUEy6V6qXNCCmCLLZQsjYTNMtnws5SBHgMe5vi4YOO7XLx09wV1EOY5xRZZyG0XBa4FA52VDlCTdZ2M/AzC44us8dfDYrwbHYla3j6kqpYPU84dZXWb+2Dfe3965kAZfH0WBt/3FMKU/mTsfC0DUcUKUGzRzQqNZDCWPqGg5kq0zSLh7TNL/DKZDcfrC2HdhiD8w3t/pe7OChaZ+19574bpKSj3hfcP7/30WC7KG+XA77nTUDXYApvnVUB1JxefiQbBi9Tx0Cjljhs/MDjP3p3YckSwzFsG6739qP6sHEzriQBNo5s0PzgBhn3VccUMB9xo2UhsFp4gP9/7o+uHg4KBnnoSaiaCv495tbftJFjT6ikIeRqN5qs+spJ2R2B1ylhYFOkFMy9nwus6ZywYmIzrv4TC/Es+MBhtiqOrFxI3aQkoG46mX7QyMSRwgNxung1mOkHgXhYBw5QBJ2lbELE2J5d4muguzO8ec2biPeEz1Mzl4o1bBJdfZ+BBfzdNOcBHI7MIGJlwhtxfloLyXhFYOlsNRPuUwTOfg4ytPgaUDNEZh2XQyMoNTq5NwKpPXDg+/hWxMp2ER/XZOPs1A7fXM3icOLI61CNgnWAPzVUYIRobe8iNJeY4Mfcn+e2eCz8uzUYnmRv0yf4MvHhLHg1yCU5TiUGBYTy45C4hI37REP4uHN+PLUPY3EiiznrBPZFxeHkNA5cljYi/O4HEu8WkwGg8bLgizO01MXg4ygijm8XA0i8YUhui8W9BOJyscMaP1/NJ8xJP3M1XwCWlKvDkEoOaRx2wrdXzD3/Kfv/QZTrWvAgdnl6lb6S3OHa88UXtzRYChxXhGBn4jNWwQpwomd+hD7Zr4Jdvp6jnQg75PsYWmn8xwI0bj8/faMDmPD5G8P8SOKwUwcKCLTTDMwWq48PB1ScRUl6ZCLzDGDA2eUbPf2Cj/bMUCPe2wtBuHzh0LgNev+XC0DU+DjlGko7PsSjxQzgjOUGofC0YtzolgvZNNdqfZUC5Zd9ouZkluqzTRGOOGcjrLSBn33Lx3f7DRCG/GLOtRcH+i3D/1xykZy8MCl40CO9D2FPW1Y/b6eNVWWhguIn2r8sgEZ+m47vVxthzoovuEXqxnhIDnLY6Dv0XAxBOJN494onSY3OgPisYFZc747qtf9Pck6rg+qWZfvQqgO3LEuDyZn3wEurTb9WFdLoJG5t0Z6PS/ky42fiB5muHwZlGV3RwCIYTv0vR6L487paThrC5l2hGmDfMV1xJEup08J09D8N9G6jM7NV0d5k6LrrCh2ZHbdYytXJSfXo2xXHBGFe1hwY+1sPqbwXoIvCnU9WPCbQdGHxSOgUmjrXA4xbuMD9XOJNi79FV3ytAzE4aJmsIfcqRYjC5EwRLItXgLccNh3nnBOHnGHr1aiKOZUsQHd/JuCvnluBbIR+L55XCwvfZ1GJ/tDADJcMz3UtEVssO5+0OwhoBAwu/zcWRAh7sUt5A6wpjUWptAYhKT4KncpJw5Voy7D2SSc6o3m3f2GYItgsYMC0ggPOraXtREDzyHgvbzmfi9+kC6vDtqkDGLEIgv0sCzwkU8MmYfOivFkWPzyw0VbchYaFXSFVcCKwOjkPp5RXwZHYQuiTdpPcKizH1pxOav1Wlyy4EgMOgA1YnyEHDzGKQsbAV9MkUouuJM+Tw5U6WcuVDEv7dnn5dYYRvJwSBnQwXu7t5WOkYAscrvOl3tVxoODgHJIakcM+riXCgXQkmKphTL4UJcLk6hk78dECQ4cMDzrcj9MHDOTj/iiWy/vRf72kh8mMHHBLl3EmS1iWiWTwG/IYjBJL6uRi/zwOCD9oCEyOGTZsUUaLmDS1V4MFx4ToPOn3IbqH265vy8f7t2YIDG+xQfxIHO56L4LMKfWx7y4bOzmzgZYaDxKRZeNsiA5UkLfH0S1Usq5iDp1SzwWpZGS78VEIkA84LYkfK8Yehg1Pwhm+EI8ZBqXc2WHa8j5X3Jg9G/8rE4tV83F2dBt3neuiOVb7wqz4Ugm+lY/WUJCgJ/ZulNrCPSgmzu7T/EP1hkoL7rOPJ8KFAlNHnwdAtPq7v9cNpca9YDvYM/Ir2RXGz/U4bBqLQTscA8yQSWR9OaELtsR4aff8lWRDziIacaCbPV6XiiY0M1B+uwI3PL9JAZTYaZkghGPjCvv2BqJnpjL+MlbBpXpfjjxd88PXwx86YB/S6SiiZZeyHYQPxgsWWZeD+IwrT2WJELlUCVmd+ppkffTDzZx7utjfD1uM5wI0JIsUF/WTsbQLTL3YT8Q4XeFjAgMbPcPBxSIYHJw1w5VENiKHT4bkwH/KoD/hLlVC572p4f9M6pw+Pp5IZsaN0NHCU3JS5RN9dYkB0XhzeaoyDdZUe0Pb4ErXZkQPWdbnIjhmm5Ut8cL2SMazqfkMXvKrAhyMPWctaGLyXX4T63kgYJhD6Zw2RRzkppylMgyvqbYJMbR6MCTMTTHksjdEhPvDqMA/Wq56gy0KfkuF39eTkViVc+eue4EDSMTozZA1xHnucFSjMpL1a70jO2UTS8c4dmD0lYKDMgMEUHxyofEHO21TRs79dYeyF+UTCnJKxx/lAyg+SSVYnqcd1TZw1Nhzj9iwh51XEofHZUSKToIidG2/QB2+84VXkD9ri1NE+f18sztoaSr4mcKDApJ5OYW+n4w2j8JdCISxdzmOZmyxhWe2eBTX2ZnT3XhWMLdHE8lZvvDuewctXG1lHx9fS05JXiVmtOTh1xKJG1HqyxYWPr7LL0aooDdR/yVGZkCxQ5etiLrcYRq5yYYTvDSpHtLFkyJUMn2dg8+lj5GRqIfWKMIQLDrU0facNHr85A25vF/pDu8P0xic+XKgvwC+3H7Z/NdxDM728sTpuNXEM3kCeBbrT3EvjIFSqELO+qcL48fOJtJ8UfXfRita+/yqoOTMWHesr4KYFB6sk8tHynjGiojd4iRbD5P1ioMNPR4mfP4nksCd+i03HYXYBbQ4XAbtLXfR8QQ6q7zDHiqtRUNoAiDu1YbFhBXB7XXGtsxfYUjVoXmYDcRdekk69C6xnX5Jp+otokin0RQOCYKiMc8cLX4JwTJkB3Fy0hS69JQlhhlW0arsrnFOpFQyLm8GmjeKQZKcFy+8mQfpnBlJVPOm94lJoFmqLrLYteeWejU/IJaK4PYq8N6/AdXPsMT/DFY+oxqC5gzNohBfj7OlK4DwnBnyuz6Ldc6wI30UVjM5FwMVthVDG4+DphBOsp59NYdecLtaaD4XgaeEKrPFpKFGbBDP7FwoWtUrBvm8zWcGq2dhcoUUlN51ql/19j9qvc4Jg4dysGEmjSzVKINkmA86+caBS312wQWUynd2fCxsfFMOZEQsIVp2Dl3kTYVW+FNHI1cZYmXEU9xdQ57oKjGBJ4NbOcgjWMMNeX09sOeMCnnmOkFpRBjsmiWF1uwHwnXOEftIc9N6/FuiHVWDXaj2olinDkaXpeHrhbIx5ZwsmTe9pd3Q2uC12Qdb6CjioyYOsLlv8ojAFXuqzqOiESDz7krKSl6fj2uoSnKxniZIr/LDgXha2PFxM7jRywXjkiaAjwAVu1rWRVg0GBhLDIeKFC5wycICVmw/RUwvXCsKj39FFhSWwqWkd2XfJEPsUGEy5Op9eUDeip0uDUW+SC87YJ4+pfhx4P3cXebPcEWSKy2H7Fh7kHg5hmQqfG8bpYX8NG7YvGKET/SuwWcUG97gmIDMvGUoeOcPeUQYbFxeD/x0G55+3BJtdu0hn4irByvzR9s41BCaISRKXyfrQtrWBbtaIwo5p5Th0yhHOXT3Yfmi30Ht4TMAPzzKBf3AJGdMeDO7MFJyvXYrTb5g4fYhgoKjsPmEzLVRE667gSpE+yt4vQd/FN0gBNwavZjvDSJOAuFbEQteCOLiS7gurDvHxsc4QORKs66T8IQSKXkVggDkPS/2vs7hdXTTgiRleGORjSWYUpL90wtZPFXBZyxGWXTZFi/Wu8CS9j2o+coVtLb/aF8buIDOHhB7QZzJuLKsntkcOCbwepdCxdvlE8nuoUNecoNvHAbQm6cNuzgtqGJkAdbtE4UyzJw5eD2IpJu+nehuCccjuGplT+ot4BmWQqMUbib19OEwLTMVpeU7YcVCb3pk0EdbMr4D7GlFgeoWBmqQXdPOdNYIo8TBM8esiSV/5YDbXEBwXS0DpAxZcyfZAm+v+8MjKCbZ1pOGCrmFi8dmhXSE8BPrPK4Df6TJw+CpgackxmHpzJimfPgHsLP3gC86G14fYcCC6if48KEIWfHPEmlnR2H1vPsmaXo7TNmWDc9JZ+u7cbNZUsdeCMCUnPCtmhtrkDb3hlQ7pv7JgRWkhvH/PQEp4HioLHGHlOkbA+qROzJ8OkkXz02DKVQ5qtHFxjZoUWTTPgKrwymH1UwY3h1aApzoDQVx9KFy7ia43cYEV8xxRwssbbBkL1t/FhtD1MAGi+mbgfJGTJFxci66W/EjURwowpFIM7wjX1HaogFnL7UnmFH981+dNPro5wld+EdDiasE3/0RsiimB/ZYccv1GKDy7b0nOTc6E2E2acPK0KIRu4kDQUCQs/3KY+ugFgUZsCc6WdkS+6Qpi0lpHD5/l0lJOATw+VwjnmJnof8aDGleUg/jQUUHcKwZWVWVDcp4e+Lu44Zm5tcSfawPsaw5w/fs0/HjrMh0WevpJ63iwebEHKBfLYH5eNClZrAV7TwxRx1tvSbN/MDT7dpK2U1yUPcrgsmOryObVDnjkJR9TV/+i96KaqFMtBzLMegXx8bmwQ7eADp0SdWrbHgfTG+SwUsECuCPbWXc25OKzVSfJDXk2Hgh1gK0FX6nfQWUMtZ6LvW/Kof1RMoaPMSTTb1aR6x6TMT2HQa5wxm13lQT3mly6oVMLJf1d4dZFXdiu7oBP2H0k8dBsDP/LEZ7dYYBvHI3iFxicKVpLLfPKUFbcDxalXKc8vyNkV+MYLHgqQ7/diMTGiGPEzmMmnHvFg1cHg/CK7A+ip8rADiovuLyGS0wt1pMkmoVB9s0UL6oDP1aRXNrqjB6ynWSbbAwVy+ShvNRMLLptCKXmCbi1shSPtPHgdXIQxJ5PAc/Z2+mmaT/oBccEum8OD1znVgpOVmZjtYcv9OXxQVtiEnRftgLlSg4NkMvB9gXHWEwmA86p2SB5RROvwUESQFSJyGRbCDx6lFxhL6ZaNgyae/LhQ0UvZYnfJZXLrHDd8nxsEurN4SWF8PITAw2smcQQp+MWh2N0hp09iH9jo2yuHv4l3U8735Zjj0sm6EbYYelOBlX9rYDXHIC7BAzuXTMOpZ4yIHPMEdcfTID1CkjsD/IhSi4XihrZGH9BGxc5RSJXoAwTVibBxVxJaJhghWXlj5xaOAxKpFuikj8X8GUZKi1fQQpfXKStwqz+o1oJxZPX0zFTPPCpiR4mP0DWtEYVkuUoAtI3LcFnfhi023Lx4JNMMNDmw5jwbZQPDNb+H47OOiyK9+vDdIhBl1KKKC0oIjHPAekSCekOKelmY0ywUREMsAP9mqiIxDxHBVHsFlARW1EEO9F3f+/fe81OPOecz33vNTO7/jYJmdRAUv9ZM08XWUHj50Qcdaid/jzXTS8/4GHBJxaTNljjlrwM2Pq7EA+GWJLxIodNczGAPoE0OTnlEe3P2Ervozgk2BYDvCnCZpcd3NaXgTgm8CfZ6fiBHAyyBhtHN5L7Iga/FrL4b2URjBGtTZAwFyadekl0dgXii/dJ2K5fSNxWiI7pSTZZGsoiFZjC2CmK0KNqjWf+TEe1YX88zmwmPbtK4UDkc/LumRSqaw3RqlrRNhlC+D7ZEUcYAdr9kIZjpclYY52Bt54I8cs9K7A9y0OJFQT3NkVCuC4feJskMfecPX5L/EHG+Dmjmaw72r/Nw4V6OSgCClicMRqL9Hdw0d86yO8aK+wPvU4PPTdBetgTz07LgiP9LJrdCoDLN8VQYY0HnX+1lrD1djSOC8MYyUz4ZOrDnZMqgQofPRgMtQKzRlmgrl/of133qeKSPDB7q4iTN10iayOk4bfMS7JjazkRfncDvsYsNJ7Lgu+oQTL9pwLKZ/JRxtEUI2+wIKF6nsZr8yALimDLhk7yfBIPyx/Jo9q9+ZD3gGBrZyZoVY3D+i0lwGiHw6srOqjftot8/mUCrlKaELx1DRUwU7nr8xLBbjofjq30dtzlORZOMGWg+5UP4+RHGLmO83TG8nAwf82C1LFceF1lDleaTfBj/UHa/M6CtvMy8MuDVJAm5jDqfinY7RP5635DcDmQhj/H++OSQnsqMWUy3HSMg6gRMcfVu4pxXYkJlGUW49woAXPypSZMupwJF/8sonfN7lLPH+ogu1uX0Z+iALOaU3D9oQ7HZZbnSPAdCaj5nYc9rjziZGuC/cXOaHfSnJDjkyBaIwRWluZjqf4cDJoxHsP/qOJUZqz9x5fLuISqdFJVuZGxv9rgELZVG5x2M/D4qzF0bfzAvBdfQ9LOJ+PyliiYPt4d/8kA5MXqQb1/IV5LUMCBI6Fg5dZDstVFrpQlhlzwOPC3zsP5p4zxs0QiMIvOkb8zYqn9vQQ4Of0dk/jkLwllJ6FiYxZ6XkiDVz/iYN1vLSiYIIvBz2LQfqQIPU5bkYd5xnBocTyENX8mkukCUNjiDsH9yaCxLB+NKqaA7QtJnDzAx5WPXjgsQSc4dtoFX4g4dN2AgIlHL2CsjPGE4j2m9oAJ3FrD0QSfOTBXPBI6T3ZzuxabokTWJ0byAAEuOhaXyMZCvGEm3qmMwKrzCyCvIANXDk2FH2tcUCaYgNzbuXjttSmkS/Y4LjgmhKhkS/jSMh2ilcrp+/xwnD2+ABaHsFjy+SM1KZuKHyqOMh1HpmJDdRFeO+cPB584QYWUDbgeCULvg7X0+4zpuJEKoC+ojsYdDmEuoBBOLy1BsQ5dJtxEDNsvBMNAxlQIrdlDU7gYuO14gsxrYMBjWQb+qIvBnd9mQsrsaLCd4Q5zvedDzUIhyEaEI6qPgk3qPDDfGI8jxlPRqkYbNr8shN7pQvSu1oDT/qZ0/kpj2LHdHs9zSFLyhdg3LQDS4/JB74Uh/hZ5MNcoQzWlZzt6dxnAuv9EfmzCh6vfPjLaKwyglhHVRbsm7LAlMLBgFZW5lgRhdc+4v93RkLXmLN1VzsLi+GQ02R8D+csM8PFPKdhWJMpusRyQ1zKF0sBiuCRIRbMGZzyoMAeXLBbHmswF+O4egYTeFVxlYTDMrbCBpWphuMLZAOZPv0uNz7Fw1m+qaL6Kwza5OpL5PZEq6rvB7PeiHM2x5UaP88TNSpqwyaeMzL34mCSITGjeXD63ekQfbebyMce2CL6NqKBJ5Gjwl5qEa76xYFPqiQtD3fEZqoLNWjMyNHSXtIIcqDQf5H5/FWCIRBrwm/Vh4LgVWptlws4rmbD4lDpkmQlRadiZ2rf5wGHj+bRsahbOfJALym7xJL3KFZLLVDDR04/4LfDDiEJ9bLkZQPRD4uFVkBKRuHSX9Hs9oT/+ZmD4kzl46vp98vFtGbgmEti2IcdxeG8mlBuk4banUTDvozydbKUPMccXwOvuQJB5KMTElHeEq5CGIyeVYN7XALjmnwhub2dimm0K9XjgQn2zRPlaI8Ri0bU1NsmBx+/1cGP7PLBeZgq9rU9JbYsYKIg4r+GLJRh9CcZ5nTxwNIojg8FFGGR2kaZYlYDyRz8cpXSWutQ74+IDeqBam+24+Z0aNPa6QvXIHlIuPw58rSKw73YoBK83AUobiJLSDKRflPDSnjBwN+ui1zexkHo8kaol6WHNmgD0mDBCCvz8kNnRSr5yl6nlxMO0fneE6PqvZbaZ+6GszHoySd0erzyaBPseixOzzBXURKYIqvX0oC0nGxfOv0JcX1+iL67eJm4b+bjpgAACIqJhmB2LenksBPwuhZliEWijSMncjRNgqV8QMn+sUPKhLh6Qz6ZnXayJ7yUWNL8kEJv1prgms4q7/jEWU77moSobi8UGxtA0nIPS+hXMoqpMSH/rCxGWNTSmRhea7Mxg52AF42HhQTP6VpAd0eWkeFwR7ixJgLgFufjBboi8KjhC92wU4NPtPOo3RQDL8rYTWFsGLV808FiJqMdXmXEL56fiqsFdJNCkAB7sGyJFr5Jw3KZRSCefZqwu8rE+nsULK+Rwm7QHyPxiUenfJPQ6qgFNbU9JW4sWlX2vC9pq35kbB13B5pQrmM5Igaf7TjKJt3WwKkMHx7oWYfoVV6w3O0WmdNSTf6eb6YUUDZRXGKHVzkvouU4dEM4zJ+7TR7j5vZ64PDsNm5QtwE01D4x8u8jmHQmYfkE0/7ZZUN6gFxg4CmGagQa0mrHYZ/+JvkstQJfGGOL/eD4OPHlODddnwP0aPnzKKEFhmxBsfdzQeymLH2Mn46B8CPrrqeGKXnWUEOij8L4iXp26jNu8r8aRux2OXVeK4dqGTPTVi4c71y3AMEoJXr/Tw/BHxXhqbS62P2LpYPZzeqtKHZSDq4n0QyPstOPDbduVzMxl69puH2aYJQuzocbjOolunk5z819Qh8uvqOT7UFxgx8NbriwsO8uCjJ86HikugP1v7HFM9RV6W1uOJoicNeIoH68F5aL5lhK66GYnleti0ejJEhpwdyLWnwZ8YNhFl+RKo6mMOhT5eYJYYAzuyfTFuimHuaod6dBRdoYUjs6HET8vNDobBPuvSeGo5RHc/vWt9PPt7XRtLsERkECCapibsoWLaU/HRZfcwDVmHBcXM5FMPhICm/YV4Om/LHYOsyg3+x/RvTmBwJ0S7qRMJPEJioW1z1mAEjW43ReNZrkF+HLCE5LglU1jzxN8fHIGnp9aBLaOPqh+Lhz+fBW528VT5G3qQvKs2xw2/LYjoz49o5ZWaigRNx8dCQ997on2pUooaSlCb2Y0tq8rxm0ei+nbb73UsGaI/G6Uh+5D8mCgVAhCHwGOUVXGMQOqcOK3Kp17mUUTzZVkd/poqnJ6E3UoyYKTPSWwh71E5CbIUX7saHzXZQ1p+91g+FoJSKx8Sx5OF0LvblUs5zmCwFiAkBYLRr8O0uFzhuD9fApxVihDqIqFzmojsHZhMWpfMPS7ZMF92VnAl7WGpXvOUq+OMTCLlMK+NcX49bM52ilsoE0vBODUweIsAx7oqwngZHkhavvcpQ/SMqF2BQv3jVeQlGUp6KVrgCm8Mah2uokck83G+pYxKLY0n26f/pA0rxeQE5p8/BY4DVJyC8FpjyYEyfEhtbMY5BYtoXsSX3FaPjwotB4D/EElvBMUhL8W5kByeS8NKJTFgb3JUPydD9JtVxkaZYRHRPOou5+F722h8PaXDWarlcE972007+1oNB9h8WrmbDwX94qxX9JOPUS9vfmfDga1iOZFvywmfZIhe5vOkICuEuhYNxFktUpwQ7IUnrSwxpido0Hwbwx2Mpa4T0YoYu3DVHzKGPy72hATY4WwMDIHNwkfks3xhhg2LgWufGghY18V4lVqiIayxeAQOhrV15wlG0e0sP/7KZKRKIVOL68T1+vxsHRAiG/f5mP5m3D0t+Hh6wIfGIrOZyRHLFG8Jgv2W+VRmbGj4YVnGcwelMByb2MIHvxEnYb54KVYSZaGi/prvBLqDLFwZnEdHVw+BdyUF1HPkvlUJc6ScbQGPN+hgN8c7KBmYDMNTEvAbYJH9Eu/MZS5shh6mEX9pDTmfXUQXqeWoOb7g3y+Ogq/O24hB/YXYn9mLmTyFGBr+xS6y8idfnnNoyv0tPFC60o6f5cMKXvGwvkZU2GCeyeJelUMdm9riORCd/x8joHqOBVoa9CkctYKGJ+ais1ux7kdv1n492Ic2twvwLWpuWAlWpf91jz42j+D/FzBkgc+bczBs1n4dG8ZaB9Qh63mPlj1ZhRMKorEiOAmR8OXSZQfII2lM1zwlbY0/if6PneaBHulLTEiwB7PaL3jwraxaC9yFBd+Adq8TIUxO0ah2NB8LnnbItIzIRnVGQt8ViNLDCrtMaWPRacdj4i8sASaaD7un7CbmmYXo/XWKdDd8dhR/UUtZ7NCEuztfaBs6Wmy4rohuN80xIP3UuDotwBo3csC2xdOb/zOxTfTXWn+bHH0lgzFQ8b2pFEqB5wUQ1CPSOIjyXwodHtLTjVrwNoRbby6IwwqdS+R3jksGsa5ovMmHZB3dgULUS72lhug5+Gj1H+PAd4rioOfnyTAcOZqcmzPaLAaswC/LnNCOzlFx/HWPJzTJ4TFd1isuHGSe/5fFkj9nIPVWWfoBBM3lL4ni8r/WbV21EsgGhvi+yB7CHBvYuzEZqG9VQiGvItwLJ4rxCVD/riHDlEXE3nys0LkQR+iuYyTfLwxrRQiGuORjZKAJyhEZydf2FqcB1tvhKJpoyi3LpbC+v0CeFG6ntbviQXTZX64YHwZ1v63hBjVZ6GBXycXeDsEzJQk8HDbF/qrIxKi2+Xwx6W52JWmRWtP3KVmz0UcIdZCDDNZlO1NxV+ieshSV4CNz3Vx58w0bG86znRdEIc99gMk6GQqrABrZJy90N/FCTb/9MPT4nxYcyASjon5QV5ALec4VIa2Te7wSzCGvh1vCY3f0zC0TByVNIVgpcoDj6IoGNspynydIqixFvmTFA/jjYTwxqqBNs0SeZlRBpr3Z4FrRzsxOz5MxKV94b6FOMgenQpRkizMK6gm8b0xyMZVkbuxf8nnD6VQ/9UGrH3i6W4mAWZuDUD13SycsPSFs0f4mC01ibg/F8NibiH98FHE8GtYfJsUhq58I3Tdko9cSwleKJYm7Tq2kKRd6UhfWuChsGKocxHCKk0Tcu1bDu7fJAbbVHNR47EAHqnaEMN52fh6kwDX/XefU8sohs0W4nj5Rxk+HRWDiU1SON4zBDAoHd65xsOofAZ++Ijh8q45qHOlBBx90iF8Yzp+13xAP9YLQV+sCKdUZOLr0VlQ+jACi5oP0ms8fSh/rAarmx3w3JUFxEFMDFTeyTKeqklw7oYnLpmdjOZx8qBlVEc7hQVQaFWE5xerokmnKYPLpKnKYCNR1TjHzFtxk34/UIRrMz7SFaFhIGEYAjaJ92lhXDHG/WNhEz8Gpfry0IWvgU5nrZlP2skw3q6OyRd6MrFl3bTlg4hnfKeB9a1hknQ/HYKOukHoZB4EP8jHemcFKDhnDMUTcyGwotuRtIvD3upAUJhRSaRml1PrG4F42yaHft63lTTPHKZFZkDuv7CGwnZTLqcpG5Pl7lD3Sk0YCcjGV+YT4eCHJGx3vsbdDRsmc2eOxsPBLJbbhWH8RR4a1g2RHlUb7KyXhH7tIoy+zcepHgIRt6RiW3wmXFpbCjoZQjibkIHH1Q1BudIDp61KxrwFHnDDaiLkiw/RyGIe7rO8T0w2/yHpK8vQ13wqNnYnUO/QDFxklAqHpNzx2xkvjBzxhN6IbFy5ejS2Rh8jYiUXSP38D+TMkivUauVu8vqqPVTWF+MkjUqy+zsLPLs0uLd3gDzpqCVLvhmQbb7JsP4ki/P2+HGD/3teixUivTJI10nKQ+PXcKIgGYK2dSxaCgoh6uNsuvf3fHzwwZ6S3eUk/1MOHL5eDPCvCEnHe/L5XA3xs1WBlmmDJPExC8XmdW1d2w7Tmloh2rc6Q+6zDAzYmwT/zQJU7SX42dQJxJpZmKYZgt7S4dDp5w+Xtr0mW6vf0/oL6qA+T4KOxAugYoSPaesVacVbJXB2SEC1DBabv/Jh2Zlo+ucUH7ba6qPqKxaG/81F/W4WY369I2fWHaaHa7Ppjm8cXXpaxCldqah2wRLkT8RCQEow7tufhik1hbjjuaiOvRrpY/84GDJfT2eN/COyse9o061iVJA7QlNEs+flPxYXp4bi6dAInKgYjVeX76Q3JinAt282uNUgFIicC+f5VQKedbFw0VkKNrYPEPsCJ/w8qIBUyhlG/hNt38efdX7/Ido9LwKezDxEFBV07X8/XkWW/7OB7oFyUrHmgsPddWdoww4WZaYOULEHv5i/wRNx2Oc9+aMvwIsLQmF+gwA0V4ei14pQ0JmvgEcTS+GJ2hjgsaOhT9SzSXNcsYP2kTPWD8myM4lw4fZMnMKWYO1JISp3poL+mWpu5u5gSP0hhJedaeDz2QQOeLaROxc84Nn1GDxyMwI+jKjAio29dFtnHFQYe+Ep39bWoyKO1c1TJJEjhfhkZwBqLXCAFW9Ffp58gKaeMCPdd7Jh5XcW74+LJhEDAnT+1kMkL11issNScYxlKVQ9K8aks864VWuY9KyfA+ePG5K4fQS+hfCR2Glzzu0sJO6NwO/ZQlzr20aehPbQUatdMXxHHraqv6YzWsrwY1kh6i9zhYcJPnil5Qvl5MupQ4Ue6hiXUPW9RaBl94kOlPSQoL+TIbG5m8SbFKG4iFXf3FDB37EZ+E22kv6T8MIMHU8oZdMxKbKOngzvIGeP9FGdyHng6zuBWRk+C6zul+EN7W56+cBOqrBMiGx/JiTJ56Dz9KmwOv05nXTVFSczJdg00x3OFAjBP3oCTjLVg5Kv8nC1MhEt7m8mBvwHJPXmePghqht+gSJJ0C1ACXcB2DwqBtNIZ+i+oIP384XYUqkC/52ajdGdJ4iKcRLEHVOnZb3TIeLxfWpTzcLnkRI8fE+Ixvdi8e/P2+RwtR3z6C6DrY0VrU5XkiBPrZ/GNcfhz2pzRlrE0C8e8KA4pQjLyH2yUzR7GkUZ4KRZT1b+l4yengp4fQ8fa1ztwXv0BNBmxdH3IIufJxai0qIVdPOqYFAM+EkbBOFk0Y57tFL8AX0UPB+rt4bCu4/pGCo6l59fzhDZ6pnocLQYIo/Zcg4asiijzKLe9jFYMbaPmAhNcSJ1gmyxe6R7PR+v9oXhwq5Z6DIcgIvD5TFzRQgE3bKG0ugM0GpXBf1Nv8hzLSEO1HqikvAAF22dATmQja7xd+nBqmn4j52Oe+6J4funYThnygMCz2bgBhG3zc77QQaNszD5r6husvIw+VM2akao4C/DHYxp7jgidv4O2dsbSl7+p4s8z0Lg+8Xin80CKJIZjWPHm8IfxQnU+G0ZXO7/TFaHROG3ThanKaXhS6kSFGzyhdDCDnrTLROqvsvibgVPevC+M7oUGYHsuiw4qDMFjz+zw0WvZuJT3VG44Y4j1qwuQrpVtJ4/FMBzIAUb77STaW+D4W3wE3prTiq+v+WFu7Or6Nl+I3JnzCT41i7Em+pxpCWfRS+ihk0m83C53BwMu6oDA44rmBGrdvr63mjH5e7RFDT1YP7cAHTUK8KMcyyu+qGLPofDcEpnAzlp0kNPVF6nVa908UNBF717NBctS8PBYt15si47EJ18n3D+W3Xg4ElL1CBOuKlSSJIuaMPWwGP0l6UfFpWqw9O6IPrP+Th94iROthqIt50IjkXPD+fonedZONnWpiVNKw3H7pmF8iu0yIeKZJi6SB2XudSRhDwWbosLYXVERQt/oiY3unACtP/Og96Np9p8fM+RLtk0eoD3iAnf5ER99J0x6Uwa6E/XxQI7ZYgwDIIpgli8eJojHdX+JFB9Nbm4NBNebtHGgGsJMOu/s5SJNofurEPEvJGF0WfG4yuzEOiKMMQjd8dg+hoWPlmL3HfRRAw+cpVcaB0Lpuai/po3DletnIfj5M6SCU4C6GroI03dlXTVksn4UVy5TXswHkDEITn/OYG922lG+zQPJVJVYXOOFxTqeoDqSBFkhLZz15ORpnzrIxf+k4X5Q9F46o0Fiu8Nw4INa0nIgMjxzrmQhfd0UHrbLhpp4AjuaTmwregUWfZUwKn/73mmdkpunflHzybqwmQRt1dpyaD/7RRM9mLRJUwM7jx+Rdeq5cGJS+YgeTMSku6yoC5hQa+cX4CqBp6oYEAp+LAYAdNh4sI95PZ+RVxTLEtN98mSMTnf6SFIwb/pJRi5tgS8tPLAybIIOp0nI1c6FfoYM7pcyJEcTz3ccjQMZkiwaDXzIq2xc4EzgblQO+0DkZcvw+ErFhBbtpT4yAvhfEYw1C3KxfsmYuBLc+HyyyPUwWMzmW0+H6J/aJBLhn301Z0C6AmUxr6WF1Tp7SS8fnA6ZYtnYZoOC43dOiDhzEOeOg9X9jmgqfMRcvldAZh+ywFL00yQWvmFVh5eSUiCPb4wfUjOmClyXQ86aeiaAgzPKoTD246RuzcBly4fYRSeFdLSusOUH+IB99+UABcp6o+n1bTvlTFOmhAIrUtv0ZALahBzPgiutMyhY2zjoeqQMqOou4P6usyH2x4lcOL7IdJweh93W8SMSvJT8NnZI1RuiI+D+y6Sd88vkp/VhfBIxDpjl80GiHWBTXQ19VsQQcomzMDMzRbYHXiIvtgdg6fXCqHrciJsn3GBXrl/nTYSHvaOo3RmZBZIYDgo7cyGReY6EC01DvtbzeFUyAt68+h++vbIf2T3iWSM1y0DB+8cYvTFjW42HwNDPhmOt4xOER0JSegY6hVliQrZUNFPVu73whPShZB1UgDvpvHhldx/NPnJWGrdlgfVciU4S6WYmv0SuURzKah5/kfXjf/FnTCVo07jfNF5ZhnxrMhB+wclMDXLjarH6+PNxINEqpyBoYBwaDowRNqvVtG6LFnoML5H62btJI/TZ2DnZiPw+pKGnkcmMuUPWPSZNAv+PovGyZ1ddD89QJ89K4Fl8V6g0gFwU7+Ne+X8kB687YeFGjX0ZZ8Ab1UU4sYzx7nmPVOp5c1C/Fa4glyPc4IqTgipWgfInjunyBRlW7DtD0C10Yq0+aUADWIAKqRWk23XotAzpxAEXDCUxLRQ/0NBeGDGEnrlQjtZ92cMqhfU02wrFZSsnQiLslqodFYB86LVEFU/jJA17Yto/ieOcuFGKMZPhHt2o7GlawJOeGKDAyUsnuh8Rxuu7yeBP1jc+rCfrNgngFPS5o6mFzaR+CP/+90kiwx5riEtz2RI8q9Hjp/dbPCsxhHqPq4EN61bRvb8YWGW8X6qcno0rj1xhBRo3eAGpHW58LYi8N11l5uQFUGnbp8Drcseko6GKEzbEIIrMn+1yX0sRPvMYPgx+Juu8qkhSant5PfvHFIvyMXOwpOcNkdgxi4hxGxzIy4vWdgyEI59jlF4/0QW1gVWkhNa07DqlA9SJxlk9lZT3FoGK9okSaKnAtxqU6A1gom0b3cdvXV6Gj3U4gYyQyzoPRRjbHbw8ay+Irgop1BVp910OZ+Fx382ivhnFoQvrHds9/xJjuhuJ9zudJwaFYMyf8aTnUkJ5GRVEE58mQgtHSzmlnpD5cfN5KMoT9vKe+mCeRvpix5zWlEVRn9vL8IfYuXk3/tgLP1kDL+HZeiOym56h3SSVK1gcFIsQfk/OZBRz4Baez1V0NXCgSNV5MfoNJzbt4u2pIzhbulHEoXF6x33RmiC9tHnzEBzIi4ZnkFSnjZx5FwynnjGwt26Mrg76IVZtQIIkamixw9G4L+V92jUhTKwOWNMXtwPhyWnUlEvvpnT3MLDcjlLHJWRjqN3inqnqRCeVObBdvkQ2FNwnDbEbCDe7nM4+QhJXHm3h869NsS87JkP+j3xNJatZIq7jfGPsASmR/ugoN8Qx5QFwZgzU+BrH0v0O6bg79Pr6WpjX/i9xxof7vTAT9WrmJVf/jD7frCwa4I3p7OznFv2KxedIzTIfONPdPnSCaB8wZ48cmVxKLIMZoxbT4TledBY4Iaz671hwp8R0nfUFWs/udBszwHHlRHjwHCjDryRyYE+23VUPv8w+Rt/lF6b1EVqGlbTqJR1lM+sIHM+zcOy4ZvE7P5r8mlKAX65lIECnlTbx9n5MOr4Sc5+yBlKMvdy49khx9eKbvghl0D6YhvMx0oyNWwyqr5LxIRGHr7Qv0Vi55TTUD1lUD/g37ZNKxe2XxiisvZBdGyuONn+cj1VO8JHqaBYnFRXAGUalfToJiGeaizBCWpbiN9AM1HeZApZ1ZYwbVCOOdMpD1X7YuH7zQKcdqme6DkqYXZjFl4dsaNeYXNpdvZakiv9iYbdEuB+QSS0O+0mvzfw4XlpBG5/GsXF9LZwTxew2PrAFgfuScI7bx9cI6GHVT4CeO82G7sP8+msH+/JyBCL5rXT8bX1SpJ37DLxMj1EL/RtZvouWmDAX1/411BOlCdbgFZ2LrLaY6nYirf0e0oW7A/nEQkPIYDC/95HLov/thiDiuI4NGwRwCOHdm4v5aPQpIHKXCiBjRIBoPSeRcJrJy9krXGluQwNkSmjckunwDOxfOxWyoezP/WgT3Qsd89v4tylBpmxB+NwW4SolobN4O1wOhSGFOJDDV8wXpYClSFz8e2JEnK8eQmV70nFT5MOMJu+qcG/hdJo8a4Qf21WoA+SrtJyfVsgAXvonLtl0NYUiOIghEmpr5iUgfkQG1tMUzbnYtLnSAwe5mNjjQJohT0hfmuGSfpOfaIUzicbfpahwjve/95vB6YF43FfYjqsVSvDKHtzrlmhiNQ/8cM7Rb54alE7WdovBrHmQtCJdAWDGhu69JUX5ClkwfztBqgeVAab3h2ka0bUQD1ZCqpPRODXpgJav1mf2TR7EmaET4J58cFg+3cirDT1QmU7b3JVtM+VrCq2xCNJ+JwGwVUdbRhwnnFAQ7hsHgNqSfmkwzgcooZGo8aXZHyn5Qf/adRRva/P6HGpWGpwzBs0FlkzlyYLYWOVP/iv2UDOWnfTq9VZ+Djyn6OmYh4Vm5UOTa/e0kR+NG0IcoOlwcVY+aEYTjTlk+OpC8n+TnGYgmrgFWwGk5YqQceZQNRPs+Ry+hNRrC2HDGnbEWreRKd/YaFQhQGvWQyWCm2Yv3JL6Rn3yzRzOAm3k4d0ZpM4jF7oA9cDWLy1vxBG/Q6AKynZ1Nt0Jh7xXUwTdi2iATEzoD/LlFht4eM+97UkWSEOj6QLIXoxi6E6e8jE1blwqC4Z9eJU8M2ne0yZShaZasZD2frzbf1L47HezhTilmfAaGEzqS3YTIfe89FEwh1HjOTRwP0zY/KBhbVHRmPlAXVcFJGJY3ABPXTkEl2cf4T52mgC1qMy4faBINifHALJ+buJ7Q5LDOVqicnLr2SN1ikunSmEE9pzOU1+AXQedIdbxr7k6NxRINbuSqSjlKBPLBWsxXrbahRt8KbwMN0nf4HrO1hOtorWsfbjRipeHgimwvek9cTjth35NuT+bW8q/1b0GbOZxH7PQvXUBEiRjsSXlxTQuvg0STghB6cfEWyp/kE+Cr/RpgRtCBoOg9dSOTjlXD6287zI3mUiRtshAQbFX5lAlyjoeZSM7fVZcJk5S6/8lwPzvvEhoVcWC84Z4ZeEevLlBQt9uw1Axt0ethh60hfFx+jURi1o7c6G6Jp58PrLAGfaN5mojLlMArbng9y+NFy2hcW9E33QbP8WxxcJcbDwniyMzKggYdfcyf7pJXj0twXkzxmH4clzYIa+O5hQFpU679DHcxRBqK8AllJ99OPd+SirV03tNSThyLRSlHpQgv8K3eiVLMAb1rNh/FozjGjngVZxPiSvYcl978dE614J96Pb3D7bRxVaVpag/XZlvDGnx2GHZyfztt8MKnVdSe4qSXKnIhEk5sWBkUUpCAcXkefpyRB09xUd9XsqZqYp4O0gIUzvFs1Bf188bCyGD2emwTHNQ0Tu4mz6MyMJLF9nwZjscvL0byHcdzHHlwt0Mcx1mLyO5+GiS2mw4fh5MmLDR8WRXExvjUF2vCUqV/MxI9uZlMiJrkdRMci/ZHGxbR70WfFx+VRDfLjlJ22dk4Tv3/Dx3AJ9fHwlEz2+smgs4iWzhmE6Jmk8nNZwooWCNubneAFs6Agisb2ZILe0kz5TSIDxUeJg0d9Lud0Ebljyca3tPJTqLcDBMxF44Agffo+7Tt4jIV7L3XCfyA1+yqWA2UAqHB9SBFvpteTPgDS+mR9NJkiU0xa7/VS+fDrePReAjcJP1F/dhM71ZlEhlaEX2gqx0p8HXev88JqoP7L6WXCaw6J6xiio/OsK7xLqaMM1U5CYOgpfTNDGqrtazDS3eNSWHw2qio6ENm+jsqsLoFbjEs0V1fKpdZLw+oAc2XN9LBo/YsHWzh0iLUthxcnTdDW3n1rclQeTZgeoGhgioabjqbm8DuS9TgfjVXzodWPhVtpzamSfCy8sVeBCjD+4FQjxX/0Kcqlfn346vbZtskYSxKy8Sf73m0/iLS1SIhDVszAG/Odu5DKlhPA+W4jnV4rOfaEGBi+sJGVdSZC/eSZ+1PBmEjurSdIqFo58EGJYz3cSVaxJjYMlYKmRA2y+n48fJ/PBjTHAOTb2uOjUeEi71kobeOLYObsQh58uIm9fKkPtrmDYMT2HLKxVAFcdDXLmWg89oTcFMuWlYe2hUpBN2Um03QPhznM9nD0lChd/sOUeO4TTzZ9fkJ86fnBmXh+5axcG74JY1G5XoxNv8dHUQAmKYmzgSmwpnHldDMT8Ilkobgi2ckKc26kKW9udYFBHD7wX5MFol3pOY4EHZv67Tp6kqpKOAkss7P1E+NMjcNEAD44ccwDiI/Jt8alYEWiDcaZZuGhtDjz64gqbRG5z0U4Fr8Qco39/a0PVWBW6+lk0OT28hNBb22hBjwBaombQtMMCR7dfpiC5W5zGvSmD3jRdsjQuAV4WFMA9sTQcGGZB00OANqeUyOieTJA2GE/ijPiQb8qCd9oCfPpfKazpt0RZKQmgz2diQaY3bvtdCOZ/5kLvdiF6bdhKHcYdJp3hivTm6GCYPcOaGSvK6sNPWEiJC8T+nfeo/IHpIPs8E5Jk48hX2Qy8O4cFlVFa8FNvDqrrz8HwcmPw+DeWHGamOgRrL+N2G/viymsJMOKoxAzs98PAFFs886YQ1l1rIDo6HZzjGQFkT99LNnBddN/kQNj+tQRP7x1Dq/2DsHx7DVVRdIfTUalQbhkFO1aIdFXZAcpOqUOItR+6OwbBTK9c+LbOw1F9rxj5ELqG+FstJ2reo0lDcBYanb1Mjwb0EduoTLB+mASu/BFy6QDB2/nf2/bmsfD9ZBLcXhEBXSGbace8CTDhJIsW7+0gaWgURclE2iD8Q+Nml+HXS7mwZribedKejy16s0FzcCLa82LxY5oWM5JnD41/VVCmKBPKvNqI5Nt8VN5+hzFiLUBntxpK+l1xtNkpOt81Hvhr023uwyo3HDlXil9Nh0heYCQ+SlQGj0A/lJgugC1SLpBl4UF6X97kzKUF8HOvMx52L4bqJyUQfS4PU7YKIfSZJ3RwKdiYdJr5UxyNE5W6SPjyPPS09YDM7kicEuoOy82vM1IWj0jxvkDU/PCFBve5gZQqj96x2EYHNX2x+P0bWtetg6+6Dzk+6PcgB36zuPbsKbLHaR5NWZWJeoVXuAnW4qhsGo+7osagbp0f6A9awEGHGByY7Q93Wgl5oJ4nmicJEMpZtD2KKMCQ+eZ0nY0QuiR7me1tl5iEFBblRmXi6vocuNgbDCeu8CBvkgnIhQRiwWsA+fQSnL5pHvxziqZbuubipkmT0U7tMVU/HgDSUp1cgrIurtctwv1PnjH3qiNhT/B5slJPE7S85oE3j4UxXRbQx9vNLVS8Q9+0a6KwIBfh60zc/DIZQ3zamZ+zaokBx2LlDwHqdMVD0lVlvHwjFc+ah+GlR0EAW6fTzUFpuIDRwINr9tD1njz0LUkCj15ZpnLDWW5ndCGk81kcvNxIruSkwOMrLAp64uiaXxHAP7GdKD3ppKd/++JmHkHn5OetfwYAz3WOJf6vI+FED8fUR7vDETtRNnkHwfk9+nBhmzg2/mRhWn00vh84TGs6gsCnaQzRnReNOq+2EpO0HdSwjIe8bxGI+q1citQJjvtbiNYTCGowJvCz6DW59tmVHnCJg6yxYfhpmMU61WxwHJuPhepq8PxWOGm78ZU0jZ3MnU45w7xIikXJgRgUjPpCTTKnQ+wRIaQey8CEiwnY8VSIH9rCQeGIC2xPFOJp3TnQs90c4n6awdWG+bDpaCOnJ5mMx7/5o0pqMfhZOIJmqR4cjRT5k3UymLXb4443vXRToyeFHiHmfM0D5lcBGMxPR/7uuXjCaT7TcU6OVq0jKLFtPlXjuYL5yb20NdkSDvNT0GKnPo1ykKN7pApQI/3/7ycCYX8J3B5/jLu74g0ndSWOo2sJaL83wYyBJIyf4wtfCktQoi4Cu0+lgW7mN9JZawTOFXYYsjwWqz/nQxuZBcOLSsB8VhY4zotkHJaVwP6wL/SO4WQ4FvGW+E6yx+M7/6MOdzIw+Fk6DC0vRLvFiwhf0xnHS8wFlfMrSIn3KLS3cMOIJ/O4xUMN5MfU8/R8jAZkFP0jX90Zcvu8AI7eyoSZYiUwNbsQYvRKoC5FG+kxDXB9s5ieidHDC6FzaGzaXEaqQBU1j6+hnbMWoFbZKChyywThtS2cTls2+ozTgbYQI6x+YwGHDR5SbYldxLfdBgclcyGjsAB8P3tzqh9ZjNK0pNwVAxr7XBXaVQKg6X4haopYuzoX7cbdkSF5s7ZTv41ZFPc3ObDSJUh3LXTc1GeDBgI3JiJDAfPq+YyKLQvVw7rwvciGGxb7SHId83H6zQnYt/oh8cjkoczGl22PVtfSP78LcVZtPr5tXEWfyDhx/c6dpKrcnGhd2k6XFcQQB1GdjrVyBacHBRD1vhgPvQ9H6YZJGJQ2k6yOVsOYmdNwspgWvI4rhYp19ozbQBlMfr+a7B0Kx+P7Ksjnu4n4M8wLFUKLMOxrOg6JmKbyYQtZ9ugsWRM4B20Pp9DOP4rQZzQVtbRtODUNW3BdcI6sORYFhNtBel53cnvYZ3RHUzFInPtGy9YHYcMNAXydOA4+K+bjpP2bSXhBEb4WHqPVeyyZyg/T6YJ9n0lG3UYil9dE3n/zhO6/xWAvXoqaz/XISttOEjugAcHNtvBdxCxPVjMQ+CqaKVghgBEzEy7pWBqa7DSBhioW5zdeJ+0yeRD+ehoTYVkGS1fbg5KYDdkef59enBYGm1bm47mNZfBsichTyrUw8LQhY/MwHPN9CYy5NZN8KXlJJDzzyNK1fLxBeOjSK+pFBUskwSyeas8GIzt/FNblwcJvRrhg3m1qPudX68FaG86i1R8O66SDwfJExzcbpuGUH81k2IYPGlPDYIgtAVcRQ79VF8Direr48t4UWP5SHHULWGiuvO7wXd0PbjfFwMfd49FYahFTEsfDXCYEGCUB2i86Qhx+5qKn2UdSs5mFtMLttCDOgdQcL4P9yqPQ7Nbu1j7R+eY+K4Qncga43OMlE2pA6VNig9tvinJKezuVXa8DW0XzzM+8GCyW0ubjGkep9NIRbp78F6oxOcthjUs5mTqVD/ucUmCD0JBM2a8KI+sV0DFPiK/OBeHadSe5Qy4jxNIqCW5pbyeHv7viapf5eG+EhdZ69VbZZUb4tJCFDV0B5NDnecTkBAtB57Pg8U8hBAhZ3Hp1iCptkMbf+R7wJ18dmvfzUWXCDHh49D2ddZBvr5cpxPpzLJwW1TWTt4Tw5v4jqx8akqhkFnVlnZCtj4WQNhYNqyaCqY4/nGjPwqbQFySyXxFttpy3++4zRFfYF0F81AgZXzebCth31LqVxTcnWdj7Fei2BSxmPHhGHfvek5e8fFhsoIcRJwX4cDILVruetGw5OUQKp2XCYdYein4r0saXQhw7byEJfc2iZcwb8vm3L3T6KYNJ0UryT+Sh4k1j6VCGG1l1+jId8fNwqMn/331y8fDcKQRDJSjX1KCDl/2SYZVoVqYfTAHVSQ304HYhHFu1nNmQnQ/W1u3E/Xk0yPzWw/3Jla28MCMIehII0HWIVk6bzkn9OUBCd+rimZ8sltwpBbn0ePps70VCfcbRVml/+PfPB7LinDBikA+Ohq0Ox0+V09o1pvCpWgi9G447zPhXAprDk7C9m4WDJpPAc5ItSATp4q6Yo7QlSB3i7IXwoLOdGu2vp23PXrV+0srDzi9qcHWrDN3pEctYVc2CWpME2FjL4oGsEcfwX2WYsZuPe27J4J/r20h5fjr0l7Cwf5YFrpsr7/jwsy9+txwhVy9loeneCzTZ3J36V6whqS4iVl6jjn/lRXPe5xD9GDQN204KW9bpqANbuYwmh5aC8tkpXJurHePFu0LWdypBjfVR2j8hC3eI2N7yuhCTHxZiIF8Ve68ZQ01zIM48t53OX93OrMoMRv2jS+jJmebMz5/hEJVjTXYveUf85tXQD+f9sfawPOlxFAJfLAuHnK0Zv9ZS8NZZgNErNHDCuwfU4ZQPrFvmgE5HrLmsTWkg57mcsdhagnoz+XTOkXDG0jAXiioFeNVTAsdeFgd6ay2xTeNh8Ct/DNwdiFOcdxP7t7nQNGkWc1x2Orlx35Qb6F/cemqyD2WLYqCwSxp3XeKD9gsxYv0pCV1PzcBfInfVf5mPAyI3/PSXxWAxQrW2MtwoJ1v8uEaLaqiVwtlR43HS5QT4339tnXrOQ11tG3BMF8JsqSLsWl+Kf0XbGRxNRqnz9STxhgcmK7ow2/vK8LjpEirf/5oe7VbEqQ9PMXkZgfDnWRl4/RXAwX9uePt0It24OBWzpPKx8vpotPjgBT/HZ0BtuSd39t4F+tboE83aOxvH+MmgU5Q3tu/rJBqXS/EQRMHNxloSMAVg1BI//GQ/BxU2BHJz5EOw0+E1d+aPH7O4TgGmXFPEvfP8oePRN+rvkSvqTx4WbiiBfKM2ImwoJ6ZFuRCmbYKmjBoqz3pPjJ6y1FA6CGleEDfnk8iZbxvhtA3nyZXyfjq5OI3myFQ4JkMxZhtW0rAnBCdIGhMpp7/0lVMtDXkYBq9dWXgmnwiH3oQxjKsqvOm3RzMpAf7b10Un21jDpq2lsOxeIby29Mfhb3z4fsETdTK30ccbrzjceM7C+YldVPhKvo2NieGyo0/QBN8YXDpYTz57OOOM4TL4oHGfxvoW4Nk+FtP3pKGk2gKUFz7gLvjtIq8L48A/j2CoRjTa3U1k3KpKsJem48HoUHhzdRY6DjSTJ6wfuh3Igwk6bmiuowCkrIcTfrjBdN/QgKl2ksAFxoI2DYdurzSuKoXg9cwCLNeww9onlviBr44xLQDXP+Tgue0xtH8/Yz+hMQDebqslwVpzQVlx0EFWzJ7wpi5jiHUTM/U/eTLOkYcqBWK4VjSLQ0Q5mDhqhBxUywb07yV0gQLO7UoGvy1/yIF/+VAkK4YHfASo/CMZ575r5vx0kuHCZRZPpBbC6o4J6Bv3m/Y+EOX0xExUtEsCnctpEBPoythIGUO+aKYN1sfg34q3pMzNH/z2tjGMZyyYmgpQShjE1O1WRo8lQ2R4zk3uknYGDKXzwP8NHz+LGGTPR1/oe5MH7T9Y9JWzhlfpY8iMaOSmWB1nKtYV495tPJTOiMUhTRZWVAmh6HMq/tY1Bu2dBCquxWDChFQ4neUH+dci8N3SFKwozkFp9fPM8cXuyClkY9+jW0RKIhzNX4+HxYe20ewT86H4poDh/1pGcwZNHU0iS6Fysir4dHyiexa84kqPugC93sE99S1E24gglArzhnvKgVjqfIRO1IjBr6FJ2D9aEXPT62jd0nhw+8jCmYbN5EGUBnNErQxMhKEkvuIiY9+1lZoUzsafHml4fpc3Fi4pxfhgE7jxLh42V+Zjk4M7/BkbCjtqeDBwwKHt6Ql52Jokhb0uRTgIlzmeky7Iulvi14eEzj85G+P/zcQ7It5ZFBuLPy/k4KkCIZ5XWsP1lS2A6T17qJV0EvyHhjhe3hbif1xlIhNZ2CmjjcePsXDA1x4fjvMnQR9ScU1LFDz9KoeNF5OgxiYVy1cFwbPnmti2hoXN1VnQM30d4Y7f4Hj/xODvNwmc3bBOlD8FuGQoC670xNLd0hFYLpq1XiXiWLhxDkbFWUF9/1zc0hQMFnmWzLvWMhRPu830bnpEZ4htoSpN0Rinm4NDmXOg+hcLa6eFQjzfEv4zn8WZqUgSi52y0Da/AF/U9ZFf+wsxN3ASmBnc45THClAlyJ3uLpgGLcUZ6PFEwIWqulPOJRjLX/KhoksVqsdnQeuU48RS5L8Scvu5A7EqEPf0JLE4rkAcgy3w747j3N6Ho0AA89G3KhavVmfgac8AGLGKg9tmWbjR0Q1ejVUhG3l5mK+lgtIH1WFdjwDvjxpDk94kEsmBRselB3LxeUkcdrbEY/p6ZThgPgfTei6TTa/LYLEpoTe+LuSK7P2wYDgVB3kFuC1VFbKSxxKl9CzgVSwiJcpXuHuHWHxweTc3ZZol9Pz2BqnRC+mv/pmYZ5WPa3ctp2Z9Kji0W4jbdj52jK25T77iOFpiEQRzs0+TbWKZcKBZiMs4H1ChEVjPeaC/0Bt/fI8lujEOKOEzQA+N2kyTbeegtnQOlGwWcY22Ern7OKHVx+otCW5Swz/GApRckgP6k/+jh2Nd4dP/7oW620DqtpWQk/kG2FcR4DhN5jLN2m2AbQ1ScKtAme4xD8ZZW0eD8KQRXhopwy274gjvqgBk3zvhB0lvPDnDD7WWFMPkh+44LNxLvGXFyO97sjhWXBIm31Ahoe2ZONXXHqz+JkLSzBK8cWc66nbXcpqJDNQoC+irPBaePJwG5ecTUcZEC0pmToDV00rB8RuLqcZqdPNIFF1g5wuxVVlk+b0CfFNWhqkdheh20w4Vx1wlvqWxaD1+C107WITTRG7llJIJOq6dnHzFG7JliTrJ2WEJphgJz//3HtIvfJx/rJHuWPyRWOjOhObFcfDTcIjkvinDspUsal0tgDuzBKBhm4bTripD82MN+uIiH7YeTQXOOY+4VAuxYa4mfJRxBbEwa/S+wgerpacZOZyMoxwEqPeNBfW5HqA+3hLCPwnxoq0WqU7sIb+VeLC+MhEeitYkPpuF1EVe2LPCAu4MWMCZhzqY3dNIKjRzsGZ0HvBGNRKzE8NkXlon6VivTWvLJTBagoVJGVMwTZTvKhnfCff7Kf1aZ4JrQzhGXzMPCvME0PQtDNMX+ML6y2Z0/RE+slcN4Oi78cRwEYvGogwldc30+AUWXRd/pkrjSoBsM4LEJfKollaC5n1akJwwC2dfUIYk6cmoompCZoocJ/KGI33UOwFX9wng1CkB3P4gxNEeLI69acYs3jQRF8vnokmnBdQIb9G49+Nwclk1iUjJxUK3eBDa7SFuUwhxN95OLl0rAWHtbc56dzYu1tLGwsV8PJavC/cD8vH4yun0gy6L97efp74nqtrGz+XhxTMO6DxoCdosUItrhZCplQQTwnLxZUc+5i9pIDPzWkg80UaLxeMw/FUn3VE/BhcY2dN3cvVUX9oJU1bdpI+uFGH3fSfyM8ENbE1CQKpeAZ7MLcZGiTJYoxUC0tLqUD1xKXOjMwi2LXhP7lhcdDztqI1Op3bQnR+E4DJXSFiL2bRwLuWWNbpBa4clfHApw1cpNtCRZ4OhR5XwaJ0xyIr8fmiQhcNP08mh7gDMtw0nGVY59JM2A4rlLuSmYRwOv7eGHssYdK7jo8mHUDomUgFrPMYAs44HvS3hQL6KQcuibrLnZQFeW2QOBvZh+GhiLq587Ep5AUnIb5AEuf61NCFMiFFcNn7oyYILfXIY2ZcIvKe9dMRNEgP2acGqLyye/1AAbRkeYHJ9Bhmc4U76dkvQWa73iWMrCzWuYXhF1Q99mieRxeFS4K/fQ2R85OiXByzmljiBwnAucm9L4FHsMRLp4gV2qz3oiWqAwzt2k5M7e2ltWwyqyqxnTM6w6LnqH31zMJ52bjcC972vaYJSLJyy9cWpEbPg9TCLDWbpmPXck7hMLoGfBhFkUoQHvv4vCbcfioPndxaSSI9vxLnODceuLMQHGoow16AAVgmU0XzsSmKzfisxermRWWXvTZvljxHt9q1tBu+94ZZCOj7bHwdK5smwQ2qQ9j9nUT6rEEKeCeHsOSFseLOBPq5dTBfoz4GGecGwfr0P+T+KzsIviq8L49KlIAoondIlCEjsPSChtHRJh3Q3uzsGWIiF3SgidoAJew+iImLjDwEDFCxEQbEF9N33L5g795znOc93PjNzrS/Io66+IF2i9ZAIP89Ep0ubWPSxGp6pHCSu93ywKUAXbdTuklQuFw4eoA5FR60wYI4//hmLRe57L1rA57yDWVn0xe0y3PHcFBuVvPEaTxtrf/fSjIEqmnBdlDyYp4Idy/Pg8uZjpGG8CIXDquh3L/urS9gZJEhdEq+MnaTB++NYJk9s8M2VLBTyjoNb6Q/J+l08Irj1GRmexsFW6Sjw81GGYdFC1Oazaf2CKLjyMJNWKN4mRa+l0E27AGZFOWLlrRAaKVFFfp5oo/pHlgAdCcPhkLP0piq/Tnv9QYTPUemLmunM5aE4rpNNCm3KwN5LCxlXAZTNWoR9mVYQHMqgfVcz0RjiQM8IAw639SBVYDZ0LyoAPbdk+lPYGQMf7uBplORQeUFr6OmwxhtZ81FVeDFun1IKfx5IklqxBvpg/Txw3BjAZ5ZSWHLnP2LL10Fj+E6i3RMBxf6pYHAvl2wQsKOrtNzRwCoPKmeH4HD2OaIvnQMbxI+RzTvGWU7mbaTkZzXZ8yOGDHNzwCeXBU8XBBGp196ooJlPa31T8XRAMpodrGW9UBXBRE0l9BIQxc4f++lEjyAekLEifzss8FllmsPTpZ5Qv74Uaqy4qHdSgb7NLyANkmHYvCcXu/rYOHFeBn3cGOzoskXDsa1EnGSih7QZdtJC3GO2gAbHy8KsY7mg+aSLkAU5sLu9kIaOqvCqxUvRxv8GcRpVxMHsr7Rpkz/U03W05GQBDiwrhmijTFqc2EY/OG4hV77qgUazLawzXIBzVYpJqaAP/LeSi6q3fCD0rjZGsX+SX8ZtpLtsGdH5KI9ffubgxggWTNsri8s33G5+NG5PfhyKwx8HYujxrBL6tzwHxvmzq8LJGQ1cTFAim4FDiWV4PiidXjSTYg1VqYJkSC78uWOLt62P06h7p2jEXjV6+nUhiF0vJdGLVpJtU0qwwOwt7VlsjS2LZsNh640OFj0h5NwJbZDNaeBt2GNBWiTDsataGTP6XFDAgg0mx8zQTYFNw87qYcXsHNzvVQKLtp2gfdcuUIH1JbBX2Jm2/C2BbY9HaefshbjiZzYuue+PoQfZSO9/pxdOb6XJKRwy15GL98KScInMRFPxzOt0jF2K+fue0BXKpsQscCmkHYuGDOUMTNzBwa7YAvz8KBPv/DbGf3PLYODuBrpg00f66HUoxvvGooTiU3rrriW2hXpjaZQiBct3JCydQcdHIzynGAZnDDOY76iObxNT4dVKTRzM38jnmzEyY40lTkgYwbUdn+mc9hjS+gagwE+YqKxZQLf99oKp+oshO7YQZ03NQSZAhgZpnGEJOl6m/6lsonf2caG9QQ1n/psOzx2FMHFVNqxeN0FOpPXxHhY44nHlc/T3tyRoeBSH1ds9UGuolcxcUYyOOlxouLGJxBrNASMBcbymnoVnXk/DdZc84eZwHibsaWRt5meO3JJY6mqVxXq70RWVF01H1fee8NddEjKYGbg8fTPNPL2Weqb00xW9glS5TIruVTzMW+56n/fFZh2v2C8ApXRtIPdwKFo+V8eppzfSK1ZcMIhWh16TIQpyW4jDrly8nhtJZdhc0Ndj0SyMxZZKLrDXAWvns71Uc7wU76rEYFDdW6pY+s0hdmUqNAvlYNQlBj42baH6ol7oV9fP27r5FHWKyqScDdGop7+fNql3N9f/Ok3WiWfC8K/pNh6XV9JZ9CTZyFKA5w3LmjeeFcTV8dVk9qgfa6Q87urzpiCcWLaRrnXi8Cp+ROIi70b7ji2hoFQih+AQA21uYTzZWhUQv3WDlbgnDzoERFBWaivVaw+Has3nrJg59qBaepS25fmAfaUhaEyUOFh0cOGWoAXr0rJQ2D5HmSzfFAB5h7vpuX+z0XiIgXXnt5Iz35JhWdgJYr+tgsj4fWb9HM6EmRdnwd9lM3ibFO2g8J4AbHPN45lONpC+1gJ4VumPZHI6OoW/pl/CttG2MWOSsK+PmLznc6nSepbBkA8plUjh7+lS1gkvVfLjawJKyi7F8XAJ4LxiYIoXgxb5hTDQKIcLBbYT+YD5aFgjBircIFJjud/BXdQGH1yNosyc/bxKtV/0bgoX4vUWo2+PBVSPZMMWTES3F5G8uHscWF+/neKGUnwargaLepIA4781w/ISTBhnwGBZOyvAJQ31Jl3x6pPPrA1nA0Ey1gOqRqfi7PIMHJzVSklFHfk73kqdbSzh3w1NFNhVw3tleopmyC+ke1zHeDGtReh+eg95qpwBETkZcO+fDERLBPFm/NCF8t3aeNLwKJV4NxU214cCa0AJtz4VpIKjyvC3OA2f2kiT+RYqqHZlFXk66ga9UmyQFd1Ivn4ZJiJvBeB7fSmIPjxK5v5mwEs/CRYzqTA/5hBR5vO7yq0ZYM3Sprobo+3/vCI4Y68llX/NQGnyv+bAtlAoOVuKYbf3U++CepovMBNCx7KhW8GSPPFfRx7ljJKIKWaw+4sNWVutCpU/2TAlPRePDRSDqcsZeniAgVXMJEt5/1xcpnSMLHa6QM3fFoNfBYNP++PJipxxav86HF899KLbDIrhXk0KNpqx8BJrKVg2qOLrNTFwxzkJrAwL4QAeo9s+F+OdfwzcThEBnSoL8q6a78XHjtENu2JJ0YFUeK4oBVf1s4jfRU8w8AtEuQghOKk8D3tIAT2ecJxsj2Lh0EYuXJO2AvG7U8jnfAUU6eLAfKlC+kz6O2nP97bTeVgIl0v0wFirEOuTpzg0hb6mHicJHpI4QXfckiEdGqWQlhmOmwv3sWonrpDGgX08+cUVZH+jNpXOlkKzDgsY0v9FVJ4wyO74Rxc2cCAkJBsqTp8g3LuJcAdzYeLPXtozMgUMxkpB8nwB9lZupMes7PHKnRSo/V1N77RuJwdKsjFtaxReTHGlvT+laFDASTpSEQUfQ8JgpzsDVzqHyTRTC9D88YXArl1E6RsXG96xsS+lFJ2OIut3lTsG/2Zww2Akijd44PRfJ4nV4GHWymR3oGvm0K9WXWTfvhR667kzpDUeprMPh2DRPgIiO/XQquUdz2tQGr++CkeZGEdInp8Cl/aeolC2EGWVbGDsaRqGylOy1yMHO4654yreKTL/zAn6bKyCmneeJUO+NXQ0Zz19JvGVDt66Tv5zWMHzcDpNGp/m4/XDWvBVczZYM3XEMpjv0ate0aKGS1TxywHyM2cvnSXNAbc3U3BUX541+40cq7uYwYs1QTjUw6Ou3Tvo19GXJCRNB7RWbCB9BhdYdurFsKLoGmmwCwQb4gZDliqgdNAI178/QzZtloClGyoIdMdAdBklfoXq8OLWKSIVXorjrkZw+X4cXLqsz5rpcod6ZPK5roCLvyTv0V5hX5T/pQq2X+PheUYgvPDVdGhRRdrnxoCQfg55kNTr8OeSJuzJiYLaIjba/PmPKMYDjN5MgNj3XLTelIL/lgeC9CYGBhs8cZZVPEq2ILEO+EeKhoRItEERTNycBWyZMNb8j1eJJLuPDug7kJf5Ajj5SgYvqpeisV0hbLkXCHYLDtFMCAFWXAsdie6nDR/qHcwbv1O9wmnQ2O6Le74HQarge8Lz/UAfpc/neda9I5KTDH7sZEDqdR8Z+8vAaP5xlrTyeXJ0EQdhaShdX8LCp/cFARQz8cdBK5yb9YX6zUhGelcO/nYzsFy4DDdFZMO3u7W8sZaj9JdjGsbWXaMKFibY2V5D13UGwPIDQVAj6kZn6khh5pE/ZGJvCbqaZqK0tBCo+KXhRIg7KP+egatH3NGG6wUzJx8S1Y8pJLmwixaKNBL9LB+Q2D0fSr9kgtqIANanaeH9t6XAMRHA4xv8sOPmNJB+mYz/xIPB/IM6MfzQSvt/ZcLixUK4RYGLRugG7yeKYdtFLWJWJgqjywearfusMfojn4m7TfHh87XkXx0XQwcZcl8sF5c4dBJ9H3/o22OBjkkfWJ5bCUQN1ZPpFxj8PSyJJ4Jmwfr30WRP6kc6t04Yt+015N318cEvLgxed7WHXSY36H2OcjOT4YqukgGw/asVMFdU8az/cjI1QgY+qhdC3PNzZFP9NKioPkYVMs7TDp8OeknzNvnRnEJcyh6TyxJBqPAxCM8vJRgzYAx3XnOxWyYRfK/OxDe1GTDdxhuXK/LZcK4Y7RRRx5///WWdzSXAWlaEJ7Jv0ltmGTj8Jh7/On+ly19mwFjGMFUSU8fEiVnwPkQADPIYOC/2iGQ+38m65RiEBrNUoNA/Gie+msDtK/+R+O/BVKShGH+XF8G0L8mwJ/Yfia7UxnY7Ffx6RZhGF0WjzQwl7EzYQHx3FGJBaAY4CdrRRN1dJK6hjZ8pzKHvAQcXC51yqM6JhyPnGdiXFgeflmrAhXYrnFAdJV65HAxmfybuNxncTzjA8+TAxUNl+OFnF3FawwbeKINvpEuxODESVPbLQVzlehKzUgdbVLh48vBFll/3BaqmqwOc11nYn7wQJlYN0b5z6tD0/RbV2PGE2E5l8IvIS1L8MBjOWFI6uYTBmmX6sOpRBJ8rVLHu8xy4L+EOOisXYXzgBdIoboHsuDvkqFU3ed0rhLnCeWhU6ApHL/pCSFwZ7n4sRaqKjLE17CpNVsyFsmelUBWUCEPsadhuakxPlS+FlSlcvGRzm67aySDvRTIyTV74QJ8N1yrNUfZXNuREmEPqv01EI7mYn4t3kRUNReAyby01eaeDOjlvWSrV02FKXg95WaOMZr8j8Pc0DTr/TAkkZeYSJVtRXKExD1+3+MLC66Zw7rk5vuYzp9FMc9bc8By8K1+KrwI/kYK8d1Rpdg2p5XihrikDjnsKYVwiFy7U26L3Cxtw6mJAs8ySTokrwjltbJCRbSAHQspQLUEY7j9k8NuZXpLL95TFntq4a/8Bem8kF+5/XgQ/L/jD/Qp77OK6QM2nNtpzsIj6S2ZBmIc6gLojKrbowM6TUyD5xHtakOQCD39NxfDGMNQ9mQmM9S2eRuNN8ksaQLYnnIjdDoCSdfbQoBACMn+DIODfARq/LRPP/BOFFR+fEt7uDt6LP0N0roQ1fOxKgfzeePyVU4ZfqxbgTJnnVPULg7Pm5kH8Fyu4MFYIf/eGEvsyM97/z7UdGRiidQMx+GbBJapuJwy6/Qy4LI7Eu0k7HAp/ukL07gRsuxwG0nqEzBKZBk8vMEATTED7aAHs0n9Jjus/J5f2xeO2uBV0XD0fx78zINc7jxfBKQHi5I7P7csw+OlTem+DKy7QbiZNaVngNJgPFtEzMXm3AogrDVNG5DcvQvZGs1Xrel4Vf14UKfpBglw3/cn3t1cJ+ng2XJzWCGViU4QpeX1yIXwfK4bSYwqYyeZgasIL0rfbEewUT7FE49lg7s6FFqcMODjXGzXqfLE9OpN3s0YXbk4XpNrsmTy9qKmwqWA+tJblg1crjzQs/kgnW4tg3XYXUjqhjLtWsUFTg0UUZR3B45Q/bPs6HYwrC1Bjmw8ErCqkI/tWEZfwdWTZhhfNUolz4MaBPuKrdZL80dtJFM5p4Jo9BSBnaIcrZ0+QTXVBuORZHgylF4CKVTJ8f9RH3Stcefm+K6iXSDaeMy7Fz6s+UZHvCnhwmyD8yF4K558Vw5a8Yjz4Lw9d1oRCXk42XF2si8MSlSz7DBUU2XqB7q+0wskaTXgUtJtUmUyQ/GcMKp5UhHMyi8nk8jIoPlhLhR3u8mb6R+BaHXG40iBKPCxC8NK1BXgmWANffinCwidiYORpCbR5hIp1imPKV1M4/ILBtvUc6FymDljOhcvjkeDp2MGbs+cZaU2MRYHrMShuGoz/yRZimnkJeq8swNeRk8Rj9A7JNFoAlQvWkI4/XIg6yc+CtfupxrYYdDQ3x0efw/B7rVHzTaECvChRhPVZ5rgb2lh333sRlbFRunhFGRxaEQ+9/dGge4eB8MR3pLcrElvk46FMioPXRhiQvx8FLeIMHjrH4PSXyTTAJgU22zrCjaq/JH54HqgOZsITRxP0uhYLHnUCOKPZEOwLE1H5ajSKVgUgdoTBB10uNnploZPMTnJSyQzPr8pCKb0vtHOGI92VVwSRFdLYmLIUnO4yuGj5LGi/sRQautvpT/024hK/iPdxaS7OKlyIV0JYYNzwmao3CBIT3j9y4VMquilx4HwNF8RWpoPXjRm4WScFW3+noPjwcmKibkX+vklEH68l+E1uKq4tL4XSU2yIVfOCtRFj1Hh+KG5XY2Aev89VJXNgKKmVTF6NpFel02GJvCcYy5phYaMf/r3pgHLfK+iJC7nQLWtIdSEZZQqnwCYnYd4lvzIQPZiJRhP5UHqvDLQnGNCdnomG/FzftbwYwjkvWGMm8ij0I5y1RlYfrrrGYqF2NGvbxq90xkcfiN2YD67aMwCDi+HDG1MoUlhEvX9mgQfjDkd/5KCxaTrIBj2gF+QvE6MdDiThsj180Y0Ar2kC0DclB8aGU2GqgCHkZbuAv10ZXRCdhpdoDr70LyF316nC6n2LQMY5iL6O1QTxlado9vqH5NLFaDS//o3+9FxGisujwORnLMa5eMGyG7l4c2IGiOfkwbHOe3SrXwNvq5gBrdts1pQxFAwGN12weoSLvLw9zV61AqDyVB+JqT1k72SoyO4AiFm9EMsfmgJHvADtf8fgpqejdH1hDq567UtG44rg8y5+tjTLJgluCXDzuji+m85FN309CP/FwOP0cNCAXpaJaAQOlRfCnv1sDDkRDQGPCJp33iVfDjCg/zITs+xCwfx3EE63F4SwziG6zFAWyk1j6ZbLMXCgPwH6TI5RrYEivGtlivLpDN5rTCAJxXLw36NMiI1Wx9i0hfhOnsfbGCKBh7S/k6SRMSITsQBjSpKAbbgEmBccCNAvgbZDF3g2P72ApVAIk6YOILffDf6MytCPMWdormE67HgkCDU3ubBLtJxc2+EL2wbVQOrAdNY0k4M8sYBSHAyQBEe5c7SwYR4c2hkBP75JgvTBVjL4mYHXym6oNCyBdEIb7norkyXXrhGpjfognuSPew4UoFx0Gdzwy4EXa2Lp8OMNVPNHDmyMXElsHnqCxqbd9KGOAX21VAje1K2n3kMWLCd5Drp3zoVlielYnTJGWvzZWCScB/f1baC6uwT7xjlwLy2GJzWXC60aPmjZngsJXEl0VcjDISznSZ49w9p61A66rObRlFPOsM2QA78bpeH9eCkq7DNAd10Gvly+TG2qU2HKHH+4dsIWRyaFYIOCJ/av2U4jb+ZCoYQzJFy0B/VOd8x8zUHNp5U8BbFM+HXzBDmYXAA3j2hiZaQCKIdXkd//5dJMWSlkdqSywk500NnfxVB6x33i9KYIiqe/okf4HliTYYgDBXK4+xAb2Cax8GF2CNYJtLHocwZXKhQDr1oYvLZGwETqBNUbsUCfiDfEJfAgcZ9bDG18v6rnleH+63NZ6v5q4OjvDLElRXgzoptOf9RI5O2N0bpWCtt8k+HjTnkU+BMBdPM4+eKiicE1OqwLWQzMyYjHvocC8FClnR5SVYW89QxGF/jjlu2usBNPUh8DEXB8rks273PC3lwvlt92cejK5OL1dDb2NDFYF91Nm74nYNwRLqa7vyILgzJxujuDATfC+EzBz43WUzElwRbFDvhjy5oyXCwzE1RDB+nMgiZyY5iBJWMJZEkKF3xmz8CjPWXk+A1XnHyYiGmvE2jec1WUVZsJlQ7/P1snFo8c7ySxJqpgsksI9j4Kgvh/DNrZukHdH1fM6x0gOqkGrMnHErh6+jQ8XMeFM8cfUKHbUzFdo4VeE8nAhzM84N75IMJ7yYDyAyvU/8nAsQ8MtNbtIfHNDEh/1adqN5fR08tCeEpdXTT0GBc0T80EJ3NpyFMuRNb8RWjmv5JUvpsOoW8l8Wwng8L2PsBrWUXY4SLwaeVVWv20gPXgxDApb7bFxsPSKBYxDc9PMrC/mUG2ag9tK9lDWNvb6HfxLJjczIDg7Uqy9zXB3sZJetRjKk6xbSLe50Kh4VAc9BZ+oZ/s5cCbJwtaHH6tv5eCvVMD2WTJxv8aNEm4NYOTCUh1f7Fh+28G5wRaYdUZLch5xcDO/Wrkwf+fuXychiu0L9N99myUqFThPd3VQTY5mOAK/pyr4XHBSSsSzQ6moLvhF8oNCcaUzhB4vicGlCzYeMNzkuZ3yEG1Zwn2CxTi+shCtCp4RXZPWUHyHy0nd/W5qHybg5UKUrC8SwquV2Zi2kM+I7OFoOXed3L+jzgu4kpjV4ADbh2NQv82c/BYVG+vW5oHG8YSwKiUAx1fLfHsgkg7xpeBuio2Wq5mwNld0sH6jzEsdGbgQqg8nB2UJ56TvtjBloTzJIo1pugGZx3V//+NEZp/+MqLypNCz+Eu0n+dQXerTBBZ9Y+eTExF/bFntEJGBqv4nqe9k+DxezlYXPCYleQ8zPsaqo0uN8ug6JQQBN1LgUXDs3CRVDHKVrnC4LIo7NU5w7vIno1n38pD1bVIWPPNEGdvusVLr1UmFq8SUX9dHPjdKEXDnZmg8I6NlXy+irqciPbrJGCQXwvrFcUwXZcDzEEZbK7ewdqQ/o36lHPglYk38ZgSgeOTleR+bQn4RxbAf7sJCA1foUer3bFJeRnJ/BGO5e6SqJx8nCblK8DXnEX4idVKD89tpsXzuWT+jZ7mLbEMtqQU8zOvPNgPV9CZg9rw02IOdvmW8RJ9v9Ld/BkgZayHS9VV+dycj89vVlHZID+8lORJY08HwNzuqaR4VhFMOaLE45TspX9jyiD73RjtfNZEgzJmsNYL5YDCiRLoEpgF7wa30n8uZtQ7wREW9CrS5eVpYNWfDdPrCvB8ig5ecnQDD+v9pE90MzW/rI0O2RIwfnOZg/eyVTT5mCoeP64B1WNlrLHxTySs9wHv40Ij8jJbFDco5sM9fkbdUsDFlCFT1Lh3mJewLhTtX8XD5j15mPRtHk5umAUC+gx8CjMh2/h6D/kXzPtMWFhdZosqubnYOpSEBjcSwFumGJ+0CJBLc4oxfAsDDv9mg153PqhbqqHckQkS9WQ9sfqVRx5mL2n66VeEZ/r9IepCNpiVvKHFOQJoKrycmo4cJfk5DK4aDIQLk9MhkOdM+5Vnw+icAeoZco10RMTDrtqXtFd4L7W3v0k+9fAZKcCN/Doxn6fSaAu674Tg0MUMNDbqJ5L5hchrdcL1zWr4chkHk9/LQU6LHpbeuE27RLSBcMtQayAdjpk4wfcHcuAlnA+fukxg6JwLDmw+zLP7L5I1ejYTSupnQ4C0MV7eYQiHt8vhnqXnqYUxF/q3H3TY2p2KQzsZuGyWibrnFpF5NxaDVLI4Ys8M9CnPAOvuMd5pV75OPmWRfhcCgqsyIGPvburv1k9qM0tA6lQyxN/0B5G3pSDc7Yy/b6bCYuEyeuhBMtZ+18UN5v54d54i5FVkwaZGX4i0lSN1OSto+loRdKvspk+Sk1BTeA/JfWaNnpOZWEtLwSpqDUv8BBdXvBXAEcelsPq+Og6m+YHD+SjI+8jFnI8BVOEzP2eLeGPpjATIrTpLNXqjieOFqTSb71d24ZSYJvngkrOlsKzdltfToggXFnCbPfoLwNLzCKl/pIIDYjIY7+AI43tiMVwhDO8MniMJE17w0qcUUsa1IXpHBxUrm+QNhS1BuRANfDQYhJa6bDiQawaWZ2fg9msdJKtunOyYGg3nz3Exw9YbNQNSwGY4BQSvhIL/nkmWlEsSpmyOAt41XRDcnY5zChh4uDQKp84Txx1b2dAln49N6UvQLcsW0/IYyMg/yLq/2hcM7IvhyU9lcFWeB7YObx1C+jXQZHgx1W1lwElsA11y6w+lG42g1KKD13AyDPaNz6Y5BdEo1xSEgrXGUJPGQK17BmDYa9Lpn4hhbnqg+GwuHGYVgm8LA1sqX9Os2FX0FIcLdf8Fo+yD3+SfwiiJqY9DUzlBUN6eAfEuHLBs4KC8xmLwXaqJLm5suDwvA0x8XWhR6z36wTAfV5kfoC8/BUDhkTTQv36RdbzxRPPjocMkussZapcvBM5SV/Lk/ANWfYcenGh/TPpjYuF+YzIa0gVYWe+C6hciUfjdYjy0sgi+vFLFI5cjIblujPXWRQwW7r9AugcjYUA6A8dHNHHMSATxjT+IhqmBWKgdflswyer7aAgqt73BbbUynFPKhcfmEXD7SiMv8UEBGd9iRM9EO+L1/S5Q76MPa04xKLTdCGaFq8Ep7lw07EuCKW6zgLvVA7srhpuMlxbj4jJVntzsbXTtzGreWLsOjrZfp48jLMiSQi3MXaqE3yblgcVff7WNPjY8CoOdkIJbAhaCV7AaJrSYQkljApof+kE9uluooHcO3i9mIHVlHoZ16kOmSC056jxJxutTWEVr1FDAbZODWtMS2j/mjG8Ti3FzmiU9ZB+Aa7J0cY/gHFSJLkPfqgraPm0+ZE9qYfqRQpix6Q6pFyyA4CI5LBsLxNd8ftrAdgQVM76Gv7XRQM121tBLD5TLCsMwW0c0zgzE2sQq0hxkAKrJzvBrrwXp5XPn82UxeHt/KubNcqd3xh3wfsRXCncCgGqmgXVdErpsKMZYj4O0ssETuEeL0bZcG7UHn/CyD61h5WjOR2PzcKwO1qKFrml4KMIWKiv0iOUXBje6esC6jcVYE8BguWAo8XY+RJZxDeFhjwEk7I/Fa5pGvFcqoVAdGoBnrxMYnT8TRI9aoUafPZy/GAaBmWr0TQ+D4fIZeGKhCXz9y2CazjnC+meG/72LQ91YRXrd+wjx+euJl64UwqGDpvByaC5YizEw9/FT+npDNq7jZ47xvc54le/bL/EoEcln49YIQxhr/cjrkF9JA/j+8X6OC3qf30Fk9SPw8TRT9FMNgfBUcXpppyOm3sjAHX9EUX+iGF6OuMMxIRWYwzVH2O8EX2eM0Snz5EieXTHmX9fBGYfr6RY1Q9j+/Bi5WasL2XbhxCIvCPeV5dNvm+bDGfFgzBt5TMT7DCElqggeXp+OK/n+Gf00Hwv5uf/9Gw5c0tdFoVs11GN9AaYEFcDrTkvcf28bK6RuHVn7hqGJKxhsnjBHL9tj5MltXUxf54zvBrJxqvkzWmVziFctpwXLngTi7bQCqN9hDXdWSGJIyTUiJrSCTrgwsODcVDCJMYI0y9lo2GuHtxLfUG5wOv4tFIPedQVYNaKKgZEMXvXWRZPja+nPqou0wHUOtGRZ4SmN2/RY9Xfyr3QuzrjH4A23KDROKoFvNkvR+w8D35+PEaVLihBHCcrUihInUoJhmzWo1b4wEH9fhLJ5BA/8/13QASPIWymGQVdTMZGl1Jy3LhYj5kwDnZXOcKZMHrMj5lP3Q89I3D4jdA3zZB3oL4ZLKoVo7h8Nn+f/Jp9/8fV6/T7JTs3HYpdijG9j8MX7+9Tvnzv+0ZKFffeHyeBYGH5orCNPxFeQJ7YMLFxvjmXCduT2mCSsijeG/zrKYMC2DNf//kQCP3Nxw2c24u6XdMHVqRhVmgzHRxgQMGfDCovXZHRLOnwvSaGdnhw0nslgVaEFCj62BNs7DFokOsIRNgdL+w3Q16kYNviKw87EEri3LgClxDUg/LYaXKQLoOP6DvKnYoh8VbxNrr42Bi8tN9przEGPQD+0Sy/DalxHBCU1WHntQjj9pCac1mwjogbXad8ZF3B8IgLqLBaoSFjAjQgl6P5qgTvWpmFKdRHqz2umES3FePtJPnTGsnF34l9ivA0dmkytiG0EF9+lxMEalyySwC6FrPKl6NXBRv/O+VATH4IdMlnYmMWF06UFaGHpDOMjTSTI7jsVVJ2N0nJmuNJDDk0FCuHw8oO8+a25UFvwlndy8gi5n22JynO6m/fEBOCTxNN00y42XtovRnWPBYFDzzCpii/Edn5WtRXtIw/dGGj4dp0GmunDsHwplIzqg52nLdDHvrCGvQBPcIJwWTAXmSdL0WrSClWXDNCsRRm45HsO7j3I97MLHJjrqYuMBYscihdGPZ0SzPlkictKs2GRjxl659qDaSGDdH8oep6RRMHXPeTGvil4ZrUqGB4WA9OMHIjwC8dT80/RNS06OG1iOblz3RZ4e0vJSRklVBj7Q4Z+MnhnrUwzSzsLXG0e0MUbxcksi/8/v/lNw5hwkP1WjC9PuKDptkxQVnMATJmH3oKGeGWKEJrw+SX6GYO+FsEgfk6DnMi7SQ4XWkPlq1G6i58tnzXbwbgFgyuq2bio3Ix1a04+ijrYAfRzsFN+F208qMI7rMHgksYUcD3NoPTlJupuxwXnU9FgMvidRgbMgLycdAwfyUTPowV4qsqOXH0zD11mvKMVjBv9xOe3aU8y8XesIqmPXkGsOs6RG04R8CiNwe/VEjDnYzt5VbEAPbnq2BQ1EzIiFsFQA//6PT+J85+zPDHlHho0Mw/3uTiiTVcC0PajNGH9ZdL7zQu8uQxIrnEnhytFMHf9dDp6TAHkn2RgdKwVKg5z4QTT6BC0Rgn9rhZiOV+fP/PU8XrjPtLXu4U+mXqO9rc/JQY3LJtvSqXSb8GtZEdZPv5b8JJVaGQP2+5qweac5YR7MhWutZfgxSXV1E9DG8VOV1M1v8W4cGcsndkij+v3RNDV806wvC57Qhz7hcPHlQFw7rkVuoifprrhF0lE+FyCDnyu5OhCbX0cVBWvJWp8b9psZANH5nBR2+oxPZPZRkL5+ZYrkg1trCQ8ccgeioSKUUN1iPz/W7fcP754aAcHPjKV5NFSLu3ud8YzE6X46j2l/m8VsNVXD0QPp0JVjSZWp8ixjlg5YcpnwGubpsI55x2kSiYEY+W7SW0SgzenpJJt00NojUgmvH6oCaJrCf4SioByjgJ8eSKGJmv43KfoAJUrp5B4Fzs4/ycCTh+IxJ5YCcSn08iQUxDdfe8FOZ4rhxn9ZTA/OhLtnEuAONxieZiXwOWFYehc74RpMxNgp4UXmH2ywPa6eIzj99pkezasSnMkh2YlQmH5XNZs38PE7WMujiowICvXTWfbt/HeMrN50zY5QHdSDERciQDyQww1NFPwH0hh6hlRnIJGtE+qDFz/EwCrL8b4IpGLYx/ToC/KHzfapGB4eBKsVlqAMtWtrFfHkmH0Ri74S2XiX6nHJNHHFk+0KBCD43rA/EzAAxf4uWGQi2+POIGSG4MLlMwg5ZoP2ouy4LiUK96UKQM1Nz361cYEcxdy4J5wADSeE6B2I1Gs4WQu+MxbRZUT8lE9SRjiBhLRqPEzCfYc4GWvX4A+XwswSoCB8dNpaOlqjfeN9KG84QbRqu9lTTkii3N+uWJFjw30NEejyKVkwp07DeP5bO88kgHT2Sw4anKAJo4xUMeeDS2vCI4obaR568rwZdAZnoNuPtof2EMOPcyHRVJO+P3TKd7Pn2G8PTKGJFvVGf9OLsANmSog8JIDt+V0YJH4QqScAhxrq3dw+1zOmqaSjZHtq8gyRz+yj6eEX2Ru0SOJw+SvayycHQ6n67+woHwtA0aRWVC9JhGxwQdHrzihdFQJCP2ax2vwF0PyjuBlxQW4//Uz+nmzHjruDQXBy/aYaZsPc7OcMXBCEDarJcB9y1ia0h+EDrumE0XBHtr/45rD/c+drNZfbLD/nIVybpnUZm4E2r5koLjqIX2VZo1nkwisSOuhaebBEP2LgQMWQmjWkQwxfV5Y5QZNZYPmsGV/CrYFzaI6Ml+bVUJK8P7pGKgY4ULWwGYq8dIZlc7zPb/ZFcyrGfrqyTTMWRoC79MdoV9vS3O5OQdsZ0rBYKQnFF22JstlGHy1XwiETbKpoWUJzNFxh5+C5ij53QIe/ozFHO3Z+KqOd9lj3jhJllNlJTtFw47pGXZiNqmwSa6FtCel4XMDOQeeXSguovpYsdgFj74NJ5ZNQvCfbCCMymphT0YIbjDIxxVCubxJj1vkdKEUpj+9wKqxlIGqG464xukfFU1bCIUVF+i8Te6QtykDOkKfEFWLBjJ/rxE63kxGi6FbvG9ph1nVt1fSL/dT4CkrEDUzbeAkCGFwRhSufnmR2PzgwJEWF9QqDQS9C4cIS4eLic3z0DN1Ol1XyIUbLvd4J054YslbNr928Tj8IwQs2fE09Q+DA50MrtNTwjfj7vDhzQV7B59w8uftJ9bN8KP0zDUnkDx1gK5a9ZOVO8mg+U4+W9+ewsOOLAg8cJn1WjICu4e2kLMmPTTV3BW/vfTH5tBdzStNsmH4I6WxpQUQtCkS2U+m0/TORbh7sIJKRS2B3aIcMF5zm3zY+pW+LFkMDbUM2Cz1gAd7MtF6XICyyqVRWOIJNRLeSParGsGW3XOI05YCKrR8L23atATydzAgtk0fNnyXYS3u8sUHTZLYecAVHf8m0mrnLPrEyQRvHeQvYJ4+asjOAlppR1fseEBsnN3gqeqbZi/3InjzKgLe+EiAotg26nDiNzXq9IAXoyaw9MVJOrc/GlQDBPCyKwefRaZgULIvGc6MBYFCLlaw41FWPwFm1JaAzYN4uP6cgd9fGOiTdsNaLQ78eSqF+q5ryDW5Gl7ij05q3hRBj4zG04gyNsS3JsDz5kBcf2kh38e4cGArP889bKRnfrylLQ6e0LfhBSlx1sKaHyXYsVOQfnGXAmaSAfbeYqK1zARkMgXwJJ5nZbzVglCXNCj4HQgj211x6N1U3MBxQ621U/BDqzXeefiS5AfPo0H/GLhm40HHtq+igSOnWOrD83kN0qmw9epH8nmdIyzK7SdiN8LgwAsu/KzzhGthfCZSdseFpq6wrzaRdodIwB6jNLz3cBMZ0JZC48dy8OadJzx3rSSrvwnCveBDvJq3CUT7zg0qsmsxip1Uw48+fHb+54WJFpX0tWE3Lc+cCSbPd1PO6Rw0mpuJBx7OpKbDq1kjunEk5vIMPF/HxVU2Knh/uhfMurGLqO3MxZHu2zTV4TB12sXB3lRTMPheS6wu5iLrTSkUHczC6IPTaERoE28uJuOM/7qIVgobXP4uxs1zCmCOaCnejWND1zQBcj1fF9+6R+CX8dOUvJuGp3v4PhO+ALPLT9OJv3fo1DN8zmq2hfMba6hrqRdI9rrBnhV87Vxaz3vyZx8rNa+aBOrV0+pHl4nR5mpW4+1oMnrFl/QXiOLIdgOs5Phg1qE4nLnJHDXj/fDEpxO8yf//C0kjH71fvXG4pV4IrqMceHO2lZpLG8G84XPkVFU+WKI5im1egqP9HbTroBp9u6MYoga9QEskGq2FSvDHfGF4efFk820HRxBcUMtrKr1L9D6UwW17b7xYOB+m3PTHhaMlWB87wbK0YIP0h1K695YfLghPxACVHMztFQLZKxt5Tzcp0kXFBfh5+hOqMh4Hp/IYuJVXQcu/pqHaIy4u61GGa3e/0z/nWKjg5Q1d7VKURiXhc6d8XD/8lWd5JQUjor6SFyf7ieX+LlKeF43D+4pxSkYZKjWn4j27DKj5Y4wWWXlwwcgfP0sR6PoQipWstbwt++vII6Ng7Ax1BhXzIcq2TKGC00dJ704TbONn2zdHAbU8OJCbmApS4XNpVqM39DcUw5xNlii5mA0TNRtI9LIZDuv0PDCPM0bKuvj6GrjIMj5QQ7yrSvDjXQGM370QlhicYXXcc8f0jf4oGdVATjSrYVttB03bnU5kjoTCNoWXtOLuOF3x/7On36ii78UyOHImGVU+LKF67y+QsBXFePhRCixV8YHV/83EG5LiuLjZC6uj7Enxo6Xg6lcM7xlBEKj3xj/KuaChpgVTi7xRXryNZfjTkEq6S8LfY/d4LWP++OMbgyc5/dS3wBVenpYgl7PEyLoxZZaelBhaG66jNknFiGaU+kkZYuaBu6TxuA5OjVGHts5AvLExBKxkpuK//ZE0VvwlUQo8w3ph54xFQWyUL5WCZrNrZDnPFN5JcNFu/C/N7DGAe5CDEUI5qLQ2Fs4HBeC03hskbspL3h7pUlR+IQj+j/OxJ7mZ2LyTwcr9kSC4xJJWFitgj9s+msBbAluTylCxZ1nzD+m+5u+fQmDe9VKIGFp9VVJlLanrGCXX6tdR8YogKHGRA4N6PvNMtNEW417i0/OUpyfIZ7cAR6R78uDSuSh0uhyAVQdsgHvrGUviWDQ8dusloe8NYMDcGleVz8btxpZQdC8AvhUxULHhIau+nQvp2w7S9r+z8bt4KFzzDAWvj+ZUz+UE6a02wXqHVnJ1hA3ziiWxo0UFhu0/shZ9YVDkvQQseribaO9KBZHkEGLydzpm9rFYmUqB6BadgpNr+sgWPWlMvHqcmD6+T3bYZUHrsAYWLRfG/Y1h4BXwjS5sniC9GQDVhiYwmFcIv5YsRf1joXDMKRw37hADwcBscPxnBcRsGm7foE54v3Qg40smTr37lC79y+eqmQawtHYKmN7NB9M7kbyAKG+wYQei+yMn3jc/NRDJ+cDaM2IMc/jz9edxBn/808dy9QIULhWnutv+0KGzXtB0IoQEiZRARXkGqugoEQPRMDhX4wUmZpbQrMBFiTxtGE0Xw/C+FWTdN2PoCpPDm0d1YZ2PGLzQSYa0iw95ASfng/V3XdjYnon7XgSin1surD0UCh8FG0nyFkUodn1G5jcmQGGHOV5VNGQlVxXghyMMHFtQAtlyD+jSsXXk31sBGNvhBG1xYfC4eTUJ0U4C78cB0OkvA3/ecMDyqDqe2jkPFPXmwLHUAvA42E/O8efdkx5D3HFxOSVnS/GCVALpPRmJPp/nYPlINmgYx+GBq8IQe1UH7n/Swp/+83HZ0lK44qGLdZtmYunxMqhOjcS+ACtcUzjJsv+QjxM0DMr43rt2sBicBjmU0xsMX9p30FsdcTBtxB443CRI2CpKMpO4aJiXDnqepWjYV4QeLkmg0DYPjISjsODIZ3qdP/P+/GYgtMUXXjYXYpdAJVnBALrTQZJ+7BmJfnWdWi4LAH/+3NmqV4h9SvP4fL+d3FMKh8thI0RlMxda38ZR1F0E7U0EFfjM+mR8AaxV86A/voWh3lV1mH16LjyWTsVX9ebYGv6avl9ShvIRUfgokguW2qWgHpKIibKOEF+hRb5OSpOjcW5456oTbhlh4GYwGzQ2ioEqn8Eez0ylGg0cKPyjC7aF4fD5oy7EX88F71QdcK+wgdV8vrt4PhuuHHPnr5fB6jUB6B1fTxaNnSVNoVNwcfEO4pLoSmsGrpAdfA1a2lfQ95FhIKk9Gy6sNoW3w4Gg+k4U61974Zr5/pio1EYCA4zQG2eQ6qBgulyLBeqv+XUgXCSPwkHlaQ4uz3KH9KdpOOd8Jjhe29x8RNkWJ8QXg6LxLupOrMiXwBI8bV7Gq8t8zLLNLwXfaEfQXDQbzotF471kbxzXtIFZG+bQk89T4FN3GPYZBIDPtAC8uus8+axghkdjKqlv+xqa3viAtNzPhfHb14lf8nPy3CgCHkmbsFac1IKimlzYXZjAn8dJWG53gxwXCgaNXWHQtLIYWn9awfXMD/SbohGoeOvDjWIG119jMCEyGk2q+foWfkXqA8XRbWUUPHs9wZuxswxTBkLReScXZnzLQfntBfBBXRPnKUTBw3ssYrO+ggSNCoD8qggweZcCRi9Okbh9FhgpGgpaZ11g/MdZB8GaCPC9MR+2lKuCVksEuf/ODRO2lILPyEnKyQyEUt4kOXU+Go9JWBOpwBxy2HIJDAr5QbhePlj5lkC/bxSqHZUne2Y08L4HO6L55nicu4IL1Vtc0LLBE58PMFh6KB7cLlljgoIw2Ze5nfitZcNGj1UkenUPtX4bA6e443T1+1Ga+jwbn43x+3ZFCkhXsTHG5RCd3j5ANorFoLtIGWqKdfHsK/j7YVQK63Ya4fRLXhi7JB6pSDqanYkGiTFNOrKSQcnDLFg9kYnf/vKz9FNpCBiLB2dyi/60C6XLZS1Ju0URHNvCxlk7HUHlWiJMOhdhFV8L9VfcYdZ8DupH/SVz96njfXqU1hziotiSGMz6JQmBDTX0GZMKQTV58OhEB9Vs/EO4U5PxQ70Slo15gap7GrxiZaO3vB95qy9LbUfdUbO4hbQ6JYAQXx+PYruowSslJCVZcElWBuaz8vEhn7kO1lg6pIuco172d1jvzcJQU7mP92RaK4kRD4ZRNx2YdTYGOYNysMNNCO8v7qa+v7nweJSB9f03eVMXp2PZbHU4fzSWHr3qj7mB83GXXieNPOgJTe8YslWQjW2bEmD/yFl6rdoCe6TKoKyDAV9GBPIzuWDtkwm+NcUQYGoDOUqJuMF1OvbONsYDelwQc2XQVW0foUKxaJ9QjO5Jrqi7Q5u3jJ/bj0b003KDg1TeIht53SlQOFmKMo1GvNqSVnrSie8X51PhkPptshwsIKY/AWZ+IPjiTRDeJoXI2sLAXhMusrMi0e5HDvQf/UJMhDPBYTIcmu/b0wW1pVi8TwFy8wiEiBbjeHAsvjwkSlpfxmN44zj5voYL5ya00DXQCKyP5OHzICOamyWHWR8iQe78bYfGYi52Frazhvyise5JDtlokgh3byZA0Nli9NK1B68BNmQo7yU2zxXA17YA3lU4oI/rXFYN2di8hpVCvkelYNjdJ/SCqB295wTw8Fgs1h6NhoT7HFRtCsLwn/kgGlQErcUpKHe6EL+qMXiqWwAMSBwq025im8zCkEX++M6TAyclstG/LBH2LLzIYo8yqLn1FL3kXwzfrFlg2RRObacWQ8v2YHT5nYBDX4Iw6YIikLZJqurM4K/yIVqjpU8KpVNwXNQNLacxOFTGhXVyZXC80IhK8vt7im8JBkrU09JkLkyE7mM5zHWE49SXtn5NxAoJfr3tPeFEeyKsLy9AAfE8zBfiUXV7DuiuTcOa+Z70e24phOw9TGq/u6Ls92QgytEwte09K9d/Gs4tXwrsZ8lYF5yCz1l7ae5AMorL+aHmTS5cPBqA52TTcJhbhl+9g6Bxxx7CLfeGa1PywXvXf8Q+awlvTulb1u6pTayd8kmwvU4V/vyNwJOnHWHfOQYS4KmDdO1M0MxiQ5YnAxMPVpFrphL06nUurFUwgUL1YSKpUIoKqsGQuj8Ft2YzkF/uheLzPpGaszGQIRoB/XamoOfEwV+zCbwcIljSm4lC67gQMhwFZqtDQTdeGLakxaBiXBI41guARKA2bokuRv3cJBh/lgin0iLw7W8O+FXNI5G/2WA+ywWnHM4D96FSGO9j4GeLDoi7VTtce5OCddbPqKvgNHzcPNFkeSQNLN98YFWeOEr3b+eiPf1Ev+1NwW2WrqRFOAoC/bnQdC0G/mzNxGVPosHsdBLM+8xGIZkP9JNXCVS6ZUO+phuu+8kBx1kMbHmVjO9lpEAG84CTCPD5YTJcbxGC4Coz3rabwXjcOBVh0BSnhl6iEiZvaUBAPrAL8zDdq453uZRBD5MSZPrYdtWL7cDhyywoWeYA2ZJqDvlHCyAy9xyv93cSrPoUQzjFK2jiAmeUvF0EJ4eW06OJBRh2nYGX/Az5r0QKVZ2X0aqNXeR89jgRnvxLXaNm4pHwZNDNS8XNWplwuuxms+DZDLj8dwF4uJrA3kdfyTV+T6sfrW2uepgC234UoXZ9DKlie5AbYVGYcFwBn/3wxFmuS+H63gD4Qk+yzpbOIEfee8DquWzIN3WGt/xM0zE4G56/YWPGRjOUX7DSIZYzzBvm58v0snwYjNGm866k4rPSP1fkUl2I8AAXVYP8IX5ZG7no64H6fC5ZWFcIVdv4PVC+lxgUuNCs+V00UD0CXVNfNLfuy+LfewaMYwCGzdhJeBp1ZA2Egd+X6cg72E+N7zKgMpwOf55V0NVWZbBuXRv9HmyCyU+/0nMi2Rgg7Yhdf1Nxo1QmDlQKguSrEDh4OBqiXzJoFFsMI2sYPBDyj6xbs5fWmumAs7Ms+ihE4w3ONJZqjSY8182gAyaZoP07mkoeVoT/MucjT2AmCG4UQrFns1jP9RkwTZhKL7q6gcrcK/RBry1u/1OKlSYSkNYtBvZXikDcJQ2/TTcDnWFTOOEmgB4aqhCRMUjiPNh4+CQX32rbYFYhFzPPcVCnMwwNDbloyosHffMSzFKxxAcZmVDxlg1TdRbAgFYBnH2hCzrZXKgfaCYuClx4HczFfS0JEHpoNnr+zUZ3p2Ts387Adus21vbO3fTemjTMufmUZHPiYbIhjvXB0hx49vx7iwkCVhwH7TrDsS5VAA28/+PvAweX7rDHaJkguGOTAnvvlIH3qUyw7xNAqf5MOFuRjBaqttCeepAORFmj4xY2hNGL1M3Qlqe8iIHeKAbjFH/T6Cv9pEfRDGdazkGde2l4Np/B89lFUHOOzyBbncD6fRHKO0li0LEybG/YSuKfyMEUg6nwO6IQhW1daYNyDc/owmfaZU9pxEf+9QtUsXo2ByIF1CCx7H8cnXc8Vu//xxOhpKw0bEVWiMzc19smGZmRvUf2dq+TEX3SlLRLQ9o7De7rnURDe1BJUdJOWxL9zvf3933uc53rXO/36/V83fd5XOcGCTw2H1/6ZsCEPVyo6VkEO8zScdbmAVonHot2ur84o5sUcOFRPkxjWUZWPhVVz24n5RoMmLxKwvTFvaQ4bh45W6lOpjFFWNrMalMfF8R+Maha6oKOBzrJxWNJGCVrQHc/mYx/DDLAgWWzF/48SN3iSGceGCQTU1JhSz2DO+0LQdveCd4/skTJmFGaYqqJav7GMP60EJ1co5Gzs4euCEhF1xcucKEqFzwbGXh2XBLcIv1AIryUDL/PRx93f+J+MZL4jhsHaSX9VIxGwqtqb6wYsoYvjxnQT86ANbbnRdlfUiC9ew2NtJaH6jB5UXNNHMzNzwO/yhd0x7eTJLCNgZYVZTSBwwOuXjv17ugkBx/+o+ftsnFKWyTKr4yA5aYROC1RBVaYhuG+Hg307c9BQUE86hIvTMnqI8vGBYv2awjh9+xhzsT/hDj79Rh8uz8DDh+Mw98W84G2mEDfR0/i8zQOFc9qY4N9Nvx8ysWBHAYUw+VROU+FZKgmgc4FFTi/MYosuM3A9LxsXH1oHH1qOhtmFO4jlr8FpKPnsSjjdQyU8TLx0WddrMmKxNXHhahifpo8CJ0PXwzTIC14CWypfk6t+jKgVRHgobk8rP8lRAfzTeTWGjc4U7qSXN+WDl08H9G48QMk5OE1+n3peMhZ9pt0+Qgg9KUzNvhMxfFHs1E2sBAWj/1O1n+bQt5NPE5El7KhMIPBh/WpWLpSATTGnuN82KkGspVmsK7XiaifdoOa3gY6L0YIylqZoFB0gvBUy0hgWBoEcdtI6EQeSROTI783pIBVZi7ov1VD8at8oI0BINn9WyRu1kCe3WdgrFYTXdGXjcnKU7Caq0IivirhhLRusv6zMe7dkgm2CUl4tTUDnNSy4Ur2UfLNPQ3euj6mgaEGtPltFdWR08Xl4ZlQ9pHB4J3mHIPU2Tg4Lq0xqJ2HVTpS+FQnAcKXSYJTfCEG+eljxGRWJ7xNMGvVLJzHXYT7a3iwUSUHH7RMQGF8mejbmlhoyLTlDC+6TF9/t8ao1lh02vtEtPOiMcCnIhS+YkA7LgqrzLLw70MfEBzcTEvZ69x+8iqxzqmhYm7ZxHFXM2fsbQXo9faHKwujQa5mLuZ4mlOl/9bRjuFCyGaZQjKdh8bSG4n2B3M09cvBVfI8qKy5Qf+3f5HNVQVyrZ2BguYW4nkhAmsDBFgkfpW+kJ8HUywSAX4KsDjNHq8+YzDIKQ/kH2VCfKUluC3+S92bPxO5OGciv10MZu3jYbpyGPRdjcDjd/xhZ0A+bBpkuXllIub01NJvt8+QKTSJLlmRg1pjrGjj2il4bIY9nF9fQhnuG/pG96JdvH8I7nZrpeFfGay2SSAlKbPBQHM+Ft8KIh6bXGDlLXvMHsqDLWNSUK/GAQ03+mLgnABadVSIhaQIogqDYE/9crLyVAhcX0Nw/evfJOnNIC2UnQONLSGoYpUBCjQHZ2eHYM1mPzSPaaTjz2jTVQUaWDPwmJ5d4492Dp7w5spTer85C8O215BxKx3ptku50PE2ju56o04b9POhTUdftFAzGk9vEID7kV7R+4Z95PRaT/xVtgja/zBw9D9pSv8zotOlkBN0YRzhTSqC5v1CWFvnicHf2fHXeEGRWBoel2oXvb3OFaXnF4Jtiz1YxnnhMk93aD2kC2WR7rAioU8UVPqMHL1qDTIPHtHwnkQoD8mHNa1ZUDPIx18HsvDSMtemr1q28NThD9lu6QGiu1mkfE0eVgekgmL+bopuppBLKomnyAZ3a/qKjPRy0fFyGZnRxGDx0wkAB/voBW8OneefizrvXHFL6D26MjkPcmt6OOIrxuBx6WiYlVJMjfbEwIOYxdi3Ih/iVWfjYwGDO/JM4VHPLTrPNQ1n1bSIRMud4PqraDznxvbJXF8cd9wQL4p48NvKAhX2qhNYEAlrwnLxIuuRqzh8bOblo9aWBlqr4IuKLHMHTrXHTZ58vFAmSe4vSECfVm+cdF0Arg8L0NOMj8+Lz9ndPpcPT99/omLj8rGweoqovGcnrayx4Hyvq6XJ/9lBx3UjXHuii4xs9xZlbk7GhMEEWGqWjy3tP0nxrygUrsvFIcWvROdnBK5+wYVjreXUJigcqr3D8Xi/NYLNfpH0YAg614TAPmk2hy23gYJWU5w4bAS1gkBw7c8H8/+E8Fp8EVa+eUzHHQ+jsg9j4VOZIXTZWkDkdxngsfm0jxMJn4YZPN3+nRw5mkKt41OwW/4K529LLm69pAOndhE88c0N+RvVUf1xG6e9Vhl+Z8xFy59ZEFLujyVprxvXjOHhsUAbct9oE+mZc1wU7p+MHkoFsN5iBf311wBNFnvA4zIl1Pntjr01yVi8fg7IJ2aDimIbyRyKwlLeUnRRkqLPpTjAKymE8ZUekKlVhGVLsjFv2Q+6+MU9EjVtAqpLZqN+Ywjt26WP/6bJQqeXBYGkxahW6ovztLqpdCoX1rlPQOOtRuSpQwHI8D0xpnuYxu3g4YSjozTQPpuWsvnpcr0OHHbbJrrPssaLTxWc2zomqFthik5d2SCI0sAJk3OwaEERTr7owgl4pE7Dkvw53ZVvaVA1W6M2njRivRZe7DbEo2WF0PMwEUarJTDYLBpfpOiBgXgWbF1GYDC9AMJilsCAsYPo9aVQePP0Hp0exbLmjxScnawGjVP4ePVbFnqr+IB85CG67hYPVZIJsX6/zO67XT6ZlFWExS7JcJrD49znmONaU2c8okbJ1B5FiLGbjmRFDOYWq8FEhRFRQLc0LRUJ8OPeI/TDthxc7lWItdsKYN1PE+I2Zw6prpFC4Yv5eLleCYcljGHaNyX43fWAqn8/Jep4noad/C2c7fcS0KHIGQNGp8HP9LlgvLkIDdqtcNOlVWT/22xwnmGO8qcYjF8dihGb5LCQm0pKBgao4W0XXLZxNWl0t8GN5XzcNyoHzS67SMfVAqhq4KKvYzMJtTrC8ezXx0le/+g5k3qqaDsRxLLcQOVkJByvtyUdu76Qtrnf6bEJGdiUk49f5vHxKS3ClPgrtErwnURLCNBtAZtXSj6QS/8EoHpJClN/MrBgKYMb32WDy4OZkDdiRRWbNFD7cjiHb6gGm78XwDanicBNk4djay8TeUjEvtdCqK45IDKwEgdQ8gK1L55gpmMH9soLYFaGNFncKk8by9JgaB17/OsiVPrFQFf8bMgM8oGFVdGof0INji7KQ8/QEbp/8goydvQEPaGWT8t14mGIdpMzV5rJtR+6+ApZPlcvhNjdirDyfBSs3uuIE5eEIv47QJ/5/SCP5H6QtqHLVPVcNHVxL8DmxwVQ1GYKSSsWAv+NGCq9k4UHE7jo9+A0lTieCgm8SDpnawAe/FeI4yWcIOvwezpuRgQqz7UCueK5+OwMA/2h0dCjLQStE1yIuNNHbpziwYnLhTjzWR6sYefxNMgL76+3xocFbH4y6iUe5QLkjHJxTsImot3JB/du1vMvp8CE7BFR93sGNU25+KuXg2QWH6buW0bXV7ugroUr6TR4QWVPdtK0KjO8MJbNGFe5oCGeAV0nBZAsUob/nBPQvHWYFN/nwfdLE0lSLgOnLkej65VC3KqWDgPZhZAsdptcX9Unsk1VwpV2uqDtuxjOH06A6JPXOQUz5sGs8+NhKICLsvnxsPcyA4sDDCCGycOrzT6wzkmRM2H+FTocMA50JOdynOecoZxXFuA70Qa2lXXQttA+4nCGweavPAjrj4BxkwRw7Ko8nNmThc9vhoh6WD2/z79I7i3IgCldYnDfV4BTI2VBtOkbuTnXD2VPCzA27CkJK+Ji3p+DnDMRBeBzLR13maZhUUEEpoaJSOGqE3Tx7VyQHo7Dx1dnYNdII5ldZ0lU7BmUiXUht0ynwyo5PuSNcUNOySIwqpWkLw2T0Pp6DO61C4R9K/aT7rWKMIH1kjW+d0nRRGPy/fp4qjYuARP7wiFYVYgLd3BxmWIeZpsCTK0RwqZNMZjSfkbUscoS36zbTh+tthFpf3dBRXkGPtOJIrefvTQmRxa/RNhjT+s58tKYDzs8t9CUnYbwqXcSfuxiMD/cA0zFJ+Ce5Cry+pUOGq3bRqQTJkHsw3xqmm+GvP4IkVR7C73zaDZcucRF3W510JZShlOfjVBMm8H9rlnwbWkGWJqtoIMbuLi9s5yQjvd06+TntDcvBh2evCeNhQxMLSrCZk8+OAeya7d4hHROiYLAGwLMyUgmBslqZNUUPolWSMCSjwQT+xnUEajiGxc+TssOBo3K5TRnbjW59oqLg35joGtfO1mztILQRzxcZ2mGBvsYNEpLpGvGvqS/h/iwplQMa3mO+CJCgOe6GbidHoAfqyzAMIUPl/510y2rNpHZvQxmuRTh3QEu1nWVEqfDfqymeZHF25Ix5PxYqL0ZhQ518bj3YRQ8O7UQbDfm4vfDAvxuKMDZi52I8xVfMk4lH9f0X6Dzi+LgnxyhMlmOsGmiDKx5wcDm8XmonGqFe16KQ47LHrKqNpJoVPDhdfMcLH6oT20dEuGdDR8rhMpwQ1ETtWbqk/ZTDJgHlduq2r0jnv+E6J33mYp9mG33c5RBg0BHcGoRoPe3BLJdvZLek9LCSwpCtN7AQFGJNm7VVqLvD/mCj0AGk3Qm47JZQjy/dA7lnR+lTaJ6khGZjtv38qHwVDocPo503D4VZLwZ7AobJjNlYiFv5iinxH8T0b+fDCVfhbhiZTZullMkc1wjsGpgAWx98YGj9lSAqSI/GBn9Srfk8SGWzRKt5QN05biVZDjuoSiW7Tt4zcejzbeonGMYQtVLksByhJx0EL6d4A37kA+aFo8aiy7K4Y/560Tj2e/PmM7g9Rz2GLlDHBWVJfi+igAwKnBrEgdfnDGDS+ah+PEdQz16ErD4iwAn7AhA0qzLelIOusxIxu2vVbBbQp5cWcSI3giEGPJwOdUxFcA2tSm4e2cM9Gr8oAoubP1dMeRYdrG6VZSGqfdsceUWCbAel46mLkeJTWkmjI035ByZZo1ybB8tEZwiC03v0PMBfBgcJ8PZpH6d/t4bDGmSQvzyNYSuPOaJTRctYEx4NvaXFoFZuBQeu3Wk6V6APLV0d4WNnrNx6rsg6u0ngDkTx8ODHIJ62hHo+4sPS01NoVk7DD99DsFP6vn4b8QSljVa4+n1x23C54yQ5/lCLBNycWq+SuNKYTxoHo6FPb6REDJLTPRKRYgTD/DBZ5SHR27spEfdi3DzVB/akeMP//nW2enFp0InjYOMoxvpC443ETexgulbGPgibUsa5GMg/q8Qhoc6SU88g78tGVj4mIc2X9Ix3L2FrK6fLcoSLwCSKIYfdM1AZvAmSXr6iTjrBMLfhQJUnRuAr0yFqG4mjkfbGYxp06K/DnpjS6o36pzgYecGPmf473RIS7YRrXqfD9maSnBVwYhkjukhv0rfEeOjHaILMgzYr2aw3FiAgkJPXNgRiA9GBZyOoMt268wOiI4wWU1bnyhB5ZFMssvODvpSlGhNzy7Rp/Fm4BqbBktdhPjbThU9agpQVuQCycIwdKqOBL5JESSuuM+Z8lQD5ealIzTb4eCeUFA1eUU3Fyjg91LW52cswI0zGDi8bxfpVMzAMM4LYrYzBjsvRIN16RMqcfWv6G5XIR56xYXU2my8cz8LuN/NkOxMwjphv+jGHQsqCBFivUQ+3LsUi8d1ufhsbhJOadzDcU/yxBe5iiQ3Phl1H1jiJlUrGjPKsv/VGBxzToj6ylzwvpYDf/TY8V85QygGouK3i3a/LNLwHF2Ivp/t4ZGlHpX9PkCfHYokkidd4a8xA6mxfLQ64o2HYiTw7mc7nJYmxIvLr4h+NShjalIBvF9riEElmVh8Z4hU6XDIah0TMu7PSRIfkgb9MXlQ6yJAsax2mtA8C697MfQtu+bfy6Kw5u4oVZH+QetPzMNXW6Lo//aZzvjnS0eNOFBi74UeH35ymqKi4dzieHT4PMRprU6H0meFYLdMiD9+xSIxv0VV1Agejuag1SNziF3VLLpamkAyTRmYy3r7sj0+aO/iBE+/WuLVLSoQW2PCud6hj5HODExW/ifKNNxCvRb4w6YdLCdyufA1Qgj+c3g0cXIQHMiIxHPZidiSrQYJ8IlOUkuDHfV5uPzsXvquSogPptvD4GFXqvGlnxMQ6YVnDp8lwdq7aXV3Bcm32E+kxw6QGGGRSF3mJWm+YEDXBArgdrgi9G7mQ6QvAx9YBlwRcFLUIy3EKXsC0M1vGu6cE44dv9ZT+aw6ulXrHMdIPxhOruQRjdYp5FG7Gzw5pAxZviqYeYP1y6qjxH/KaTvtjjPklrMY2mcwWHnWEDrVt5PRN4Ho8kEN5SqjoFfXlPO8/RV5ymsgjrxc/NnbRt+EM/A2SBm1jMaRpxv84d+gDO5IHKS3lirDTov9dLs1ey71bPi4MG5+jKM+HWNmDjzUp3rB3XbwWwhu9xg8yfqWVVw2ybnCwboAZaTLv5MDJ5NRXPUEeZbNQe+4Qow+FolvLl6gdwdiwDB9Fr4alwnLfsrg72QG9ArG42QpOSj68oMGB2iimRsDco3BlMhepCupKb14zhiWPw7GwtCJKNfKB1+dLDxSq4UOaoN0fyeDPituEANVXfBzZevviz5+njoDPrL56du7FrJ1/gB5ZWdOn43EcT5KWWDtBz5k5/Gwv1MKAwcZqMpjYG04D3ba2UJ4YbsdV6GS6lWcpLa2XvTihlv0o1syNE+4SvWPJUCqahD2e8TBmgp7lDItgogTg3RNN4P/5VlAi6EnHHTk4h/VvVRbnIvM0xSMc+4kSj8K4e09f/A4sggmbUjH1Z4WkG+4myqatYk0jsvgmWUMOCpwaOXpUGi6NAtqL1+kxQkWuHqKEnh399AZhpqQ85XBq+c/iN7z+Bit/I4MmgURSetkbKlm4N9LBnEXm9XqksD22lS4+eMe7fu5mLx9OAeSwt+TvvPFdvknQqC4OxXFWhhcy9MiwV+M4K+eEZRqdNDISgZ21QZj9ZNssJRMRb26t1SWZoDc8VAsqf9OJQ5Nx8uip7SGccOXOoV4UEqDip/NBQ06SGonqeDgGwZPFMqJ5My4sPnuTHq4dZRuV/SCx6f3U+Pvf4nK23xcvEWarlA2JcLj8Wg6/IZun1oET8bwkbufDxs2stxd6A2+3gIwcWCgcd5a0eMp60V9UkIolBfHD6tSsfWoLrjIZsPUjgVo8WweSryyhmdkBrmXcJoWsX5d6ZgFJrcKwODODvow8bto/ZEofFt7SOS7WgoaekfovrV5uHMnBwKX6+DdsXEoWFlJenVi4dUmfbw7wkBuWzExic4As3lL0KrEkA7UReF7Th7+kJqI/1nLkZSkPLA+tpRKX/lGn91nWUKtmM4cskb9L4Xwv+fbPrvq4ZmZcVAcbgGF/WFUUL9O9Kf1B5mZIweCu1bQ4KyL/6ryseXRejrhOR8nqVpgYGcntW9YS3X/MrBNT4i7LINg44RxoNBbTl+peMH6J44Q3qyEVdlCmJQnSc6kmeNVo70ix5lxGHajAP+uELL6bIVKPxiUMv5A37a4wIRVDJYO76TmFzLg9S4PjJs4HQ7aL4FT57qIlVMVOaM0htBJFmjHyceSDVH0dW8OjPvDwN/272TmiBl03ZpHtn0ZT+7hYtJ75jmtcFXDvQNniMoQywU2XDBUOkBWcN/S8S9N8ULwEiyYYUVNBhm0PCWENzPUYZlfCrg+Ps5xyrTAUyWF8DpGC2avrqGxnxlocdaDPSvF4auWJl3mtpSO9jrDvsdOuJTtwQtVs2BWmBR8t1VBu1Zljv4YLg4ZTuYocXVwYYYvapb545JZx+lKR2MIitYkvr01ZN/LBnKkJA3uDzF42LiSWo3xICvStdBW1qPJMu0kPWDGRzFGHy6dY/C+TBuZ8oGPnXcWgeamt8Tqkx103mfrZsETUegPBg49zsbHU89zYqUTYccRnmhIewbM7BCR+VlzsGGXI8aHW5CRt6rYszUFlGdYwL1YLhx+U0W6lvrgCMuS0asXYsTnTU091/2JntpLkZ4rA6HTAuGq81xwZnzoGVUTnJ/YTk5e5sJKgzjAN9mwgrsIPLri4cEXBvJjIziut8Wo4msGDz45RJLTvhO8aIn/yVmgtXsvqV1eCN33hfD/70Lu4cOuFXpQ/9AHX8ZKik6OsGtuEgrlUzNoZmo5Wfy5ijxkBulo+3j0vcb2ybd4EHzNwsm75KDegwtBd/kQIwHEafxGGvI7HB98ZOe/OwmmXDtC5z+xheBFecQw83//FWVAT5Qdah/kwmKhGjTJ9tPfyilQNhwLy0OO0JvPGPD/FQYJNQw+eqFP9g2/oDpDgRho5EwM3KahWYIquTAA0GemC7sfnBJtucOgNN8HntydB+8jnfHlrTN07HJdTthbBoK7WcZ1lUaKR6jkWgWyrvUsOVTLYHyyEzZK23MEvwZozn1x1N9oBBuevyFTlvFx/yYX/NaYgHZszZfHczH5wEJUYntibdRkWCLgAmdJGUltcYUx8xVR6nQknE4cA3JZY8H+mjq5My0DM1pCQL7HD9JeMdC0spk2OIXi5Q0S9OY1CYhsN4e60Dh4Ju+B3hdycdxSBjd8Zeu0ZS/5elGLfmWZTuZDHHnv9I6zrZhB7mpZ0PdIwmeD6fhC9J08vcP2lqsm/lvgBbXpGYifPIGM+Uyi//3k/DB7R69EK9C7+4WI9uko+ZztjWGWT6qz8c0Secjy7yYKFZn40EcbzWIW4cqj43DTrxO0JEAWJZacFfUuZr2B10zE7z8iufvdoDSfwXfhCtDAes5ggwEYRLWR4IaFGKUfC79S/WGs30tORVs0XWDE4OYF1VTfLw7TMxfCCZENlIkB0XvMwJKGBhpbLES7smic/+0YCdffywmu+I8kRAXhPzczeGxuBZYH79O01l9U5TMfZ6gL4e3uGAzns/kzhcGH6kU4+o9BC98rjU/WWUH4pWgYX8jgOfMQuFMgAIl/XDQPnQ53hLoYnMHHwPsp4FOnTlVvW8CDewwoPbZDKZ4QTcyLoGODMZSvjgQXaQVybzAU13lFgLNYOuhUCWACq2mxb2NwcbcyxJvxocaojHyMYMfpViDyg0VY+aufvjLQgREtAedSaibU7Y2A+gstZGyyEnZsaCADxxJwpa45fvxwg9N64yc1TShCyXYGDGYlgPvRIAwecsS7oSrwNdcbr1dL4TvLaLRYXkH/tply1B11cE9eIWoef0Uf23XQ/GEjtJyiB2l+DI6dGQE1Xq54RdoHjhnJg+OHdFRfY41jD8XC5A5PFDwXgyZpHtY8YbB2+RRqp+KHI3Ep0K51gGRWC+Hi4R0cq83maFnWQq7vlYTUXm8q2uwPW2QScB6rrzf/xmH1ra102JmPE4YMiAKHwViz45z0QT4WujCo9NsN17Sqo/oXPjpODobDUux63blMbvsnwfAON9ySKITvlqbwsykNO0I/kWzlBNxZUohTcqfDrWIhGNXmYpiSNM6afoCTJ+sLt5cVgXITD3/xUjBoUjrd/cUFhdsLYGoSmxHYbOhTLYvK7hMg7MoTIsPy4deQGipb9YK6KORAwsoFYGB4lFwov2E3uC4VPqo/IHUrthHRNB7ZpTEPH0fEw5nwaNFAlAboGTJ4Ol0IdgqasJrVFAmRNV5c6QDVX1iu1SyC55Xp+GBxDMx/UU4SPttib9YE/OkmAMkN78nQ2CwcGB0Psw3Y3rRiwMw8nVq/HoOxijbwdSqDd4ae0B8JmdA4mw+zBgiaFeZg8eJztCmKDyNhwRD0d1lTp9kw0Q4VwtDvEaLW85ezyCkPduqbgn/6AL1r74l+c7LxnQKrOX/44HHIgb510KQPMAbSWO2buj0HAkcZbFdh8FpbDzW4sZTMORjKmcFkwPfcYbJkugP8fpwDa0t3ci5sScavf51h0g5nzpi3ZrD8TRT0jExDUvOdzHATQnC8AG8bHya7z/LwTEsKSLBcNz18iOj97xmYyxPRdbwMVCxOhyaTEhKi8IoIJCRA9Z8cau6VhaGJLH/sNMJNuc+J2Oe7RLcgHhbnKIM5NxdndeXDV59snM1ZY7t1+zhSHNRHYgyVyJr2uXYbc/JB65oQt5kJYeBHKvnOspLyAQ84tz0S/MZcpBtLUPTKohDHT83GXctkcK+sNx7OVYeJfkb0Y84oVb4QLpJ/fo8cHxGirn0GyC5Qh7G1bjjOOgRWf+NixWIh1GcVgZ3HRLzyIBmWRSZhU5MzPPj1ia4zi8CvZWmo/U4V9qsJQeOZLyjPNofRjid0nJcf7DEyw1M7d9N7G7m4dWQPZ08fy71GVaKsYTYX21phF+u5S37owkInPlb+5qNnlFeT/jsXGN/AQE8fn9WWZLJ1YjgG7OSBucAIXNcRGD0bBFQyH+qWx6OueygOvRGDz+LHia20ENQLA1EUFATTWR47sIcLzguzQK3fEJQq+Rhrz4cLN5fT9/eiwc+dgZ0BAgyI/0El/rf/dZAhcLv5mNyXivUPiyCW9dPC1gIYCfCAVktlOBwigFVKMVCzaTtV7Yqze35KAOJ+fNRdI82pTBvhvI8uxMXfBNDw1p/WJAvh/BRTUPbJhR02Z0WRN5Px8yQhDk67Txo8XCF4ewQKErLAYFMM3nwvDdedE2BaaQH51sxHjvdcaGTZoHyBDNQ5ZMKeOzEYzF9M9Q8I4cCFTOpVIMQTjkvwzdASiDlohhO3xILcjHnY0WENI10CsE7kg3eVkL1XWVjcIoTdmRHozJ7v70wGb8m4o5woDlZMt+VMP24LBXkvqec2Bq6H9dEw/Rz8UcfHUqNHZLsO67mz7ohunmDz7ECw6JutN124nIEEls8exYrhj2XxcExJERTuKdBZ8wNhiL0Xca/nkDmtArhWJoHDbdlwvk4AY8Ysg67rPhB3KQWT2WwrYjyx+oCNKNHKD2ZfvUb+HJpDBG495H6UC/2y2AWW/cfH9L0M7lmVBFO9puB7dvzl09JhyEMJRr4x8CejnN57pgSPozpExjYLyOn1n0le3U7Rib5w7KxNxf6jArAXV8G4jWz+Up6NInb+vaMu5Ma+OfDn3//eT36FlFikY8mjpTh0w7rJrcAKjBcr4LoIe3hyrI1Tk8pHWrWHXM7QgDWXY2GCjQASt1nj88eheJ5lu5PVEdD7kgsxhgtwxvtQGnwkHppWJ6HizRZi7RMI+jUCaMnMh7k3pSF0XSaJ9M6FnL+FOGC1n75uZVAYK4CYu5aYukyHSFTco/rRAvh6UYinWH+NuReHlj58tO13grppvYS9d/j8SgHsfX2A/p4shBIhA/T6XMjcFkl0DLJg3S9lNHLiw/IWb5RauZxsF7tsd6KFC4p37oqKvh0lv5wziSFb/7KNamBy+IhoYAbLzs+ecdLFTpEtyQAGHzgYnx4DJRfaqUZGAKhoRkL0XD42OkSj/LIsktQeC8Vs/23vE8JUwzy02CYEcT1JfOHoherW0dQ3ORyPpbD6d+Y3p+zUQrzaGor2tVxQd4xFXycJMkF6NhatFEJmygcibfOTxIoJMWh3BubksvXWwWa4b9nYCM9JPFcFyl33ULWXC0VTFPm47+E3URIetusMm4KTNzAgFf+PHtjnCGaeAvizJhcb0yLRM9ACVZOEGLJyPNGdlAa/RhloaEqEMD4Xpqk7QJDeadG0vQJYvS0SRKdYprrqgSn5fAxtVUCZ4c9klupqUtwrRO8+Szo/VACHmurpaKcnBvzgIXddAbhWudG3/90WTc2Ph2XblfA5eU6VVvLAUWwBR9nhHMejrggGDubimDPfaHmfG9sjC/Dwo2HR2GAu2B06RPIeSeKe0iy8dDkFllhREl9VCHtKuDi1VheiP6shuBuAwulITJxqg0dbG0izVBBoT88AtUc8FKuZioo71dDBRgachBnQEMODYCUZzs/gIthykwsPNbKhWPkQ7atzwKOnWO+fqU/VPzKkVcIBzc25IHjI4N5KM2yfaQon63LAb6UpRM3m41iVQkyvi4eQ2EK0EfBx89EZKHZGgD0+s1CN5Q4DE3eayXrWPH1pVIx0wYufw7HIsgAU87NpSRolty7kwxjXHtJ5xhgCd6uj56UumvSUhzLTsnC12nViYZgHA7Jc0Ay/Rfa+CETfrrXU4D8HqJ2QjZI/AqDfJQ/HnvKlAXPMqUlBG3k6mEYXBERjJ58dM5dAt+489N7Iw/n1PNjSFYuqZxn4tGIBVA+447a3lrjuXA58/euI4xWuUt5qIfyp/NLUfdEUa3bbY6hvJOYdX0Gt3hbBaPg8uKWXgZJV9nTKRF+wdb3KuVw9HldJZGMQZVDzoT+IP1SAbVJWRGZPN83L300NZGxp5fsimDdYgLu1nejcugLUOeKGw7LBEPo0Fiy3ZoA9q8UVqr644iuDl0gv7bS5RCpMpGny5kAiz9bmxEW6OOFyETTYLcXnB1j+3u4BHisjwNUnHExjVYh8XDpGcDfS6GMh0Mjm2K6MSbDFzprMe2AEP0re0VNljZTXXoBKtxZjjj8Xk+TvkXMG8rDswjyO1XYG30elwlzrBrKj4SndqyaJs/lWIOH9ncQfzIXACWa4cZEAkrYVQVn1QNM6mRyceoqH70NUYOsMX7z5uIz88klBj+1LQOygNO3R/UvOiQXDLX8bOMeyWUNpImRvmIo+uwrQL9oHdN4kYUTOEo6cvg6IO6ShGImDFp0k2BEjhCsndCH2L9vD9zJg9eMECBq1Bp3Sfo7dSDf1zS2CBaJMaJUIw+fueTC1yhjCpnHoryJNzHsYj2GpVnB5XR4eFX0j9eMFsBcFUNpRSdyVuVgbzMCJ7ALsqxIQ50pPLHg4Bk3WWkGGoS3oqGnjy4JYiP4nSzwS82DjFBn0rWGz7g0ezLu3jBqzITh1nDS2eRZBf4Y+uhy0xcwb1nh1hj1MzymENW6xcHNsNGp1jdKzDrr4MpLBac4F0GAQDRpjLPBR8yyY7PSPbncpQPXfAjxZrYtPtyyGltA2cnH7PuL2fgWpjY+AhpZUjI2XJPdXdtIjVXFQwTOF1EsFuL46Ha93MXB1VhGMy3hL3648TunpA5yZPt0kvEQDrfkMNmMojjbnYwftIh1mU2HwkBs4SnQTafd8zNplIvqYP0CVpxXgxeJfxOlmLlX5zscjF74TgyNC3COjSOaNDYEQtek4ZzgExT8J4H2HNtQ9lCSbRk2xrT4PZlRPRv+/hfD0sRBe10rR8KgWcrlNAh5Ev6HWQZkgUgnCpcc22Xl+ZmDexkbyvuYjuV1tQ716l9LgcEWcXikElff56Bs7E8T625qMlHxgY8RkGK0Xg+CbUmhs4g8ztmlCf7E/ipnPRD2NSvLhzXd6c2MN8TmsjDqbb9Fr1bnA1V1DD8Qd4bR9dMSWHdMh8Cbrm1GshxouwqQpRbBCoo28quThEY8Kjto6G6y8bwdT/7f/9SV1PNaYj2sCcnAHx5noTn9Df2k7k2OXFWBjmz+r8Z6wZWIyfE+1IBHvdlGpL7LUdzQBq0O5MMG1u+n8Ux70RuXCg+0eeP7vJkKNi0DRLY+sT7lMxYMuk823F2AY5zOp0TBBSUdj2HntCNVvF6Dlj2zYH5OF57K30Vur8rF9QgFnldxNwtl1sfHKuyry6BYPB6/ykL/CFbb8MKMbyguhyK0Qhssl8c5FHobu1wOVYnWYLT8H/zPKhdo/oRAoGiE8phD1Hx0ihX3GuKbhvp3LNCeMe8BBsy9s1qElxEnJFKXForFuF5Dx1jIQcbsAisLz8eBwIqQ/dIayAwbwib3uYQceCZAvAk49gIIrgzY8NfBQtsHku36oameKMyYfobmcQtYX0ykdzIGyazPJl8RA6E6RhxVRT4hmazYEZz6la7Ts0KUiEGedOSVSX5CLi4aXYkTCM7rW1Rl1vfdz+rJt8eScfLz1yQZJYSzsvLoIB+s8mmaWu8GeyZ6YkmUNf6P20cKRAXrezoNey+Shl9KwaG0RH206U2DP0mwouZwDOgpc/P00HQ71SFCFXcqcTu8nomBnB1BpmYdzXerJ9zOR4OvkBLlsFh5/3ZF80M3AdXpGuO3ILFowkoeOxyi9kVYAN1k2sm89zinuicPa97+Jn9hcmBIRBM0pDP7wzoBlzUXIF/9Nbg5poBoOUpkrC/HBqhyQ6pgEL5QFYLftCNV5M55w9cNw6+pc7FOag12zGIj+4Am3QI441iehquF8MB5VoJfG8OHnhaVw42YeyrOsPLuVgbRtRphQM5/kMXKixkKR6LORPky+aQdzBk7QP9memCrujU6jiRBt1kx22c2g7yafEZ0OygGF6fKwOpNl1XZbOPq6nOol+oHsIiEsnKGD6gNG5OujaLqWn4zuf/XxjzLLR8ELYZZuLN7bFISFOzPx+KYLVMwhGbrWl5P2gTBK41IhzGYPfSmrBfb6X+k2SyvI3VgvWvJ4lCou5IHPdCHgb2vYxPKqlmYO7C8swoqcMCiekY+1lYZU1ngeqfgQhhpfVDHtaDK2/+Nim10gVGnuoCd6ImHSyxQIe9xJdqenN5XGZGLWzfm4S9YLRgT+5Dr3u6jp3Hzcq2wEV+qnwY8LhXj3XxqkKMRCuPVTUXKFMTiMKLM+LcQZTlGo9i4b1IemkVOc+bBVhwdX3daKWvoK8LSYMkCSEtJkDbwRr4QJ3tfsnlpf59zTf0yydG9yPNdHkWkLg0BdNxNn3k2AdJvZwI3UwjNfDjWVLaql0m+SsWmaHNzd4NX0sMKU0+8Sjjt+bSVbw39z5H4+ohIhS6DbNotz4GQ21ISHgx5HHrbcdcekg28p11ULNPhBtHenDB7LmAF9vWJwbn8ortUaJus0nWm9hB/u7nXGFeeT0P1dBlaErue4Kn8nJzxSYPbQPbojXQhb2JpIN5UCaf0CWDcah79NJnOStPWhZPtvKmHL4OlXDfRV+kK4XJQNnZ/csebRJZJ4aYNI4epFuiHgFdk+34cuPT0Wu+ekgl6iEDy84uBvkydcTpiPzbMT8P59BiTnSpGxRzKwyiAXT/usIqarGVymtZe+2CqOtvtT8Ou0v+TM8DDR+OyCW5r8IWhOEpjPTELrs7K4sb+KLv2YgSkO2fAifgMV1NaJJP5IgMLUSnqzh8GwZUrwof8bDZuoRNWLt5Hesc0ijlsR1GgJ8d3DWNEz7iJk0QE80zPwkYsWJLy1p4P5YeCaFk/fv5bD2tw5cHHrO5LMt8au/nLyoi0FF39kwNeWx3rue+L/0Acmsnlnt3Q2lE5lYH14DZU1WIIWQZr4cksbSfoTiveGX1C/1UJs/Uqw/J4H3tjPA4uGSHzmFgwhEzaSew/zcLZpBu5fMExo8Q1ackUaq4OU8Y05H07vqKMXBR2k6U4IGg/zof71DGJyKxXC3YxRjfXde88YbOWZwSHlVLhQ+Yqud/pLzwADo0MyqDp/DvydKoTytzeps+ZpqrI5GX1TM+F5rj/oVnbSmD9R8ODmGbJIuJ2s/5qOUdsFuLJ8Cqr5nuN0u4phqkQJTT17TKQ4toUMxW8mP8bJgKmJA+zcNRfa9/8UBelKwNjNLrAumYvt95dCGNcQd+sZ4jnZVNxS8ZFe5ubgvGl56OxwgdqN+ILKrBrbo41BVF4oBf8KXMjUI/PttFtCMKhiFqw6lY6hP7NJVxvBOglXsBl4RE9Nc0WpIz8Jk3qcjD/EwECpDNYr5uHKmyfIUoshEmQySu0/xdGYZ0gEa5bCtvYMyF+VjhvuzALVXS443f8DR4nrCA7udbR61AGiHiXDdSsrfBmnTPz2zIfP6hH4fL8Q/PcJcFpOOr74vQhOWhTgBsJF11gOyr+Mh9ohDwz03kYKzy4k4/r42PFmBbmZxRdJuyeBx18G/959RKuPT4HmBUthssLmpic/+Zj2Po4uPjUPnf8tgfZr3fSo03rqKfGK7l1NYPf4IsgRq6BTjIRQERaKitQGI+a+JPPM09HSNwhrGhkMWlgAx9cbouTyx3bSewph74dy4uCbgNNal9MvrkZQWvWeZuvzoPnQD1q3Owa82DrfJLEUKrxSca2Q7cmXPPpNUhsFx/3x2F9JzrOyIhqXz0XVXdvpC/E6utJEC3dVRoKfsiRnVb06ERsvCb3f0rCs24vygI/59zj43CwJG0uXopQqH9dOjSch+Tpo6eUKjtLs5z9ckSNlDEnXpUFoxsf3+8XgH02BFDYTz5TOx5IF+uC9Kwa17A2p+htK7JUW0TfVpqLGAiGmXdTE+K2xGP2xg3wrj4W26z+JfiuDixYm4sRZfHD6kIrJf86QWiYMXew5IGoLgAUF84mXhhJ4NMbB6uJmMvt3Pj4f+Mbh7mCgd9gIBH4fyJ/CFJgcfp/4S0Tjgz8C3NYZiLm+BZCqbIXqbbq0WCwPUqPEcMBeCAbLGQy+40pHFwVCS1wKipfIg8qYRLTqGIenumfjcGA/J/KwNyqLbSfC8PHoXiZPtipnQOMbG1GHxBQIXiJAq9tDtHRND3ljeVy0x12IFWYpYMdmm+4BeYQAf1jyzxVDpWai/ckM0IgZEs1/KgHSPUqQlvyMfrbMRC1rDi5K2SFqLikEPeYi4Z5h1/fadAyr/kHLL8ii9YIi/GUoxFlRM4jGntucSeLvRD8t0/H5BgYddUJwtUMyNlsq4Ha3CahQrQn1H5LBm6kjnYmXqZnBD7pRIx+vOL6gVZsZ/HV9n8h9yAynveSCbLw0ClxZJs0U4jxBEkZFa2BbKg/TkliefJoPcV9X041JkdAano2qlePAOn85mXIpj6OwVYA7b1miQ8Eg+TpPDi+f4mI3FeK1sVlw5m8NCdmXDIP9juC004SmRIjspkumo9YXBrw140DvvoYo7X44iq+K/N9vO5ARcJlOjY/AO0+9MW9SsZ3TNWP4oZKAXOVuUc+6EhF5xM5RJRlBcjL8PaKJc/dtsjt6ppl2Sp8ibrmD5DAzn57zMsJ/Yqyubc/BFWzP7gpLhpS/0bjF3YqTN+MFWVAbi1atN+jS91poFCM+f0HMf2S+ixeK7ksjY3XErvvgfdrCHeSkjkwDN/94IncnHg0qYzD82zXilTYVsj+yubOaj8fiY+E0t6lJK8ULzuYysHAVg48rk4mrf49oTKgKbPgnBwNRDlCgxM5/Xb9o6vU/5H6qJx7xZtB1jSGOSV1Bkm3NOMe+MXBraSSGsdy54WUnKRvxITb/UiE9Jx+bDmRhc3E8rtN0hD+PlGBIQZvKKAoR4stJUDGD2gqlohvKbjDpx2diLTkFuK3TQe3lCOmNEsD4E+Ec70wVcuhENDQZFMDDcgOotOGDny0fSwxz4dckJSLx+yhHfmEKWR91jnNsrhDKiD1I5HDAfMQNZi9PA00jL5CcF4+pl+rJOQsCt+p20n2tRehW7wy7IBOOt3eKuL0PqeCrGnquv0Rm6ETC6vhk4E0+Rz5+10WZ2yn4PDwaTj6Yhr71PuCXnIsrGvJxr9+IqFtcGideGUMmtMegqlY2LNTQwg7JXMjlCkFUoUfz3oqhVtpuTs/rOHxzhcHJ9RGgeNQCFj3iocSLBLQyt0YxRWX66hcP42IK8ZcUH/WOCGGklgtVc2dh/3EudHQthoMTomHkiSQ2SS0F47OWMO6GAA/eDcHwD/5w+74BWRJqiD9OGsOBMXW0u0IIZ20iACdk4cTkEZFKVwQu2hSHUqZ3yYR9uZD3OwdvazF4U3Jd0yxsJubLgC68VSO6omZOr+83YnNWDvyeaARWZTyUdknAtwdPzpc5FwVrP/PgSSUD8istyeMuBjd8D4CrD3hoXBNA7q6ThUXTr9L8D4XQ9CccAifcJ1sa7fHg9HngEx0NAe5xeMXZHEzi+HCnkA/H2Yzwb0cKBPY/tSudmkw1N2bD4g5/uPepgmgHA5Ru6+ZkNu+mt8L7SKpYIErERoHJ3AX0oVM+7KpWwR6DJGzz1iazxxZBIyMgmm1ZsOMrw+KyFcr2TgXtAF/6aCSA9J8SoJ1bpejz71jkdy5FO04aOHt74QWTNBzbSvBB/1M6PPM/+osrA15HQkCf9d8xSx+RsdxROrHBCLZEuJCY48mwcVIUlA8G4Y6l0fDWOhHe785C0wv5KPN2IfrO30C+njokKjz+lPpcCqDB9nHk3lEONu/Ihd994ZAp5oFqB2Mx3YI2bfzmBwtoLzkrzMOKY5toS5EDXHfcQX0XzsHD56Xp3DIHOEeX2WX9l4w39YvANMIepSab0qYLkSDuvgiujDgA9jJ4tqUQB6/MgqmFv0l7Xj3RTOHjldO6UMfW880Oe/qhVIh7p9uA+q0WuvFiJqxaHYo3LDLB+ZIxeMSssetP56FFnwC+1uZiV8hJKl0bjS/W5+GX5zPItuNtdMZaLpZrryIHChlY8p6DvLhIKK0pIKuHWoilRhSkBXoh16iLPExvFN0xaCRmgXepQUIerL3vgff85dFmUgSeL1tGv6xYTufuK6Jh5qHosckAj1etJeJWQfhkcwAeLpTCUnchHNJupZr/csgZoosTN6Tgnjc2qHPTEI365mFmRgAaF+XD/UmR4NwqQNNfZkT82yS8FxyKAX8Y3HV2D224fps8r3OH84GS5M5Qk+j0sxA8m3KP5jcUoNGphfDipim2PV6CT46+prD4P45sNeuR66Px7ExZ9Lwchtwlj2n2XwaGWrvIapE/+B7LxPpXT0W3/PJxi9seYvWlll46HwFzWCbbe05ftE7sG0eGJCLv422i4WYE4bIvCf/9QrAfCcHlnnHgVynA+kN+REtlLg6OMsh5xYXrgiWoMSKEdVLTqeeddPwWmorPvvBh080p0Jf2gXoGlJF5n1kNVZrM+ZnO4HX5f0SZl0o3N9YT1UZtdI2JgFgzbfj3roS6254ipX36OOa5PuwWS0HHTz9IQ2UbPcL4o827UCwOyseKtxxw8qghzl8WwB+TKzRRewn+VF1P97aeo3djDGDtNHOMallJH/MCadL0UbpqUiTCSCtZJxGOIX9isJifAl3n7CBLKggPPxJCvEwEmNzOwYSC9zTxoR2+6J6PYZ25+L9nXZQ3ikMrCNjvWBAsu2bnm++Ch5qF4OvPBZ/PDC29pog720Iw5tACHL+Pg7XXl1EpE2f8MNcJX3EX0XgDacw1nosF2iZophgLFzU1kSfPhfoxOaDnwsCRqH6qciYc3nVWcH7N9kGH6UXwUtITV+2XImKrUrBm40SQNXokWl5RCP0/AsBn9lu6sjcMFK7oQYnqdWrM+nRmSgiebAmD8vkReMYkktZE+WNQZyKcPasEtV/lcPUfSXj7OQu5uw6LPkSX0WeyBvDK1p5szCXYOF4Cj74OAMY/ERacTEGdE47Yf2CUntwRDEud54BM8BT4M5gBqoVqsPtkMurRBFG09i1y+MphquQbCf8dFMdnsiF47LgaORWbi+bHZ0K48gQMl5ls16cShv82TMfr83OhvvqU6EeWJ268okI+2V/jmMxxxjhZW6J0kAGfsgCo+GuL5RY8XH36MnlTooRx5kF4UquOJqAaLPojDzkSX2iFqxxpCz5NpB+EUvEcPup6CvBX62P66kQwZjYVobUCg/N5S3HHBnVcczoZu7WEsPmbFv7ICRY9b4iEfn4GvOePB7+1YihmkYHfit+xfq+Ia7QDQOxuI531mcGr2w9zXB31UH/JdLoDT4hmDuvAkdDZyE9hcPGAI+RtMYNlhYB7p2miaIIZ0DGsJ/sHo72aHKxck4ZJoQwoNpfTDy724DAuD29+1ceCQS6OfecJc208qYF1AKzdlYLVWxVpYWkirtRuoCmX/OEvW2su96NRq1wK/3btpEMq+fBT/RfZ/WkO7LWKx5Y3J2jtglwULEoD6xtCnPvuEtlf0S9KidPG5p+L8eX9yRCd5o9HS8fg4eoG+jl4A8nxcYR2OhdrVzynoa+jqfoUG7zwmgenR4swzX4hJCpzIX+IC6/C/cGjoIk6Vjrit4ZQKHh2izozuliutRBv8S1gGzeQhNtYwaELpaT/JQNOZ8NQqCSEU9ESWFBaRTdvWoy6d4vwRgRb4zsDOEP7+mj4JgHeseimIRwbrDnpBFVjePgrgeXlwUJIvamPl7+mgPNllgFMLXHojx9MRlu8rDUDn+cxqNbzh7bLd9KtJwsh8oEdfLvGQOhoGlyzN4Yco0QYW3KTzlabCmHV6bB0Uz7I2CzGaYoaxERJHAcb75Ebe7mi0tjx8LdvLEc20B6jVy+Cqhgl3PJxF834sBAWrWgVfYu6xREeMOF8Ng+jRzf5wWBRIigNPqQWvSngtaVXFM+ux045Pu7scIJjtRvIYXdTzrxVPDyhqgfX9MKJ09FHNPZSPoq9cUQFkReq1ITBJ6fdtLrfDbcf0qZBRAaORiEZnOOGp2+co56vxWD0tjYuvfybXHoqBhM/++IfBTk40pMJo+Z+8Gf8co65riqnvkQbR2cG/B9HZx0VVfe2YUIEAQlpEARERAUBaZizHzqVTunu7hlmDraiGFjYYmAroogosx+xscVARcXAQAVsfEX0m9/3/6xZe6/97Pu+rrXOOofWK96j3rtCoSffFw5PjMYwsVSsXNxOpOLrydJmgCFbFl++0oJjFhPo+dA5ePalDyoeViPHerigUzufnvXiY0yAHY4enYOpngJsTROA0wAfhtIoM+98Fsg7pGGdWzHeW+WPA7eDoLm5GM96zCP7V66mgtbNNGi6CVoYj8OEi4GQiLa4/aMHlltqQPLYN8L9XQLQezgeN5Q4cn4I0kD502zMLVhN75ifIqPVZXj4zgTSpreKOF3NB/cLIfjm0Di6oXILES97RJmtDfTWx0rQnJ0m7E4qhHeXVwq52UEwWG4Ind/fk9XpFuDe3MHMEZSCQqwWudgeDn/cxIA7uJjczWAxbZMYvloaA9VjP5AD1XoIOlEovmA2qvwQQBdfHtxuOMHTDY7M3tMu8HQdi1NrozDNIBfSvxPcsV8Wh4Qz8H3NNDg1KxIW2MshpzkWGscGwejGX+TOY0NYvasQf+7KOqvQoERPfl9EtDxjYA9bBldHquDQua9k2l1/XPHyOrn8mIUgpgL42sVOa7RmY63/W/rb1w7S4o7Raa8ihOLXM9BjJBlkh+OwsEAT+BsyMUPUw8vtMnHr7fOM4/hE2OXTKDzJTcL8hkAYf5ePM+dmQcwGPoT63hHGp0Xgr1593McmQtHWU+27JeRg1H2EfJfnYp4gjs7wHgPF3VlocyIEzh/zQ8vQ6XjQsATEtf8K05SPkXnP/9g3thrgkr/JuCzaABXktjvKrxiLxn9YnGQsgVtWeHCmmJjCi8ZnTLdTICz6VkHC5XgwZ2cQ3rHbSPrzIoAuSIbmsDTQ8K9EWCKHC46zuOkPFys/AZ79MEp1zg0TaBBlmYcf3g8tgPh/Irb/kQxl4gKSezEDmkS8K7suA1PqM/DPw0w44fSS7ijOxog3WeglF4eNYiJHW+UNypqecFglEJ6IPGvD60FCfs4mlUlKJEHk+hqXs+D64k80/XUVZjyYACvLRffmiQCbRftIdNQm1m4OZGyPL64s6mz/8IoPH/+xODtHh/x33gXXBKkR7t0c/HfelogtJ7COFcBkDXO8/qgSXxdEY7pxMmxdIPKIAx4QYBKBlx9UwEc/p/ZwR2XoqizFYZMCiLqQB5qVBMqFi+ivJSKGiJNCjfBonPb5O43+ZY/yERzQLvDFopPZuO9JAdzfmAyvj4+FBLFVNMpJBjMbC/BzGAu7GrfS8dKT0MjhIB24MhV52yWZ5MfdtDpqHjnk4AHpd7yotaYb2Ff20aDDw8TJ1gSapUT+lFIEMX9DoSXAE75UifKoVoauWnuFiu+XgWdLBXBVqRLvjPHFgYVWcPePOUwWV8PW6Jck34MPGkqhwJMuwf7lu8j0CgHOUTlIQXczkz21SXi/JBk0ViXQWdrmMNjpDkue80Dytja42vrDFfYauf61i9z36CDNm0rhpOYBurAtBcQPTIcp7bWkaWc6adjBQmDUXDwWt5dM3eSDjctvk+23JMEX1Mku4xNkk58cJr/i4VXpcoz9GY6SseJYvLQSQi98Jt9NxsD9tDDweOkOI+1V+DDKHdKNpXFw3kMyKa4QH+k5QIjoHBXUX3OcoytguhKLr4wdhS80a8npV5PBqNwFFj14Qc73nSKZH/VhkpkP7r/Kos6LXSQ+ZA9ptLEAs/Bx+OFrBUzYUol1RhyE16rAhtvg0AJvMLubDVPNtwpvdLGYxF4lif1uMG+5JnY0ziXdd82hutcYuDO7CcT2UHKaC0qTptF0+zwMeFiK/Jmp8EJJdPfVJ2D1mzoyIsfimXPeGJK1n9R/f8xwL9mhqaEuLN7Eh4xzfph2hoeae7+SgJdc+MDEU+9t5VAZUgn7cjLQghcIfs+VYU2FG2i5lkO//+iZQxKr6HFJFXArNobde1ygvrEKVCZkwZNrtrgTAe7+7/kCk2g4f16Af+8FoNLyD8RrhMHXb1ygctNJav20BAJrZMCqfC0VSP0iHYV8HBBUwMaqeLJ2WiGM3KuCW4umgspXf8yTW083iTwjVdoNuqL7hYsK31GPOa70unMUBOuXocPJEtwsYqhb+/TAWamZtFZOIR9G8iDh62VS7nuDPnzKwvipP0nXIgb/nkxC8atyeOwHDyvDg4Hn5IwvU9qE7m0CzGuR5tR7zUGW54MunhEwPu8IM+L8k7l4/SSNPCAFvetdQUfEPt8sDVHnqy5of/CFg/Va5J2zABosWbhqrYPDJSwM7crGoaW2sPhHBt3opANx9R5Ys4TFMUoMGvk9EcqfsMVAtzvEXcwNxM9kwkPL3VSwUjRHmeXYmxmPjyO7hCH+46GneAx6fs6Bh+ejIGbJfTI62RUcDdyAHe+DdfwMfLrNEdqnh4NmQwzK3mXh681sVB4YJPH2wZBwUkhSmlxwfhofyoZSCC9UDZ5v4uDx8//75nUU2ggSmQlXLeHm3wntq8pNYcpPFs0X/qby1a2MdXUq2ArNiOm3bOxR84aBRgHmnuBjx3EXkH+whwRVpKHYZRbivApgYmMkFm5Ug0MiDmns1SCbvutiutgspkxMQeijyaJHVCv5fvsBkTl/nCRM5uAPRW289VGAgzbiaNiYDbvupOP0iVn46BOLB6Y6QIqhKEcWz8VCw3K6Biqx++NUWk4d4AZMhxngAp+bWXwkWuvMA3E4qzgNXqdL0wW8ULqCsmC0TwD5iRVoEROHD62e0/0jc9FZxQgTG5LwWjUXaw474cJlUpj9l4/bHkbApeMJYMRxgwHR2ptLWIznRIHSsZlw3yuVM+u8Dm51m4p5owXYeUCPakuvJI43nOGf2D1SLlOJ00vkITt1LtCuEsjnBdE6eRaYwOPERsGKtnYlwomlrqgVdJmuWc7DjbOqIOqeHcTYOOFyqyoYmlyI1i8EGHtJAN/21JDThVk4frkAGbKZ1utfoa754ZAanYBbTh4U1j9wxqtXf1C576W4PdsGXkybBdsz0kQ8epFUOvGhq38ajvmoCuM+8CGzJhhXPQoBZ7sNwkzFUnTxzSIZ8rvo7D+m+EoxhHQLHfEZE0B7fkaiTWsuSKjxIGewCmeomoGqJR/3bmex+QuLt9dn4ARxFo/TMcBJDiXLUrLgY4Y7/BuyBuWCHDQb9sYnNm44O6QC1ty+RYOYGHwYUwXCGBd0OBgFDfL+qGoowJX9LvCi3hSynI4xK2bm4oiHI/6aFI7XNtkiMTJB+V8l0GcxHvbs3UMT0nk48MGLhux/QiN2SxHNhUWoqJYDmnWi38yvwC8DMkSuxBrq7q9oN5N/RBd6V0KYVCHo/Wbh7DQBWMdwYQX2kdCrMpQuzoOPbTG4dhILPbMqwdBLFvSGAuDqTQe8ZBoE93h7aOd/peSkqJdcs6XhQmezUMG2Es9eS4OLp4xB4qA7EHEGa2dUwidrB+hKnAS6Fal4cdgKFA/nQ8lwLHVPqUDtOhZkfXoo87UE1QfKQejKhdG6UvzZ5EK1n6mh7625sChvBdkoUYiuThUME+aA9zcWkvknXwi3Pc4ivFEWnN7y8d6an+Ts6jK0ajMC2dRSOF1RjAvlbtO/vClgb19N1+rPax8bPhdMeFagKeLCyXYtZ/LPOsO6ayw8P22El4xc4ZZWKXTpT2JOOBkjM+CIxZvD6ZbRDmJ91RoLthZDw8IYjHlqjwqr+Ris0sMEauXj4v8E0HpjI4n38cNNgmJ8cksVx3uOJXWKfLJkrvTZrdL1DPdwOQ7/0cTty6yF9n9nQc2HV3SrVTQRTBXSOzV8SBeWwVZ8StO6CuGWdylOvNlD5gjLYd8Kf6KUVkxrY8tQVkwZ3Vclw5Rke/R/O0KKjHaTMQIBbNpZDOVnCOh2VICRfAE6d+Zh8L8oPC3qxQTP6+SZcCw22lmiZ3EZVCQE44/qWSAlOtOGZQYwtfiCcNqscrCoNaTzNo9yPPXzIH/sGNQdZKHrzmb6yEIVnuW54SPVXDraGCu85vRZ6JJnit/+6cNmbVvovBSGr6ZzIWhPFhpJVeHLKTn4/tgXpogfgjKGPMzf6gSxv3Ow9e96ErsoF7VWZkO85CzQmLCOti32hurxsjCwtApufghF8ydIxom6b+JaK/jx4xIZeW6EfkeiIcufReNtHGC/GcMeLS+8+GkG1niUYNmGSJGXjJB7vgIoFavjXOKHQkZ3Oqaf5YPiChuivriVpD8uhLAdFSjjx4Wv78Uht9iVli6yBKVT3pillQYmnFgU5LMw/CgKJvDt8MLqVEjxjIWoKeaYWMeHPpHvd6yLwgnji6HzGBfX0BL4kTUDp09bx/S8LYCpN42EmvPjwOJRGjwZlMFEy2Qsm3KOVCytgO4oLUgwY8HKYQpefmkCtSmjTJvIe+2lLaGlJhacnleAyos0WFyaAqYP/jDGvfkw82ACPCXL6baBQly6/h6dEVsKO6M7qO5DZ+LlkQZTni+i3e+ni3I7DZa848N2s256oicTUo+740CUJlmiFYfFSv74t0QX1oqccVZEImyYs15oxz1EXbWssby+Eg4stYD49Rpk9TcWNn3WRu/jeXDeuATV1rD4tSAaHAdYnF7aLPR7L48Vz9zgc1sJuabijnfi+ujVZDPMTJuOC8dxsCanEBZ3+IDXtyJAz8XUpHgxObQjEhvSPDFX7jN1lhwk75J0wUFGGgpzqrCrmQ87Og/QK+MsICOlEkcup0FvrBXZscAOxDfPgIJnybDsShg0SqbTa6fjYesXFqTfvSSh+9xA2S0fE2I86HwtPhx+PQ3FP5+mzzOjQLqAh3cVnSG8rgr6Jxhg3KNgdNn9kNhnsfh08B85zeGRDJ1cKLXTR+c3seilOQlTVk0Gw2Rl4PF9YLnPBuI6xRss9l8j3QsNsBMDYOrsZNjzdjascRmDxW9mMtdCRfNhq0rvxjrD/Gs36I6Eabj4Lwu3Le1A6mgpfKMBMFdFEbEhGX7vErnxHR5EvmiiS2+zeHLNNJpTUwF/dpwj029dIE84IjeQmQzRyxXgr9AQBKtmoeWjcLg6xGJE/hKiW+ADprunw6XvifSE7Fua8KKNbkmLglQTP9hRweJel3Ho9swEjcKQHouXhyDdbbQ+JxZ28qUwahmLXiqe+FhtNTOJYwdlO5Xg53o5DP9pBI2KAnBZzyNFTYN07TxDCJ3CxXUn3xOvLUXwKFARbOJ7ye1QddJz0hXOxylglDgPgy6L/EndBle1nSMF4tnQkR8KtaKOr48ywaFjDnit+B2zwmUKZVdrwSMzAcCEAjA1BewZKMK3f3JBtfIaKZwajuzaOkaW0QAGfHBY9it1+G0AG2dMITOGNjlJT+OjVeFkmDFiCDeLZPDwbif4ISjEqHhVDPswESM35+HJ8WU0b8st5rC3qIMbpXH7g6m42yAD+oLvCwtVNOHRJ1OwnLSV6j+yw+9j7dHskCwuGGfBdH2oQg5bhYuCpbEw9QE9/T8fT2BhbZEBrL0RDWmvIujYNwFQN2oDy8LK8YJ1KDSBNRxJO8GcEGXFAoujdPJ5Flt63TC3OwgXn9lHHd3v0PCgqZiYV8hELG2g1f6peGkbgZ02DFbcOEeceRbY1G8GWhKpmNNoAp/1KzCck4v3rURnFz8bvTbr4q4P+oD3fLHB/gFtWNdPif15UiqYQRQGeXh9tyl8Ec2yV5oAMvW44K2RhuOjWQz0TUND6XA4kcCHczeMcaV9MRpZSSIj6u1q18fk07JsaPP4yDnYYIKlOokwsd8Df0IdOd3pg2mlFSg+PB5NTywh7bFHaGqSPuyprKXLiDbWrPtB8y4PkU2NIbD1dQX822MEh1bfIopFW8iHPCscLZ2C0fZR6HIhQ2jlweLrPzOw+IYWLtg/GWQrzdHGJgkVeiRgg+BJ+7Z7PXTDBn0M5FRQ7Z+LiXhBGe7Y+ZWmnjBC63P95GlbB7kkwYXnjyeB2x0BPGtLxcyrITAybzwUS0dja70fLo+dCNkzfHDcMEFOSya1PbaGZmicozW6kfCjhEDLkmU011kLx/T9IBLbCvDiqgmQfVcDJvakYvltF1hrqokr/7KY/KcKDvxgIGfFLmH441XCrEcsWITF4otCB5QLmgROiv30+8EKjFMVoEWYE6l+8EF4t2yY2HxUAVkXP+xdl4mt8kVQnSJGhIYskddPhO2KjvTGqyLQP6mJsZIilr/Ox5g0WWqwJoecNPSEgislmMEq4pUT82nwP1l4qGhBXMIbyfc542A0cwyceMSH0Hc+dOiKHhSYTsP/BllsmlsA6dtWkhelxfBqbqvQTU8O6kOuEomvex0v3PHCV1skYVO8B+jkBnCGqvvp7rceuHyqJsYNCYiRoijfRw5Qxc2N5PGbecTJQQAGR8fin/J4WFUjh+aHc7BovYhJdfJAq3GURryxxi7fHFhK9OBuTwGsnclDrV1mOO/vNyF6mqHk0d10UEUC9HX4sPFRJgwffyes6AmEe/NZ2Dwci44L0mFV4D9h02YNrDENgO+RJRhXXQ7nAx4QC8sq7GZj8Mn4ERqQYgHXXlWh55FA+LQrC+a8Kge1UZHDmEiS/3yCsPukrsjJf7fbSfpiussywpE80o5Vl+nFE6Zw49EXYlV5i4lZSiD3SBoIL3Ph1q8IcFgUjYvVuRifkwK14zUwYzQCAzrt8PwELuqMiMHXr0rgmqQG+wPf0/ob0viwZRG9/SaGoaK8anprCDU2P+jez2Jw5Nve9mfTdcFUKxu5qZPx6Dh/cuPdNibgBQt7bbPw7bvn5FJkAZoc3UojdaPwCo8Lw8KzpMFYnGk+bwRGK6LRqFod8zpcqKnaeNw7zwO+vTlDA2slQZ8XQielP6DPfxejTbAnrDwSj/cVsqAugMtsWCwDOVq5GBwcDQsOK8KlbdYwx+4DVfbVR4VLs8ibgtf0r2itu9ZdJ+0zlbG+QoAP/6Y6PZUKhJuO1bQpKRlWHDcXmr+JYkq+quHBDaKOmd9Kk65eo39sl9OYAgEO8IJpytsOKv0mk6GtKfByeQIELDaFCxvH4YUGFmrOlcF/21NwlY4i+J++Qfz2zKc6Im+ekVdDrB0NcCgwC10mnyaN6iawOUoc172OgPPxEuCx0QP3VoVBdoyKCNCCgE1Rw7BSGRyt0KQXdy0VPom3pkvzdhIffTXQUDpKZWbyYFyao/D6tWrhjm27yTSnPJw38SUJe62FnaniJKJGAU52s5hUPlXYLYzCf8nqePJ9OfS+NoU3U/aQ2O4UKH2rCoXF6Xg0/gdjupTFkW+bhR3cEvibkY/uD1VRMEcDr37cQe5dlYRrUxuEo4edoCkuBiWnb6YF156SJw55OL8hAP+JHxWm3CjHZz+UMSrQsT1qhwc8+zUeroQcIt5H7pE1PmPANEwcvXOs6Yppe6j2i1WkGqPo4hdccP7yH6NWsI5e7YxAtZgM5I0tQHaHH7ngpYpB+8ohtUICDuVF4Z3/dnBuZeXgUksBtF1YRAnHHdWv2mLvslxonPqDTtttiA2XBWhXrQfF2aVokD4eJF47w0dRbxoH1wrnjn9Ip8YFY0xIi7ChpYzs6GJxRm48MYr3BLWVxmigc5aGjAvAiMfSkGnkAKOnVfDv9h/CzU9mQuFkFh51fGamVykz03tL8IZ7EtVTCsT9Gs3Ea7IdDT3ki75PSoQf6m5QM6YCh+JriOEDeSjPTkYvNVfwOdlJveP30RrLGGgYM0zWn51DSi6vod995uDwziJsbE1Ds75qYib9g1lhdJm5q83FL9NU8Kt6Iv4bGwK3N+SgOrXG3McpIPvbBfy329N/99qIQykLyzI5yOVW4IKX42D6RB040p0Fk/dOgVeu8uAjeUvYVJUI/QYeID01j+rdfcRML/TDxR8mk1vLo+FLejw27Ski6ndZkU95wJyXOWCpaY5DFkfJCRspVNF3h21HRXkaHkiuqhii7GovjPZ5StWnKFIQeazq20FySL4cVrpUoWF0Hujm8EGzY5g2FPAxoUkOoiPLYH8/C4VfK/GxqxydeK4Ig+9XgPuub4zTiA0s3dHKnJgyCxRV8zCtlgXBTG8YPFwFgc/VwWTnGDR630BSDFi4NGkpef5oPB4fw6O/s35wNj/rFNaFjSUtw7p4rGk9HYkJw7hT5SC3u5UY2WtTN+W7ZJuuHFzz1UHTSB5saiwH+aUxKPF5Jmz+j4cWkjuZIS1xPDK5CKZs4mKosjXu3xwBgwFTaYRXInplsBgnOwY3bmcxa1MJrLIyw+OCz3TkGg/2hpwj78yzhNNrcpGbUIGhH4xxOEEWV8Ubwofvk3F42B5kvaRhU40s3LQbgwX/ZcEoEjRp8ceQOeeJTVQo7CHqTAqnDM0Oz4Sa6mby3bKV2lacIeqfC3BP3R1y9s5XEl4iiWJ3HhC39XFww7gA7P47TxtuT8CzB+0wkbuwfcUqDdz7dxZNteDD+jeT6duUfOZNHQ9vLRXguH8shg6PA/OyKnCsC4C761pJ5GuCJ6+z+O2ADE6J3n52as1dzoupV+gbvg6kP/dgBLXz6NohZxQ3LCCt/QNE+bkEhoy3hPW6BOqMBWhkVIaxm8VR/Gc+3N19m5OoYwhXnoSA9rFblEmvJoaP5TAmYhfdGjUX6BwluJ08DrZ7h9HYto+kTHUCtDiYoZTgHbkQrkE4N52EkrWVcPWLyDlHBchy+4iXjxduvxpCINMLta1ZXOUlgUVjc1Go0UaaJoaClXweXjoiAKaWxfrHSxkFkgFG+6NAzEgRSxfqgY9VPjbxTOHum1N075/rRPaOqAs8w9DZeyE5MZiI/aJcFCubBHVfFhMt2iyMu/2BmDU8oZOhn2zqdAftoyxmLC6Bu59ZMC2YBHebxPFWn2274X0dor71J+3dl4z7z+yhtZgMEaueMs8mONK/3TJ0Sy8Lr8ckQebEcuxzqoSsvjq6SoaPx55+J+sUgyFXxoZR+cWCUFkK35VVIBNsgHN3S1IdPcALbcZw0VsCZsuXgWFYOFoGOsBlQ39s/7GfPBDNp6Z+PuxvWiBs/tTPeaQUgRv0HfFvkAHWF2qS8kwWddaVYuw3cezyYWHuSn2cd2YFOXAvAAJPmYHegW/kzYZs4m9qQU3WpOIENQHywxnSLFp/prkZXbQ4FPuTSyH6LYviXXyo1zpFc7pZujuhluxbX47b5ouh/0tT3Ge7kuLGp4ST5o1fPXPh5+fFhES6EnJqJ222N0aHex/o9fhCGDviI3Q4bI1PnrKYILWI6JR+JVWRfOx9pYkOKzTBs7MSq5SDwFV0fn8GwumnqYdIku1YQu8qQNXQXBivuphU7irH8S7tpFDeCUIe/qN6U17QQ04zcdkEPvrdsMNJp2yEkpEaZE1ONlHPjgC430NeSrEQ3uAISYby4JVRhmfrPUB/sihTnb8QzfV/SegSF9xdN5WqqhajXUQsvvmRDz3G1XTFbwGkiM9FQcptYj9GBYyrvcAk1BJ1HkZj+Rln8r3xL11OWDi5PBjVr4v87Qofje8lo91BGYzwqyGb05xAapAFq05tNNl1laQH58P6o7n00JcUapjgBR1yn4nkjUkwTy0RTf9toUpjszDl6mXhAtH9/Ziwjqb4TyH2vBBcG8THwA/eJKuZhakn1hMtWwHIJ2WjXvkofSy1kb7/lYPjR2ahbw+L8f95Q5IoP+J2byfnVn+nhwYvkXjHE3Ti3pGz1VLlkDufAwO3HtCjE44wOQ8Hyc4zJag+QZQzx61hS0sccqWKQEGUXxUb91PJzCzctsWGaOwwhv32MZRMCYc2Vy5k/bLFmJ3BcNfzD32JojvpUYXCjI0kvIzFffZFZL8oP5NbmsmoawU6mZhj2BEutBy6S0199aFcqxj0GpKgIzoZ9+0cILKVOxnleTzUXsFD3rRcvLjeArVKf3B2LGinGjd0MWtZBQwmbSQ3x2qgZFkLKftnArmlMwkv0Jbzvu053R8QB+aPS3B4vCX5tqkK2+J5CKL+WM5cJh2Hd58N/lSIPvMtce8vBrsXujPBhS6kaVYV1gljoCTrEzn9JxXCA7Lxla4viHNLUVyb0gFRf62feIcG/acDLwt1YYNxOWYUi+aifTE9Z36Irjs7SEJ15uJGwTO6v1KZeIrFY2+FNlrPKce4A6rwX4Efzjr4mCjtLIcjYc3C31NK6PdfRViXLY8Jv92w9Y8paHFY4V2rj6RDlUEt5UCcSR7RK9k8/HWcBRD1d7T+azojNxsyJnymUcYcyJwjwMFTXvgktQpJRwa2qmVCr+tTqihbiXeOEvy3PQ2stuXjLp9/REXEDzdOfSLaW3+RU2fiQb6/CHb0CjCzMB8nBuXjIBwTrjwSgB1/+om3winaNX4GHsnSEep+FuC8xR+pq4hfEn//oM+sdOh3UYaINTgSyT0luO2UyB3C9XFiYSV8PChLz43toQf2TQDPTDXc+pFF98BoVHkrwO7AFmHp5H/EqcsBPftmQ7q+J/oeiEfuYBmG31Ol+z1lMcaxANwvvSdHRd7cNeYP9brzmwSKfHHzUV0UiPhtwGoMml8Q7flRE71yVJTx9S7o/CMMz4efI2sFhtit746O1x7Tea7joXvmJPpjuSwM2KYjV/IAMZ6+hxodlgHFa5FoMSaJrh4zKqzbpo0fpfQxUskP0iQKwf5KKJ1b+47YWZ6nP+pdhcNFYXhprQAniVXCYhG/7hQ5xRr2GH1F7MB/YgD8t6WHRnWdpJ2yAti5TozRUCmDgNpH1KyCi9/uVTLHJ10Var4qwJLV9qDTZwpMjTL0BW8l+hI8xG3pYFlbSNcogFBLORu2bS3GuieWKBn2lqjtsMdf90zJqlwBtMhb4VuOJ5UW8XvlfTW8qx4CRSO5UP1qAXNTZrD9SOoS0h+jAWfPq8HF5yvpv4BuusJ1BpFev4KoK7XTZw4VGKKXh0pBWbhaQRvurWTBIEYVHY5G4GCaFTl+Uh7ePLjNzBdThgcbuVCqdJUYX8mAWM0O8tg/H+ON3HHHuQBs4gZTqyA9hNvOMPeaObPzVwJOjD9Htg94wSbdBzQyvBS1eyajz4SH9K9YNLSGfSPpkc6gYcrHvE9RzP/8yTHTEPZNcic6n8RBYV3f2dXVChCwOh86zwXhl22xzJJ1zXTyrctk5eVI5sFcZfDOmQFOESFYXBEHsVUhtCPfGLVPvKXPJFPRlR8PpjV2wLnDwgkLFvX+5EHkeT5Y3L9Pey/4YZeoQ4VQgSsOs+h/NQkWivzRoWo67J+ajg9TBdAVFAXz5COhaU4ZrIpOQ+Vvxsj8p4olyy8R4/YibDqSAEbNxeQUm4CHz55kFoj8VTJuJs5Q4OIWkefntIQLewT5cPNAKLzQUMDjEVJ498JrenL9PbpsXg1NbMyB5VmVkFKqiofjfHGpyJ8lzWbBGFUjXJK1haz2TcMT8nzwstCDVMds4aFjXADR2WR6XyRru42g60Ep3G2eirP52fjHoxDFqmMw5J8Nxh1SFraNLiY1FkHos5uFU3lN5HVlP0nf+Ey43EcKH4R3UbUDAgjWr4LPG9zwio0JdqRn0qwju+n4fQ4wcFwNDv5H8Mi2PuYVKQfnHVrMXJlIWOKRhr4ZEigpdYF0mH0jA0UseLG5cNe6BEfsZoGHgwAuOzI4bXEp/P1TBUGLc2D7shSY/345le5mYb6TJSj+x8JajbsUb36k2y6wEOd5iaSEO2LOEBfSY1uF00Wz8zk5D1zWZuJxBW9I3O4F//Yp4ONZiXA/vo8kX+inLh0dJMBKgIXKfNAaFOCx0lO091UbibxSgo4O7nhYXRLE/shC4WpTmnrFEN4k+uCN5WPgrgEfWccJ+Ln7NtWVNMGlKeXQOG8nk3/QF/Wr1TG+fJgYr/SCwvhfwp+Pjjgmni/Ej/aDRD2xADfMieJMnetBbCRYMOKfIzkl80niqkDwnBoJs5f0UZ21LLrddgGNl774Ksoae2vlUNWeYMmvGrLwHouFiaPU6Vk8tDRdoTKDTXR1QiHMB4CXcXG4fFwUZkWN0Pf1fmgo6w9DvXy0epIJv2dlkNK9K5krNmX4/nUsFsY640FXAzxqj4Tr5g/2x2bAmlZxlJ4lg241Q0TuXg/9NTEITz+OQbNXVagaVgZHQYHKV4Zi4aiI282u0ymuPKjYe4yeVVcXVmmpw3xkoUjtpdBdi4VjryvAYkYo9OlRenoRgYNK1lA5eJiWfNTFmk3RuKGCBf2aPbT3m5AI7kTiScEoaXuexVy6PRHl+nRgs9NHjoaIJdOfrSC2RxfR8BWrqXr9WMy4doy+5f6gimt5dGyOOW6IOUpP8bIw+9Z1EtyegdqvQ+h77yP03wAPtw9GwdjV/nCkZSu1HsfFHkMLjDFRFS6JHeVMKAjFh7/L4NTRdrK/yYk+b91IdljcJh/7zTFiYSUq3DxH/PfE4Jc3LLSEesLvzybI8Q3G66tXEJNUGXhNfMCmupomqxyhQydtQKPJjbhm6MGLjZvp9j+2uME6FBxcRZnx3APGVgpwuPQI0ZHMwa2fFWBxYDKgzy4yZjcHbTPPkrLZPDDmTaMq0l7oGuSC4ZMCSNQjO5zanYi/O5LIcEsKeF+pAjl3Hj7b6APP/3cP/7DoG1cIvZ2HqOLrTiLxrYbO3uJONc510sHPebC0SAa13FOAmf6IWoaL4257XdAfZlGfLYCCujAoadtHFv3SoHGTz5C/58fh/kVIX0mXwvsdTkx9tj9ETyiA4vo0HDmajBZ7HYVPz3dSNWElHn3IQnnyS/InfjbEP/BG2eqDNK6PRcXuL/Tvy1q6+8dPGq8nEK4KUSfwMgPmb+Ph4X5VqG0X4PPtGXhetwyun94k7Bobji4lWWD+7DRRVVDHzn1K+MvMHm0ehwsd1sdj6DLRf0pm42yJ6TjbKQ+PRCsjb3gOzvIOhbQHyuh2cgldnbdZ+NbmAN3jdJQoNBljdYo5JPUpYf/pRPyWv4SskMsFo/glwvZeUc+U9JGkw9NxVaMu8P0ykHiw4Pw1BHcmbCJOshUw1OGI7/L3kru3tWjooV3CzQ3J8EMqH1s8FSH4ki3uvc5CRctfOtb4GOFeE+Dt0x3U2zMJij/uo2HPAYZ2zsZXMwqRjdhIPqAfqC8ywGVPCoC1LAYaG0/yN4yB0JfbafpxBWalfgCGlxJ89lDi7KYJ9eTLkBi10IjB7Z9UiEluFT0wrwAOTjDAaeuLcJ9BIShPaiaSbv9oVv1joZRDLY3suUHfhcZC/vVEPL6zkXZNSsHA3zzkXOaDk90cYlaXQNY/SwYGSuAjjoPA2XPw9w9WxDGdzDQHRYy6+Y15rOxIjvZGgt/FDaQpKgGDUlh8tFgVfeYcpTseWuHyosXk6sVSPK1zRsgrZWHy+iH6eeIWQm/6YNsfbbj8iI9XPeuYXWF76cK2DuG6GyxKLX5Dy5beYzS4XNRaw2K3eTkknpqJkmnxWLZeHja7j5Dx2wowoCsHBth+0iybhadK1xNtXQ/oteTji+BkmN7ylErub6EzozXJ1yUVeKalCtV7B4mHyHFYy2lw9jGLzYV8oG+sIW+OP3jI7qFtH0tBfms5mp5Va79WecDpeYgmjOvNh+/tIo46mYCS2MpM8L1PxnfOBpOyUpzo5Q1/5qej+oAK6TFeR878t50M6eTjgn8sCipfEr18FvSGZFC+m4unV9+gd3/rYstuTVzXmQ7PfgbjzU+tZOF0SbppbCFupbuo6cKJaHI9Gbc8HiDZxVRokjJMHlfZYU8PD+yreeSHURFYH+fjHb9KWHFDB8VPGtO96lxYFsJAe3cdeb6Exa7rYWjnmAInPE4Qu2Xf6Dd+KBhhFd7s9oFF8RVoDqfp/U8sLBIepKsfTkaTz2KY9iGWNBY10J4+CfToVMd3PkHYc8+VRnqxuOkxEoslfNgl4iCfAzNhwloj0NrNg9+8rvY2ESMaXDBCXksZRixdQ8Zse0ReDP+lCw3V8Ub4AjJXXRtnSVXhcTsBotAX1IZmUav/ff/vUg78bF1DZm3moY1WLjhPtwBu2TxaeaoKMOYm3TKPDyqfjOFZzXGyc88oJ/GqAMZsrCZTza/QPfO2kqWZITDfRBmWjXPBTWfMmdq5a8kx+9XEYMgcDoZvop/cjtLFnkZklSQXxnVUwFaGxS+9nfTf0xA4sNocDddYQiHHD/x27aJN58qwWpGHlyaLMjgomj4e40Wl7xeiu7IKTEy1gZnnnlLDTSzc+RePz76waPewXMQ1MmCWXgh/zELI92gVWDPeEC68X0l2bM5FpXHNwjEiV75bV81sG42gM3388OBDFhs+V6FSYxQ8XiWgNxqbSOVHFnhxptD5WgtSbl0iBxv49E1WKO5ybuJE3reAd+tHCKX5+J+RK+z8xULfFXOU2fyE0iBHCFFTJy6xpXCCWwIbuCW4R0GA6+tryVJm/1nfrjHEM7IMB+KDIXfCHPRY85N0iVzE6uol4b53MrSGFsP7Zr7Q5VIwvHOvBB1TR84fJRUIHuHS/M/BYCZfRHqPrqE5W26RLVPWMGFfS/DHexas22TgQrsxpNu6EcZkCdVYogOF2Q5wZGwB+H55R4y8V5D+NXkQ+l0Rqv7jgoScGywQsShzpVkoLnKtXbsKwGF9KVZkuaPLqQ90oeZBwh1bSy6zenjhKIuLQivo5bWpZEDWGCzvc6jHrh1kdE8CHJ8mgOgMFg7uLYbOpFI4ejAPTcAATWuvcBIJS7seS6ONjyQUDdeQd5IzUcXMEYLcvVHsviW0bVzNMI17qMU/AY7nK0GTjTFKmYyjx+I98VB8HXVYbY2yTAU8zRwDHkdK6So9PmrqzQbvvVmgwcQRZd05aMLEoHy5AD6ElwtbjceSeYvmQP3uZKwpUMbC+nKU1b8snLyNRbZxGdka+4z274uG0ZtdpOSNHsw/nY/pqmYQ8pCPiv0mmNTTQ0wxA/a2lWF9oR+OFJXh2+4snFD/hqyRKqYdO8WgeUsGFLMsZnHTQPN0BD3/QRVctPngH8HDMsm5mOS/iUlZzCKzsBD3FljiVQ1z0DdSBc2qYHI8moWMriIY4fgx0/JkYV+hDTwuzMJipgpro1JwYIAFaYdwLI3g45YBFmd751H188MkoY+PObEFdMMGPXA348JHyRhQzpsL7f8qwZIbRD19eeDj0UP/VoRgHJOEv80LkLiX4rr09aT7cBUuSLhABm4FEKtfm+jGID7sFBfDTJd+Mj/zOPnf83tSHC4WjYslSe8uEU+ZVe1btKNwxdpgPJbOgXc22qRVeTJ0n8yl+Wal+LyrHBZJyeA9j9lQa6SN56pCabpcJXgMucOMQzbkz/Q4GOy3x1UuOpjiyIWtXonQdYOLSQZzyIMrnmiTmQ9K6hHAaPJA3JPFcROzsOVyOfo8YWHiDHP81jYXQoyUsKD4AH04pRjqhu1RQmcn6ZLLptlXDjO/zVJgo28Tc9iIA0OZ3+iJjWoQmF4GN3ICILamAuxvzxDWy1ykntvb2w10p8GRtXfoP11LKCv2JTUT4kHKLRykTmXj0z8l4LNgmISUm8L4vhL0OLqKSH8sRCudVKy3cKKB0mvJfHlXgksFgHPLMCEpg+r5p4JvlweUbesg/FnTYXYBC64NMagRUAyhqWfphTxdaGoLxhtnjnLmn1YGjzcH6c1JBbTxciQxveRFNsTIkdDxVtBgaARfFuTAWg9NYCx30/4dhSirE4MWts3Cg263malPMsm+Rj/4+sIN30XNxdU2BOadTqUcPYKtpnpQ4O+KRmtYcOlsoV+cBcBezwd7aQEma32mOlXZICv9iMxqKAROfBW+yfCC2esKcJG6B0k5UoFx+dIwk5tIc84nwgU+F9PT24Rd73IxO9kGG9rjYdtpH9gpowXbh0TcHp+PL96eIidbFcgwMaLPp5+gslG+JO9nBYCIk4Y7rdDgZAku+JANP8Uk6LFMFvGXAb3u5wx6M0pBqVQfb31mMcTTF+alu5J7H1Qg5MZpJmQOS/u6QmF1dwKRG/uL4OVMfHhPHN69nA0cmULUOx4HmBgMzflqNPR1HSmZl4k/lhvQku8s7NRa3761rQK1a/mASwogb48LOGxIh5lPcmHB3SbqIe6M/fIl8HNEGxaOP0cmTvaHKVeWMJNPAwn/JQepJeJ0xddpEPTAA6a0+MGmc35IZ6di3FApbFZJomfPxBKrowJwHLuXeP7yhbB7AfhiTRTsf6RHuXKXaDbm0XchJ4jL0wrQc9gtPBLTQ9t7kuDwS3XY42IHqzeUQssiATxJTua8kRLAmcRaOjIjEbNlGNJ8lYt52w7Tf/V5GDOVC7M7EnH0awUsTEjAlLLv7e8keLj6pw3s2VAM1vM30nb9bOh9yoLgev9ZpSkT6cIVdvi2q5ceihhg/C4WwuoV8Tgk95oz6WYsfO4wxNH/csF87gramx8HD3ar4KUqDfgH7VTqdRJERzqSDxGW5LXI+5p3C6DfWwAFEnxhRS8PPZWj8Wx7JfZHqqJjUSVe8gVoU5yD4Up3GY1NHpi+IwItCrUom5UJRzf7oUupAnaEZ8DEMn/I8uwgt+dEwuCVdLRcbs70fp2Ox7drwq6UIhQ+uE8VGyIg5dMe4fd9diTRai6MsY9G42Xh+PkHCys+5MEHxVCQKQ5DxcB+On9IHOan7SeFng+I92+RwxwMwhffUsix31kgI1SnrROdmd3OGVj4MB5k81Lg2GN/8kHKHmXzg+FyLCENUqmoEFEFQ1+SyU9eCdaasSghw+JzSW+0G7Ym2s2z0bW2CJVc/xL/6SzIj7NDNScR/6UF4LZugNRUAbSGToLmsirOwLgg7MxJwzc3HxAtU126SUaVVjFF+PBvJUo8Y9HnUB7EVleiyrQCyLSaDS3lLGj1eeDQnYu0c4oLna6rhe32CjiBy4Gmm4XAf/2GmbJ0Cf0sEGBS0wvy+x0PiqYeoWLKamTZqDfafggBseQt5GqJAMv/XKQrEvbQhcdzhT07k/A0dcI3IcpUW8r4f+8tRahMh62by2DKVx2IXsSieLMHGO3eTqKklhC3VSoYl6KMvxfOaa+16icHymfSRzLp1LfkBbOUFQCfL8CY77NRwq4MLmuwMHtwCQnluGKv8DCd/Ydg1O5eMtXDAQOn1ZMp3yKgc1YVbAkuh4HtCtSp9DP5NlKJT+QmYqpJFTzIGiKbFiSD0mUCG8UjcdNoFWw4LMrEd1Ho8VEFdjxgYd2ZQTqjbCYeGdPDKNcM0u1t4XglvgpcWwIgpWKEvmk1QMbJCYffCnD3mkyMfzFW5AYFON59Duz0SATDUVOMkt5HT/TL0TcJAuyP9sTUx/bkkycLButYbOvVx6FFdrDsniMe+yULE+t/U62VfGyrLoL6I2NAoiMDbs/jwcF1DxiHR8YYvd8K704R5X1KLNBCdTwukQJHu6zw1F992o8tjikiFtuey2LiCAtyM5M5he7HmHUcfbSzHke/5i8iwd166LpjLakcZEF8zBmi+X0RUZC3ADm5n8R8jRwGBn6nMkFl+COxBC5OHKWcwkq0nniNuhndYZReF8LyqTLYpW4LSp8z4MtXLrwQ7eGStineepQFt3sywSFMFuS/JGFCdyBY3onFmZ6ysDBMADzBWMpGzoFbZ3ppKacE81Mmg92MWeAfp0VmzjQBJYUKWP2yCiv/5zTynijZrQlXR1mUZA2hforIqU9cY7gXZgptFx6mhRd2UG1OEYgdyaSTruZDhbkRWhjqgtoCAoenPSayyvp4/cxu8vBINNpIFOJtrU903lVJurohCoOWi5gkpAref7GBAb9CLDEeh5pj9WF9ITiub1xEP1aqizjoPZVotONwzv2gR7eJCTVHWQhzvcQoP81Eb9M/TEzHJJpuWQlXQ3xxcoAdSvfp4H+7J6KVcCs99x8flZ0W0rl36umV3F+czq3FoOwpjnJ0vtBwuw0dLcwmk6/MxoNCT4j2aWFKd4XCjOMakPK2HK0jPOEOEdIvVanCvbNUsdVNBjzOq+HmjLHYdQeZ9Gu2oDPmEBn3XR0qjB6T2jtJoJ6A5HqICnb/zcLSSgFmzzcEFdWJxKdiDkaQZPhPTBlrK1l4sYYr/PSPC1f9npM7Ji+IxfYw2CfK2cNKVRgqpgj/DIep5ckUEDvFg0LBCXJiaxF+28uCeXgpPEl/SX/EnWGG5K6Qs1xp0DmWgjdvSED0nUhG7oMEU7JNFpcvnEQ6g8Xx65YGcqJFBmL+sGi7PwzerIsV1suqo+2+fGG8ohyCnR583XCQcyovCSSuB+CS4xWQpDQW5OocIaLPUaj2ywyf7ZsFj7xKMPPgSxL8Jw+l1/Ph/KeTTOMgi+/UreDW8XhQuqgB1zXyMNH2Gb2yShz/dgvg9SdVeC6owEy3UDizJxA38uuFu78ag5E80ontKUK3m6p47YkHPDf4Rlcp5oL+sAcNM/DDLInf5L11JFBOHnpVLqTSkblwbbUmfjypjKaRUuD9zAgduceYlwbHyVzfcKizVUAb9xlwaas97AtwhcdrPtOSD0fIt+dc6Cu3B/9+HvxU3U17HTNw5TEuGgX/75svUcKHWRVw8nYl+P0LhhX8lWR0pALS9X4Se3hHVAa20drztiLvKIHZr0Tupid9dvYtAb6te8YsjVhBE6QPMr3dBJTSRb4i9pGR1mWg/+V+EilViofvPaPFJ0Mw2D+edAnnkEpRJ8t4bSMX7QLwskcamN+YhUf6/IXmitL0gIiX77zjY78SJVvbdVFT2CT8rXaf3J7HYqVxMxE/5orpa9LhhZY6dEy+SbWdLDGtpRRs1+1hxnel4PF1Apj68gTJNJwDuX0CvGcgAD3eNfpF3RkP+8/B5zvy0avXDw73+AO79ivTndBH+3YG0KRfbsLIzfGYM50LC+8XQNWtj2Qg7SP1NdxLC66dI4l3nWl7KQtvO1Xo4vEmkBtVDB7PtOGsAR/kLwIGzOZyvuS2CKffKoJ5OQwumP+aU5Lxj5wRcadN5ym61Oo9UUuJx0/v/cB9xy/yWbRPvws5MBjhQU0v5cNZOQehr6YnaG/IAKNpArpllxw8bGZhn7sZrB06TBx3pIPWo0Ei9yQTdtjXk09risG5nA/zn8Xjo2IpenilP+dWVSnINERDkFE5rDovRvo2q0JxiyrsuryL7uutBMHzVuZ7y1Hm4mUhY+AVAKfivWFCrDN0FAYi1ZspfDBuK+HNmwQZO5PogzH6IOVEqcGKLEy7Wk+0mMkwMKqLt4ShyDS+EV48og41tqdJ3Mzr5NCyDNh/x5CjIzqv8F4Z0Ijj4vKHyUQ9mYd7KwPpibaVVMZWjGGMi0DIS0b/pzvoiV3biaKmMa2tYPGAQ5Yw33Ky8GIei2EuLwk3xghWDqbSvqUW///+iBuxC8mSBHVcHVeB89/1tXceFEOzhcRx1unJ+Gw1j+QtDccmo0Gn5tXiqKZ7lJxL9oTj0btIB7+IqPHDqE1SCW27X4ppjTOhtCsGWhVm4ELRPIc/KMDkglDwnBAD9900hWMyeojeo0KScS8b5fskqIdcEVSLZyFvMJ4sGM3Fxb2z6IBODEg888dW0Xq4DTysIT4YDYs4Dfs2OGV+r4Jp28aQ3PCb9JbjXRI96A2Pb6iB6Rtfmhorgfm2V6hZn5jwAWTiDMUA5E52wS7fXZRvk4Bfw8YL/W7PBJ11Rznj/FjQaPre/j3sivC1pym8d7UlkxeJEbveEEh5k4yuylxcVlKMdgO7aMsfFpgXQWB54I+jbmQivVPcLEzxkMELqn+JS08ylp6PQbFaA1r31Rq+fmHR4UcIxoqy73+cE7o2mCSd3kd5K1uZW+kSwg2meTju4lO6pGIugvQKxss/FqVadtG3nvLEd4QPI7ujoOt1G7N7Og8/lxWjm4sphCTMxrrXeWizyfVMcm0IWp0fiwXTY0BPCemnTdbCFSIOnvFiUPjULwZe1z2h7tlZcCY7G9tVAvBXXgnMHxsOjis84SJPhnldneXosEIXw7hSYP15GqE6QRBVeFZ4Q/MSU76ynQQNy9CeGD/o+huEYunq2DenEqJblpDz6cZkYkP82ZDbIZDOm4X6C1lcs2gV9XDOhcc7S/AjZ7OQ93MlPY/GwMZo0HIZc/T7s40eWewg8noWZcRK8WKmClMtngd7Az0wNL9eGD9xMalXYsjeil+Euz2BkRGdV/iGUgzVLcC0OVK4r3CSY35wLbnQ2kH6pjhCx5wrZxesjCcW98JQfUkW1Fgw9LqpI66sdIEPB+WF83tFc55YCgqXXNBW5SAdWLudBO4V0NZSAjGGBYSN0GNeGpmjx91kVDK6QptnaeKOogrkvPfD7zvnchJMPfDYLGMknBDynpGhVfvUwbrEGreHxuJoZyXOTlVs93xVhaf183HsfQPcM+aE8E/9Hur89gkdHfPRSbsynsprjCUGd0vA5kIds8ZVxIcry1FF1ZHGUlP47D6NkVv3hBbtKMcAGZ5o5iSx//84OsuwqLqvD0sJSCONhCAg0qCoMGcvQRokpLu7OyaOgYqF3YCoWGAHGMxeKoqBgVgYqKiYj2BgI77zfz/NNZ/mnLXX/q37vuY6+yw0ACfxSmi6/qK9eNIjsujFYxL+YDpce7GS1Kwqhv38Muzt5cMmrzpSYuzcfvGbGFi1CPDH1VBiv9aW86NFAI5PHnHs4z2ZGdYpsFRK5NrWS+hqoyHyRbSv9Vfa4QfBO3rHiY+q9lOZ8jIW7CKqRFxRhtF6j+icuDfk1B0upu5YKSwq5cL9jyw4363AKV+M4UF0MYz259FJr6filAx9cOtPaz8SrwcHVIuxS8SRu748ZY7tF8MzeJ4MLmxjPtuaw8WyYBwessS81V4Az/3hr+4GQhekg8o+JM/1gdF7V0eXf8rArfbOcGVLC828zqLNh2CYGPRCuDWAz2g9v0QC5+4iRdNTwcJQAz76ceCipQS5MofFkICl7d/qSzFCOQbDLC1x0VxHVD6vigUOM5AvMZZo/lLCU2emkbPPuJC0sxxuh9aTcfZVcFGjCnWmK0GonA/z6YY7vtjsh8qtB6nekzI6M+8C2Swhgz9v6dCZNcU4amWDATkTwbKZhRPHR8/Y2hdDy9AYwNF+Ile3V2indpq5dMoJE5vzwfqmMdre5MJZKIBnsrbka28BBiyvgMqHSHu6eeDmtJ1edTKH1R2T4N8/FrN/hDARd+Lh5W5TXL9CnLJGalT6pTWcVFAjnsfcqNY4WWx/V4gPTQSoEZgPNoMaeHCvKxnVysWQeaNERumgMKk+BcpOHqYF+kH47YQSNnlzaZBobgvcI8nXGxeYPdl6cIyqY7xzODg1BuKlrc10T4eN0F1LCfRfxjMziBRhJpVSrSFlsNwfi+cvFWFdTg25yuTRpmAbOK5xTOh1/DG5X+UIi9xLcN7l2fjgXBksDGTx5Pxjwi8mgCd+S9N/wR+JauEwfV3tiTWpydAcMJ/c7OdBnOEnGrKYhVpGAQpFGbg5gof86PHYY99HMm5mMZn+FbjnzNV2a49EnC8ViLVxD5nm9xNR7chy+tMvAVbSeFhoKgtSq4TEensRGubmI/NKG7/46IPicJvwivsRqn6khdnzXLxddsJlmtKfAsIDhrioYwOJkuoi2ZksTn79jd6KsyMfeHPwwTpDzK4qAfdZfHQ7XcrUW6rDoVezyarPLLTEz8KKyTFQbJgCdzq308izLF36zA+2itZq1EVJ+HuGGo6T16B90k7tx46fIappQmHvBxZKbm8i3fKvaQKW4dN4ZaaudBHx/rafqMb4oPa1I/TmpJ80uraYWj/bTF0uInk2tgDf7bSEB0HzmKcHxsHo6y6ibp2GYU+t0StjLm7czOKuu8fpL9HvShonUS0VGVCOeEr7T0zj2D73gRBWHwx6KlH44ILw3cvL9PeGv+TQWX94/iQP/yTnYEqiDtnWKCSNqj6Y5jYd+FtYMOvRRdUDu8ifMQKY80ga6u1F19y5iOnX4OKXS26gcUGVXD6gix6mU+mko/mgue4yjfjApwcluGgu8ZRc3OCKV7q1IMo2G95wKjBa3gkUvK8ImQpHzIkOFXkeD3+3pKDaaWe4ECuL26feJlP6u0jNz1x47auG41MT8EVPINTXaUKNbidzc1wp5vpIkR3Z08nD3GRoGN1NCn6PRfdX5eCgMR3r8RHdujgBjoSqYY6fAGeql+Lu+BLo8OlhDEVeOet8Kvl3+obwgFwWdPfkgaNzMfS5R2CHXyN53xkCQTmviNItHsj42TB6My/QB+IsVoj898nKuzRO0wZpiQe4j5EhZ5Mj4Vd6OQQlzcB9EQKU8JqIS15TsqLzI/UdsYb3X8TAPtcQtIgAHRcWQeG6IFTr+PG//6lxzZQ7Qo1CM6ZsPB8Cfjmg7VVvTM0ToPoEHk6W/kFU1JBJlU7FVhse7OEX4VGvOZAwhcVzB3hwrfUU03dyHMltmIO2F1k4pFcK+W5VKBv0klIxacbTfgw65Mvi1xQBnDi6h5w9HIN1n8fDX34aNjvo0E/XguD81l7h2vZClCtiIU1lKzkfORPX1EnAtx0VYNwmCYVihbip3x2X/rXAQ185aCJi2A2RitjvmYgfVqTjtGwFIlhZTUN9MiH2BhcbLxXjk+csxF7to8esZNE+uRgroreRJYaVeP26Oj5e59teN7lAaP1jBpCBpZzl454K1e3MYBwbA1W79OFvJR+9XylD/i8fNDyiAP82KOFP+SXkzZ6ltP31TXovqInafXxLWtd3E7d/0eBjqEzqgwVgrTsHvv7wAc//snBxfSf1qnIFTxlVVEpZwAyOEny42RFu+IbRTxra+L4vHA7XCHDkbRwOVr4UVpz5SIznOMPRxxm48RKLZCoPK1q+kaOBGnBumTk4jPLgY0gpcLafPlsoFYS2d1nYoldAtrxJF064p0qmuH2m633M4GLHI+p+Mx73zbRCX61SzN6pg9eXc0FqdSaWQATJfexP9lWWYHJEIjhERcEuiVR4/fatcMIKAd4MUAGDdhb+ivYtkVlAXh87JNxwVR/u/EiG2G/jQHlwFnpld5Gk3ywm3pKEZRmeqKISis1L1cmVFhM0IV+JpYmU0CEgB+O5xfBFUwC7nhnhHt+HpG+P0CmgtgAZFzUwSi9CrmIuZ+hNK416Lkk5ZY7UKauGrPx3jlTNjMPGYR4WaXqAX8NumjZiCjV+iZRd/7930upTvWZniFebg8q9I7TrEwubunLgC6NFApKLwJ27gjQm/iT37pbgjaQPQvgdjfzBKXiowh2z+jJheL0P1LdFg1XzeCxtj0XyYi3tEGXLNdOZdGu2H/R9aBK6tcmB9LNyHHqRik9KrMDspjVYvmTxy3Mehg1mweizQpBbs4lqjbL4/PJcnK4UQL981iFb8jyFmatm0c/DAgz8W4VkNAp15qnB26P2uPCKPxbOI7BN6yKj8pIPtCiRo3chGsfcCYAEv2IQe+BM6wqi0NV+I+2adkfYkVKA99x726P3ZKF0tiNkKtfS4YZFdBMnGDssWFj0o4mskPpOOUfriNtfBpt36xFwysSJEdeoxv/eR31agFuOBEO8UwAVKDuhRq6QnL5YR1uPp6B8YymMjdWGMYKvnFdPWPyntII+X0zof0ttqPGEEapVcZqCaRR+W5D9/+eXp7cxcLkmBk5EesDGRHXhnjmJ8HZ6ELgaycCgx0Eixq8A7UhDYvV0OpiWqOPNSjP83xlA4TMsSAw/jA44uKDlf3x4WSnAU8Ui2+meDS8ml4KxZTIUb3GHM78mYZqjC5U7WgVXNC0wIQegiKThFhcG37zORMUjbv//bFbWsRQ4UR4LcsF60DXEgvg+Fs43huPT+4cpR96YfLO/SM61EdReWUT23xDgl8ByHBbTAFlxL9wS/JdJCBBDyQ8t7Z8u9JOOXYXwUxgK6t9TaEAYD9YOzKYlUfLIqc7DU2/KcYWI0+f+3UYzNCyhjPhC1kVxFBg7C6fsT0KZWwIS0+ILoZ/7iZIMC6cna6CaqPNXfGfxpk05fvLMI2KFdtgvYwxHPsTgu3J/XGaehzqf1GBDcShKNVkxA1QFElw3CZ9ev0evNffQpPUeVGX5eIz9wsOtprPArKUcT1sI4MKMnbR5SRDYHy9FW5KHKGhiXgyaUHuNdSRnvQqWVHHwV48A2hzSScXYE2TFGRY7m0zx0VFXnAbPiGagAO7PC8Gl/9TR/GkV3A3OhIdSZ2jH7rPtnCkZuKkxCaBmLN539aZGVyvgN8vixYVn6aa7POxaIYWzuEJhekk4BB+VZq6iIyb88YPuPXLYmbKfavcI8FWlNWmKHqZRfZlk1y0Gwmfz0OtXDlSozEXLTywuWFmC70kU+mwxhgEST/6claeRk6ZhkrMufvMPwH57HtaK1vLDJ19qaaVAUxTLULfimlB3JYvbJJXQ+sNP8vZzDGz6VQkvrI6RZot8LLUPw4mr5qJpzg267vQguR79H121OIdMXpCKD77m4DEFFh9nxuODK9eIkZsLHKqLR4OcdJANmoPJC2djpVseSrqWYmPkZ2blX0soiB2HOxr86X+9cWiglIhRXeNgb6MqcjbwoEfJDx0MkuBL7BsyWspCd5sXWbGAh9c+snhDqAOXzmXgYhFPh9nkk0OG+4XXB82QHYgA6a1ZeGuFBXTmP6JRS1Pw+l5XapYUj7ckS+BbF4u3Fxeiy4kc+OZ1WFiTcI3+8A2iu1Q8oUJcFRMOxiJdaIpKmYtpmkolnHibBjJps3DC7UHy+dhMkXdUYECZL17l8SG2UFSXG9p48EEh2SpXCpPbh2nvCQ1h6OJi1LEswbwDPZwLhqJcWMrir79tzKR/82m5YhoOlamgb4UXnHAfh+01XPjwM5jytLaQc2su0DOHWHDtmoGSe9zw7WUuHoNskCoMRlkxfXS2qcAIoTeuyltLXeedIsWsEe6ds5euY0vImi/6+GFCLX3a2Uvl3I7RuUl/mS1hF4hPUC6aytbTRWZFYPHWGnenmWDz+k6ODs+ESokXwp3MyfgqcRn1CWZxyQ8b+i0gEyV8vPHZszRI7wyGY5x8ONB1iGyLrcDFMxtICwzTPtlm0r3FGIV7xjD2MeUgbVZOgp3H4Cn3hUIjPx+Yv+A5bbwVCe8SZmCwYiF+KYmABbusIa2SC/tq1tAP83Th4+UC/F5XjA3BC+nxY8vpL9unZMnTaUwHawDMg3DcdHkVZ/x1aTTsLoLq6wL4fsueRrPGEDawX9h7bi607+Ni93l1UNNzRJmbleSfDR+OXrlClLdyUcp3LB4YDsGQsAIYzSnBYXCCVuYyrX9yhvqeGEPGRZdCGuOChWIq8OphIbhIrKT3DlrBph2/iYeqD01wvSWcWp+FiePGk8VDpTCrRY/k9QfDxVsiLuHtIS0iVjeYsYuODStFNy+GrivhEYxJpys61MAxrxQeyejiTw8j6sBj8Xp8OR4c/kGlv7D4/p0mTjwoARcyC2HvzyDycHks03rXFVcG19Kw9jwcLJ0Cn20dMHKCLw3uJPDyfQ1ZdqACXKyz8c2OSHhy/z6Z2WQFU1x8YewkS1g8kgf5stmwX4cl7w7OxZWmBD5Mmok8T2vcf6oMMj2Tgfe0ElOvlKNw+SPaH8FC29xAeGutAtOTS2DaF1NceXlAuHHXKsrte+GssjUAhl8UgTBwIz3fvIkk6xlh958qsHKfgH/GytKuDDn43pwFjzrXkCdhlbAvQgons/446blouB4Igefb4sBUdR+JfABY1TkJ2yoN0fQHFyV3mDlfuGAKXlaf6buNVcBdeLv91Y6ZUHi4CjjDSRDzdTXtz8uF86NZYCDNx47tl+jwSz6akhQIecCDA/P0oGRTKdb9YvGwaCY+O9RMExtWktTaPqH9AgNhm/0CYrkkksTWi7g+Q5FcyI1n3k5+TG17q8n2Rj6utqiApQmlkHlnhG4vr4KkTl2cKS+gUh5p6KmTiM9c11JnN1u0W8mCZGIAzB37lUZUSQOnjsX13gKQ2Z8Fh2xM0eiqFBwamwVlX0T5GjEVjzrZQdnuSPCw2EmadvHwVzoPniaPx8xxDsLo3ywUpOgSi0cCfDbjG0mUHksCRGw36XcAfC0uxcDqcnyc+IpopyswemvX0aIrD6iETxnUib8V+boSKhkpgM7WfKjzZeHCAyO6791jkn7/BfGbagV571gMqinD3RP56FCUiarlu8habwmg7rnw+LtoNn9RplqfbtKxG2Rx+Q4Ww0yccZt0FB7XVsUSf3HY7FeIa/fb4clQd/LHPAA0+9fTJ7UsRJQlQm9s0dkP9dvIlM88TH7riCNPWBj6LED6UfS5SYCpMbvJ4h5vbObm0yt9IqfufMKsvtBEgs0mQNa/uWDflYhmZ4vpoifm6DAcAiZ/WJxxPhyXlQTjV6vL9BcOM1tACZO715NjMbNw3ZICvGazka7r3U6Xy7uCfeIHIvY3lFhfXkb+y6Hk1F8W/J+00Ju9l0hJsD547DwkHPZcSU3tOZBh00XdV8+GCwp7iIN8Gd7utob591VgUK2ZfnpQAnlmPLwiYseexduJx7VpJGSniDU2rqe20xSZ/eZVsLCChRl3Z5AZgk10ao8bphlfJJ3cQFwQNcA02c6GDG05ZrkojwOtQuCZrQ0uut1Mhn2+0VzNw6SaFENzLZc2Bpbh3oi9hMRfYizoKnLkzVgas0cGEg9J07lrK+B8LIv9+aJMX1EBpqPmGFlvigWNepATYQ4nCkIw+q0jxF/fTFXXJkBagAP558+Hp4YpHMvedCyAp3TpoACcakRM9rYQrc5WYl2bDz6aHQJ3XWcTW+INb43mE2bXPrLofTIoyC9jWpta6Ez+ANlVF4GnLvugTpoAfy3mkJoMPdjXXUsVgtJwu24ZuAY20wHfnSTqkoiVdbZSJ2aUqZ4zWTj6xxI7Nx8QDq1QbO9rcgXXC3ywNnlPHX8cFTafmAoBbAV+u8+CzKHZeHyExbOzzaH6437y3GIW/jyoRZOLnIBf1kYqp4ZBlajvbUd5ePh3OkaYfaLxX0qh9zaLV+cXwGAQDwPb9hIv891k69rTVP9jOdrTBTS8dglZLOVEzx3LBGuLUtQ05oHvzkqsOBcP/84kEUPCw7113qDNzceWBD/Y+oyPvOktRCn+II3a1kbDv5eiT/MSEjlXmzTvOyDsseRiTqoRmmayeNQMsEU+DxV/KMK2Y4boMDYMOSd1YEvvGSqRqwEGAe/pXn4QmVs/GWZ6zsJBJQHsVauCLHagfUqfH/wpLcSnQaHYvn0Fid+nQYtal9ANoQNk0vwD5LHId2+vZ/A+4cISPSuSq1uGNTP20DNvKjDWVRMOXThCn48xp7ISU9HswHjhb1HZQ9xZUN/G4iz9djohQRb26ARg1y9DDBxQA5PQdfRkbzT+2V4O1cplkGefRP999AJ7Mzm8ffYwVb/iA22t5+looBxwrx0k3DPXqKdtHLZM9EaL0rXMxXmmILXbCrqmluG1g7lg7jkdJEeyIU1eVKMBPZQf4YNWGUEPcwlYnSykZ5K4yNmXhXfUs6C26h/R2sSHNh81mNpWArU5YlAiyUVh6UPGdqMxbS0JwS8iLgkNiYXMbyzOH3+YuCVNw68xJSh9YJXwWFMwTFN8QKrPZSExELHQEwk6/kcuFJwMwnMGnsxehzIsbNeEORpOpDriO1myj9ImPkPM7bh4Yz2Lksd7iFWqLJ6Y6U/zkgrhcds0rJirhjCQiha188iNOyxaLLrDfOjl47IvquAYc4S0iOaSjzKLE5ZJQh3PDdJFDjRbNCuaNuTjF3UBPpe3YSp8Zgm5Da/I+eQC/OZ+l77jvyMagwI0+YhUeW0ENA+ymHA6Bg6/2EUOceRwPT+cyhzOhat90Rj0XQxuKEYBrNeHo2neSHlBuPz1VDhy5i7Z3HSUvOlQZH5vzMWygUh8mucAduPvEHI9EwN3ZmPo4jN0smwqwo04SLV3g6JDimhlxuK8BenY+WkiZNqdpw9q/HHoSSKOd5DF2t9pRMqLD2Ona4BWTSYYrF9NFqQI4JCKZXuwYhIU/1hOLVeloYSNBF22mIenlnaT9JmFaFoVjLXTWGjYbQh3J0+G/gELaA1JxzfHAkDjmTuuTkvEC9dKQcFYm8mqUsSXGlXw+ftRElx6gW5bM5/+3uKKD5PPMt2F6VBttIp2yMeizu8U0BPt98/B28jQhzAMEs0jw4Qj5PD2IqH10BicrWQLpldvk4XRk8E73wZhVg76BshDn6AUtRt20CXrk7B5pxtsUltC5NRXMUm3+GAxRR9u7XlL/nVMQs0LFfjiVAcNMRFAr4MObj7qDkqboiGqzQXXhAtASToB1k2LJqvEAO9MywDNyEKo3B0CzVfC4YpWLtRz44idzB3C/n1PdplJ4C4HEwjmfCT9+5SJziIURhfE4h69J/TmKE9UR39QHoiCNb5lcLaHhRPXbRjWiQGNkYtUauQr9TC8Rx8aVeGy+jlwpysDdKtK8Or5KNi8KgXTazPRb7wkFbg6Q023iP+yE+HvuTL8srsYH3rcJd9OCPDf7dXCdW/T6OJFXPBPDYc9sb9I2ftwLJ7FRfkv48DNIQ8vfFWFBvcCMP0sCdLR48FznhoZJJ30On8SyrTrMAr/e4feXgZU5PRPm2i7ACsfCn9rJ2P3yktOMzju+Dz4Gtll6w2vB8s4R19pg1eCN+gvukeY5OUkZ646mfSmkvY3lsKf/HAUc/HDdZpzUfJpvvDWJjlstyKkcJsXsTkrD2HrRiiI7s98Zi5mzr9M/fuLYEB7PrE2J+jVZwHbOjKQ8+shtfvlD4+fKYPUs0zYblcMGQ/F2/eXLCHH98Vi48uZnCX565gxHffJLZ4vVEi1EvJmMkyIzYU9Ucp0oWEldN33xaBrxRD6nIfa9s7gsCSGZq4lwqRRFlWlC+B+fggs6bhC+zQThIXX3xLroZPEO5cFt4susIV3jZPX4AX7KwvgBp+A/uZKmtp9m3Z3nKDx96PxhUAf66tTkT+ml2xnRN+z5EFXToC7byaBa2UpyLRzUUlUs1fBMhC8aRElC8tRzUsDH/8zwPs+3+mVR9L4xe/sGSr+hrpuzMDQK0ZQ5f6CwWcsrB+pJddvXSCBvS5wZf5VIVyvo2WPbdFgLMGRT6E4tXaQ08GpgOeRcbCBeUjUSszpFOfpINYQDOuVwuDU9Mm4enMwHI4n+FLfFh1iPMBzwhZydCQGc/8rw64xMiCl2SkcXhuDvU5vKefTDFAKErGlnwJ2POSDiTsfw2pn4L52JxgR52NQlQBnfqsCr39F+KeVxXfayRDVm4Wv3LcIvaoekXvxVdCrWwImyt9oFniC/VAPXbReGt/oTMf2S/HYdCQFbq40gVMBApxcko9yIm8e1rCBE8rpsLjoHTWLOE9ej+XCmsnxEJpQCS5z5OBJTgHkpDrAaYmrIkcWQ8Gjk7S9T4DXznriiTuhoHrxB8n9zws3tD4mzk0KGPtXAHNP3+BIt1RCoCh7JaVVydxFNtgjmE+akpyFS4v4qNFQgipF8ug84wXZd4CH6ama9PL+91RFxBJzvrKQ3M3iQkERLAiRBckKFiftt4CrY30h8KsaPFbWwcO6CfiiazE1kd4ozLHTIV41ecD98oSwnqrYPcAHav2UGEmKcsP+AR1espSUnzFDg/+dIwR5mDx8m6xf7oJHZdyYD8UqaOKVCKbtHNz6/ANNetdKNB9XwLn7mhDkngldW/jobJ4MujuMQa7WGxL6qgAmsTA3Uhd56SfJOu0MvPZ9Bh4OS6S/rZ4SPZNKCH5RAEN6fpj4TsTshaY4PtAI/WoNsPVwJZESOqLpXgVoKn9M9dwM4fk9AeKGburTWgL/ND7SnZc48NR0NtzUnESfZiTDn6p1ZI+HIfpl6cKMnBu00iaeNOT+R6WhkI4YhoFhWgA21ghAZ4kNqmQ+Iw+9x1M3BVuQEvGSRhgLfYZlcC06BU7rauKtjwnoM+0YyfqPxcZFH9rLJMph0YkbNE23gAbKbKDf/AapgW0aKvTrwMMD9ZT/MBesnGZBY2Q18buvCtydPEwNnoPFwjGo1+pAuo6xcNbgE8NGRaHt2kpQ/zRCctXjUPaFBExcXYIHfXlQ1ahNDzWyqDBfETvirHFWGQsZfAlYt5+SyaH5ENSgA2a+rrAsfiLyFgzR3tt1QuWW+6QzcBrMGC2FW2cysSh9ApknNg7KEluFMxPiceiBKBfWKILkIlF/786Bv29bSfymU9TQcJRMlPUCBbldpEN5OjQuIbDXPAzHLMyHlpVS2D1pHOhfGySzd70huQOO8HKyPPgt4MP+Jhbd1Pmwru0T1d9cAu7/osjh7mL87wIXnPnjOa7XZbFQTgw6vQuhRO8UYzdHGX00I0DOeDF5vNoLzPvXEslIVTwR8I+0mWwkl/c9YnoXTyVDyf4wZW8ELD7jiEfpTzoiVwwmkvro9USAw4vy4YXAW/j96ilif2UupJg9pjP/+0xXmhniJLHuM6G1PAjXZqHrVjrUaIXCnwVDJPWlItycXATWetcpVeKDqlwrnfh8Em3pCcADTqK6whjIn8aiWv4Jus1SF3VdY0AxQZqsuCck3sMD1Eh3uXBPTAW6TpsAjjtZmFNjiFlcYww8Vkd9asQx3/ArXXL1L9X1fkPD6+ag0y8WcpjZ2GrPw8RFT8il6Cm4vdEa7inwoXlmObwIDEP2TAG+E7lhfc0IPXt/DPT9eE7GnpTDhjHp8PhfOlj0lKCMnRLTNtpDK49zMe2cPW1+episnJcCSTrl+PMwF/vcD9Lc1SzYhwzTmZl8yDE0Qr39L4jH/olgumMZbVhQAeWmV8nEimQcdbtMf/8yw3Y7L+AfYtE/3AVvbXHH7LGv6O1cMTi/zBKsdjnipgEWreYVwpwP5lB88intK0IqFvSA7nkZDF3+Vfi8zQgUHPNRYuEuWtwYDmUeili95Bt1is+llf+8sESBx9keMxX0B3lwYrsP7mg+SZQ2pBDdLywmf19Hg9QHiZm/J36LDSI+P1MIucyHFklx6PPOxx9/IzAyLgHWtlTA166bpPOhKyxe20LVp3hAz/ZISOuyB7y9jDQ9k8POsZagK8Pi9PPnKWMki5XXg7AkLw3JUChaznWHZfYKmHNpmHTu2kGmifJkc/UjuvFgGYxI1jImVYtoQXgJ3O+6R8ZV66HXNnGInxrCOf1OgNVxSlgy/h7R/M2iGCnFrbs2UVtvf2jSkKXmtrkkZJUiHRn2Z27EV8L9Kg7kbq+CZydl0W5JKMz4waLJYgH8iw+BDZnSmNTQLNwYupIo/FhOzl8yhc/RDM5VO0hp5hmiKZwJIY1J6F/QQPfZSYDCoxwgYgI0VvnOlDwdA3V+j8kjaQFUTltIZ/6ldM3mUnwcVoTTbknhykgjCHy/khjwrOnfNxY4y3sc3uIHk5MKlRg8mocDCYnAfc2HohN7aIFiEbHTiMWM+FJI3OBAi+xssWJjOOgW8+lvex6E/e9Z2ssS8DnRDpdaFOJNt6Nk37A6RLdNgteHorFfJ5bmdPPxgKUpJrl4wvHQuZhxohwMOFNx3alB+rumhCY+GYd7k3jIK0rDjItT4NTUHOB3b6LG96zg2XIvcktyKrb+kCKKErnwalwOPlnJwvQqUQ//tcScuJl4JFYS7v++S5+LR+IQl4OOm83ho7AMfdXVwOWNA3XbNyy89ukxOXd9kHnlycWfsXkwvLYMl57LAAfPAnDJl0PTOkmYKuWNbzrnU3/rZcT0NMHtj7LwxhF9whfl5kHZKEycHCiaOSJ/vuMO4p12WDnnMMN9NoWGf5WE500sNO93xIDplbBuszOMrjciIzPmULc3spS/Kg/2fGRB62Ea7jtViYpvZSFQvYDyCyvxxqYAfC4mjws+6gBM66X7bGPp44HTRHd3FnwYZaFd7z1jklIKgUXB+NkqW/hjTCY6fp1HLlWvJ07u40DdJBPO10jB8s6d9MyeJiotNRtdrvrCyXch4NceQR+EoPBEoDpU375Ezu2KgGUf5uDMuBJhtCYHXrFGcH3dOSGzXh5NK8rwTUAc5fkYw4fbb0gzrG1fcjQT95VuYJJMtYV3H04jiyckQGr9FJHDn6WuE6vgrlghPpSOwurK8bDr7mxszOgXLt/dQdrubuOMeRSJSjkaIJQPF/bnT8LzDjb44efqtvVrR6l5vRH2+BRD1+ExwgfnEiDSVAFL991m3la/OHPJMxsF52Qh9EEaBpy0ANAUY06bVeHUlRUQH52AWmcb6NZL4ShfZg5G9cXAqdrfnimjBif8E2FS/CMmYuleJurMEMO/NCK8bBWLG/ymtmsMp6HZm0LkxEVg2c9eYeG1cvzRN5tEnVfHvT7JaH9KAZdOSwbtx1FkRQoLy2frw/5jbkAn7qAuD3TbzfsGaFaqHjT6FzPildeYphoWz0g107diKqhiGg6ym9Rgz5/PzLxfO8lYTWkUaEaTKaYlcDlBAAeHf3EufXMnpx4cFXqrpINq9BYiRvJwxfrZ+CS0FAqNIphr3oooZeuCihUT4cbwGJTfl0eHXBeTt7XnhTmfA4XDei4QbP2JKpkFwq70P8Sxzh2JLw9vLIgVZgdnY7KJOixaWo6HxjlCc3E0GOs0kIptFtiUVoL/MtYy5zawkHIhHpoefBHazpsMgQ2yqLA0H8+M5KFMkh8YPFbE02XNdFeFFwRfjWE8OuWo5FoeKnX/IlleQuGPnrnwIkPk/FNzUf+eH/R6z6PRtw3x8mdT6P02hpQfVweJA5qQtCwCIJdFnHiDrGq6RnUHZ4GW2hvmdN42cv2RH7BDHqjXI3LbO5XMETdNOilSCuQ0i9AuRwl5d1jgKqWB6o1uMt5SXziqpgSVGX7o+0KGViccoW5nj9A5DQ4kKbASFh4+Sw7ZIQk+XIX1C2eBh4sGqKrKgO2zAjBRzxbV6jdpH2LhYOpDWndAj7TIRmKvSxLdGF4EgaJ9eOy5EQwf9oGBpX3MCmE1EfurhFNW3aAfeiugbb0nvvEsYT5xpWCVTSUMpdrTwDYBfvCeDjaSDMjd5MLO4XDQ+MEHd480sF1RjjdvagDnABd3vOLDsTdGuNyxg3YkPaOtfhc4m5Z6kkvsTHSS4uLglHz6SpgDsc/LoTogG7yc7HDqNgfYXq6HW32mor6Iv0vPV4GU1kSiZi1AjSNcPKkdQ12jtGnDt1zw7S7DqYPB4L+QBdcnmvA06RWtPryPKMdowsuVs4lqicjN162iobfX0d2/jPCrVwfRXVtA0nJ/kVkBanj5qj3uWWAnYuQ8qMW7zOqJfPBv4WKhmD5yPbPhSncaWDWJnPryfDJsVEf2FweDVesCOqEuG9puWDOnf68j36fKwPIUdxg3eofygt1p0xkTkH+nCYk0HO6URsIOGVuo/utO5cLKoatzBoo/WEHWzS+H07/EYCSzCuTnHyH/c4uxaUGo+DQes051kZ/L9PHylP84xcu9Qc+EAxmj2ug7lYGPMdLY/HYj9dyqC/+NS8XNm8zgwiV1cm8WS74aszg3ZhacyBPguVItCC0U8bjLdHAsL8SRvotCWGxBQxcWwC3THeR27bOzKikCcLlzhJb9yRMe9WKFoY1pWGTbRbS/aYCjmgH+PV6AVbeMgBNyhKpYS+GxZdlYL6FCCjRaaOgUfVz/xB2tFPk4NmAJdZvbTjnbFLFK5EcP934kVuLaMNacT7cXycPkfXywPynpzN8djB3FLO74eoLoJlRgpt828vuGLq5TfUeLDk7Cdo1SYcUKLlndw6J5vQHGV5vD4JO3ZGjcb6G6+VN6RfiIYzp2GWn9SWnnjCx0nLCEiMvfIJ3+ExGmRmJGwlsiVRgCI3HSMLdWW5QzfHhg9pYJ+hCET6u8sFImGrZIiBz14BXiQL9SW0nAvLcCSMnxwCXFFZh6PBzuM95YZiYNPSaGGEbekvWuu52DziTD++7DlHmUCm0LjDDOr4dOMDMjGf94sKhfCboOJOPWaBZn3Wbhp99O+uMui7Y6OrDt6He6p+GOcOTaEEl4OR1vXHrPLBtviNWKT4jeKWe4uiIT+cvfUjEsQ1fIxQf8RDg2wsIB3UFie8gQ8x8L8M7S8P+dOYvRK1g6bOyB8ZUpsE0wQA2lWDgzOA4kelncWs7iyVez8Z2qFJDWvdT1nAYs1DADL39jjIzdQRdJxLTHLjJBt9JilHm8kPy4N0i8p4TDxZIbTmsVJHB1CB/1IqVxd/o8jniiHw6UCnC240T8FlVLCpTu0Z13koH8m0ClX8yn1iI/Wev7k2p5NBODbXKYgWPAYX4B9lmU0U3+I/SlVBg8LTtEBreawbjYYjhiIIGPal9S51oVdPCygqwsPfK2dQxs3bKD7nieCTH/sfBVKRIXhK5l/lsfi8valpDOGhMYRyeiwQofKLHUQV62Og5fisOHtjy4n50P5k5S6HA1BNue8TBz8Qci9WQGTmk5R/aMshjvLlqHNi7KGUwGOawQLs+wwS1v58C+uJ9Ut7KTlEiMQ5lrskASWZzCJSC7XwFPjlfG1E+TYNv7aGw+eZ4s27uMLHY3ws0yMdh3OBWV3bVIZ1AQrh1WxoYVjnjWRhHDbxI8tmoRLdVmsfBlB52fFAk6q7/TyjIC63McoHfnZND9lI7uGwqwfmkhDjt1UXjO4i1OKOgtVoXDzjPopTt19OmEbvr1FqGraiogqeEdRz4kFnvqyrH1uhHuizGma3dXYGL0LjKj1wUblkvgvWghWfVaAyc2fSDH4jygWVUHtMhRTlQoC4HPpGGJlhP1sPMjEabmsG6ME5Re4WO/rS1cDC3H/JKX5CPDwwUTdCGiIAOwSoBKCXwwmp1GQvVzoVk6GUZ7+cCNmAVbg43RV74KigdFPpo1HWtYW1xfIIDtXuMZ6mKAdkWV0HcoBVzunySsRS8JOzAbFA+w8Ib3jnpaFmHNfnNYyaVETYXFob+r6NN9+dgyYyI8LqqAkjxjqKcTodZLDNU+myDV0oYeRWnIfhcKn19PxYhfYsT+kTEe/U8Rv9hz8dXJEthcpIfeRY00w+IZnV8gcq61FzhdO5yFp85XoXLkLJjQW0ImsC+I64J0aHTyg0zLKTBVtG9Oxpeghe8surklFR3eleOuch/0rLAA8U+auOS7GgxrbqGyZ+Ngm9IG4Zuu9UJXOwlinJuN8YG2uFpcBQ1VfLBNnof85zK4u8UZGcnZEB9qgwqxZajmkoe1M+3gt2oJMHPO09RgFRTXrMLo6THM70NTYMu102T8w+k4drke7C4TsV/JVDJn7jBZZ+AAGbG5uPFJFTQvKAfOEQG4OZjggEwGHhkcA1vEw2H8LVvsFFTCuzJjDLW/TWZtEuDq7By4GFiKT/9zxJJjY3CWyOV3N5STzQ0C2L86Enc8FGDexUnguj0aOXYW8Py0M6hOVsBmjwy0meyLY2PngtL6fcLNq5ygyFaGyBvGk4F+ZyxZXQzc+gr04gfgGq1c7Juzo71kjh2apmrTDfHv6bTgSObwmSo8v01F2PqPDzP0CSYHOMJxzjHy3rYY36fGwYg/i89HbtDz92yE83YkgNVxC/j6OA13HmijCV0srH/phCvsYzDQdi/1PugCFdMFcD50DjZZcWgGAzDnbw52Dwwyk86XQWJ8EC7sssMnAbGo1b2A9l8cg8lFJbisNwU5K6NRSdUd9fyjYX4ZC/sZd1heqwX9xbOxR6ADq6PG46tlVLj4+HRYMX+1cGOPFt37Lg4tzPiYfr5X6ChlDVsSvaBt+BTxjYvHHIk86J94j5RfHIt6G3PAZEAb5cZlAnjbY8eoFyzonyw0XB0OL29ko+caT7j3bDx8OuSLEgfW0okdg2Sb6hiyaFcygepiemUJH9SqSmDGqDsatlpCtFEJ0HPbycJ+eXjazCIXi6B5OAuze/zhVmggDnFOMWUb/PFy6zbOrqubmPT9XpCEM9uX9JnTgj+muMJuHZGuvkN67ghw+YsElPlbRRxn+NG6ZyJOuMmii7Q+Ph1Nx0slLtRp6DzZM/40cZMuwG22BbBpzwyQi56Ib1LEQQo6yI5WFks+W+MCkT/9eBMCMDQbNaYUwQ/xVcINEy9R5TITmOQ2iRYI9GH+8pnYPtsBp6mz2CXQhv1vcsGrIBsbr3ExtT2SKl0OR98P4fRodAVusQ+HDyUqeIpGwphYY3R9LA3vlxXTW21WIKfzhypcswb1flcM9+Fji/cFem5jJr6pjYKynX3kbL01HJ2cCNLBM9H7WAXIHSzARk4F+suVge1VgvvbX5L5Y13h4eLJ+GRKCcJtVVDYoQE+or74Kp8K94/Y07Z5FpxbnyNxhiELVqLcX6jayeRqWWLA5pmg1cgHy99zYduUSvibXYGNoj48Wm0Blm/ioUuDC+5+l6mPQjyKtxegslYO2Be64PreLHj7tZ6+F7n4+DQL+F6YCA+/lZOPS6tw/2YxEPrH4se0JCTtZvB0IBjqFaLQLJePjYMu8OrGLrpMw0aU13xwOmUNbHwxXdPCh+6FFVhvVogr5r+hkbsb6Y4nKXAxO4mx/cJiUI4X457DYtglSSAdgUTp0g+6rk0P7x8FnHIvH5XfsTD4OIOc5Qvw10EBTsgQAJ2SjkFrVFBYHwkF+mnwcnUZzs/IIvPfWcCaAzzQ9zpPD9Q4gca7QHQxZqFKxQW+/ojA0hID0T10CL/EZ4KMQQkW8x9R4xNBIHyThAfLmoVNviVgY6NDP8eycHnqNJTgbeNUxHAh930eLhhhMY02kQnBJbiiJhvzbArQ31SWPp7fJ4z/546drp30VEYl3PiZgt08RYxvswGThDDRGpTATh1Ki11SMbG8AtJdf5Pty3JBPF0az10xhYGRbIwQM8SzjBl69M6EtO8zwHPNS6qxZRZKzs6BXPFsUF0hgFpxFp0NnECqYxboFOdjgn0Pc2hLKVrVe2Hdszd0548odHasJZe0npGD/0QMVe4MVUVzSHFjHNxIQKFvSD66xDH05So/rPQrBJmztuAVVwonxC1Io/hiep2pxJ8PeXR1jQBjf4/QIAtHHCOwR+72ADisEIa3b9vgEc9clBjqpTVjivGXJR9azE2xvawQTjmnCIe0WFQwnIoPnwcSiUBxnJnjAUEGPDToU4Dp22Vhg2Y5RigH4XJSCRlbS6CUiQe5t1LkazcBj3wtYvM9G7UKyvFaVBS0gmjNBS4IFj10ff8HMnB6hCxwdsHj96Yxlyec4ZwxKkPV8svk44/H5F5HEtVpOk4qTrrg6voyvOrjiuvqH9OAVA6wH6Xgt8tpYvc7Bo0qeDgtIhbUm7iouaQcgrctpCgXhakaZXi8UwbmVUli3gs7lH/fSsOMVXGJ4XSsPsWCREk1yf9lKdx4ORChJxnkL3SQK90VeHGpL9aKvNE/YirObrWHK+Mz6edDBXRHEwtqBnwRJ3jCrGPacNhqApmnlQ6Lz36n+091Msf9g2H52CrY+5dFjYJgFLtXS/b4h8BE24OM8cz1ZPzztTRK5TSNSXbFMcd3U7bIkfkL92lDQBxu/u2FesMzsFebi679Xuh07RI5/qASOP9yQOlVJ1k+4gzVV+biOcuJuCaIgVv9Zdj9Z4HwRIQujruqjPcki3COicifNHmQm11GE99NI6Uj5tD/HxcGWTO4r1UF4T0BcKqxEtLVZuCse64of8eGPinwBvNjmfjbtojSLQWgNczD89m3yE5HG5S93EJ/vWXh3aRo0A0vhpTjdSRv0Qa62tgLTg31MRPlaqlD4jRSdHMS2m+yBcsxysT9JB9SQxLBtLYCXNp4eOVjBbQGVcLedebw8QmLnnucMN5jNhLZJDjerYVzwrk4/k4RRsZzoSifxVNxAvRnefBkoBAPBunj3Zp0WKrCgnqlG7nudpCUXS2BQg6BAkESqFhYIxOwgbCc6RimQUlg9TIiEDnPmTVWMCdJDx7rXSF6d3zB4W422JzdTy22ukNvaBIdGjsTe9oGmHkVLniopxTeKR6iv5vV0GcgCW6EsTjpwGHG+dRRYfkxAUSNmoETzwXkubPRpDEcXcQ9Uf6vF8TZu+JtwVyst7PHpgFPfH5UlHcnjdCphYUuJTn0n/aJPtBUBS/n32SbjSTUuPFw3u8UOro9D4bN3LDBWMTB4xIxVqOB+lc+odM8/tHU5wZ4u5/FzXsKMeBiPDzGl3Sm6mYyTNPJn435OFmGhQkZWnTQBODGir007FMOJhQWwfHGMJRYJtpLG1Pgnsixfl0uwvoMFrPaWazQu8TMe3dYqPnJEG9du0fdfefi5WJ/3PpOC+7LlYmY3ZgmJEqi62sBpMcsJO/+iPz0jwLm9vqBWJQUqnu8Fs5adJn+fa5JZ3A5eNcyHZTveUN7BwsdLjwM4/CRyRYK5Q/qQf1eFnWesbDq9jrm4XGRV68fJQvEG6nSBTOMe0uJcmMSbA3YJpwg444XL4wFn0VcVG/WhAGijdOk99O1B/TJ6T0sJo5MAdegJLr+FwtT2segxtxovLXJB6IF1uRsYQxKKc6CocRu2vWsBH3tAqEuyJWuH1OJdSH5UBbIwqsBLhz0mw4Rnp/IAT0LXJBaiTpf+DDegYeuVXEw7BKA6V+56BKRjfZqqbhzhyWqr5qBDU7jMPa5AIfVkkFsLBePZsnh8SB/HDuORZX4bFjbNkjuyBuc3driA7PtHDBVLh4L+O/pnrFr6V43D7I/zhFvhgjg6NV7xHw4F0oCWNT3yhR5eBYqjWhC0qglNKSn4p3X7hhY50189AOx948ozyMJKhjkgWxiKNTs56PH/Qwq7aeG68Tu0einlVj/GfC/X/cYv3MZ6L8+BFqHZ4Hui2z4T0jw7KFhuop7nlZgAKi8aCfrX/PA+ZgH1tZaM8IPmrBWNNtuWYqjdusdOhJ0WFgdJUMaFAPx8NswaNG3wLiiJKzfNp/5Jxcl7PLlYYIMD/8MsWj4VwyaKB+oKRc76uKod0AQNhRWwnLDXGye5wsru85wmrnqIo4PBo7qEM2tjEOrBE3mnAwXfK/OwSPyqZjae4bcSc5GXHOBOWHI4jm3cbirxhu0Lnrgq3DR/LYcoWP+qILODCVh8uUklE6ugGNfK9E3YhltTBbAwpfGGPQ1CH3vdJGKo2kYMs8IAwVzSYFNKIRbFmBEjQFlJAX4z1IBB1tOCAN2PKMnOyogU+I8UX3kjiu9uHjtlzVqLDKHce2KuEusEHrXFGPHw0DkWZbT3+JuwjG58jRGRYb2x/tAdYAPtorm1OIFsdDvto1Mn9ZAnTlS+Kk6Bi6uLBc+OjWGWZTGAAnxg70tU8A2q4JpWFIA3v3bSbB7Iey3ng0N48ogWcRF942XEAdPG/jvVgs1eF8Jf7TfkoD6YtyFK8n77Z7YfJ4P+6ssiYPkfhovmU2VEzLJnwtfifiyEjzAOEDPWMAD1hVgOyhN0+4IQLchHF6dU4f6YxWY+SkUJPkDpDxzBbEWt8YNNI6kzbTH8uRSHH5VBFE/PzISY/igLjGf7Etex2y9lAV4i4UEcRVoI7aQuCsIB1J+0fTBc+R6XDmQh7upzFAVHlm6R2hvU4h/cyTRYnkmVss7Ya5sLTnUBrB0dSIcDy1FD95REmI6B5b9E+Ajq69kAbOMKE7+wdg9O03n/eBDcXw+BA/nYeaHcWAoWYIGHkOkqF8M9k2VgvOVscwlLg89q7xwlp4R2nipwA7rKlRYNg5Oz9eASQf20q+ZuZhpWopt0fecLpzyoNvEJoGBZyy5G76IMziSB+nGRnDncBhEBq0hz8TnwiUZP2Jz7jRndC0LC7OjcXpaNoTmqmFegxw+qp4NH+u8oNTnkvC27AiN/GyKh2ffJRXZc3HB/WK4szMBuu+6QzHvHfkV+od8n2UNg86ZuHmbKcN5kQmaf/NA6qaIZ4uvU8P5sdgYKcrDtwJ8/30OrNziDcL6HPj00gDpma3C8yuHaGjzeNCWSodbaiag/YyHXW3jCHdvOdR1muOrURYdD1eJnHU8fC+PxuGjNUL2VinsPNVFD+06S/K1WdyjmI0JJ1NwUVIVLK4gcHbEDx1E8yT4aAz2jEZjyqI+Yc/AN9KSGw6LbsdgygQebK7OhOUOdrTyhS6mTPelsiQJbm5MA5WaRVRPPxP33eCB9N9KjLz3nVZ7pKHDMItj7YOxyzMd8wtZ+LeyWmjwZzlp9DtIZEX+dKI9Hk3fvSQ+j+vo4Zwm4WevfPh1TYDLDr3iPJ9pBxY33pMxnf7QNluPbrjIitg1Cl19p8CV6kooaI+is41jwX62N7VP9ICUs6HAtQ2H5p1Z2COUhhfJr4iujhwEW8SgtZIzPFbjw8oICTD+PI1w/jhCZGkm+iVGoUmLC1pxBTgvJwnO5jjCPj5g6rckqJrIx0MNLH4yj4Dc1kqIyiqHFacloaBVA8abBVCx+xowtCkclHeL+OtgCqzY2cJRFlcEqY1iOO4QC7OD3El3xxUi+yNYxNZnycK6QKg0j8TdIU1M0+c8lA2xgLC2EMyLmIsm7mI4eCoW92UKSeLnaKr3LxRCn8bDhPk6sG5dANxxUMbV/gD1aWuE2gfFIDfLH5vyY7H8YBBcjNxCvzx3Fq4sisbNSnZETeIK8/PSC/I1ehb+ecNH9b4Y/P6PRalzFrBBi4WggRD4wAiY95Mb6a0Gf3yWc4wIf6lBXLUeU71QWThJ7QbZvXqdkGfXRVV27yY7X0WAV7sS3rPQgM0dV2hl531iszEeEvIF8KrSF5q4kbDYNQ8NB1Og6kkxVZCZAn32KlhYwcJuVXOYdyZENHfbifKev+TQlRyMee8Ov9sS4eLETqGOhRd2bBXA9cdVEOgcDN68p7SuLwbsfqajxzIureypwp3xgaB3vxJ+/+giATfd4KXBPrJu205mEZSg5SM3OE1nAK/NER+YXBHqivrytlIVRPcJMLQwBMt2zP+/js47rsb//eOiQRGJNqVUUoqWxjnvq72UUkrae49z2p11Z2aUETJCIquQUojO+1KUlREfhErKXmXL/J3v7/97XI/7fl+v6/l8PN6P+yZ2peEYccwO9Aa3k2tB/mS+rBVotS7AusafRC7HEW/JqJDKSZGENZSCRr8dMb1NBClbfVHI9oQt56yBStxQjtGmcuNciOz9Qpx7iwu/ijg4sN0Btg2y4dfiX1TTpYFYu3rhhYworGRSwNr1JlVYuJ+kxORhsiAQ961KxGszjGi4JKerYnQhf10UdTzjjdUBdjhxOBWnBftgQIoOqbktcUJngCe179n1klladM+WBA5FA3tNGCZeXkcrL+fDsbgkkOu2AO+CblpZZw1mXvl4MU/CrBs5sEuawXVeUuAbqQEmTYeJfIIztNpH4GuzleIDoUjV7prg8TmKUFfIA3UJJ8s9Ncdf5zUwPCwSg3c4gNhehJOu+pPRxWmwNNobtE6moEtgCKz+5Ak6dR+pqSogPR8Gi6qbSOe7eta2qWbgdkifpdfygVyZG0Yt87Ow9+AjMZQKYatMEO6/2ce+yArCgJpsUNs2CZeMZEK5+UeSE6iGY3bw0XW+MZ6X0YaUsLPsMknerz0YhsVhXfR9qwdUnS2ma8b4wsQtMTCGtYn07DXAKWvPEtuPC+FPBoPq9peIh6khfNtSAGHy0zGHbY2rytgY02GM/uHNJMWVgbElMXj0VQSOK9VFToMdzrQWYf97PjREfacnjQJA2VgPC58mQ4RxFrjUriXClqkYKh+Bekf06N7GVvpYoMQ2VwiFik8iXLTFG18dUQJLQ4D6ks8kdoYteqhow6Q7H8n5kAKMkfjo8LbbdEUpD7cQPXyUR1BqxVIo/RWEtkk/6B6FMOyJXwxTHfJA6M6AQr8s9t3XQPfC2fTBLlN4wTYGgz+BtHWlCDdXcvBiQDh8Ml2KulaxEMS8IL7f+ChfH42NK+fA5SkaoHaSAw8rVeB0Lgvdq8zpn6ZDVLC57nxlSTKGUz62adiRsjRN8fCWSKz4TwitMQyGLPtO983LgM0b6mnDb0WyI3gyftNisDTgMv3tKkCL2b5k7Ck2Kq13RKWHQvx0LwZWY+j//0e98AaDr74ykObCwO8xPFRKZfDTiwlg3xQkmVMBkJGqg2abrhNh8V96Qa2aPue/JlcPLkR/20QsCi+k1lZC/Ov2k5V2PxbOxFohd/R64lw0DtdaH6GunxicPPsiqStShpKsRXT5LD8S/NISqnBQXHIvFuwjOHi9Kh+/XYtH+yEGlsgEk7cdMZA6JA37aw+QIxVLoTpZBFfP2ENLeC/bptMYpJcnomAoBbaxo7B1jB/KzszE1NrHRMljCXAwE2y+6KJD6ig87RAIPnmP6ASvKTCPhMLs312kC3JgmcUYqvXPCR2ywiC+p5MluheLDl/VIGuGCEDJByyPayO/4Tux8mHgu9U4evt9H936ggvz5STcLWHXGao8MHBn0WM3E1Fz/hJw7w7EY38FuPOgB0RabRaHN8aBrv0HmkwIXLhuAHt2MhK3E8JDqwjgfLYFiw+jMLU0D3xy4uDK0VTY8lgFR283oqueFELeiSikvcuJ7sAzsvatJenclYt1++Vw5mwGkip5oLd8A+m9NBOlF/xvz1QUsuLdiZH2P5rzJQs6jQpBvjoKf9VIfI3zmN7f8p4ahueC9qsREr0rF7z5IrjlIKAhaukw595ySWaFQ8IOCySrb9ArMhX0p0U2fCveJM6q0yaXj3UR/2tJeNTAAHf8Y8BdHA00KRl+jLHEW31m7OmfeBAYk4QHHvJxcNlOujbYAiPPCcFhug9UzZew2/grtMO8EKO6ZchKqa9k00VnYBew4XN8Pi5ozcOJFbMwRikeGpfeos3h7jRysSX4mqXA7TOZQLny6BdESZjqDJz07zFJmzUOn4yfDhVDf4jhkTzUCFGGMXcO0a/ro3EnWw62ixvEZ2J2kBsPGIg4uYQcnWuNL3wbqbj6cEvXhBB0ETH4YO9TMqzpR+7/nYc7n6uD2m8h9N0Jh46rDAynHmdNis1G+8gYGJ9thVarAqGQW0N23FaDvPcMePU/lHCbiBRN7Rev9+DS14U5qD9aA72/JOGmXDXgFkgTIm9GegQM+P4rRMekFsl630fdt+fB4/QAXCPh7W0xJuzGijtU9H4LmVlhywp7dZVePxYPb/YcobqNXKKR4QVjx6TDjfh62hckwFTHcmL8JQ0+trWTKSoicCAG9Ie7CZ55ag/5chwwPMzAibkxaOyZjzufToT0SRqYXiOAGQYasCxfi01NS6jGsn9k6IcVcbuH5J/PWUqiIzB2Eg/+Cqdj1lZ5VCtNg/H/OeKJEX0IMn5IpSRr5bEkW/UCCuDb+OVk42oFutLWDBd3zcCaMwo4t56H4bHZIH4sh+KXb+nGlcn41vINe/U0O5iUkAC1smzITfhMDFX76QA7k5ooz0e1jQXge4WPYUJLbOaKkNQbgYBVAH+GGSh2S4DjrZJen+QNCkRZvDyYD7j6Ow029MInF0PIrSUi3PSMizILD9B8dhC2uiSj6RRX0uCyWrzh5hd64W4MstqjwW13KNAPDG5S9IEMDoNyS53By9yNHuwLQbPIafDGdw7cXZCBYWw2VnKO0hvdKbjdTxblJqfDI+5DNqssCb68HSFtL0Twx1kNm30/E48J88iEt6cp3VEvbrhbAI9WCbBiyR+6OtoZo2YbgcVDEUxXTsFX5s5ofz0L5o6LhkrNUPjc2USqebl4xV+LnvPYSjxYw9TaYjUxvM5Ial1Ib7AFOKHjO/k54y7d1pcAKefzMHcWD3bNDsTvEm42FvHAnakWc4rlCN7xA7uOSHgxzwbHqhagUcMn9mB1Jj515mP8Xl2cnRuL/lXV9PFiIeye1ynWUEiEjVcswG5IMqPDnolL2xgITpJkgYQ9qmUS0L3vPrmlOAM/mWgQn2iJ329Lh3Z9DVjXnQ+fI0Qw/EYb3J5E0feGR2jmxnNszzhnULzvDZ5kEbVb6IXrpEzBXahGFUfi4OSik/R7ZQpW1DNgx82ENVrn6QrkQNoaDrZ4huLcE5vEPw4GkMxjJ+naDkdYsEQf7xdHwrhKSqS+2eCP+rn4JScRrLJkgNjEgbUwE29cNIfPpxbgMMcUgvgL2EcNntDaqBg6djgT/LW+0I+GfDjEeME19Sf0VEkBtF3nYLv5XZriGwRDUg7Qo1MATzwOsM8kKoOl7FBLf1o8ZvrOh6lLpxL1mM109BwRqL0XwrI1IrhWWkJPqWSAnY0Aa/pdIeGGMdwacAa7tmgom5WKX79PYJ17wsPLsny0j1iAcq5Hxd4LkmBHRirs7/OjSV1s1DBORUcVc1w5zQIOaReg09/bZOf6SFhh7UWrjb1YNuqG2PgpHTqtU1BFWorsOmwP3ZpqoBIagiejsqFqXBL0HvElq8l0cChYSaunBcCzAQGb+cNAzAZZMfz0E9/ymIIdXc1iv/2HyB+nEeJq+YCd4uUEMRP96ElHLq7lpUH5cjl4VVwIsx//Ie75CzGtJZTuXR7GHsybDPG+pmxjshj2DIwGYbkrmZlxl2q+PU3K33hiZnkG/J0UBD31XqTb1xCPLIyC+68zkBOSjCfNdoi3RObAmlGjYehoLPxrmANqyV/Fq20W4c5qBlOepcGqxlTcnrKW1VnUzHZ/64y+ulPgU9dC+kBVFWomtLFn3cyEHMV4/Iyj2AkRfqAofEAzpzH4p6+MNA/K0D27k2AFaxEOuJS1+CxMh6h0I5IhPx8C9YxB9t47cVGhLJjFimBxRBj+bUiBY46muL0tAR9c8Yer7xbj4yYu5uzPxBUaUvT2/A4y+MoQ3jGFINwUjmOcFMiLZ93iTQ9c0U/iuDIVPMhoiMR7PYo4+o4v3teJh7QpmVA9h0sXJmbQdyMiDOlZRc7pHSecY5V0mbwOvbtyHoawK8h3s+esvZIZeNeGj+3SPHFB4256SanS3oaVhku7jOFy1jHalr6bzZ1kB6c/a+KWPBG0gBVZ7y3EXONiEnUmFvKyJS6uK8K1d1+STz0MdF6JwJdzksGrRAgGNU+oZ3owGJ5pIrtVGUwyviIeM+hJj9+NwOxxgOtbFsOr+BQoNEjCzjVL4LzzcVqnEwZOs7JwY/MAadQei/tfyqB6mhcsSRyiioY5uJYTQa5lt9M9o4Qw91KDOPmDL0xIs0ebm5Lc3eWKa132sWs3ZULv1N/0U8kCDJw1Gm/cHiT3+L/IC4NfJCMmm1bbryIla5PxqFMKCvXngGrND2p1IBruVy6Bf5xENG9LQ7cZDC4yCcUtzQw0u89ika0MNKRJ/KlvBbll7g2qr+Sh9OUCnMkeDRPBhcYFyUDT7zgSO7wIFIdyoOmNjoShktBmXDb7xGM50OeowOY+bxT/KaWFvxioceq1S78xlQRaHyRvtH+wR3msY1mOZdClV4M835GFsW5byejVWXjmUxIOvOll18mdo8VS3fSU1w6SGOoBtEoBa/ZlgrPOF3aLZjeRaf9Fr1y9R9IfZuBsBQ40+eTA15FC1He8SQwZLVj8mYfDkYXQ7rmf4pws8mAwGRaFOEGJU5343DJTWJgejrv4LBi71w24bumoIWE+gUCEf29Oh5mL/UmnkTn2txxmLZ1/jDBfb1NBnwAT6xfR7jtl9F2PxPslvn6jMga92uLIx4BcmD5mm/h/34U80uiGj8PyaUNjMulfz0ezhafpqu5oHMgl0F2agw4ZX+l2Tz5IK4dheZEp7f6eg21y2Ri7Ixw/MiIc2e1GBfd2knJvLqg3XiKaUy+Qaj8BcopW0cJpNvjrtT6+VxXBZo3v1MRCAI62dvBqBQcW70nAkF1x2FjAg7aDU2iZvxdUSgdA4OWrVPWzCd4e2E0PC1eSwYteWCKVBMfpE1bZ6Q90X1M5+XolHX2NVOnn9iBMeVFCa6SvEvWlTeI9C21RY+9dMvHHIXF9rBtM2FsAd5cmYPJce/DbLQc3/prgzYuFOO5MLozXYoFdiht2rLFBqT+O4vyuGAw+IosOer10xk8FqOCL0NVuDsxcWYDKSj+I5mAMTLog8eR6HXiUk42sCEm/n7VAZRkz+O30jRzLiMDb1zlQtukFiXvPBb/aOIB9VXTnHYk/v3QiUicZKAqqYz+1PUee/+SAovgsmfp6Oq5NPERWCvzY104Gi81GJpBt04ZpW0YWtDhK4d4bfPwZL8B993+Kr00JAuQW4nFrIeqpFeDuQiVsthuLvf84OHPpZTq70xf2JjC4bckIOew+AdR0WOLaIl10kclAv3Pd9Ob4QeLmIg8V+5RgiVoeBP6VONrgE7ryhRI5wVIjk1UzYE3RehJqLyV5JrvEX35Pgc65lLy/G4inWk0xMrBMfKlIF+619JDZ/dlYk6OIvSeO0cUPhGChsYNGDrkS0ZEYiZPIwJ/rxvjDOQOPqkvy6IsFBvyJwkWTlTBLiosNCxjU5nDhdjGS9MEMaqI1BT8+5ZGZe23wnIEezlogj98UBDjmMQ9UClRgkYoCxjfzseBlIr5qHSX+2SEFa9ILcIYxT8K/v9i7T/vjno6FtPhXAeama6NenSLEbonBM43WyEx9RB+8OE3ndApx0lMRCvZkww3Fv6Q7Pw909tjCTfVUUqYSCxqXNdG07wht74qDoWnKeNXxP2ouFUkqMxm86qZHa6QSsbjWFzMvTMXfkj5q8MzB1O36NOFhDUmUfXO+4NR98Ww6DcZHLid+MaoQOC2NrFzQQNa/HId74m3Y+dtscXXpTfpgtBJOzRkg2we5YDAnCb/r9pF+yXvYoyKH5JkIPE9I7lenif+i3aEwxAf1rb4StaYisvXrZ/qa+469NjAX+kdNBK3WAtxEQ3FFohyc/PiOiDg951Nj3xKPBh1y4o42xH1hQDpf4m3/20/yfYCEn7KlV0EDGjnh4s3+kvd+Lw+mnMsUZ+pPh5jcOMieO0Cfpv6jrjqZEPpMDzfr+kBmSCRebxwhctXT4UGTLBnITwPdPxkYOOMCyarPwaRnXnBa1gLS//rh9l5P/NDAoMdpeVz92xCustcQg6YPxHtAFn3BBci3HlJXJoS3wwbQbjkdvAKzQG5uNiR3ukJWiQrulG6kHjun4O7R1jj6jTEGlLVJXP8se+Agg22qMfAidy5kPvIAzRAZiHURYv/kQtR1LSWaDlngmCmEfRxbPEko8T8aCCOXzGC6YgQWTJiOT8720ajbuTDGMBpGDjUT9Toe6rTvJrnz8mGFWA8vXCuEwY5p5LrmSpoby4f4D+dJ5wELNF4lxEMmWTBtszLE2jLgfJUP08fr0cGiXrbyBhNIVs3FqhsiXBMngod/ubjQ1hDziudCf7QARwfPh7ID5tixtAAjEkTQ2imE2gtCmKGjhxcl3mH2JBe/jOKDauNXWi6Z7a3j7eFVtz7NWJVFf2zQYef9sMOxC56Lv2w3lZx7iSw9zIVXjxNAqYyD+bbzcL8Ol9yJKcQn8JoobRfBM/N5cDhOGV8rZKE7FIDuLgK2+jJwZrM53XFOG/c/CgGjpbZQ3M7H53HP6MgPW9iwPh9aZUVgZ8aD9ZZ5GN4XiTPee+BwST9NVXJCX8dM+C/kMv3lIMArlvZQF/qZvK12xPgZQogeycRyfi4OqRrBknuSnDpaRiPdf9O6+Mts7X+eeC3bFYLu5YDPmwCwjzlCvQYnw+xQG7IztprK+7rSnmRD8sCFD2ZLGbqpMgmMlPkwvNMfZ/XEQvAKZ2j5zwNLxcoQ8onBCwHGMGR6jZyfNJa2lfqSCkmdb2Pc4OPJQlR6zsGaY6GwocuODgTeop4+4ZDY84kcV/CGoIk24q07l8JZbiHE23gT+a299O0JY9gS6gDSV5Lw4WA+CLV80XyLNtBtHvjD2xmDeZZsi9Jc3G/iix7N+miY+YhuilxPnPrlWkjTXByxyIHV5tdJ6fZm9o/eXOyz+8vifRoL0Vwt8H2ZjwNai9Dua5V4lac/TNzIw32u46i6vzUmvw+mO/2iUcgrhMQCBi+dWwTPVvNBaXwwjEl5StJa7hLXaeZYDOn47IUQvrICsU8rAfGmAbzblovKX7vomId8LFe0gylr+EQuMArhTyGGKqZCnfsJsqKYwVkcDv43lk1rDxyhKerB8CphgL6uyMW34lAaGqyC8k1ClLu5gRziLoae1FJWoF0B6AU5kllzs0i8kjM7wEobDUzt4KVhKJoOheDTdT5ELiwX2kc7wA5lggqnE3DfYT7VY3vi/eFQ6BovRJ16S5T6Lk3OuoXB6UMJqJ9VAL37TMFydzgURjvBET0DtO3hwAPZcGzXssW+lxY43Xc/VTkWhJqX+bB7Mx9vJNWzMnX+0R5JpsK2aqJrrgxL50XhRMmsbLoiQ2Z/0yYVMtl4vPUTvRKaj3W/Gfg4M0Hc7M5Du9Wd1BPrxYtzxuKsAyGYs0obxR6paCGOgeqzi8BCtoeMPueJMv4ifDp6Pj3bkIcajYdpVm0qSG0sAG7JWJhV0UNHbgbB49sivFgrwnIrT2oyIx5VXbaTwYkh+ExyPeFnEW5IHUfsNM2xPtobSx4twF4HIR7uZmD4hCweDG9km6/wxsshL+kzSX25aYlQPqqRRrxnAO0viw135GDLvD6iM8ClLjJBkMuRpgIZAeTOV4Vw3jS8dFKEMiZ8TKiVgpCWMDpQm4xbFEOwKjUAR875wzelKFz+ZS7cr9pCmpuLybHILIjcxEOn4lpqlGqO8rUXiDiZhcJOBvIhGXMvpcLHFSI8fL6dfOuPgxSRI07hRUNfhz3kujIYMgTY+7QA7E7ko5WTK/5ePRPCNHKRi5OhyVsVKi+mo7eRDQrGWhOpWmN2kKcmPnIUwFs5Hmw+1Uoa9i2kStlCVOhywHMdgagz4ojRtwTYz22jx17MhuTGTMlxSjSeZYbFf0JwYcBj6hojhF2Z2WCfT0C2PR9knQQwnP+NPdQRi+cLYumTmjRcf5OPHvXhuJLh4rTO30R+k6TGmnDsGebQjrp4uHPTCHgB4fj9pzmaP+ChVEEsVKll4t0Hksxbz2D5U1N8Z1ZJy+2y4UdoGrgOchH9NrICGGlxg2A+ONW5kt9adrT54Xr60rwQ3o+Lhoad+WjtL6TJn6LRZxGfnBaK4P7tHNSanA9KYdPAmhcDC2QsSVblBzLe4CjVeygEhf/4kHpAiL1GN9j84yLM6y5Ax1sMRndkwT/La+Im7zxQm6QBDoX14qvzg+D5TUp2Xp1Dfrs74yDvNV0wUR+bBQKomueC78J4uG2TEWBJCh5d4Q4ByjmofTofPVamtjg5D9Nl4XvE36sUwb9hMlWYL0D60hL7q+qomRYf3/rl4WX1bHz+0xFUK4TolvFQ/LbxF9H5XQAxJyT9fCUSnGYno+izNaxJHKAdDZl4eVki6BSLYHqjCHwbq+jwSBJmJWeguiAb3QfiMM/RAXwMeWh+X8K8SQwarvRBJd1g6JLMsxVzZuAFqzFkM6jg2y4hFkVpwhvNMCxzf82e+qUA8wzD8K6MD9ay+eCRKfH4u96wyESBTCjNhBIJj5Wqq1MHVR4aWw3RLU0CUN/DgOiQN8rXi2CfEg/dHSaJT617QE3MBfgwcJhKHzxJi2ozoO6tAE5JfKIukJLT70fB2kcMLInkY5KONLhcsUPuWEU8/5yLScn+cCajEIoUhMC6bEqmcmLoXRUX0K1ioCVQiPmJXuDdPlPC5LIYfl8EB6Tu0ylnDDD7dhBOYIvQNjYNrpdUslpvCDHDnA3z/OeDjok7FeTlg8raeJz0Sggz1dLx9XM53PxXiI08CaP219Gu8khIaF+CzHRLclLAx65MEawxZWCDHB/tMRsF+XFwEOajuigVel6aibMkfVr+TgDbR+ailSCX6ur/I8pfGEw7Z48T5rSTvG8iXK0VBPAxHfx+2KKFvgdslf5A99qrAbc1Dx6UvqM1+/jIPxVIli0eS3d2M/gV4rBjMB+XPfGHh7EMQOUhUu46BfftEJGQI7pwnGNCd2R0sw8sVcbCXFXEbyK4OCkdvDZ2tpxhOYtzzBn8PyfVf24="


def _host_constants():
    c = {}
    raw = zlib.decompress(base64.b64decode(_ROPE_B64))
    cosj = np.frombuffer(raw[:S * 16 * 4], np.float32).reshape(S, 16)
    sinj = np.frombuffer(raw[S * 16 * 4:], np.float32).reshape(S, 16)
    i_idx = (np.arange(128) % 32) // 2
    for nm, tab in (("costab", cosj), ("sintab", sinj)):
        c[nm + "_hi"] = np.ascontiguousarray(tab.T[i_idx, :])
        c[nm + "_lo"] = np.zeros((128, S), np.float32)
    P = np.zeros((128, 128), np.float32)
    for b in range(0, 128, 32):
        for i in range(16):
            P[b + 2 * i, b + 2 * i + 1] = -1.0
            P[b + 2 * i + 1, b + 2 * i] = 1.0
    c["ptab"] = P.T.astype(BFD)
    c["id128b"] = np.eye(128, dtype=BFD)
    c["id128f"] = np.eye(128, dtype=np.float32)
    c["negid2f"] = (-np.eye(2, dtype=np.float32))
    c["onescolb"] = np.ones((128, 1), BFD)
    c["onescolf"] = np.ones((128, 1), np.float32)
    c["onesrowf"] = np.ones((1, 128), np.float32)
    c["onesrowb"] = np.ones((1, 128), BFD)
    selE = np.zeros((128, 2), np.float32)
    selE[0:32, 0] = 1.0
    selE[64:96, 1] = 1.0
    selO = np.zeros((128, 2), np.float32)
    selO[32:64, 0] = 1.0
    selO[96:128, 1] = 1.0
    c["selE"] = selE.astype(BFD)
    c["selO"] = selO.astype(BFD)
    return c


class Prog:

    def __init__(self, nc, es):
        self.c = {}
        self.h = {}
        self.lw = {}
        for n in ("dS", "dG", "pT", "pV", "pS"):
            self.h[n] = es.enter_context(nc.semaphore(n))
            self.c[n] = 0

    def inc(self, instr, sem, k=1):
        instr.then_inc(self.h[sem], k)
        self.c[sem] += k
        return instr

    def wait(self, eng, lbl, sem, val=None):
        v = self.c[sem] if val is None else val
        if v <= self.lw.get((lbl, sem), 0):
            return
        eng.wait_ge(self.h[sem], v)
        self.lw[(lbl, sem)] = v

    def barrier(self, engs):
        for eng, lbl in engs:
            for sem in ("dS", "dG", "pT", "pV", "pS"):
                self.wait(eng, lbl, sem)


def build_nc():
    nc = bass.Bass("TRN2", target_bir_lowering=False)
    dr = {}
    dr["x"] = nc.dram_tensor("x", [D, S], F32, kind="ExternalInput")
    for n in ("wq", "wk", "wv", "wo"):
        dr[n] = nc.dram_tensor(n, [D, D], F32, kind="ExternalInput")
    dr["wmod"] = nc.dram_tensor("wmod", [D, 6 * D], F32, kind="ExternalInput")
    dr["wg"] = nc.dram_tensor("wg", [D, INNER], F32, kind="ExternalInput")
    dr["wu"] = nc.dram_tensor("wu", [D, INNER], F32, kind="ExternalInput")
    dr["wd"] = nc.dram_tensor("wd", [INNER, D], F32, kind="ExternalInput")
    dr["cvec"] = nc.dram_tensor("cvec", [D, 1], F32, kind="ExternalInput")
    for n in ("lq1", "lk1", "lq2", "lk2"):
        dr[n] = nc.dram_tensor(n, [1, HD], F32, kind="ExternalInput")
    dr["subg"] = nc.dram_tensor("subg", [128, 1], F32, kind="ExternalInput")
    for n, sh, dt in (
        ("costab_hi", [128, S], F32), ("costab_lo", [128, S], F32),
        ("sintab_hi", [128, S], F32), ("sintab_lo", [128, S], F32),
        ("ptab", [128, 128], BF16), ("id128b", [128, 128], BF16),
        ("id128f", [128, 128], F32), ("negid2f", [2, 2], F32),
        ("onescolb", [128, 1], BF16), ("onescolf", [128, 1], F32),
        ("onesrowf", [1, 128], F32), ("onesrowb", [1, 128], BF16),
        ("selE", [128, 2], BF16), ("selO", [128, 2], BF16),
    ):
        dr[n] = nc.dram_tensor(n, sh, dt, kind="ExternalInput")
    dr["x2scr"] = nc.dram_tensor("x2scr", [D, S], F32)
    dr["wgq"] = nc.dram_tensor("wgq", [D, INNER], BF16)
    dr["wuq"] = nc.dram_tensor("wuq", [D, INNER], BF16)
    dr["wdq"] = nc.dram_tensor("wdq", [INNER, D], BF16)
    dr["out"] = nc.dram_tensor("out", [D, S], F32, kind="ExternalOutput")

    with ExitStack() as es:
        _emit(nc, dr, es)
    return nc



STOP_AFTER = "full"


def _emit(nc, dr, es):
    T, V, SC, G, SY = nc.tensor, nc.vector, nc.scalar, nc.gpsimd, nc.sync
    ENGS = [(T, "t"), (V, "v"), (SC, "s"), (G, "g"), (SY, "y")]
    p = Prog(nc, es)

    def sb(name, shape, dt):
        return es.enter_context(nc.sbuf_tensor(name, shape, dt))

    ap = lambda n: dr[n].ap()

    pb0 = es.enter_context(nc.psum_tensor("pb0", [128, 1024], F32))
    pb1 = es.enter_context(nc.psum_tensor("pb1", [128, 1024], F32))
    pz = es.enter_context(nc.psum_tensor("pz", [128, 512], F32))
    pv0 = es.enter_context(nc.psum_tensor("pv0", [128, 512], F32))
    pv1 = es.enter_context(nc.psum_tensor("pv1", [128, 512], F32))
    pm = es.enter_context(nc.psum_tensor("pm", [128, 512], F32))

    c_ptab = sb("c_ptab", [128, 128], BF16)
    c_idb = sb("c_idb", [128, 128], BF16)
    c_idf = sb("c_idf", [128, 128], F32)
    c_nid2 = sb("c_nid2", [2, 2], F32)
    c_ocb = sb("c_ocb", [128, 1], BF16)
    c_ocf = sb("c_ocf", [128, 1], F32)
    c_orf = sb("c_orf", [1, 128], F32)
    c_orb = sb("c_orb", [1, 128], BF16)
    c_selE = sb("c_selE", [128, 2], BF16)
    c_selO = sb("c_selO", [128, 2], BF16)
    modc = sb("modc", [128, 48], F32)
    capl1 = sb("capl1", [128, 16], F32)
    gacol = sb("gacol", [128, 16], F32)
    gpcol = sb("gpcol", [128, 1], F32)
    oneslam = sb("oneslam", [128, 64], F32)
    lamcol = sb("lamcol", [128, 1], F32)
    lam11 = sb("lam11", [1, 1], F32)
    subgs = sb("subgs", [128, 1], F32)
    scalc = sb("scalc", [128, 1], F32)
    swcol = sb("swcol", [128, 3], F32)

    for dst, src in [(c_ptab, "ptab"), (c_idb, "id128b"), (c_idf, "id128f"),
                     (c_nid2, "negid2f"), (c_ocb, "onescolb"), (c_ocf, "onescolf"),
                     (c_orf, "onesrowf"), (c_orb, "onesrowb"), (c_selE, "selE"),
                     (c_selO, "selO"), (subgs, "subg")]:
        p.wait(SY, "y", "dS")
        p.inc(SY.dma_start(dst[:], ap(src)), "dS", 16)

    p.inc(V.memset(scalc[:], SCAL_HI), "pV")
    p.wait(V, "v", "pV")
    p.inc(V.tensor_scalar_add(scalc[:], scalc[:], SCAL_LO), "pV")

    with ExitStack() as e1:
        cvecs = e1.enter_context(nc.sbuf_tensor("cvecs", [128, 8], F32))
        siluc = e1.enter_context(nc.sbuf_tensor("siluc", [128, 8], BF16))
        wmodb = e1.enter_context(nc.sbuf_tensor("wmodb", [128, 512], BF16))
        modrow = e1.enter_context(nc.sbuf_tensor("modrow", [1, 6 * D], F32))
        lrow = e1.enter_context(nc.sbuf_tensor("lrow", [1, HD], F32))
        l11b = e1.enter_context(nc.sbuf_tensor("l11b", [1, 1], F32))
        lqs = [e1.enter_context(nc.sbuf_tensor(f"lqs{i}", [1, HD], F32))
               for i in range(4)]

        for j in range(8):
            p.wait(SY, "y", "dS")
            p.inc(SY.dma_start(cvecs[:, j:j + 1],
                               dr["cvec"].ap()[128 * j:128 * (j + 1), :]), "dS", 16)
        for t, n in zip(lqs, ("lq1", "lk1", "lq2", "lk2")):
            p.wait(SY, "y", "dS")
            p.inc(SY.dma_start(t[:], ap(n)), "dS", 16)
        p.wait(SC, "s", "dS")
        p.inc(SC.activation(siluc[:], cvecs[:], AF.Silu), "pS")

        p.wait(V, "v", "dS")
        p.inc(V.tensor_mul(lrow[:], lqs[0][:], lqs[1][:]), "pV")
        p.wait(V, "v", "pV")
        p.inc(V.reduce_sum(lam11[:], lrow[:], axis=AX.X), "pV")
        p.wait(SC, "s", "pV")
        p.inc(SC.activation(l11b[:], lam11[:], AF.Exp), "pS")
        p.wait(V, "v", "pS")
        p.inc(V.tensor_mul(lrow[:], lqs[2][:], lqs[3][:]), "pV")
        p.wait(V, "v", "pV")
        p.inc(V.reduce_sum(lam11[:], lrow[:], axis=AX.X), "pV")
        p.wait(SC, "s", "pV")
        p.inc(SC.activation(lam11[:], lam11[:], AF.Exp), "pS")
        p.wait(V, "v", "pS")
        p.inc(V.tensor_sub(lam11[:], l11b[:], lam11[:]), "pV")
        p.wait(V, "v", "pV")
        p.inc(V.tensor_scalar_add(lam11[:], lam11[:], float(LAMBDA_INIT)), "pV")


        for sl in range(12):
            for kc in range(NCH):
                p.wait(G, "g", "pT")
                p.wait(G, "g", "dG")
                p.inc(G.dma_start(wmodb[:], dr["wmod"].ap()[128 * kc:128 * (kc + 1),
                                                            512 * sl:512 * (sl + 1)]),
                      "dG", 16)
                p.wait(T, "t", "dG")
                p.wait(T, "t", "pS")
                p.inc(T.matmul(pz[0:1, :], siluc[:, kc:kc + 1], wmodb[:],
                               start=(kc == 0), stop=(kc == NCH - 1)), "pT")
            p.wait(SC, "s", "pT")
            p.inc(SC.copy(modrow[0:1, 512 * sl:512 * (sl + 1)], pz[0:1, :]), "pS")
        p.wait(T, "t", "pS")
        for j in range(48):
            p.inc(T.matmul(pm[:, j:j + 1], modrow[0:1, 128 * j:128 * (j + 1)],
                           c_idf[0:1, 0:1], start=True, stop=True), "pT")
        p.wait(SC, "s", "pT")
        p.inc(SC.copy(modc[:], pm[:, 0:48]), "pS")

    p.wait(T, "t", "pS")
    p.wait(T, "t", "pV")
    p.inc(T.matmul(pz[:, 0:1], c_orf[:], lam11[:], start=True, stop=True), "pT")
    p.wait(SC, "s", "pT")
    p.inc(SC.copy(lamcol[:], pz[:, 0:1]), "pS")
    p.wait(V, "v", "pS")
    p.inc(V.memset(oneslam[:], 1.0), "pV")
    p.wait(V, "v", "pV")
    p.inc(V.tensor_scalar(oneslam[32:64, :], oneslam[32:64, :], lamcol[32:64, 0:1],
                          None, ALU.mult), "pV")
    p.inc(V.tensor_scalar(oneslam[96:128, :], oneslam[96:128, :], lamcol[96:128, 0:1],
                          None, ALU.mult), "pV")

    p.wait(V, "v", "pS")
    for j in range(8):
        p.inc(V.tensor_scalar_add(capl1[:, j:j + 1], modc[:, 8 + j:9 + j], 1.0), "pV")
        p.inc(V.tensor_scalar_add(capl1[:, 8 + j:9 + j], modc[:, 32 + j:33 + j], 1.0),
              "pV")
    with ExitStack() as e3:
        tneg = e3.enter_context(nc.sbuf_tensor("tneg", [128, 1], F32))
        tabs = e3.enter_context(nc.sbuf_tensor("tabs", [128, 1], F32))
        tsgn = e3.enter_context(nc.sbuf_tensor("tsgn", [128, 1], F32))
        for j in range(16):
            col = 16 + j if j < 8 else 40 + (j - 8)
            src = modc[:, col:col + 1]
            p.wait(V, "v", "pV")
            p.inc(V.tensor_scalar_mul(tneg[:], src, -1.0), "pV")
            p.wait(V, "v", "pV")
            p.inc(V.tensor_max(tabs[:], tneg[:], src), "pV")
            p.wait(V, "v", "pV")
            p.inc(V.tensor_scalar_max(tabs[:], tabs[:], 0.25), "pV")
            p.inc(V.tensor_scalar(tsgn[:], src, 0.0, 2.0, ALU.is_ge, ALU.mult), "pV")
            p.wait(V, "v", "pV")
            p.inc(V.tensor_scalar_sub(tsgn[:], tsgn[:], 1.0), "pV")
            p.wait(V, "v", "pV")
            p.inc(V.tensor_mul(gacol[:, j:j + 1], tabs[:], tsgn[:]), "pV")
    p.wait(V, "v", "dS")
    p.inc(V.tensor_scalar_mul(gpcol[:], subgs[:], OMLI_HI), "pV")
    p.wait(V, "v", "pV")
    p.inc(V.scalar_tensor_tensor(gpcol[:], subgs[:], OMLI_LO, gpcol[:],
                                 ALU.mult, ALU.add), "pV")

    if STOP_AFTER == "p2":
        G.wait_ge(p.h["dG"], p.c["dG"])
        p.barrier(ENGS)
        return
    with ExitStack() as em:
        qm = lambda n, s_, d_: em.enter_context(nc.sbuf_tensor(n, s_, d_))
        qf = [qm(f"qf{i}", [128, S], BF16) for i in range(NCH)]
        kf = [qm(f"kf{i}", [128, S], BF16) for i in range(NCH)]
        vt = [qm(f"vt{i}", [128, D], BF16) for i in range(NKC)]
        mT = [qm(f"mT{d}", [128, 32], F32) for d in range(NDUO)]
        lT = [qm(f"lT{d}", [128, 32], BF16) for d in range(NDUO)]
        lTf = [qm(f"lTf{d}", [128, 32], F32) for d in range(NDUO)]

        with ExitStack() as eq:
            q = lambda n, s_, d_: eq.enter_context(nc.sbuf_tensor(n, s_, d_))
            costab = q("s_costab", [128, S], F32)
            sintab = q("s_sintab", [128, S], F32)
            hch = [q(f"h{i}", [128, S], BF16) for i in range(NCH)]
            wbf = q("wbf", [128, 8 * 1024], BF16)

            p.wait(SY, "y", "dS")
            with ExitStack() as el:
                ql = lambda n, s_, d_: el.enter_context(nc.sbuf_tensor(n, s_, d_))
                xcb = ql("xcb", [128, S], F32)
                p.wait(SY, "y", "dS")
                p.inc(SY.dma_start(costab[:], ap("costab_hi")), "dS", 16)
                p.wait(SY, "y", "dS")
                p.inc(SY.dma_start(xcb[:], ap("costab_lo")), "dS", 16)
                p.wait(V, "v", "dS")
                p.inc(V.tensor_tensor(costab[:], costab[:], xcb[:], ALU.add), "pV")
                p.wait(SY, "y", "dS")
                p.wait(SY, "y", "pV")
                p.inc(SY.dma_start(sintab[:], ap("sintab_hi")), "dS", 16)
                p.wait(SY, "y", "dS")
                p.inc(SY.dma_start(xcb[:], ap("sintab_lo")), "dS", 16)
                p.wait(V, "v", "dS")
                p.inc(V.tensor_tensor(sintab[:], sintab[:], xcb[:], ALU.add), "pV")
                xb16 = ql("xb16", [128, QT], BF16)
                xsq = ql("xsq", [128, QT], BF16)
                murow = ql("murow", [1, 512], F32)
                exrow = ql("exrow", [1, 512], F32)
                rarow = ql("rarow", [1, 512], F32)
                rbrow = ql("rbrow", [1, 512], F32)
                rAb = ql("rAb", [128, S], F32)
                rBb = ql("rBb", [128, S], F32)

                bankx = [pb0[:, 0:512], pb0[:, 512:1024], pb1[:, 0:512],
                         pb1[:, 512:1024]]
                bankq = [pz[:, :], pv0[:, :], pv1[:, :], pm[:, :]]
                sx = [b[0:1, :] for b in bankx]
                sq_ = [b[0:1, :] for b in bankq]
                for ci in range(NCH):
                    p.wait(SY, "y", "pV")
                    p.wait(SY, "y", "dS")
                    p.inc(SY.dma_start(xcb[:],
                                       dr["x"].ap()[128 * ci:128 * (ci + 1), :]),
                          "dS", 16)
                    p.wait(V, "v", "dS")
                    for ts in range(NQT):
                        sl_ = slice(QT * ts, QT * (ts + 1))
                        p.wait(V, "v", "pT")
                        p.inc(V.tensor_copy(xb16[:], xcb[:, sl_]), "pV")
                        p.inc(V.tensor_tensor(xsq[:], xcb[:, sl_], xcb[:, sl_],
                                              ALU.mult), "pV")
                        p.wait(T, "t", "pV")
                        T.matmul(sx[ts], c_ocb[:], xb16[:], start=(ci == 0),
                                 stop=(ci == NCH - 1))
                        p.inc(T.matmul(sq_[ts], c_ocb[:], xsq[:], start=(ci == 0),
                                       stop=(ci == NCH - 1)), "pT")
                for ts in range(NQT):
                    sl_ = slice(QT * ts, QT * (ts + 1))
                    p.wait(SC, "s", "pT")
                    p.wait(SC, "s", "pV")
                    SC.copy(murow[:], sx[ts])
                    p.inc(SC.copy(exrow[:], sq_[ts]), "pS")
                    p.wait(V, "v", "pS")
                    p.inc(V.tensor_scalar_mul(murow[:], murow[:], 1.0 / D), "pV")
                    p.inc(V.tensor_scalar_mul(exrow[:], exrow[:], 1.0 / D), "pV")
                    p.wait(V, "v", "pV")
                    p.inc(V.tensor_tensor(rarow[:], murow[:], murow[:], ALU.mult),
                          "pV")
                    p.wait(V, "v", "pV")
                    p.inc(V.tensor_sub(rarow[:], exrow[:], rarow[:]), "pV")
                    p.wait(V, "v", "pV")
                    p.inc(V.tensor_scalar_add(rarow[:], rarow[:], EPS), "pV")
                    p.wait(V, "v", "pV")
                    p.inc(V.reciprocal(rarow[:], rarow[:]), "pV")
                    p.wait(SC, "s", "pV")
                    p.inc(SC.activation(rarow[:], rarow[:], AF.Sqrt), "pS")
                    p.wait(V, "v", "pS")
                    p.inc(V.scalar_tensor_tensor(rbrow[:], murow[:], -1.0,
                                                 rarow[:], ALU.mult, ALU.mult), "pV")
                    p.wait(T, "t", "pV")
                    p.wait(T, "t", "pS")
                    p.inc(T.matmul(bankx[ts], c_orf[:], rarow[:],
                                   start=True, stop=True), "pT")
                    p.wait(SC, "s", "pT")
                    p.inc(SC.copy(rAb[:, sl_], bankx[ts]), "pS")
                    p.wait(T, "t", "pS")
                    p.inc(T.matmul(bankq[ts], c_orf[:], rbrow[:],
                                   start=True, stop=True), "pT")
                    p.wait(SC, "s", "pT")
                    p.inc(SC.copy(rBb[:, sl_], bankq[ts]), "pS")
                with ExitStack() as eh:
                    t1f = eh.enter_context(nc.sbuf_tensor("t1f", [128, QT], F32))
                    t2f = eh.enter_context(nc.sbuf_tensor("t2f", [128, QT], F32))
                    for ci in range(NCH):
                        p.wait(SY, "y", "pV")
                        p.wait(SY, "y", "pS")
                        p.wait(SY, "y", "dS")
                        p.inc(SY.dma_start(xcb[:],
                                           dr["x"].ap()[128 * ci:128 * (ci + 1), :]),
                              "dS", 16)
                        p.wait(V, "v", "dS")
                        for ts in range(NQT):
                            sl_ = slice(QT * ts, QT * (ts + 1))
                            p.wait(V, "v", "pS")
                            p.wait(V, "v", "pV")
                            p.inc(V.tensor_tensor(t1f[:], xcb[:, sl_], rAb[:, sl_],
                                                  ALU.mult), "pV")
                            p.wait(V, "v", "pV")
                            p.inc(V.tensor_tensor(t2f[:], t1f[:], rBb[:, sl_],
                                                  ALU.add), "pV")
                            p.wait(SC, "s", "pV")
                            p.inc(SC.activation(hch[ci][:, sl_], t2f[:],
                                                AF.Identity,
                                                scale=capl1[:, ci:ci + 1],
                                                bias=modc[:, ci:ci + 1]), "pS")

            with ExitStack() as er:
                qraw = er.enter_context(nc.sbuf_tensor("qraw", [128, QT], BF16))
                ptmp = er.enter_context(nc.sbuf_tensor("ptmp", [128, QT], BF16))
                t1f = er.enter_context(nc.sbuf_tensor("t1r", [128, QT], F32))
                t2f = er.enter_context(nc.sbuf_tensor("t2r", [128, QT], F32))
                for which, wname, dst in (("q", "wq", qf), ("k", "wk", kf)):
                    p.wait(G, "g", "pT")
                    for kc in range(NCH):
                        p.wait(G, "g", "dG")
                        p.inc(G.dma_start(wbf[:, 1024 * kc:1024 * (kc + 1)],
                                          dr[wname].ap()[128 * kc:128 * (kc + 1), :]),
                              "dG", 16)
                    for oc in range(NCH):
                        for ts in range(NQT):
                            sl_ = slice(QT * ts, QT * (ts + 1))
                            ps_ = pb0[:, 512 * (ts % 2):512 * (ts % 2 + 1)]
                            p.wait(T, "t", "dG")
                            p.wait(T, "t", "pS")
                            p.wait(T, "t", "pV")
                            for kc in range(NCH):
                                ins = T.matmul(
                                    ps_, wbf[:, 1024 * kc + 128 * oc:
                                             1024 * kc + 128 * (oc + 1)],
                                    hch[kc][:, sl_],
                                    start=(kc == 0), stop=(kc == NCH - 1))
                            p.inc(ins, "pT")
                            p.wait(SC, "s", "pT")
                            if which == "q":
                                p.inc(SC.activation(qraw[:], ps_, AF.Copy,
                                                    scale=scalc[:, 0:1]), "pS")
                            else:
                                p.inc(SC.copy(qraw[:], ps_), "pS")
                            ps2 = pb1[:, 512 * (ts % 2):512 * (ts % 2 + 1)]
                            p.wait(T, "t", "pS")
                            p.inc(T.matmul(ps2, c_ptab[:], qraw[:],
                                           start=True, stop=True), "pT")
                            p.wait(SC, "s", "pT")
                            p.inc(SC.copy(ptmp[:], ps2), "pS")
                            p.wait(V, "v", "pS")
                            p.wait(V, "v", "pV")
                            p.inc(V.tensor_tensor(t1f[:], qraw[:], costab[:, sl_],
                                                  ALU.mult), "pV")
                            p.inc(V.tensor_tensor(t2f[:], ptmp[:], sintab[:, sl_],
                                                  ALU.mult), "pV")
                            p.wait(V, "v", "pV")
                            p.inc(V.tensor_tensor(dst[oc][:, sl_], t1f[:], t2f[:],
                                                  ALU.add), "pV")

            with ExitStack() as en:
                qn = en.enter_context(nc.sbuf_tensor("qn", [2, 512], F32))
                kn = en.enter_context(nc.sbuf_tensor("kn", [2, 512], F32))
                scr2 = en.enter_context(nc.sbuf_tensor("scr2", [2, 512], F32))
                mrowF = en.enter_context(nc.sbuf_tensor("mrowF", [2, S], F32))
                lrow2 = en.enter_context(nc.sbuf_tensor("lrow2", [2, S], BF16))
                sqb = en.enter_context(nc.sbuf_tensor("sqb", [128, QT], BF16))
                g2 = en.enter_context(nc.sbuf_tensor("g2", [2, 8], F32))
                for d in range(NDUO):
                    for ts in range(NQT):
                        sl_ = slice(QT * ts, QT * (ts + 1))
                        p.wait(V, "v", "pT")
                        p.wait(V, "v", "pS")
                        p.inc(V.tensor_tensor(sqb[:], qf[d][:, sl_], qf[d][:, sl_],
                                              ALU.mult), "pV")
                        p.wait(T, "t", "pV")
                        p.wait(T, "t", "pS")
                        T.matmul(pz[0:2, :], c_selE[:], sqb[:], start=True,
                                 stop=True)
                        p.inc(T.matmul(pv0[0:2, :], c_selO[:], sqb[:], start=True,
                                       stop=True), "pT")
                        p.wait(SC, "s", "pT")
                        p.inc(SC.copy(scr2[:], pz[0:2, :]), "pS")
                        p.wait(V, "v", "pS")
                        p.inc(V.tensor_tensor(qn[:], scr2[:], pv0[0:2, :], ALU.max),
                              "pV")
                        p.wait(SC, "s", "pV")
                        p.inc(SC.activation(qn[:], qn[:], AF.Sqrt), "pS")
                        p.wait(V, "v", "pS")
                        p.inc(V.reduce_max(g2[0:2, ts:ts + 1], qn[:], axis=AX.X),
                              "pV")
                    p.wait(V, "v", "pV")
                    p.inc(V.reduce_max(g2[0:2, 4:5], g2[0:2, 0:4], axis=AX.X), "pV")
                    for ts in range(NQT):
                        sl_ = slice(QT * ts, QT * (ts + 1))
                        p.wait(V, "v", "pT")
                        p.wait(V, "v", "pS")
                        p.inc(V.tensor_tensor(sqb[:], kf[d][:, sl_], kf[d][:, sl_],
                                              ALU.mult), "pV")
                        p.wait(T, "t", "pV")
                        p.wait(T, "t", "pS")
                        T.matmul(pz[0:2, :], c_selE[:], sqb[:], start=True,
                                 stop=True)
                        p.inc(T.matmul(pv0[0:2, :], c_selO[:], sqb[:], start=True,
                                       stop=True), "pT")
                        p.wait(SC, "s", "pT")
                        p.inc(SC.copy(scr2[:], pz[0:2, :]), "pS")
                        p.wait(V, "v", "pS")
                        p.inc(V.tensor_tensor(kn[:], scr2[:], pv0[0:2, :], ALU.max),
                              "pV")
                        p.wait(SC, "s", "pV")
                        p.inc(SC.activation(kn[:], kn[:], AF.Sqrt), "pS")
                        p.wait(V, "v", "pS")
                        p.inc(V.tensor_scalar(mrowF[0:2, sl_], kn[:], g2[0:2, 4:5],
                                              None, ALU.mult), "pV")
                    p.wait(V, "v", "pV")
                    p.inc(V.reduce_max(g2[0:2, 5:6], mrowF[:], axis=AX.X), "pV")
                    p.wait(V, "v", "pV")
                    p.inc(V.tensor_scalar_mul(g2[0:2, 5:6], g2[0:2, 5:6], -1.0),
                          "pV")
                    p.wait(SC, "s", "pV")
                    p.inc(SC.activation(lrow2[:], mrowF[:], AF.Exp,
                                        bias=g2[0:2, 5:6]), "pS")
                    p.wait(T, "t", "pV")
                    p.wait(T, "t", "pS")
                    for kcc in range(NKC):
                        T.matmul(pm[:, 2 * kcc:2 * kcc + 2],
                                 mrowF[0:2, 128 * kcc:128 * (kcc + 1)],
                                 c_nid2[0:2, 0:2], start=True, stop=True)
                        ins = T.matmul(pz[:, 2 * kcc:2 * kcc + 2],
                                       lrow2[0:2, 128 * kcc:128 * (kcc + 1)],
                                       c_idb[0:2, 0:2], start=True, stop=True)
                    p.inc(ins, "pT")
                    p.wait(SC, "s", "pT")
                    SC.copy(mT[d][:], pm[:, 0:32])
                    SC.copy(lT[d][:], pz[:, 0:32])
                    p.inc(SC.copy(lTf[d][:], pz[:, 0:32]), "pS")
                    p.wait(T, "t", "pS")
                    p.wait(V, "v", "pS")

            p.wait(G, "g", "pT")
            for kc in range(NCH):
                p.wait(G, "g", "dG")
                p.inc(G.dma_start(wbf[:, 1024 * kc:1024 * (kc + 1)],
                                  dr["wv"].ap()[128 * kc:128 * (kc + 1), :]),
                      "dG", 16)
            for tc in range(NKC):
                for hf in range(2):
                    ps_ = pb0[:, 512 * hf:512 * (hf + 1)]
                    p.wait(T, "t", "dG")
                    p.wait(T, "t", "pS")
                    for kc in range(NCH):
                        ins = T.matmul(
                            ps_, hch[kc][:, 128 * tc:128 * (tc + 1)],
                            wbf[:, 1024 * kc + 512 * hf:1024 * kc + 512 * (hf + 1)],
                            start=(kc == 0), stop=(kc == NCH - 1))
                    p.inc(ins, "pT")
                    p.wait(SC, "s", "pT")
                    p.inc(SC.copy(vt[tc][:, 512 * hf:512 * (hf + 1)], ps_), "pS")
                p.wait(V, "v", "pS")
                for gp in range(NPAIR):
                    ins = V.tensor_scalar(
                        vt[tc][:, 64 * gp:64 * (gp + 1)],
                        vt[tc][:, 64 * gp:64 * (gp + 1)],
                        lTf[gp // 2][:, 2 * tc + gp % 2:2 * tc + gp % 2 + 1],
                        None, ALU.mult)
                p.inc(ins, "pV")

        p.barrier(ENGS)

        if STOP_AFTER == "qkv":
            G.wait_ge(p.h["dG"], p.c["dG"])
            return
        with ExitStack() as ea:
            q = lambda n, s_, d_: ea.enter_context(nc.sbuf_tensor(n, s_, d_))
            eAb = q("eAb", [128, 1024], BF16)
            eBb = q("eBb", [128, 1024], BF16)
            eAb2 = q("eAb2", [128, 1024], BF16)
            eBb2 = q("eBb2", [128, 1024], BF16)
            attnq = [q(f"attnq{i}", [128, QT], BF16) for i in range(NCH)]
            xt = [q(f"xt{i}", [128, QT], F32) for i in range(NCH)]
            zr = q("zr", [128, 512], F32)
            zgp = q("zgp", [128, 512], F32)
            As = [q(f"As{i}", [64, 512], F32) for i in range(4)]
            tt0 = q("tt0", [64, 512], F32)
            tt1 = q("tt1", [64, 512], F32)
            anf = q("anf", [64, 512], F32)
            ansq = q("ansq", [64, 512], BF16)
            r2a = q("r2a", [1, 512], F32)
            rmsb = q("rmsb", [1, 512], BF16)
            woW = [q(f"woW{i}", [128, 1024], BF16) for i in range(NCH)]
            wtmp = q("wtmp", [128, 1024], F32)
            t4 = q("t4", [128, 512], F32)

            for kc in range(NCH):
                p.wait(SY, "y", "pS")
                p.wait(SY, "y", "dS")
                p.inc(SY.dma_start(wtmp[:], dr["wo"].ap()[128 * kc:128 * (kc + 1), :]),
                      "dS", 16)
                p.wait(SC, "s", "dS")
                p.inc(SC.activation(woW[kc][:], wtmp[:], AF.Copy,
                                    scale=gpcol[:, 0:1]), "pS")

            for qt in range(NQT):
                p.wait(SY, "y", "pV")
                p.wait(SY, "y", "dS")
                for ci in range(NCH):
                    p.wait(SY, "y", "dS")
                    p.inc(SY.dma_start(xt[ci][:],
                                       dr["x"].ap()[128 * ci:128 * (ci + 1),
                                                    QT * qt:QT * (qt + 1)]),
                          "dS", 16)
                zpv_cnt = [0, 0]
                for d in range(NDUO):
                    p.wait(V, "v", "pT")
                    p.wait(V, "v", "pS")
                    p.wait(V, "v", "pV")
                    p.inc(V.memset(pz[:], 1.0), "pV")
                    p.wait(T, "t", "pV")
                    for kt in range(NKC):
                        p.wait(T, "t", "pS")
                        T.matmul(pb0[:, 0:512], kf[d][0:32, 128 * kt:128 * (kt + 1)],
                                 qf[d][0:32, QT * qt:QT * (qt + 1)],
                                 start=True, stop=True, tile_position=(0, 0))
                        T.matmul(pb0[:, 512:1024],
                                 kf[d][32:64, 128 * kt:128 * (kt + 1)],
                                 qf[d][32:64, QT * qt:QT * (qt + 1)],
                                 start=True, stop=True, tile_position=(32, 0))
                        T.matmul(pb1[:, 0:512], kf[d][64:96, 128 * kt:128 * (kt + 1)],
                                 qf[d][64:96, QT * qt:QT * (qt + 1)],
                                 start=True, stop=True, tile_position=(64, 0))
                        p.inc(T.matmul(pb1[:, 512:1024],
                                       kf[d][96:128, 128 * kt:128 * (kt + 1)],
                                       qf[d][96:128, QT * qt:QT * (qt + 1)],
                                       start=True, stop=True,
                                       tile_position=(96, 0)), "pT")
                        eA = eAb if kt % 2 == 0 else eAb2
                        eB = eBb if kt % 2 == 0 else eBb2
                        p.wait(SC, "s", "pT",
                               max(p.c["pT"], zpv_cnt[kt % 2]))
                        SC.activation(eA[:], pb0[:], AF.Exp,
                                      bias=mT[d][:, 2 * kt:2 * kt + 1])
                        p.inc(SC.activation(eB[:], pb1[:], AF.Exp,
                                            bias=mT[d][:, 2 * kt + 1:2 * kt + 2]),
                              "pS")
                        p.wait(T, "t", "pS")
                        first = (kt == 0)
                        last = (kt == NKC - 1)
                        T.matmul(pz[0:1, :], lT[d][:, 2 * kt:2 * kt + 1],
                                 eA[:, 0:512], start=first, stop=last,
                                 tile_position=(0, 0))
                        T.matmul(pz[32:33, :], lT[d][:, 2 * kt:2 * kt + 1],
                                 eA[:, 512:1024], start=first, stop=last,
                                 tile_position=(0, 32), skip_group_check=True)
                        T.matmul(pz[64:65, :], lT[d][:, 2 * kt + 1:2 * kt + 2],
                                 eB[:, 0:512], start=first, stop=last,
                                 tile_position=(0, 64), skip_group_check=True)
                        T.matmul(pz[96:97, :], lT[d][:, 2 * kt + 1:2 * kt + 2],
                                 eB[:, 512:1024], start=first, stop=last,
                                 tile_position=(0, 96), skip_group_check=True)
                        T.matmul(pv0[0:64, :], vt[kt][:, 128 * d:128 * d + 64],
                                 eA[:, 0:512], start=first, stop=last,
                                 tile_position=(0, 0))
                        T.matmul(pv0[64:128, :], vt[kt][:, 128 * d:128 * d + 64],
                                 eA[:, 512:1024], start=first, stop=last,
                                 tile_position=(0, 64), skip_group_check=True)
                        T.matmul(pv1[0:64, :], vt[kt][:, 128 * d + 64:128 * (d + 1)],
                                 eB[:, 0:512], start=first, stop=last,
                                 tile_position=(0, 0))
                        p.inc(T.matmul(pv1[64:128, :],
                                       vt[kt][:, 128 * d + 64:128 * (d + 1)],
                                       eB[:, 512:1024], start=first, stop=last,
                                       tile_position=(0, 64),
                                       skip_group_check=True), "pT")
                        zpv_cnt[kt % 2] = p.c["pT"]
                    p.wait(V, "v", "pT")
                    p.inc(V.reciprocal(zr[:], pz[:]), "pV")
                    p.wait(V, "v", "pV")
                    for r in (0, 32, 64, 96):
                        p.inc(V.tensor_copy(zgp[r:r + 1, :], zr[r:r + 1, :]), "pV")
                    p.wait(SC, "s", "pT")
                    SC.copy(As[0][:], pv0[0:64, :])
                    SC.copy(As[1][:], pv0[64:128, :])
                    SC.copy(As[2][:], pv1[0:64, :])
                    p.inc(SC.copy(As[3][:], pv1[64:128, :]), "pS")
                    p.wait(T, "t", "pS")
                    p.wait(T, "t", "pV")
                    ub = [pv0[0:64, :], pv1[0:64, :], pz[0:64, :], pm[0:64, :]]
                    T.matmul(ub[0], oneslam[0:1, :], zgp[0:1, :], start=True,
                             stop=True, tile_position=(0, 0))
                    T.matmul(ub[1], oneslam[32:33, :], zgp[32:33, :], start=True,
                             stop=True, tile_position=(32, 0))
                    T.matmul(ub[2], oneslam[64:65, :], zgp[64:65, :], start=True,
                             stop=True, tile_position=(64, 0))
                    p.inc(T.matmul(ub[3], oneslam[96:97, :], zgp[96:97, :],
                                   start=True, stop=True, tile_position=(96, 0)),
                          "pT")
                    for pl in range(2):
                        p.wait(V, "v", "pT")
                        p.wait(V, "v", "pV")
                        p.inc(V.tensor_tensor(tt0[:], As[2 * pl][:], ub[2 * pl],
                                              ALU.mult), "pV")
                        p.inc(V.tensor_tensor(tt1[:], As[2 * pl + 1][:],
                                              ub[2 * pl + 1], ALU.mult), "pV")
                        p.wait(V, "v", "pV")
                        p.inc(V.tensor_sub(anf[:], tt0[:], tt1[:]), "pV")
                        p.wait(V, "v", "pV")
                        p.inc(V.tensor_tensor(ansq[:], anf[:], anf[:], ALU.mult),
                              "pV")
                        p.wait(T, "t", "pV")
                        p.inc(T.matmul(pb0[0:1, 512 * pl:512 * (pl + 1)],
                                       c_ocb[0:64, 0:1], ansq[:],
                                       start=True, stop=True), "pT")
                        p.wait(V, "v", "pT")
                        p.inc(V.tensor_scalar(r2a[:],
                                              pb0[0:1, 512 * pl:512 * (pl + 1)],
                                              1.0 / 64, EPS, ALU.mult, ALU.add),
                              "pV")
                        p.wait(V, "v", "pV")
                        p.inc(V.reciprocal(r2a[:], r2a[:]), "pV")
                        p.wait(SC, "s", "pV")
                        p.inc(SC.activation(r2a[:], r2a[:], AF.Sqrt), "pS")
                        p.wait(V, "v", "pS")
                        p.inc(V.tensor_copy(rmsb[:], r2a[:]), "pV")
                        p.wait(T, "t", "pV")
                        p.inc(T.matmul(pb1[0:64, 512 * pl:512 * (pl + 1)],
                                       c_orb[0:1, 0:64], rmsb[:],
                                       start=True, stop=True), "pT")
                        p.wait(V, "v", "pT")
                        p.inc(V.tensor_tensor(attnq[d][64 * pl:64 * (pl + 1), :],
                                              anf[:],
                                              pb1[0:64, 512 * pl:512 * (pl + 1)],
                                              ALU.mult), "pV")
                for oc in range(NCH):
                    ps_ = pb0[:, 512 * (oc % 2):512 * (oc % 2 + 1)]
                    p.wait(T, "t", "pV")
                    p.wait(T, "t", "pS")
                    for kc in range(NCH):
                        ins = T.matmul(ps_, woW[kc][:, 128 * oc:128 * (oc + 1)],
                                       attnq[kc][:], start=(kc == 0),
                                       stop=(kc == NCH - 1))
                    p.inc(ins, "pT")
                    p.wait(SC, "s", "pT")
                    p.inc(SC.activation(t4[:], ps_, AF.Copy,
                                        scale=gacol[:, oc:oc + 1]), "pS")
                    p.wait(V, "v", "pS")
                    p.wait(V, "v", "dS")
                    p.inc(V.tensor_tensor(xt[oc][:], xt[oc][:], t4[:], ALU.add),
                          "pV")
                    p.wait(SY, "y", "pV")
                    p.wait(SY, "y", "dS")
                    p.inc(SY.dma_start(dr["x2scr"].ap()[128 * oc:128 * (oc + 1),
                                                        QT * qt:QT * (qt + 1)],
                                       xt[oc][:]), "dS", 16)

    p.barrier(ENGS)

    if STOP_AFTER == "attn":
        G.wait_ge(p.h["dG"], p.c["dG"])
        return
    with ExitStack() as ew:
        q = lambda n, s_, d_: ew.enter_context(nc.sbuf_tensor(n, s_, d_))
        wfp = [q(f"wfp{i}", [128, 4096], F32) for i in range(2)]
        wsum = q("wsum", [128, 1], F32)
        tcol = q("tcol", [128, 1], F32)
        m11 = q("m11", [1, 1], F32)
        hcol = q("hcol", [128, 1], F32)
        nhcol = q("nhcol", [128, 1], F32)
        d1 = q("d1", [128, 4096], F32)
        wqn = q("wqn", [128, 4096], BF16)

        for wi, (src, dst, nrow, ncol) in enumerate((
                ("wg", "wgq", D, INNER), ("wu", "wuq", D, INNER),
                ("wd", "wdq", INNER, D))):
            nchw = nrow // 128
            p.wait(V, "v", "pT")
            p.inc(V.memset(wsum[:], 0.0), "pV")
            for ch in range(nchw):
                buf = wfp[ch % 2]
                p.wait(SY, "y", "pV")
                p.wait(SY, "y", "dS")
                p.inc(SY.dma_start(buf[:, 0:ncol],
                                   dr[src].ap()[128 * ch:128 * (ch + 1), :]),
                      "dS", 16)
                p.wait(V, "v", "dS")
                p.inc(V.reduce_sum(tcol[:], buf[:, 0:ncol], axis=AX.X,
                                   apply_absolute_value=True), "pV")
                p.wait(V, "v", "pV")
                p.inc(V.tensor_add(wsum[:], wsum[:], tcol[:]), "pV")
            p.wait(T, "t", "pV")
            p.wait(T, "t", "pS")
            p.inc(T.matmul(pm[0:1, 0:1], wsum[:], c_ocf[:], start=True, stop=True),
                  "pT")
            p.wait(SC, "s", "pT")
            p.inc(SC.copy(m11[:], pm[0:1, 0:1]), "pS")
            p.wait(V, "v", "pS")
            p.inc(V.tensor_scalar(m11[:], m11[:], 1.0 / (nrow * ncol), 1e-5,
                                  ALU.mult, ALU.max), "pV")
            p.wait(T, "t", "pV")
            p.inc(T.matmul(pm[:, 1:2], c_orf[:], m11[:], start=True, stop=True),
                  "pT")
            p.wait(SC, "s", "pT")
            p.inc(SC.copy(swcol[:, wi:wi + 1], pm[:, 1:2]), "pS")
            p.wait(V, "v", "pS")
            p.inc(V.tensor_scalar_mul(hcol[:], swcol[:, wi:wi + 1], 0.5), "pV")
            p.wait(V, "v", "pV")
            p.inc(V.tensor_scalar_mul(nhcol[:], hcol[:], -1.0), "pV")
            for ch in range(nchw):
                buf = wfp[ch % 2]
                p.wait(SY, "y", "pV")
                p.wait(SY, "y", "dS")
                p.inc(SY.dma_start(buf[:, 0:ncol],
                                   dr[src].ap()[128 * ch:128 * (ch + 1), :]),
                      "dS", 16)
                p.wait(V, "v", "dS")
                p.wait(V, "v", "dG")
                p.inc(V.tensor_scalar(d1[:, 0:ncol], buf[:, 0:ncol], hcol[:, 0:1],
                                      None, ALU.is_gt), "pV")
                p.wait(V, "v", "pV")
                p.inc(V.scalar_tensor_tensor(wqn[:, 0:ncol], buf[:, 0:ncol],
                                             nhcol[:, 0:1], d1[:, 0:ncol],
                                             ALU.is_lt, ALU.subtract), "pV")
                p.wait(G, "g", "pV")
                p.wait(G, "g", "dG")
                p.inc(G.dma_start(dr[dst].ap()[128 * ch:128 * (ch + 1), :],
                                  wqn[:, 0:ncol]), "dG", 16)

    p.barrier(ENGS)

    with ExitStack() as ef:
        q = lambda n, s_, d_: ef.enter_context(nc.sbuf_tensor(n, s_, d_))
        x2t = [q(f"x2t{i}", [128, QT], F32) for i in range(NCH)]
        h2t = [q(f"h2t{i}", [128, QT], F32) for i in range(NCH)]
        xqt = [q(f"xqt{i}", [128, QT], BF16) for i in range(NCH)]
        mur2 = q("mur2", [1, QT], F32)
        exr2 = q("exr2", [1, QT], F32)
        rar2 = q("rar2", [1, QT], F32)
        rbr2 = q("rbr2", [1, QT], F32)
        rA2b = q("rA2b", [128, QT], F32)
        rB2b = q("rB2b", [128, QT], F32)
        x2b = q("x2b", [128, QT], BF16)
        x2s = q("x2s", [128, QT], BF16)
        mtok = q("mtok", [128, 4], F32)
        stok = q("stok", [128, 4], F32)
        gsc = q("gsc", [128, 4], F32)
        usc = q("usc", [128, 4], F32)
        srow = q("srow", [1, QT], F32)
        vv5 = q("vv5", [128, 512], F32)
        t32 = q("t32", [128, 512], I32)
        cb1 = q("cb1", [128, 512], F32)
        cb2 = q("cb2", [128, 512], F32)
        rr5 = q("rr5", [128, 512], F32)
        tf5 = q("tf5", [128, 512], F32)
        wgb = q("wgb", [128, 4096], BF16)
        wub = q("wub", [128, 4096], BF16)
        gsil = q("gsil", [128, 512], F32)
        prod = [q(f"prod{i}", [128, 4096], F32) for i in range(4)]
        pq1 = q("pq1", [128, 4096], BF16)
        pqT = [q(f"pqT{i}", [128, 512], BF16) for i in range(NIC)]
        m2c = q("m2c", [128, 4], F32)
        s2c = q("s2c", [128, 4], F32)
        gbd = q("gbd", [128, 8], F32)
        wdb = q("wdb", [128, 4096], BF16)
        t5 = q("t5", [128, 512], F32)
        t6 = q("t6", [128, 512], F32)

        p.wait(V, "v", "pS")
        for oc in range(NCH):
            p.inc(V.tensor_mul(gbd[:, oc:oc + 1], gacol[:, 8 + oc:9 + oc],
                               swcol[:, 2:3]), "pV")
            p.wait(V, "v", "pV")
            p.inc(V.tensor_scalar_mul(gbd[:, oc:oc + 1], gbd[:, oc:oc + 1],
                                      -1.0), "pV")

        def round_to(dst_ap, src_ap, n, wide=False):
            p.wait(V, "v", "pV")
            p.inc(V.tensor_copy(t32[:, 0:n], src_ap), "pV")
            p.wait(V, "v", "pV")
            if wide:
                anchor = t32
                for J in (9.0, 3.0):
                    p.inc(V.scalar_tensor_tensor(cb1[:, 0:n], anchor[:, 0:n], J / 2,
                                                 src_ap, ALU.add, ALU.is_le), "pV")
                    p.inc(V.scalar_tensor_tensor(cb2[:, 0:n], anchor[:, 0:n], -J / 2,
                                                 src_ap, ALU.add, ALU.is_ge), "pV")
                    p.wait(V, "v", "pV")
                    p.inc(V.scalar_tensor_tensor(tf5[:, 0:n], cb1[:, 0:n], J,
                                                 anchor[:, 0:n], ALU.mult, ALU.add),
                          "pV")
                    p.wait(V, "v", "pV")
                    p.inc(V.scalar_tensor_tensor(tf5[:, 0:n], cb2[:, 0:n], -J,
                                                 tf5[:, 0:n], ALU.mult, ALU.add),
                          "pV")
                    p.wait(V, "v", "pV")
                    anchor = tf5
            else:
                anchor = t32
            p.wait(V, "v", "pV")
            p.inc(V.scalar_tensor_tensor(cb1[:, 0:n], anchor[:, 0:n], 0.5, src_ap,
                                         ALU.add, ALU.is_le), "pV")
            p.inc(V.scalar_tensor_tensor(cb2[:, 0:n], anchor[:, 0:n], -0.5, src_ap,
                                         ALU.add, ALU.is_ge), "pV")
            p.wait(V, "v", "pV")
            p.inc(V.tensor_tensor(rr5[:, 0:n], anchor[:, 0:n], cb1[:, 0:n], ALU.add),
                  "pV")
            p.wait(V, "v", "pV")
            p.inc(V.tensor_sub(dst_ap, rr5[:, 0:n], cb2[:, 0:n]), "pV")

        dn_cnt = [0, 0]
        for tt in range(NQT):
            p.wait(SY, "y", "pV")
            p.wait(SY, "y", "dS")
            for ci in range(NCH):
                p.wait(SY, "y", "dS")
                p.inc(SY.dma_start(x2t[ci][:],
                                   dr["x2scr"].ap()[128 * ci:128 * (ci + 1),
                                                    QT * tt:QT * (tt + 1)]),
                      "dS", 16)
            for ci in range(NCH):
                p.wait(V, "v", "dS")
                p.wait(V, "v", "pT")
                p.inc(V.tensor_copy(x2b[:], x2t[ci][:]), "pV")
                p.inc(V.tensor_tensor(x2s[:], x2t[ci][:], x2t[ci][:], ALU.mult), "pV")
                p.wait(T, "t", "pV")
                p.wait(T, "t", "pS")
                T.matmul(pz[0:1, :], c_ocb[:], x2b[:], start=(ci == 0),
                         stop=(ci == NCH - 1))
                p.inc(T.matmul(pv0[0:1, :], c_ocb[:], x2s[:], start=(ci == 0),
                               stop=(ci == NCH - 1)), "pT")
            p.wait(SC, "s", "pT")
            SC.copy(mur2[:], pz[0:1, :])
            p.inc(SC.copy(exr2[:], pv0[0:1, :]), "pS")
            p.wait(V, "v", "pS")
            p.inc(V.tensor_scalar_mul(mur2[:], mur2[:], 1.0 / D), "pV")
            p.inc(V.tensor_scalar_mul(exr2[:], exr2[:], 1.0 / D), "pV")
            p.wait(V, "v", "pV")
            p.inc(V.tensor_tensor(rar2[:], mur2[:], mur2[:], ALU.mult), "pV")
            p.wait(V, "v", "pV")
            p.inc(V.tensor_sub(rar2[:], exr2[:], rar2[:]), "pV")
            p.wait(V, "v", "pV")
            p.inc(V.tensor_scalar_add(rar2[:], rar2[:], EPS), "pV")
            p.wait(V, "v", "pV")
            p.inc(V.reciprocal(rar2[:], rar2[:]), "pV")
            p.wait(SC, "s", "pV")
            p.inc(SC.activation(rar2[:], rar2[:], AF.Sqrt), "pS")
            p.wait(V, "v", "pS")
            p.inc(V.scalar_tensor_tensor(rbr2[:], mur2[:], -1.0, rar2[:],
                                         ALU.mult, ALU.mult), "pV")
            p.wait(T, "t", "pV")
            p.wait(T, "t", "pS")
            p.inc(T.matmul(pz[:, :], c_orf[:], rar2[:], start=True, stop=True),
                  "pT")
            p.wait(SC, "s", "pT")
            p.inc(SC.copy(rA2b[:], pz[:, :]), "pS")
            p.wait(T, "t", "pS")
            p.inc(T.matmul(pz[:, :], c_orf[:], rbr2[:], start=True, stop=True),
                  "pT")
            p.wait(SC, "s", "pT")
            p.inc(SC.copy(rB2b[:], pz[:, :]), "pS")
            for ci in range(NCH):
                p.wait(V, "v", "pS")
                p.wait(V, "v", "pV")
                p.inc(V.tensor_tensor(vv5[:], x2t[ci][:], rA2b[:], ALU.mult), "pV")
                p.wait(V, "v", "pV")
                p.inc(V.tensor_tensor(vv5[:], vv5[:], rB2b[:], ALU.add), "pV")
                p.wait(SC, "s", "pV")
                p.inc(SC.activation(h2t[ci][:], vv5[:], AF.Identity,
                                    scale=capl1[:, 8 + ci:9 + ci],
                                    bias=modc[:, 24 + ci:25 + ci]), "pS")
            p.wait(V, "v", "pS")
            p.wait(V, "v", "pV")
            p.inc(V.tensor_scalar_mul(cb1[:], h2t[0][:], -1.0), "pV")
            p.wait(V, "v", "pV")
            p.inc(V.tensor_tensor(cb2[:], cb1[:], h2t[0][:], ALU.max), "pV")
            for ci in range(1, NCH):
                p.wait(V, "v", "pV")
                p.inc(V.tensor_scalar_mul(cb1[:], h2t[ci][:], -1.0), "pV")
                p.wait(V, "v", "pV")
                p.inc(V.tensor_tensor(cb1[:], cb1[:], h2t[ci][:], ALU.max), "pV")
                p.wait(V, "v", "pV")
                p.inc(V.tensor_tensor(cb2[:], cb2[:], cb1[:], ALU.max), "pV")
            p.wait(T, "t", "pV")
            p.wait(T, "t", "pS")
            for j in range(4):
                p.inc(T.matmul(pm[:, 128 * j:128 * (j + 1)],
                               cb2[:, 128 * j:128 * (j + 1)], c_idf[:],
                               start=True, stop=True), "pT")
            p.wait(V, "v", "pT")
            for j in range(4):
                p.inc(V.reduce_max(mtok[:, j:j + 1], pm[:, 128 * j:128 * (j + 1)],
                                   axis=AX.X), "pV")
            p.wait(V, "v", "pV")
            p.inc(V.tensor_scalar_max(mtok[:], mtok[:], 1e-5), "pV")
            p.wait(V, "v", "pV")
            p.inc(V.reciprocal(stok[:], mtok[:]), "pV")
            p.wait(V, "v", "pV")
            p.inc(V.tensor_scalar_mul(stok[:], stok[:], 127.0), "pV")
            p.wait(V, "v", "pV")
            for j in range(4):
                p.inc(V.tensor_mul(gsc[:, j:j + 1], mtok[:, j:j + 1], swcol[:, 0:1]),
                      "pV")
                p.inc(V.tensor_mul(usc[:, j:j + 1], mtok[:, j:j + 1], swcol[:, 1:2]),
                      "pV")
            p.wait(V, "v", "pV")
            for j in range(4):
                p.inc(V.tensor_copy(m2c[:, j:j + 1], gsc[:, j:j + 1]), "pV")
                p.inc(V.tensor_copy(s2c[:, j:j + 1], usc[:, j:j + 1]), "pV")
            p.wait(V, "v", "pV")
            p.inc(V.tensor_scalar_mul(gsc[:], gsc[:], -C127_HI), "pV")
            p.inc(V.tensor_scalar_mul(usc[:], usc[:], -C127_HI), "pV")
            p.wait(V, "v", "pV")
            for j in range(4):
                p.inc(V.scalar_tensor_tensor(gsc[:, j:j + 1], m2c[:, j:j + 1],
                                             -C127_LO, gsc[:, j:j + 1],
                                             ALU.mult, ALU.add), "pV")
                p.inc(V.scalar_tensor_tensor(usc[:, j:j + 1], s2c[:, j:j + 1],
                                             -C127_LO, usc[:, j:j + 1],
                                             ALU.mult, ALU.add), "pV")
            p.wait(T, "t", "pV")
            p.wait(T, "t", "pS")
            for j in range(4):
                p.inc(T.matmul(pm[0:1, 128 * j:128 * (j + 1)], stok[:, j:j + 1],
                               c_idf[:], start=True, stop=True), "pT")
            p.wait(SC, "s", "pT")
            p.inc(SC.copy(srow[:], pm[0:1, 0:512]), "pS")
            p.wait(T, "t", "pS")
            p.inc(T.matmul(pz[:, :], c_orf[:], srow[:], start=True, stop=True), "pT")
            for ci in range(NCH):
                p.wait(V, "v", "pT")
                p.wait(V, "v", "pV")
                p.inc(V.tensor_tensor(vv5[:], h2t[ci][:], pz[:, :], ALU.mult), "pV")
                round_to(xqt[ci][:], vv5[:], 512, wide=True)
            for isl in range(8):
                p.wait(G, "g", "pT")
                for kc in range(NCH):
                    p.wait(G, "g", "dG")
                    p.inc(G.dma_start(wgb[:, 512 * kc:512 * (kc + 1)],
                                      dr["wgq"].ap()[128 * kc:128 * (kc + 1),
                                                     512 * isl:512 * (isl + 1)]),
                          "dG", 16)
                    p.wait(G, "g", "dG")
                    p.inc(G.dma_start(wub[:, 512 * kc:512 * (kc + 1)],
                                      dr["wuq"].ap()[128 * kc:128 * (kc + 1),
                                                     512 * isl:512 * (isl + 1)]),
                          "dG", 16)
                for tc in range(4):
                    gp_ = pb0[:, 512 * (tc % 2):512 * (tc % 2 + 1)]
                    up_ = pb1[:, 512 * (tc % 2):512 * (tc % 2 + 1)]
                    p.wait(T, "t", "dG")
                    p.wait(T, "t", "pS")
                    p.wait(T, "t", "pV")
                    for kc in range(NCH):
                        T.matmul(gp_, xqt[kc][:, 128 * tc:128 * (tc + 1)],
                                 wgb[:, 512 * kc:512 * (kc + 1)],
                                 start=(kc == 0), stop=(kc == NCH - 1))
                        ins = T.matmul(up_, xqt[kc][:, 128 * tc:128 * (tc + 1)],
                                       wub[:, 512 * kc:512 * (kc + 1)],
                                       start=(kc == 0), stop=(kc == NCH - 1))
                    p.inc(ins, "pT")
                    p.wait(SC, "s", "pT")
                    p.inc(SC.activation(gsil[:], gp_, AF.Silu,
                                        scale=gsc[:, tc:tc + 1]), "pS")
                    p.wait(V, "v", "pS")
                    p.inc(V.scalar_tensor_tensor(
                        prod[tc][:, 512 * isl:512 * (isl + 1)], gsil[:],
                        usc[:, tc:tc + 1], up_, ALU.mult, ALU.mult), "pV")
            for tc in range(4):
                p.wait(V, "v", "pV")
                p.inc(V.reduce_max(m2c[:, tc:tc + 1], prod[tc][:], axis=AX.X,
                                   apply_absolute_value=True), "pV")
                p.wait(V, "v", "pV")
                p.inc(V.tensor_scalar_max(m2c[:, tc:tc + 1], m2c[:, tc:tc + 1],
                                          1e-5), "pV")
                p.wait(V, "v", "pV")
                p.inc(V.reciprocal(s2c[:, tc:tc + 1], m2c[:, tc:tc + 1]), "pV")
                p.wait(V, "v", "pV")
                p.inc(V.tensor_scalar_mul(s2c[:, tc:tc + 1], s2c[:, tc:tc + 1],
                                          127.0), "pV")
                for sj in range(8):
                    ssl = slice(512 * sj, 512 * (sj + 1))
                    p.wait(V, "v", "pV")
                    p.wait(V, "v", "pT")
                    p.inc(V.tensor_scalar(vv5[:], prod[tc][:, ssl],
                                          s2c[:, tc:tc + 1], None, ALU.mult), "pV")
                    round_to(pq1[:, ssl], vv5[:], 512, wide=True)
                for ic4 in range(NIC // 4):
                    p.wait(T, "t", "pV")
                    p.wait(T, "t", "pS")
                    for j in range(4):
                        ic = 4 * ic4 + j
                        ins = T.matmul(pm[:, 128 * j:128 * (j + 1)],
                                       pq1[:, 128 * ic:128 * (ic + 1)], c_idb[:],
                                       start=True, stop=True)
                    p.inc(ins, "pT")
                    p.wait(SC, "s", "pT")
                    for j in range(4):
                        ic = 4 * ic4 + j
                        ins = SC.copy(pqT[ic][:, 128 * tc:128 * (tc + 1)],
                                      pm[:, 128 * j:128 * (j + 1)])
                    p.inc(ins, "pS")
            p.wait(V, "v", "pV")
            p.inc(V.reciprocal(s2c[:], s2c[:]), "pV")
            p.wait(T, "t", "pV")
            p.wait(T, "t", "pS")
            for j in range(4):
                p.inc(T.matmul(pm[0:1, 128 * j:128 * (j + 1)], s2c[:, j:j + 1],
                               c_idf[:], start=True, stop=True), "pT")
            p.wait(SC, "s", "pT")
            p.inc(SC.copy(srow[:], pm[0:1, 0:512]), "pS")
            p.wait(T, "t", "pS")
            p.inc(T.matmul(pv1[:, :], c_orf[:], srow[:], start=True, stop=True),
                  "pT")
            dn_cnt[1] = max(dn_cnt[1], p.c["pT"])
            for oc in range(NCH):
                wb = wdb if oc % 2 == 0 else pq1
                p.wait(SY, "y", "pT", dn_cnt[oc % 2])
                src_dma = bass.AP(dr["wdq"], 128 * oc,
                                  [[D, 128], [D * 128, NIC], [1, 128]])
                dst_dma = bass.AP(wb, 0, [[4096, 128], [128, NIC], [1, 128]])
                p.wait(SY, "y", "dS")
                p.inc(SY.dma_start(dst_dma, src_dma), "dS", 16)
                p.wait(T, "t", "dS")
                p.wait(T, "t", "pS")
                p.wait(T, "t", "pV")
                for ic in range(NIC):
                    ins = T.matmul(pz[:, :], wb[:, 128 * ic:128 * (ic + 1)],
                                   pqT[ic][:], start=(ic == 0), stop=(ic == NIC - 1))
                p.inc(ins, "pT")
                dn_cnt[oc % 2] = p.c["pT"]
                p.wait(SC, "s", "pT")
                p.inc(SC.activation(t5[:], pz[:, :], AF.Copy,
                                    scale=gbd[:, oc:oc + 1]), "pS")
                p.wait(V, "v", "pS")
                p.wait(V, "v", "pV")
                p.wait(V, "v", "dG")
                p.inc(V.tensor_tensor(t6[:], t5[:], pv1[:, :], ALU.mult), "pV")
                p.wait(V, "v", "pV")
                p.inc(V.tensor_tensor(t6[:], x2t[oc][:], t6[:], ALU.add), "pV")
                p.wait(G, "g", "pV")
                p.wait(G, "g", "dG")
                p.inc(G.dma_start(dr["out"].ap()[128 * oc:128 * (oc + 1),
                                                 QT * tt:QT * (tt + 1)], t6[:]),
                      "dG", 16)
    G.wait_ge(p.h["dG"], p.c["dG"])
    p.barrier(ENGS)




_CONSTS = None


def kernel(L0_attn_x, L0_attn_Wq, L0_attn_Wk, L0_attn_Wv, L0_attn_Wo,
           L0_attn_lambda_q1, L0_attn_lambda_k1, L0_attn_lambda_q2,
           L0_attn_lambda_k2, L0_attn_subln_g, L0_ffn_w_gate, L0_ffn_w_up,
           L0_ffn_w_down, L0_c, L0_wmod):
    global _CONSTS
    if _CONSTS is None:
        _CONSTS = _host_constants()
    f32 = lambda a: np.ascontiguousarray(np.asarray(a), dtype=np.float32)
    xs = np.ascontiguousarray(np.transpose(f32(L0_attn_x), (0, 2, 1)))
    shared = {
        "wq": f32(L0_attn_Wq), "wk": f32(L0_attn_Wk), "wv": f32(L0_attn_Wv),
        "wo": f32(L0_attn_Wo), "wmod": f32(L0_wmod),
        "wg": np.ascontiguousarray(f32(L0_ffn_w_gate).T),
        "wu": np.ascontiguousarray(f32(L0_ffn_w_up).T),
        "wd": np.ascontiguousarray(f32(L0_ffn_w_down).T),
        "lq1": f32(L0_attn_lambda_q1).reshape(1, HD),
        "lk1": f32(L0_attn_lambda_k1).reshape(1, HD),
        "lq2": f32(L0_attn_lambda_q2).reshape(1, HD),
        "lk2": f32(L0_attn_lambda_k2).reshape(1, HD),
        "subg": np.ascontiguousarray(np.tile(f32(L0_attn_subln_g), 2).reshape(128, 1)),
    }
    shared.update(_CONSTS)
    cvecs = f32(L0_c)
    nc = build_nc()
    in_maps = []
    for b in range(8):
        m = dict(shared)
        m["x"] = xs[b]
        m["cvec"] = np.ascontiguousarray(cvecs[b].reshape(D, 1))
        in_maps.append(m)
    res = run_bass_kernel_spmd(nc, in_maps, core_ids=list(range(8)))
    out = np.stack([np.ascontiguousarray(np.asarray(r["out"]).T)
                    for r in res.results])
    return np.ascontiguousarray(out)
